# version 01
# baseline (speedup 1.0000x reference)
import numpy as np
import ml_dtypes
from contextlib import ExitStack
from collections import defaultdict

import concourse.bass as bass
import concourse.mybir as mybir
from concourse.bass_utils import run_bass_kernel_spmd

F32 = mybir.dt.float32
BF16 = mybir.dt.bfloat16
AF = mybir.ActivationFunctionType
ALU = mybir.AluOpType
AX = mybir.AxisListType

BF = ml_dtypes.bfloat16
N_CORES = 8
NTOK = 2048
D = 1024
NT = NTOK // 128
ND = D // 128


class EV:

    def __init__(self, nc, ctx, sem_names):
        self.nc = nc
        self.events = defaultdict(list)
        self.cnt = defaultdict(int)
        self.sems = {n: ctx.enter_context(nc.semaphore(n)) for n in sem_names}

    def mark(self, sem):
        return (sem, self.cnt[sem])

    ENG_SEM = {"vector": "vx", "scalar": "sc"}

    def op(self, engine, fn, waits=(), inc=None, inc_n=1):
        w = list(waits)
        sname = self.ENG_SEM.get(engine)
        if sname is not None and sname in self.sems and self.cnt[sname] > 0:
            w.append((sname, self.cnt[sname]))
        rec = None
        if inc is not None:
            self.cnt[inc] += inc_n
            rec = (inc, inc_n)
        self.events[engine].append((tuple(w), fn, rec))
        return (inc, self.cnt[inc]) if inc is not None else None

    def dma(self, engine, out, in_, waits=(), inc=None):
        w = list(waits)
        if inc is not None and self.cnt[inc] > 0:
            w.append((inc, self.cnt[inc]))
        return self.op(engine, lambda e: e.dma_start(out=out, in_=in_),
                       waits=w, inc=inc, inc_n=16)

    def wait_all(self, *sem_names):
        return [(s, self.cnt[s]) for s in sem_names]

    def finalize(self, block):
        for engine in ("sync", "gpsimd", "tensor", "scalar", "vector"):
            evs = self.events[engine]
            sems = self.sems

            def make(evs):
                def emit(eng):
                    for waits, fn, inc in evs:
                        for wt in waits:
                            assert isinstance(wt, tuple) and len(wt) == 2, f"bad wait {wt}"
                            s, v = wt
                            if v > 0:
                                eng.wait_ge(sems[s], v)
                        instr = fn(eng)
                        if inc is not None:
                            instr.then_inc(sems[inc[0]], inc[1])
                return emit
            getattr(block, engine)(make(evs))


def sem_val(ev, sem):
    return (sem, ev.cnt[sem])


def make_consts():
    c = {}
    c["ident"] = np.eye(128, dtype=np.float32)
    c["ones_row"] = np.ones((1, 128), dtype=np.float32)
    ones2 = np.zeros((128, 2), dtype=np.float32)
    ones2[:64, 0] = 1.0
    ones2[64:, 1] = 1.0
    c["ones2"] = ones2
    c["ones2T"] = ones2.T.copy()
    DH, W = 64, 512
    half = np.arange(0, DH, 2, dtype=np.float64)
    inv_freq = 1.0 / 10000.0 ** (half / DH)
    t = np.arange(NTOK, dtype=np.float64)
    freqs = t[:, None] * inv_freq[None, :]
    freqs = np.concatenate([freqs, freqs], axis=-1)
    base = (half + 0.4 * DH) / (1.4 * DH)
    power = (t - NTOK // 2) / float(W // 2)
    scale = base[None, :] ** power[:, None]
    scale = np.concatenate([scale, scale], axis=-1)
    cos, sin = np.cos(freqs), np.sin(freqs)
    for name, tab in (("cs_q", cos * scale), ("ss_q", sin * scale),
                      ("cs_k", cos / scale), ("ss_k", sin / scale)):
        tt = tab.T.astype(np.float32)
        c[name] = np.concatenate([tt, tt], axis=0)
    j = np.arange(1024)[:, None]
    i = np.arange(512)[None, :]
    allowed = (i <= j) & (j <= i + W)
    c["maskM"] = np.where(allowed, 100.0, -1e30).astype(np.float32)
    return c


CONSTS = make_consts()


def bcast_cols(ev, psum_bc, ones_row, src_row, dst, n, waits=()):
    for c0 in range(0, n, 512):
        w = min(512, n - c0)
        m = ev.op("tensor", lambda e, c0=c0, w=w: e.matmul(
            psum_bc[:, 0:w], ones_row[0:1, :], src_row[0:1, c0:c0 + w],
            start=True, stop=True),
            waits=list(waits) + ev.wait_all("vx"), inc="mm")
        ev.op("vector", lambda e, c0=c0, w=w: e.tensor_copy(
            dst[:, c0:c0 + w], psum_bc[:, 0:w]), waits=[m], inc="vx")


def build_ln0():
    nc = bass.Bass("TRN2", target_bir_lowering=False)
    x = nc.dram_tensor("x", [NTOK, D], F32, kind="ExternalInput").ap()
    ident = nc.dram_tensor("ident", [128, 128], F32, kind="ExternalInput").ap()
    xnT_f = nc.dram_tensor("xnT_f", [D, NTOK], F32, kind="ExternalOutput").ap()
    xnT_b = nc.dram_tensor("xnT_b", [D, NTOK], BF16, kind="ExternalOutput").ap()
    with ExitStack() as ctx:
        sb = lambda *a: ctx.enter_context(nc.sbuf_tensor(*a))
        psa = lambda *a: ctx.enter_context(nc.psum_tensor(*a))
        xt = [sb(f"xt{i}", [128, D], F32) for i in range(2)]
        xn = [sb(f"xn{i}", [128, D], F32) for i in range(2)]
        sq = sb("sq", [128, D], F32)
        s1 = sb("s1", [128, 1], F32)
        s2 = sb("s2", [128, 1], F32)
        mu = sb("mu", [128, 1], F32)
        mu2 = sb("mu2", [128, 1], F32)
        var = sb("var", [128, 1], F32)
        rt = sb("rt", [128, 1], F32)
        rr = sb("rr", [128, 1], F32)
        idf = sb("idf", [128, 128], F32)
        epsv = sb("epsv", [128, 1], F32)
        chunks = [sb(f"ch{c}", [128, NTOK], F32) for c in range(ND)]
        tp = [psa(f"tp{i}", [128, 128], F32) for i in range(2)]
        ev = EV(nc, ctx, ["ld", "mm", "sc", "vx", "st", "stg"])
        ev.dma("sync", idf[:], ident, inc="ld")
        ep = ev.op("vector", lambda e: e.memset(epsv[:], 1e-5), inc="vx")
        last_reads = [None, None]
        for i in range(NT):
            b = i % 2
            w = [] if last_reads[b] is None else [last_reads[b]]
            ld = ev.dma("sync", xt[b][:], x[i * 128:(i + 1) * 128, :],
                        waits=w, inc="ld")
            sc1 = ev.op("scalar", lambda e, b=b: e.square(sq[:], xt[b][:]),
                        waits=[ld] + ev.wait_all("vx"), inc="sc")
            v1 = ev.op("vector", lambda e, b=b: e.reduce_sum(s1[:], xt[b][:], AX.X),
                       waits=[ld], inc="vx")
            v2 = ev.op("vector", lambda e: e.reduce_sum(s2[:], sq[:], AX.X),
                       waits=[sc1], inc="vx")
            ev.op("vector", lambda e: e.tensor_scalar_mul(mu[:], s1[:], 1.0 / D),
                  waits=[v1], inc="vx")
            sc2 = ev.op("scalar", lambda e: e.square(mu2[:], mu[:]),
                        waits=ev.wait_all("vx"), inc="sc")
            v3 = ev.op("vector", lambda e: e.scalar_tensor_tensor(
                var[:], s2[:], 1.0 / D, mu2[:], ALU.mult, ALU.subtract),
                waits=[sc2, v2], inc="vx")
            sc3 = ev.op("scalar", lambda e: e.activation(
                rt[:], var[:], AF.Sqrt, bias=epsv[:]), waits=[v3, ep], inc="sc")
            v4 = ev.op("vector", lambda e: e.reciprocal(rr[:], rt[:]),
                       waits=[sc3], inc="vx")
            vn = ev.op("vector", lambda e, b=b: e.tensor_scalar(
                xn[b][:], xt[b][:], mu[:], rr[:], ALU.subtract, ALU.mult),
                waits=[v4], inc="vx")
            last_reads[b] = vn
            for c in range(ND):
                pb = c % 2
                m = ev.op("tensor", lambda e, b=b, c=c, pb=pb: e.transpose(
                    tp[pb][:], xn[b][:, c * 128:(c + 1) * 128], idf[:]),
                    waits=[vn] + ev.wait_all("vx"), inc="mm")
                ev.op("vector", lambda e, i=i, c=c, pb=pb: e.tensor_copy(
                    chunks[c][:, i * 128:(i + 1) * 128], tp[pb][:]),
                    waits=[m], inc="vx")
        w = ev.wait_all("vx")
        for c in range(ND):
            ev.dma("sync", xnT_f[c * 128:(c + 1) * 128, :], chunks[c][:],
                   waits=w, inc="st")
            ev.dma("gpsimd", xnT_b[c * 128:(c + 1) * 128, :], chunks[c][:],
                   waits=w, inc="stg")
        fin = ev.wait_all("st", "stg")
        ev.op("sync", lambda e: e.nop(), waits=fin)
        ev.op("gpsimd", lambda e: e.nop(), waits=fin)
        block = ctx.enter_context(nc.Block())
        ev.finalize(block)
    return nc


def build_l0attn():
    nc = bass.Bass("TRN2", target_bir_lowering=False)
    xnT_b = nc.dram_tensor("xnT_b", [D, NTOK], BF16, kind="ExternalInput").ap()
    w_qkv = nc.dram_tensor("w_qkv", [D, 3 * D], F32, kind="ExternalInput").ap()
    w_out = nc.dram_tensor("w_out", [D, D], F32, kind="ExternalInput").ap()
    b_out = nc.dram_tensor("b_out", [D], F32, kind="ExternalInput").ap()
    gamma = nc.dram_tensor("gamma", [D], F32, kind="ExternalInput").ap()
    beta = nc.dram_tensor("beta", [D], F32, kind="ExternalInput").ap()
    onesr = nc.dram_tensor("ones_row", [1, 128], F32, kind="ExternalInput").ap()
    s_out = nc.dram_tensor("s_out", [NTOK, D], F32, kind="ExternalOutput").ap()

    HG = 8
    with ExitStack() as ctx:
        sb = lambda *a: ctx.enter_context(nc.sbuf_tensor(*a))
        psa = lambda *a: ctx.enter_context(nc.psum_tensor(*a))
        xn = [sb(f"xn{c}", [128, NTOK], BF16) for c in range(ND)]
        oall = [sb(f"oall{c}", [128, NTOK], BF16) for c in range(ND)]
        wq = [sb(f"wq{i}", [128, 1024], BF16) for i in range(2)]
        wk = [sb(f"wk{i}", [128, 1024], BF16) for i in range(2)]
        wv = [sb(f"wv{i}", [128, 1024], BF16) for i in range(2)]
        qT = sb("qT", [128, NTOK], BF16)
        kT = sb("kT", [128, NTOK], BF16)
        vsb = sb("vsb", [128, 16 * 132], BF16)
        at = sb("at", [128, 16 * 512], BF16)
        wo = sb("wo", [128, 8 * 1024], BF16)
        brow = sb("brow", [1, 3 * D], F32)
        bbc = sb("bbc", [128, D], F32)
        gbc = sb("gbc", [128, D], F32)
        bebc = sb("bebc", [128, D], F32)
        onesb = sb("onesb", [1, 128], F32)
        rec = sb("rec", [1, 512], F32)
        nbc = sb("nbc", [32, 512], F32)
        stile = sb("stile", [128, D], F32)
        sq = sb("sq", [128, D], F32)
        st1 = sb("st1", [128, 1], F32)
        st2 = sb("st2", [128, 1], F32)
        mu = sb("mu", [128, 1], F32)
        mu2 = sb("mu2", [128, 1], F32)
        var = sb("var", [128, 1], F32)
        rt = sb("rt", [128, 1], F32)
        rr = sb("rr", [128, 1], F32)
        otile = sb("otile", [128, D], F32)

        ps_s = [psa(f"ps_s{i}", [128, 512], F32) for i in range(2)]
        ps_qk = psa("ps_qk", [128, 512], F32)
        ps_o = psa("ps_o", [33, 512], F32)
        ps_b = psa("ps_b", [32, 512], F32)
        ps_bc = psa("ps_bc", [128, 512], F32)
        ps_w = [psa(f"ps_w{i}", [128, 512], F32) for i in range(2)]
        m8v = sb("m8v", [128, 1], F32)
        epsv = sb("epsv", [128, 1], F32)

        ev = EV(nc, ctx, ["ld", "wd", "mm", "sc", "vx", "st"])
        ep0 = ev.op("vector", lambda e: e.memset(m8v[:], -8.0), inc="vx")
        ep1 = ev.op("vector", lambda e: e.memset(epsv[:], 1e-6), inc="vx")
        lds = []
        for c in range(ND):
            lds.append(ev.dma("sync", xn[c][:], xnT_b[c * 128:(c + 1) * 128, :],
                              inc="ld"))
        ev.dma("sync", onesb[:], onesr, inc="ld")
        ev.dma("sync", brow[0:1, 0:D], b_out.rearrange("(o d) -> o d", o=1), inc="ld")
        ev.dma("sync", brow[0:1, D:2 * D], gamma.rearrange("(o d) -> o d", o=1), inc="ld")
        ev.dma("sync", brow[0:1, 2 * D:3 * D], beta.rearrange("(o d) -> o d", o=1), inc="ld")
        ev.dma("gpsimd", wo[:].rearrange("p (k n) -> p k n", n=1024),
               w_out.rearrange("(k p) n -> p k n", p=128), inc="wd")
        wrow = ev.wait_all("ld")
        bcast_cols(ev, ps_bc, onesb, brow[0:1, 0:D], bbc, D, waits=wrow)
        bcast_cols(ev, ps_bc, onesb, brow[0:1, D:2 * D], gbc, D, waits=wrow)
        bcast_cols(ev, ps_bc, onesb, brow[0:1, 2 * D:3 * D], bebc, D, waits=wrow)
        vmemset = []
        for t in range(16):
            for hh in range(4):
                vmemset.append(ev.op("vector", lambda e, t=t, hh=hh: e.memset(
                    vsb[:, t * 132 + hh * 33 + 32: t * 132 + hh * 33 + 33], 1.0),
                    inc="vx"))
        def load_group(g, buf, waits=()):
            o = []
            for (dst, c0) in ((wq, g * 128), (wk, D + g * 128), (wv, 2 * D + g * 128)):
                o.append(ev.dma(
                    "gpsimd", dst[buf][:].rearrange("p (k n) -> p k n", n=128),
                    w_qkv[:, c0:c0 + 128].rearrange("(k p) n -> p k n", p=128),
                    waits=waits, inc="wd"))
            return o

        gw = [None, None]
        gw[0] = load_group(0, 0)
        wguard = [None, None]
        qf_guard = [None, None]
        qk_guard = None
        v_guard = None
        for g in range(HG):
            buf = g % 2
            if g + 1 < HG:
                nb = (g + 1) % 2
                gw[nb] = load_group(g + 1, nb,
                                    [wguard[nb]] if wguard[nb] else [])
            wds = gw[buf]
            qk_done = []
            for (dst, wsb) in ((qT, wq), (kT, wk)):
                for tc in range(4):
                    acc = []
                    for c in range(ND):
                        acc.append(ev.op("tensor", lambda e, c=c, tc=tc, wsb=wsb, buf=buf: e.matmul(
                            ps_qk[:], wsb[buf][:, c * 128:(c + 1) * 128],
                            xn[c][:, tc * 512:(tc + 1) * 512],
                            start=(c == 0), stop=(c == ND - 1)),
                            waits=(wds + lds + ([qk_guard] if qk_guard else [])) if c == 0 else [],
                            inc="mm"))
                    qk_guard = ev.op("vector", lambda e, dst=dst, tc=tc: e.tensor_copy(
                        dst[:, tc * 512:(tc + 1) * 512], ps_qk[:]),
                        waits=[acc[-1]], inc="vx")
                    qk_done.append(qk_guard)
            v_done = []
            for t in range(16):
                mv = []
                for c in range(ND):
                    mv.append(ev.op("tensor", lambda e, c=c, t=t, buf=buf: e.matmul(
                        ps_w[t % 2][:, 0:128], xn[c][:, t * 128:(t + 1) * 128],
                        wv[buf][:, c * 128:(c + 1) * 128],
                        start=(c == 0), stop=(c == ND - 1)),
                        waits=([v_guard] if (c == 0 and v_guard) else []), inc="mm"))
                for hh in range(4):
                    v_guard = ev.op("vector", lambda e, t=t, hh=hh: e.tensor_copy(
                        vsb[:, t * 132 + hh * 33: t * 132 + hh * 33 + 32],
                        ps_w[t % 2][:, hh * 32:(hh + 1) * 32]),
                        waits=[mv[-1]] + ([vmemset[-1]] if g == 0 else []), inc="vx")
                    v_done.append(v_guard)
            wguard[buf] = ev.mark("mm")
            for hh in range(4):
                hbase = hh * 32
                tp = (hbase, 0) if hbase == 96 else None
                for qc in range(4):
                    q0 = qc * 512
                    exp_done = []
                    for kt in range(16):
                        pb = kt % 2
                        warw = ev.wait_all("sc") if kt < 2 else [exp_done[kt - 2]]
                        m = ev.op("tensor", lambda e, kt=kt, pb=pb, hbase=hbase, q0=q0, tp=tp: e.matmul(
                            ps_s[pb][:], kT[hbase:hbase + 32, kt * 128:(kt + 1) * 128],
                            qT[hbase:hbase + 32, q0:q0 + 512],
                            start=True, stop=True, tile_position=tp),
                            waits=(qk_done if kt == 0 and qc == 0 else []) + warw,
                            inc="mm")
                        e1 = ev.op("scalar", lambda e, kt=kt, pb=pb: e.activation(
                            at[:, kt * 512:(kt + 1) * 512], ps_s[pb][:], AF.Exp,
                            bias=m8v[:], scale=float(32 ** -0.5)),
                            waits=[m, ep0], inc="sc")
                        exp_done.append(e1)
                    mavl = None
                    for kt in range(16):
                        mavl = ev.op("tensor", lambda e, kt=kt, hh=hh: e.matmul(
                            ps_o[:], vsb[:, kt * 132 + hh * 33: kt * 132 + hh * 33 + 33],
                            at[:, kt * 512:(kt + 1) * 512],
                            start=(kt == 0), stop=(kt == 15)),
                            waits=[exp_done[kt]] + ((v_done + ev.wait_all("vx")) if kt == 0 else []),
                            inc="mm")
                    r1 = ev.op("vector", lambda e: e.reciprocal(rec[:], ps_o[32:33, :]),
                               waits=[mavl], inc="vx")
                    mb = ev.op("tensor", lambda e: e.matmul(
                        ps_b[:], onesb[0:1, 0:32], rec[0:1, :], start=True, stop=True),
                        waits=[r1], inc="mm")
                    c1 = ev.op("vector", lambda e: e.tensor_copy(nbc[:], ps_b[:]),
                               waits=[mb], inc="vx")
                    ev.op("vector", lambda e, g=g, hbase=hbase, q0=q0: e.tensor_tensor(
                        oall[g][hbase:hbase + 32, q0:q0 + 512], ps_o[0:32, :], nbc[:],
                        ALU.mult), waits=[c1], inc="vx")
        oall_done = ev.wait_all("vx")
        wo_ld = ev.wait_all("wd")
        prev_store = None
        for t in range(NT):
            sready = []
            for n in range(2):
                acc = None
                for c in range(ND):
                    acc = ev.op("tensor", lambda e, c=c, t=t, n=n: e.matmul(
                        ps_w[n][:], oall[c][:, t * 128:(t + 1) * 128],
                        wo[:, c * 1024 + n * 512: c * 1024 + (n + 1) * 512],
                        start=(c == 0), stop=(c == ND - 1)),
                        waits=(oall_done + wo_ld) if (t == 0 and c == 0) else
                              (ev.wait_all("vx") if c == 0 else []),
                        inc="mm")
                sready.append(ev.op("vector", lambda e, n=n: e.tensor_tensor(
                    stile[:, n * 512:(n + 1) * 512], ps_w[n][:],
                    bbc[:, n * 512:(n + 1) * 512], ALU.add),
                    waits=[acc], inc="vx"))
            sc1 = ev.op("scalar", lambda e: e.square(sq[:], stile[:]),
                        waits=sready, inc="sc")
            v1 = ev.op("vector", lambda e: e.reduce_sum(st1[:], stile[:], AX.X),
                       waits=sready, inc="vx")
            v2 = ev.op("vector", lambda e: e.reduce_sum(st2[:], sq[:], AX.X),
                       waits=[sc1], inc="vx")
            ev.op("vector", lambda e: e.tensor_scalar_mul(mu[:], st1[:], 1.0 / D),
                  waits=[v1], inc="vx")
            sc2 = ev.op("scalar", lambda e: e.square(mu2[:], mu[:]),
                        waits=ev.wait_all("vx"), inc="sc")
            v3 = ev.op("vector", lambda e: e.scalar_tensor_tensor(
                var[:], st2[:], 1.0 / D, mu2[:], ALU.mult, ALU.subtract),
                waits=[sc2, v2], inc="vx")
            sc3 = ev.op("scalar", lambda e: e.activation(
                rt[:], var[:], AF.Sqrt, bias=epsv[:]), waits=[v3, ep1], inc="sc")
            v4 = ev.op("vector", lambda e: e.reciprocal(rr[:], rt[:]),
                       waits=[sc3], inc="vx")
            v5 = ev.op("vector", lambda e: e.tensor_scalar(
                sq[:], stile[:], mu[:], rr[:], ALU.subtract, ALU.mult),
                waits=[v4], inc="vx")
            v6 = ev.op("vector", lambda e: e.tensor_tensor(
                sq[:], sq[:], gbc[:], ALU.mult), waits=[v5], inc="vx")
            v7 = ev.op("vector", lambda e: e.tensor_tensor(
                otile[:], sq[:], bebc[:], ALU.add),
                waits=[v6] + ([prev_store] if prev_store else []), inc="vx")
            ev.dma("sync", s_out[t * 128:(t + 1) * 128, :], otile[:],
                   waits=[v7], inc="st")
            prev_store = ev.mark("st")
        fin = ev.wait_all("st")
        ev.op("sync", lambda e: e.nop(), waits=fin)
        block = ctx.enter_context(nc.Block())
        ev.finalize(block)
    return nc


def build_ffn_attn():
    nc = bass.Bass("TRN2", target_bir_lowering=False)
    xfT = nc.dram_tensor("xfT", [D, NTOK], F32, kind="ExternalInput").ap()
    wqkv = nc.dram_tensor("wqkv", [D, 1536], F32, kind="ExternalInput").ap()
    qs_i = nc.dram_tensor("qs", [64], F32, kind="ExternalInput").ap()
    ks_i = nc.dram_tensor("ks", [64], F32, kind="ExternalInput").ap()
    wo_i = nc.dram_tensor("wo_w", [512, D], F32, kind="ExternalInput").ap()
    onesr = nc.dram_tensor("ones_row", [1, 128], F32, kind="ExternalInput").ap()
    ones2 = nc.dram_tensor("ones2", [128, 2], F32, kind="ExternalInput").ap()
    ones2T = nc.dram_tensor("ones2T", [2, 128], F32, kind="ExternalInput").ap()
    maskM = nc.dram_tensor("maskM", [1024, 512], F32, kind="ExternalInput").ap()
    ropes = {n: nc.dram_tensor(n, [128, NTOK], F32, kind="ExternalInput").ap()
             for n in ("cs_q", "ss_q", "cs_k", "ss_k")}
    xfT_o = nc.dram_tensor("xfT_o", [D, NTOK], F32, kind="ExternalOutput").ap()

    with ExitStack() as ctx:
        sb = lambda *a: ctx.enter_context(nc.sbuf_tensor(*a))
        psa = lambda *a: ctx.enter_context(nc.psum_tensor(*a))
        xf = [sb(f"xf{c}", [128, NTOK], F32) for c in range(ND)]
        hb = [sb(f"hb{c}", [128, NTOK], BF16) for c in range(ND)]
        tabs = {n: sb(n + "_s", [128, NTOK], BF16) for n in ropes}
        mask = sb("mask", [128, 8 * 512], BF16)
        wq = [sb(f"wq{i}", [128, 1024], BF16) for i in range(2)]
        wk = [sb(f"wk{i}", [128, 1024], BF16) for i in range(2)]
        wv = [sb(f"wv{i}", [128, 1024], BF16) for i in range(2)]
        wo = sb("wo", [128, 4 * 1024], BF16)
        qf = [sb(f"qf{i}", [128, 512], F32) for i in range(2)]
        qTb = sb("qTb", [128, NTOK], BF16)
        kTb = sb("kTb", [128, NTOK], BF16)
        vsb = sb("vsb", [128, 16 * 130], BF16)
        at = sb("at", [128, 8 * 512], BF16)
        rot = sb("rot", [128, 512], F32)
        tmp = sb("tmp", [128, 512], F32)
        tmp2 = sb("tmp2", [128, 512], F32)
        tmpm = [sb(f"tmpm{i}", [128, 512], F32) for i in range(2)]
        sqv = sb("sqv", [128, 512], F32)
        ssum = sb("ssum", [2, 512], F32)
        srt = sb("srt", [2, 512], F32)
        qsv = sb("qsv", [128, 1], F32)
        ksv = sb("ksv", [128, 1], F32)
        onesb = sb("onesb", [1, 128], F32)
        o2 = sb("o2", [128, 2], F32)
        o2T = sb("o2T", [2, 128], F32)
        rec = sb("rec", [1, 512], F32)
        nbc = sb("nbc", [64, 512], F32)
        oall = [sb(f"oall{c}", [128, NTOK], BF16) for c in range(4)]

        ps_s = [psa(f"ps_s{i}", [128, 512], F32) for i in range(2)]
        ps_o = psa("ps_o", [65, 512], F32)
        ps_b = psa("ps_b", [128, 512], F32)
        ps_st = psa("ps_st", [2, 512], F32)
        ps_w = [psa(f"ps_w{i}", [128, 512], F32) for i in range(2)]
        m30v = sb("m30v", [128, 1], F32)
        eps24 = sb("eps24", [128, 1], F32)

        ev = EV(nc, ctx, ["ld", "wd", "mm", "sc", "vx", "st"])
        ep0 = ev.op("vector", lambda e: e.memset(m30v[:], -30.0), inc="vx")
        ep1 = ev.op("vector", lambda e: e.memset(eps24[:], 1e-24), inc="vx")
        lds = []
        for c in range(ND):
            lds.append(ev.dma("sync", xf[c][:], xfT[c * 128:(c + 1) * 128, :], inc="ld"))
        tab_lds = []
        for n in ropes:
            tab_lds.append(ev.dma("gpsimd", tabs[n][:], ropes[n], inc="wd"))
        for kt in range(8):
            tab_lds.append(ev.dma("gpsimd", mask[:, kt * 512:(kt + 1) * 512],
                                  maskM[kt * 128:(kt + 1) * 128, :], inc="wd"))
        ev.dma("sync", onesb[:], onesr, inc="ld")
        ev.dma("sync", o2[:], ones2, inc="ld")
        ev.dma("sync", o2T[:], ones2T, inc="ld")
        ev.dma("sync", qsv[0:64, 0:1], qs_i.rearrange("(d o) -> d o", o=1), inc="ld")
        ev.dma("sync", qsv[64:128, 0:1], qs_i.rearrange("(d o) -> d o", o=1), inc="ld")
        ev.dma("sync", ksv[0:64, 0:1], ks_i.rearrange("(d o) -> d o", o=1), inc="ld")
        ev.dma("sync", ksv[64:128, 0:1], ks_i.rearrange("(d o) -> d o", o=1), inc="ld")
        ev.dma("gpsimd", wo[:].rearrange("p (k n) -> p k n", n=1024),
               wo_i.rearrange("(k p) n -> p k n", p=128), inc="wd")
        lds_all = ev.wait_all("ld")
        hdone = []
        for c in range(ND):
            if c < 4:
                hdone.append(ev.op("vector", lambda e, c=c: e.tensor_copy(
                    hb[c][:], xf[c][:]), waits=[lds[c]], inc="vx"))
            else:
                ev.op("vector", lambda e, c=c: e.memset(hb[c][:, 0:1], 0.0), inc="vx")
                hdone.append(ev.op("vector", lambda e, c=c: e.tensor_copy(
                    hb[c][:, 1:NTOK], xf[c][:, 0:NTOK - 1]), waits=[lds[c]], inc="vx"))

        def load_group(g, buf, waits=()):
            o = []
            for (dst, c0) in ((wq, g * 128), (wk, 512 + g * 128), (wv, 1024 + g * 128)):
                o.append(ev.dma(
                    "gpsimd", dst[buf][:].rearrange("p (k n) -> p k n", n=128),
                    wqkv[:, c0:c0 + 128].rearrange("(k p) n -> p k n", p=128),
                    waits=waits, inc="wd"))
            return o

        gw = [None, None]
        gw[0] = load_group(0, 0)
        wguard = [None, None]
        qf_guard = [None, None]
        vset = []
        for t in range(16):
            for hh in range(2):
                vset.append(ev.op("vector", lambda e, t=t, hh=hh: e.memset(
                    vsb[:, t * 130 + hh * 65 + 64: t * 130 + hh * 65 + 65], 1.0),
                    inc="vx"))
        wo_acc_done = None
        for g in range(4):
            buf = g % 2
            if g + 1 < 4:
                nb = (g + 1) % 2
                gw[nb] = load_group(g + 1, nb,
                                    [wguard[nb]] if wguard[nb] else [])
            wds = gw[buf]
            qk_done = []
            for (dstb, wsb, scl, csn, ssn) in (
                    (qTb, wq, qsv, "cs_q", "ss_q"), (kTb, wk, ksv, "cs_k", "ss_k")):
                raw_done = []
                for tc in range(4):
                    t0 = tc * 512
                    acc = None
                    for c in range(ND):
                        acc = ev.op("tensor", lambda e, c=c, tc=tc, wsb=wsb, buf=buf: e.matmul(
                            ps_w[tc % 2][:], wsb[buf][:, c * 128:(c + 1) * 128],
                            hb[c][:, tc * 512:(tc + 1) * 512],
                            start=(c == 0), stop=(c == ND - 1)),
                            waits=(wds + hdone + ev.wait_all("vx")) if c == 0 else [],
                            inc="mm")
                    raw_done.append(ev.op("vector", lambda e, tc=tc: e.tensor_copy(
                        qf[tc % 2][:], ps_w[tc % 2][:]),
                        waits=[acc] + ([qf_guard[tc % 2]] if qf_guard[tc % 2] else []),
                        inc="vx"))
                    s1 = ev.op("scalar", lambda e, tc=tc: e.square(
                        sqv[:], qf[tc % 2][:]), waits=[raw_done[tc]], inc="sc")
                    qf_guard[tc % 2] = s1
                    m1 = ev.op("tensor", lambda e: e.matmul(
                        ps_st[:], o2[:], sqv[:], start=True, stop=True),
                        waits=[s1] + lds_all, inc="mm")
                    v1 = ev.op("vector", lambda e: e.tensor_copy(ssum[:], ps_st[:]),
                               waits=[m1], inc="vx")
                    s2 = ev.op("scalar", lambda e: e.activation(
                        srt[:], ssum[:], AF.Sqrt, bias=eps24[0:2, 0:1]),
                        waits=[v1, ep1], inc="sc")
                    v2 = ev.op("vector", lambda e: e.reciprocal(srt[:], srt[:]),
                               waits=[s2], inc="vx")
                    m2 = ev.op("tensor", lambda e: e.matmul(
                        ps_b[:], o2T[:], srt[:], start=True, stop=True),
                        waits=[v2], inc="mm")
                    v3 = ev.op("vector", lambda e, tc=tc, scl=scl: e.scalar_tensor_tensor(
                        tmp[:], qf[tc % 2][:], scl[:], ps_b[:],
                        ALU.mult, ALU.mult), waits=[m2], inc="vx")
                    r1 = ev.op("vector", lambda e: e.tensor_scalar_mul(
                        rot[0:32, :], tmp[32:64, :], -1.0), waits=[v3], inc="vx")
                    r2 = ev.op("vector", lambda e: e.tensor_copy(
                        rot[32:64, :], tmp[0:32, :]), waits=[v3], inc="vx")
                    r3 = ev.op("vector", lambda e: e.tensor_scalar_mul(
                        rot[64:96, :], tmp[96:128, :], -1.0), waits=[v3], inc="vx")
                    r4 = ev.op("vector", lambda e: e.tensor_copy(
                        rot[96:128, :], tmp[64:96, :]), waits=[v3], inc="vx")
                    v4 = ev.op("vector", lambda e, t0=t0, ssn=ssn: e.tensor_tensor(
                        rot[:], rot[:], tabs[ssn][:, t0:t0 + 512], ALU.mult),
                        waits=[r1, r2, r3, r4] + tab_lds, inc="vx")
                    v5 = ev.op("vector", lambda e, t0=t0, csn=csn: e.tensor_tensor(
                        tmp2[:], tmp[:], tabs[csn][:, t0:t0 + 512], ALU.mult),
                        waits=[v3], inc="vx")
                    ev.op("vector", lambda e, t0=t0, dstb=dstb: e.tensor_tensor(
                        dstb[:, t0:t0 + 512], tmp2[:], rot[:], ALU.add),
                        waits=[v4, v5], inc="vx")
            qk_done = ev.wait_all("vx")
            v_done = []
            for t in range(16):
                mv = None
                for c in range(ND):
                    mv = ev.op("tensor", lambda e, c=c, t=t, buf=buf: e.matmul(
                        ps_w[t % 2][:, 0:128], hb[c][:, t * 128:(t + 1) * 128],
                        wv[buf][:, c * 128:(c + 1) * 128],
                        start=(c == 0), stop=(c == ND - 1)),
                        waits=ev.wait_all("vx") if c == 0 else [], inc="mm")
                for hh in range(2):
                    v_done.append(ev.op("vector", lambda e, t=t, hh=hh: e.tensor_copy(
                        vsb[:, t * 130 + hh * 65: t * 130 + hh * 65 + 64],
                        ps_w[t % 2][:, hh * 64:(hh + 1) * 64]),
                        waits=[mv] + ([vset[-1]] if g == 0 else []), inc="vx"))
            wguard[buf] = ev.mark("mm")
            for hh in range(2):
                hb64 = hh * 64
                for w in range(4):
                    q0 = w * 512
                    kt_lo = 4 if w == 0 else 0
                    exp_done = {}
                    for kt in range(kt_lo, 8):
                        ktok0 = (w - 1) * 512 + kt * 128
                        pb = kt % 2
                        warw = ev.wait_all("vx") if kt < kt_lo + 2 else \
                            ([exp_done[kt - 2]] if kt - 2 in exp_done else [])
                        m = ev.op("tensor", lambda e, ktok0=ktok0, pb=pb, hb64=hb64, q0=q0: e.matmul(
                            ps_s[pb][:], kTb[hb64:hb64 + 64, ktok0:ktok0 + 128],
                            qTb[hb64:hb64 + 64, q0:q0 + 512],
                            start=True, stop=True),
                            waits=(qk_done if (kt == kt_lo and w == 0 and g >= 0) else []) + warw,
                            inc="mm")
                        v1 = ev.op("vector", lambda e, kt=kt, pb=pb: e.scalar_tensor_tensor(
                            tmpm[kt % 2][:], ps_s[pb][:], 8.0, mask[:, kt * 512:(kt + 1) * 512],
                            ALU.mult, ALU.min), waits=[m] + tab_lds, inc="vx")
                        e1 = ev.op("scalar", lambda e, kt=kt: e.activation(
                            at[:, kt * 512:(kt + 1) * 512], tmpm[kt % 2][:], AF.Exp,
                            bias=m30v[:]), waits=[v1, ep0], inc="sc")
                        exp_done[kt] = e1
                    mavl = None
                    first = True
                    for kt in range(kt_lo, 8):
                        ktok0 = (w - 1) * 512 + kt * 128
                        vt = ktok0 // 128
                        mavl = ev.op("tensor", lambda e, kt=kt, vt=vt, hh=hh, first=first: e.matmul(
                            ps_o[:], vsb[:, vt * 130 + hh * 65: vt * 130 + hh * 65 + 65],
                            at[:, kt * 512:(kt + 1) * 512],
                            start=first, stop=(kt == 7)),
                            waits=[exp_done[kt]] + ((v_done + ev.wait_all("vx")) if first else []),
                            inc="mm")
                        first = False
                    r1 = ev.op("vector", lambda e: e.reciprocal(rec[:], ps_o[64:65, :]),
                               waits=[mavl], inc="vx")
                    mb = ev.op("tensor", lambda e: e.matmul(
                        ps_b[0:64, :], onesb[0:1, 0:64], rec[0:1, :],
                        start=True, stop=True), waits=[r1], inc="mm")
                    c1 = ev.op("vector", lambda e: e.tensor_copy(nbc[:], ps_b[0:64, :]),
                               waits=[mb], inc="vx")
                    ev.op("vector", lambda e, g=g, hb64=hb64, q0=q0: e.tensor_tensor(
                        oall[g][hb64:hb64 + 64, q0:q0 + 512], ps_o[0:64, :],
                        nbc[:], ALU.mult), waits=[c1], inc="vx")
        oall_done = ev.wait_all("vx")
        wo_ld = ev.wait_all("wd")
        res_done = []
        for dc in range(ND):
            for tc in range(4):
                acc = None
                for c in range(4):
                    acc = ev.op("tensor", lambda e, c=c, dc=dc, tc=tc: e.matmul(
                        ps_w[tc % 2][:], wo[:, c * 1024 + dc * 128: c * 1024 + dc * 128 + 128],
                        oall[c][:, tc * 512:(tc + 1) * 512],
                        start=(c == 0), stop=(c == 3)),
                        waits=((oall_done + wo_ld) if (dc == 0 and tc == 0)
                               else ev.wait_all("vx")) if c == 0 else [],
                        inc="mm")
                res_done.append(ev.op("vector", lambda e, dc=dc, tc=tc: e.tensor_tensor(
                    xf[dc][:, tc * 512:(tc + 1) * 512], ps_w[tc % 2][:],
                    xf[dc][:, tc * 512:(tc + 1) * 512], ALU.add),
                    waits=[acc], inc="vx"))
        for c in range(ND):
            ev.dma("sync", xfT_o[c * 128:(c + 1) * 128, :], xf[c][:],
                   waits=res_done, inc="st")
        fin = ev.wait_all("st")
        ev.op("sync", lambda e: e.nop(), waits=fin)
        block = ctx.enter_context(nc.Block())
        ev.finalize(block)
    return nc



def build_ffn_ffn(out_normal):
    nc = bass.Bass("TRN2", target_bir_lowering=False)
    xfT = nc.dram_tensor("xfT", [D, NTOK], F32, kind="ExternalInput").ap()
    lg_i = nc.dram_tensor("lg", [D], F32, kind="ExternalInput").ap()
    lb_i = nc.dram_tensor("lb", [D], F32, kind="ExternalInput").ap()
    w1_i = nc.dram_tensor("w1", [D, 5460], F32, kind="ExternalInput").ap()
    b1_i = nc.dram_tensor("b1", [5460], F32, kind="ExternalInput").ap()
    w2_i = nc.dram_tensor("w2", [2730, D], F32, kind="ExternalInput").ap()
    b2_i = nc.dram_tensor("b2", [D], F32, kind="ExternalInput").ap()
    onesr = nc.dram_tensor("ones_row", [1, 128], F32, kind="ExternalInput").ap()
    ident = nc.dram_tensor("ident", [128, 128], F32, kind="ExternalInput").ap()
    if out_normal:
        outp = nc.dram_tensor("g0", [NTOK, D], F32, kind="ExternalOutput").ap()
    else:
        outp = nc.dram_tensor("xfT_o", [D, NTOK], F32, kind="ExternalOutput").ap()

    NH = 22
    hsz = [128] * 21 + [42]
    with ExitStack() as ctx:
        sb = lambda *a: ctx.enter_context(nc.sbuf_tensor(*a))
        psa = lambda *a: ctx.enter_context(nc.psum_tensor(*a))
        xf = [sb(f"xf{c}", [128, NTOK], F32) for c in range(ND)]
        xh = [sb(f"xh{c}", [128, NTOK], BF16) for c in range(ND)]
        patch = [sb(f"patch{i}", [128, 512], F32) for i in range(4)]
        w2sb = sb("w2sb", [128, NH * 1024], BF16)
        w1a = [sb(f"w1a{i}", [128, 1024], BF16) for i in range(2)]
        w1g = [sb(f"w1g{i}", [128, 1024], BF16) for i in range(2)]
        acts = sb("acts", [128, NH * 512], BF16)
        lgv = sb("lgv", [128, ND], F32)
        lbv = sb("lbv", [128, ND], F32)
        b2v = sb("b2v", [128, ND], F32)
        b1av = sb("b1av", [128, NH], F32)
        b1gv = sb("b1gv", [128, NH], F32)
        onesb = sb("onesb", [1, 128], F32)
        onesc = sb("onesc", [128, 1], F32)
        sqv = [sb(f"sqv{i}", [128, 512], F32) for i in range(2)]
        murow = sb("murow", [1, 512], F32)
        e2row = sb("e2row", [1, 512], F32)
        musq = sb("musq", [1, 512], F32)
        varrow = sb("varrow", [1, 512], F32)
        rrow = sb("rrow", [1, 512], F32)
        mubc = sb("mubc", [128, 512], F32)
        rbc = sb("rbc", [128, 512], F32)
        tta = sb("tta", [128, 512], F32)
        gbuf = [sb(f"gbuf{i}", [128, 512], BF16) for i in range(2)]
        idf = sb("idf", [128, 128], F32)
        gtile = sb("gtile", [128, D], F32)

        ps_S = psa("ps_S", [1, 512], F32)
        ps_Q = psa("ps_Q", [1, 512], F32)
        ps_bc = psa("ps_bc", [128, 512], F32)
        ps_a = psa("ps_a", [128, 512], F32)
        ps_g = psa("ps_g", [128, 512], F32)
        ps_out = [psa(f"ps_out{i}", [128, 512], F32) for i in range(2)]
        ps_t = psa("ps_t", [128, 128], F32)
        epsv = sb("epsv", [128, 1], F32)

        ev = EV(nc, ctx, ["ld", "wd", "mm", "sc", "vx", "st"])
        ep0 = ev.op("vector", lambda e: e.memset(epsv[:], 1e-5), inc="vx")
        lds = []
        for c in range(ND):
            lds.append(ev.dma("sync", xf[c][:], xfT[c * 128:(c + 1) * 128, :], inc="ld"))
        ev.dma("sync", onesb[:], onesr, inc="ld")
        ev.dma("sync", idf[:], ident, inc="ld")
        for c in range(ND):
            ev.dma("sync", lgv[:, c:c + 1], lg_i.rearrange("(d o) -> d o", o=1)[c * 128:(c + 1) * 128], inc="ld")
            ev.dma("sync", lbv[:, c:c + 1], lb_i.rearrange("(d o) -> d o", o=1)[c * 128:(c + 1) * 128], inc="ld")
            ev.dma("sync", b2v[:, c:c + 1], b2_i.rearrange("(d o) -> d o", o=1)[c * 128:(c + 1) * 128], inc="ld")
        for c in range(NH):
            sz = hsz[c]
            ev.dma("sync", b1av[0:sz, c:c + 1],
                   b1_i.rearrange("(d o) -> d o", o=1)[c * 128:c * 128 + sz], inc="ld")
            ev.dma("sync", b1gv[0:sz, c:c + 1],
                   b1_i.rearrange("(d o) -> d o", o=1)[2730 + c * 128:2730 + c * 128 + sz], inc="ld")
        w2ld = [ev.dma("gpsimd",
                       w2sb[:, 0:21 * 1024].rearrange("p (c n) -> p c n", n=1024),
                       w2_i[0:2688, :].rearrange("(c p) n -> p c n", p=128), inc="wd"),
                ev.dma("gpsimd", w2sb[0:42, 21 * 1024:22 * 1024],
                       w2_i[2688:2730, :], inc="wd")]
        mset = ev.op("vector", lambda e: e.memset(onesc[:], 1.0), inc="vx")
        lds_all = ev.wait_all("ld")
        pdone = []
        for c in range(4):
            ev.op("vector", lambda e, c=c: e.memset(patch[c][:, 0:1], 0.0), inc="vx")
            pdone.append(ev.op("vector", lambda e, c=c: e.tensor_copy(
                patch[c][:, 1:512], xf[c + 4][:, 0:511]), waits=[lds[c + 4]], inc="vx"))

        def h2slice(c, t0, wdt=512):
            if c < 4:
                return xf[c][:, t0:t0 + wdt]
            if t0 == 0:
                return patch[c - 4][:, 0:wdt]
            return xf[c][:, t0 - 1:t0 - 1 + wdt]

        for tc in range(4):
            t0 = tc * 512
            mS = None
            mQ = None
            sq_done = {}
            for c in range(ND):
                mS = ev.op("tensor", lambda e, c=c, t0=t0: e.matmul(
                    ps_S[:], onesc[:], h2slice(c, t0), start=(c == 0), stop=(c == ND - 1)),
                    waits=(lds_all + pdone + [mset] + ev.wait_all("vx")) if c == 0 else [],
                    inc="mm")
            for c in range(ND):
                sq1 = ev.op("scalar", lambda e, c=c, t0=t0: e.square(
                    sqv[c % 2][:], h2slice(c, t0)),
                    waits=(sq_done[c - 2] if c >= 2 else lds_all + pdone),
                    inc="sc")
                mQ = ev.op("tensor", lambda e, c=c: e.matmul(
                    ps_Q[:], onesc[:], sqv[c % 2][:], start=(c == 0), stop=(c == ND - 1)),
                    waits=[sq1], inc="mm")
                sq_done[c] = [mQ]
            v1 = ev.op("vector", lambda e: e.tensor_scalar_mul(murow[:], ps_S[:], 1.0 / D),
                       waits=[mS], inc="vx")
            v2 = ev.op("vector", lambda e: e.tensor_scalar_mul(e2row[:], ps_Q[:], 1.0 / D),
                       waits=[mQ], inc="vx")
            s2 = ev.op("scalar", lambda e: e.square(musq[:], murow[:]), waits=[v1], inc="sc")
            v3 = ev.op("vector", lambda e: e.tensor_sub(varrow[:], e2row[:], musq[:]),
                       waits=[s2, v2], inc="vx")
            s3 = ev.op("scalar", lambda e: e.activation(rrow[:], varrow[:], AF.Sqrt,
                                                        bias=epsv[0:1, 0:1]),
                       waits=[v3, ep0], inc="sc")
            v4 = ev.op("vector", lambda e: e.reciprocal(rrow[:], rrow[:]), waits=[s3], inc="vx")
            mb1 = ev.op("tensor", lambda e: e.matmul(
                ps_bc[:], onesb[0:1, :], murow[0:1, :], start=True, stop=True),
                waits=[v1] + ev.wait_all("vx"), inc="mm")
            vb1 = ev.op("vector", lambda e: e.tensor_copy(mubc[:], ps_bc[:]),
                        waits=[mb1], inc="vx")
            mb2 = ev.op("tensor", lambda e: e.matmul(
                ps_bc[:], onesb[0:1, :], rrow[0:1, :], start=True, stop=True),
                waits=[v4, vb1], inc="mm")
            vb2 = ev.op("vector", lambda e: e.tensor_copy(rbc[:], ps_bc[:]),
                        waits=[mb2], inc="vx")
            for c in range(ND):
                va = ev.op("vector", lambda e, c=c, t0=t0: e.tensor_sub(
                    tta[:], h2slice(c, t0), mubc[:]), waits=[vb1, vb2], inc="vx")
                vb = ev.op("vector", lambda e: e.tensor_mul(tta[:], tta[:], rbc[:]),
                           waits=[va], inc="vx")
                ev.op("vector", lambda e, c=c, t0=t0: e.tensor_scalar(
                    xh[c][:, t0:t0 + 512], tta[:], lgv[:, c:c + 1], lbv[:, c:c + 1],
                    ALU.mult, ALU.add), waits=[vb], inc="vx")
        xh_done = ev.wait_all("vx")

        def load_w1(c, buf, waits):
            sz = hsz[c]
            o = []
            for (dst, c0) in ((w1a, c * 128), (w1g, 2730 + c * 128)):
                o.append(ev.dma(
                    "gpsimd",
                    dst[buf][:].rearrange("p (k n) -> p k n", n=128)[:, :, 0:sz],
                    w1_i[:, c0:c0 + sz].rearrange("(k p) n -> p k n", p=128),
                    waits=waits, inc="wd"))
            return o

        w1ld = {}
        w1ld[0] = load_w1(0, 0, [])
        w1ld[1] = load_w1(1, 1, [])
        for tc in range(4):
            t0 = tc * 512
            act_done = {}
            mm_done = {}
            for c in range(NH):
                sz = hsz[c]
                buf = c % 2
                nxt = c + 2
                ma = mg = None
                for k in range(ND):
                    ma = ev.op("tensor", lambda e, k=k, c=c, t0=t0, buf=buf, sz=sz: e.matmul(
                        ps_a[0:sz, :], w1a[buf][:, k * 128:k * 128 + sz],
                        xh[k][:, t0:t0 + 512], start=(k == 0), stop=(k == ND - 1)),
                        waits=(w1ld[c] + (xh_done if (tc == 0 and c == 0) else []) +
                               (act_done.get(c - 1, []))) if k == 0 else [],
                        inc="mm")
                for k in range(ND):
                    mg = ev.op("tensor", lambda e, k=k, c=c, t0=t0, buf=buf, sz=sz: e.matmul(
                        ps_g[0:sz, :], w1g[buf][:, k * 128:k * 128 + sz],
                        xh[k][:, t0:t0 + 512], start=(k == 0), stop=(k == ND - 1)),
                        waits=(act_done.get(c - 1, [])) if k == 0 else [],
                        inc="mm")
                mm_done[c] = [ma, mg]
                if tc < 3 or True:
                    nc_load = c + 2
                    if nc_load < NH:
                        w1ld[nc_load] = load_w1(nc_load, nc_load % 2, mm_done.get(c, []))
                    elif tc < 3 and nc_load - NH < 2:
                        w1ld[nc_load - NH] = load_w1(nc_load - NH, (nc_load - NH) % 2,
                                                     mm_done.get(c, []))
                g1 = ev.op("scalar", lambda e, c=c, sz=sz, buf=buf: e.activation(
                    gbuf[buf][0:sz, :], ps_g[0:sz, :], AF.Gelu,
                    bias=b1gv[0:sz, c:c + 1]), waits=[mg], inc="sc")
                a1 = ev.op("vector", lambda e, c=c, sz=sz, buf=buf: e.scalar_tensor_tensor(
                    acts[0:sz, c * 512:(c + 1) * 512], ps_a[0:sz, :],
                    b1av[0:sz, c:c + 1], gbuf[buf][0:sz, :], ALU.add, ALU.mult),
                    waits=[g1], inc="vx")
                act_done[c] = [a1]
            all_act = ev.wait_all("vx")
            ev_last = []
            for dc in range(ND):
                mo = None
                for c in range(NH):
                    sz = hsz[c]
                    mo = ev.op("tensor", lambda e, c=c, dc=dc, sz=sz: e.matmul(
                        ps_out[dc % 2][:], w2sb[0:sz, c * 1024 + dc * 128: c * 1024 + dc * 128 + 128],
                        acts[0:sz, c * 512:(c + 1) * 512],
                        start=(c == 0), stop=(c == NH - 1)),
                        waits=(all_act + w2ld + (ev_last[-2:-1] if dc >= 2 else [])) if c == 0 else [],
                        inc="mm")
                ev_last.append(ev.op("vector", lambda e, dc=dc, t0=t0: e.scalar_tensor_tensor(
                    xf[dc][:, t0:t0 + 512], ps_out[dc % 2][:], b2v[:, dc:dc + 1],
                    xf[dc][:, t0:t0 + 512], ALU.add, ALU.add), waits=[mo], inc="vx"))
        res_done = ev.wait_all("vx")
        if not out_normal:
            for c in range(ND):
                ev.dma("sync", outp[c * 128:(c + 1) * 128, :], xf[c][:],
                       waits=res_done, inc="st")
        else:
            tr_prev = None
            for t in range(NT):
                gts = []
                for c in range(ND):
                    m = ev.op("tensor", lambda e, c=c, t=t: e.transpose(
                        ps_t[:], xf[c][:, t * 128:(t + 1) * 128], idf[:]),
                        waits=(res_done if (t == 0 and c == 0) else []) +
                              ([tr_prev] if tr_prev else []), inc="mm")
                    tr_prev = ev.op("vector", lambda e, c=c: e.tensor_copy(
                        gtile[:, c * 128:(c + 1) * 128], ps_t[:]), waits=[m], inc="vx")
                    gts.append(tr_prev)
                ev.dma("sync", outp[t * 128:(t + 1) * 128, :], gtile[:],
                       waits=gts, inc="st")
                tr_prev = ev.mark("st")
        fin = ev.wait_all("st")
        ev.op("sync", lambda e: e.nop(), waits=fin)
        block = ctx.enter_context(nc.Block())
        ev.finalize(block)
    return nc


def build_comb1():
    nc = bass.Bass("TRN2", target_bir_lowering=False)
    x_i = nc.dram_tensor("x", [NTOK, D], F32, kind="ExternalInput").ap()
    s_i = nc.dram_tensor("s", [NTOK, D], F32, kind="ExternalInput").ap()
    g_i = nc.dram_tensor("g0", [NTOK, D], F32, kind="ExternalInput").ap()
    ident = nc.dram_tensor("ident", [128, 128], F32, kind="ExternalInput").ap()
    x1_o = nc.dram_tensor("x1", [NTOK, D], F32, kind="ExternalOutput").ap()
    xnTf_o = nc.dram_tensor("xn1T_f", [D, NTOK], F32, kind="ExternalOutput").ap()
    xnTb_o = nc.dram_tensor("xn1T_b", [D, NTOK], BF16, kind="ExternalOutput").ap()
    xnN_o = nc.dram_tensor("xn1N_b", [NTOK, D], BF16, kind="ExternalOutput").ap()
    with ExitStack() as ctx:
        sb = lambda *a: ctx.enter_context(nc.sbuf_tensor(*a))
        psa = lambda *a: ctx.enter_context(nc.psum_tensor(*a))
        xt = [sb(f"xt{i}", [128, D], F32) for i in range(2)]
        st = [sb(f"st{i}", [128, D], F32) for i in range(2)]
        gt = [sb(f"gt{i}", [128, D], F32) for i in range(2)]
        gs = sb("gs", [128, D], F32)
        x1t = [sb(f"x1t{i}", [128, D], F32) for i in range(2)]
        xnt = [sb(f"xnt{i}", [128, D], F32) for i in range(2)]
        sq = sb("sq", [128, D], F32)
        s1 = sb("s1", [128, 1], F32)
        s2 = sb("s2", [128, 1], F32)
        mu = sb("mu", [128, 1], F32)
        mu2 = sb("mu2", [128, 1], F32)
        var = sb("var", [128, 1], F32)
        rt = sb("rt", [128, 1], F32)
        rr = sb("rr", [128, 1], F32)
        idf = sb("idf", [128, 128], F32)
        chunks = [sb(f"ch{c}", [128, NTOK], F32) for c in range(ND)]
        epsv = sb("epsv", [128, 1], F32)
        tp = [psa(f"tp{i}", [128, 128], F32) for i in range(2)]
        ev = EV(nc, ctx, ["ld", "ldg", "mm", "sc", "vx", "st", "stg"])
        ev.dma("sync", idf[:], ident, inc="ld")
        ep = ev.op("vector", lambda e: e.memset(epsv[:], 1e-5), inc="vx")
        guard = [None, None]
        for i in range(NT):
            b = i % 2
            w = [] if guard[b] is None else list(guard[b])
            l1 = ev.dma("sync", xt[b][:], x_i[i * 128:(i + 1) * 128, :], waits=w, inc="ld")
            l2 = ev.dma("sync", st[b][:], s_i[i * 128:(i + 1) * 128, :], waits=w, inc="ld")
            l3 = ev.dma("gpsimd", gt[b][:], g_i[i * 128:(i + 1) * 128, :], waits=w, inc="ldg")
            sc0 = ev.op("scalar", lambda e, b=b: e.activation(
                gs[:], gt[b][:], AF.Silu), waits=[l3] + ev.wait_all("vx"), inc="sc")
            v0 = ev.op("vector", lambda e, b=b: e.tensor_mul(gs[:], st[b][:], gs[:]),
                       waits=[sc0, l2], inc="vx")
            v0b = ev.op("vector", lambda e, b=b: e.tensor_add(x1t[b][:], xt[b][:], gs[:]),
                        waits=[v0, l1], inc="vx")
            d1 = ev.dma("sync", x1_o[i * 128:(i + 1) * 128, :], x1t[b][:],
                        waits=[v0b], inc="st")
            sc1 = ev.op("scalar", lambda e, b=b: e.square(sq[:], x1t[b][:]),
                        waits=[v0b], inc="sc")
            v1 = ev.op("vector", lambda e, b=b: e.reduce_sum(s1[:], x1t[b][:], AX.X),
                       waits=[v0b], inc="vx")
            v2 = ev.op("vector", lambda e: e.reduce_sum(s2[:], sq[:], AX.X),
                       waits=[sc1], inc="vx")
            ev.op("vector", lambda e: e.tensor_scalar_mul(mu[:], s1[:], 1.0 / D),
                  waits=[v1], inc="vx")
            sc2 = ev.op("scalar", lambda e: e.square(mu2[:], mu[:]),
                        waits=ev.wait_all("vx"), inc="sc")
            v3 = ev.op("vector", lambda e: e.scalar_tensor_tensor(
                var[:], s2[:], 1.0 / D, mu2[:], ALU.mult, ALU.subtract),
                waits=[sc2, v2], inc="vx")
            sc3 = ev.op("scalar", lambda e: e.activation(
                rt[:], var[:], AF.Sqrt, bias=epsv[:]), waits=[v3, ep], inc="sc")
            v4 = ev.op("vector", lambda e: e.reciprocal(rr[:], rt[:]),
                       waits=[sc3], inc="vx")
            vn = ev.op("vector", lambda e, b=b: e.tensor_scalar(
                xnt[b][:], x1t[b][:], mu[:], rr[:], ALU.subtract, ALU.mult),
                waits=[v4], inc="vx")
            d2 = ev.dma("gpsimd", xnN_o[i * 128:(i + 1) * 128, :], xnt[b][:],
                        waits=[vn], inc="stg")
            cv = vn
            for c in range(ND):
                pb = c % 2
                m = ev.op("tensor", lambda e, b=b, c=c, pb=pb: e.transpose(
                    tp[pb][:], xnt[b][:, c * 128:(c + 1) * 128], idf[:]),
                    waits=[vn] + ev.wait_all("vx")[-1:], inc="mm")
                cv = ev.op("vector", lambda e, i=i, c=c, pb=pb: e.tensor_copy(
                    chunks[c][:, i * 128:(i + 1) * 128], tp[pb][:]),
                    waits=[m], inc="vx")
            guard[b] = [cv, d1, ev.mark("st"), ev.mark("stg")]
        w = ev.wait_all("vx")
        for c in range(ND):
            ev.dma("sync", xnTf_o[c * 128:(c + 1) * 128, :], chunks[c][:],
                   waits=w, inc="st")
            ev.dma("gpsimd", xnTb_o[c * 128:(c + 1) * 128, :], chunks[c][:],
                   waits=w, inc="stg")
        fin = ev.wait_all("st", "stg")
        ev.op("sync", lambda e: e.nop(), waits=fin)
        ev.op("gpsimd", lambda e: e.nop(), waits=fin)
        block = ctx.enter_context(nc.Block())
        ev.finalize(block)
    return nc


def build_l1attn():
    nc = bass.Bass("TRN2", target_bir_lowering=False)
    xnT_b = nc.dram_tensor("xn1T_b", [D, NTOK], BF16, kind="ExternalInput").ap()
    xnN_b = nc.dram_tensor("xn1N_b", [NTOK, D], BF16, kind="ExternalInput").ap()
    QK_i = nc.dram_tensor("QK", [D, D], F32, kind="ExternalInput").ap()
    VO_i = nc.dram_tensor("VO", [D, D], F32, kind="ExternalInput").ap()
    onesr = nc.dram_tensor("ones_row", [1, 128], F32, kind="ExternalInput").ap()
    s1_o = nc.dram_tensor("s1", [NTOK, D], F32, kind="ExternalOutput").ap()
    with ExitStack() as ctx:
        sb = lambda *a: ctx.enter_context(nc.sbuf_tensor(*a))
        psa = lambda *a: ctx.enter_context(nc.psum_tensor(*a))
        xn = [sb(f"xn{c}", [128, NTOK], BF16) for c in range(ND)]
        xnN = sb("xnN", [128, NT * D], BF16)
        qksb = sb("qksb", [128, ND * D], BF16)
        vosb = sb("vosb", [128, ND * D], BF16)
        xqk = [sb(f"xqk{c}", [128, NTOK], BF16) for c in range(ND)]
        at = sb("at", [128, 16 * 512], BF16)
        avsb = sb("avsb", [128, ND * 512], BF16)
        onesb = sb("onesb", [1, 128], F32)
        ones_bf = sb("ones_bf", [128, 1], BF16)
        rec = sb("rec", [1, 512], F32)
        bcs = sb("bcs", [128, 512], F32)
        stile = [sb(f"stile{i}", [128, D], F32) for i in range(2)]
        ps_s = [psa(f"ps_s{i}", [128, 512], F32) for i in range(2)]
        ps_den = psa("ps_den", [1, 512], F32)
        ps_bc = psa("ps_bc", [128, 512], F32)
        ps_av = psa("ps_av", [128, 512], F32)
        ps_vo = [psa(f"ps_vo{i}", [128, 512], F32) for i in range(2)]
        m8v = sb("m8v", [128, 1], F32)
        ev = EV(nc, ctx, ["ld", "wd", "mm", "sc", "vx", "st"])
        ep0 = ev.op("vector", lambda e: e.memset(m8v[:], -8.0), inc="vx")
        lds = []
        for c in range(ND):
            lds.append(ev.dma("sync", xn[c][:], xnT_b[c * 128:(c + 1) * 128, :], inc="ld"))
        for t in range(NT):
            lds.append(ev.dma("sync", xnN[:, t * D:(t + 1) * D],
                              xnN_b[t * 128:(t + 1) * 128, :], inc="ld"))
        ev.dma("sync", onesb[:], onesr, inc="ld")
        wd1 = ev.dma("gpsimd", qksb[:].rearrange("p (k n) -> p k n", n=D),
                     QK_i.rearrange("(k p) n -> p k n", p=128), inc="wd")
        wd2 = ev.dma("gpsimd", vosb[:].rearrange("p (k n) -> p k n", n=D),
                     VO_i.rearrange("(k p) n -> p k n", p=128), inc="wd")
        mset = ev.op("vector", lambda e: e.memset(ones_bf[:], 1.0), inc="vx")
        lds_all = ev.wait_all("ld")
        st_guard = [None, None]
        guard = None
        for ecn in range(ND):
            for tcc in range(4):
                acc = None
                for c in range(ND):
                    acc = ev.op("tensor", lambda e, c=c, ecn=ecn, tcc=tcc: e.matmul(
                        ps_s[tcc % 2][:], qksb[:, c * D + ecn * 128: c * D + ecn * 128 + 128],
                        xn[c][:, tcc * 512:(tcc + 1) * 512],
                        start=(c == 0), stop=(c == ND - 1)),
                        waits=(lds_all + [wd1] + ([guard] if guard else [])) if c == 0 else [],
                        inc="mm")
                guard = ev.op("vector", lambda e, ecn=ecn, tcc=tcc: e.tensor_copy(
                    xqk[ecn][:, tcc * 512:(tcc + 1) * 512], ps_s[tcc % 2][:]),
                    waits=[acc], inc="vx")
        xqk_done = ev.wait_all("vx")
        for tc in range(4):
            t0 = tc * 512
            exp_done = {}
            for ut in range(16):
                pb = ut % 2
                warw = ev.wait_all("sc", "vx") if ut < 2 else [exp_done[ut - 2]]
                acc = None
                for c in range(ND):
                    acc = ev.op("tensor", lambda e, c=c, ut=ut, t0=t0, pb=pb: e.matmul(
                        ps_s[pb][:], xn[c][:, ut * 128:(ut + 1) * 128],
                        xqk[c][:, t0:t0 + 512],
                        start=(c == 0), stop=(c == ND - 1)),
                        waits=((xqk_done if (tc == 0 and ut == 0) else []) + warw) if c == 0 else [],
                        inc="mm")
                e1 = ev.op("scalar", lambda e, ut=ut, pb=pb: e.activation(
                    at[:, ut * 512:(ut + 1) * 512], ps_s[pb][:], AF.Exp,
                    bias=m8v[:], scale=1.0 / 32.0), waits=[acc, ep0], inc="sc")
                exp_done[ut] = e1
            mden = None
            for ut in range(16):
                mden = ev.op("tensor", lambda e, ut=ut: e.matmul(
                    ps_den[:], ones_bf[:], at[:, ut * 512:(ut + 1) * 512],
                    start=(ut == 0), stop=(ut == 15)),
                    waits=[exp_done[ut]] + ([mset] + ev.wait_all("vx") if ut == 0 else []),
                    inc="mm")
            r1 = ev.op("vector", lambda e: e.reciprocal(rec[:], ps_den[:]),
                       waits=[mden], inc="vx")
            mb = ev.op("tensor", lambda e: e.matmul(
                ps_bc[:], onesb[0:1, :], rec[0:1, :], start=True, stop=True),
                waits=[r1], inc="mm")
            cb = ev.op("vector", lambda e: e.tensor_copy(bcs[:], ps_bc[:]),
                       waits=[mb], inc="vx")
            for dc in range(ND):
                mav = None
                for ut in range(16):
                    mav = ev.op("tensor", lambda e, dc=dc, ut=ut: e.matmul(
                        ps_av[:], xnN[:, ut * D + dc * 128: ut * D + dc * 128 + 128],
                        at[:, ut * 512:(ut + 1) * 512],
                        start=(ut == 0), stop=(ut == 15)),
                        waits=([cb] + ev.wait_all("vx")) if ut == 0 else [],
                        inc="mm")
                ev.op("vector", lambda e, dc=dc: e.tensor_tensor(
                    avsb[:, dc * 512:(dc + 1) * 512], ps_av[:], bcs[:], ALU.mult),
                    waits=[mav], inc="vx")
            av_done = ev.wait_all("vx")
            for tt in range(4):
                sready = []
                for n in range(2):
                    acc = None
                    for dc in range(ND):
                        acc = ev.op("tensor", lambda e, dc=dc, tt=tt, n=n: e.matmul(
                            ps_vo[n][:], avsb[:, dc * 512 + tt * 128: dc * 512 + tt * 128 + 128],
                            vosb[:, dc * D + n * 512: dc * D + (n + 1) * 512],
                            start=(dc == 0), stop=(dc == ND - 1)),
                            waits=(av_done + [wd2] + ev.wait_all("vx")[-1:]) if dc == 0 else [],
                            inc="mm")
                    sready.append(ev.op("vector", lambda e, tt=tt, n=n: e.tensor_copy(
                        stile[tt % 2][:, n * 512:(n + 1) * 512], ps_vo[n][:]),
                        waits=[acc] + ([st_guard[tt % 2]] if st_guard[tt % 2] else []),
                        inc="vx"))
                ev.dma("sync", s1_o[t0 + tt * 128: t0 + (tt + 1) * 128, :],
                       stile[tt % 2][:], waits=sready, inc="st")
                st_guard[tt % 2] = ev.mark("st")
        fin = ev.wait_all("st")
        ev.op("sync", lambda e: e.nop(), waits=fin)
        block = ctx.enter_context(nc.Block())
        ev.finalize(block)
    return nc


def build_l1ffn():
    nc = bass.Bass("TRN2", target_bir_lowering=False)
    xnT_b = nc.dram_tensor("xn1T_b", [D, NTOK], BF16, kind="ExternalInput").ap()
    xnT_f = nc.dram_tensor("xn1T_f", [D, NTOK], F32, kind="ExternalInput").ap()
    cw = {}
    cw["cn_w1"] = nc.dram_tensor("cn_w1", [D, D], F32, kind="ExternalInput").ap()
    cw["cn_w2"] = nc.dram_tensor("cn_w2", [D, 512], F32, kind="ExternalInput").ap()
    cw["cn_w3"] = nc.dram_tensor("cn_w3", [512, 256], F32, kind="ExternalInput").ap()
    cw["cn_w4"] = nc.dram_tensor("cn_w4", [256, 1], F32, kind="ExternalInput").ap()
    cb = {n: nc.dram_tensor(n, [sz], F32, kind="ExternalInput").ap()
          for n, sz in (("cn_b1", D), ("cn_b2", 512), ("cn_b3", 256), ("cn_b4", 1))}
    gw = {}
    gw["gc_w1"] = nc.dram_tensor("gc_w1", [D, 512], F32, kind="ExternalInput").ap()
    gw["gc_w2"] = nc.dram_tensor("gc_w2", [512, 256], F32, kind="ExternalInput").ap()
    gw["gc_w3"] = nc.dram_tensor("gc_w3", [256, 1], F32, kind="ExternalInput").ap()
    gb = {n: nc.dram_tensor(n, [sz], F32, kind="ExternalInput").ap()
          for n, sz in (("gc_b1", 512), ("gc_b2", 256), ("gc_b3", 1))}
    ar_i = nc.dram_tensor("ar", [1], F32, kind="ExternalInput").ap()
    onesr = nc.dram_tensor("ones_row", [1, 128], F32, kind="ExternalInput").ap()
    ident = nc.dram_tensor("ident", [128, 128], F32, kind="ExternalInput").ap()
    o1_o = nc.dram_tensor("o1", [NTOK, D], F32, kind="ExternalOutput").ap()
    with ExitStack() as ctx:
        sb = lambda *a: ctx.enter_context(nc.sbuf_tensor(*a))
        psa = lambda *a: ctx.enter_context(nc.psum_tensor(*a))
        out = [sb(f"out{c}", [128, NTOK], F32) for c in range(ND)]
        w1sb = sb("w1sb", [128, ND * D], BF16)
        w2sb = sb("w2sb", [128, ND * 512], BF16)
        w3sb = sb("w3sb", [128, 4 * 256], BF16)
        w4sb = sb("w4sb", [128, 2], BF16)
        b1v = sb("b1v", [128, ND], F32)
        b2v = sb("b2v", [128, 4], F32)
        b3v = sb("b3v", [128, 2], F32)
        b4v = sb("b4v", [1, 1], F32)
        g1sb = sb("g1sb", [128, ND * 512], F32)
        g2sb = sb("g2sb", [128, 4 * 256], F32)
        g3sb = sb("g3sb", [128, 2], F32)
        gb1v = sb("gb1v", [128, 4], F32)
        gb2v = sb("gb2v", [128, 2], F32)
        gb3v = sb("gb3v", [1, 1], F32)
        arv = sb("arv", [1, 1], F32)
        gcs = sb("gcs", [128, ND], F32)
        h1v = sb("h1v", [128, 4], F32)
        h2v = sb("h2v", [128, 2], F32)
        gfv = sb("gfv", [1, 1], F32)
        c0v = sb("c0v", [1, 1], F32)
        c1v = sb("c1v", [1, 1], F32)
        t1v = sb("t1v", [1, 1], F32)
        obf = sb("obf", [128, ND * 512], BF16)
        h1sb = sb("h1sb", [128, ND * 512], BF16)
        h2sb = sb("h2sb", [128, 4 * 512], BF16)
        h3sb = sb("h3sb", [128, 2 * 512], BF16)
        tfrow = sb("tfrow", [1, 512], F32)
        coup = sb("coup", [1, 512], F32)
        tcp = sb("tcp", [128, 512], F32)
        ttb = sb("ttb", [128, 512], F32)
        onesb = sb("onesb", [1, 128], F32)
        idf = sb("idf", [128, 128], F32)
        gtile = sb("gtile", [128, D], F32)
        ps_m = [psa(f"ps_m{i}", [128, 512], F32) for i in range(2)]
        ps_sm = psa("ps_sm", [128, 1], F32)
        ps_tf = psa("ps_tf", [1, 512], F32)
        ps_bc = psa("ps_bc", [128, 512], F32)
        ps_t = psa("ps_t", [128, 128], F32)
        ev = EV(nc, ctx, ["ld", "wd", "mm", "sc", "vx", "st"])
        lds = []
        for c in range(ND):
            lds.append(ev.dma("sync", out[c][:], xnT_f[c * 128:(c + 1) * 128, :], inc="ld"))
        ev.dma("sync", onesb[:], onesr, inc="ld")
        ev.dma("sync", idf[:], ident, inc="ld")
        ev.dma("sync", arv[:], ar_i.rearrange("(o d) -> o d", o=1), inc="ld")
        wds = []
        wds.append(ev.dma("gpsimd", w1sb[:].rearrange("p (k n) -> p k n", n=D),
                          cw["cn_w1"].rearrange("(k p) n -> p k n", p=128), inc="wd"))
        wds.append(ev.dma("gpsimd", w2sb[:].rearrange("p (k n) -> p k n", n=512),
                          cw["cn_w2"].rearrange("(k p) n -> p k n", p=128), inc="wd"))
        wds.append(ev.dma("gpsimd", w3sb[:].rearrange("p (k n) -> p k n", n=256),
                          cw["cn_w3"].rearrange("(k p) n -> p k n", p=128), inc="wd"))
        for c in range(2):
            wds.append(ev.dma("gpsimd", w4sb[:, c:c + 1],
                              cw["cn_w4"][c * 128:(c + 1) * 128, 0:1], inc="wd"))
        wds.append(ev.dma("sync", g1sb[:].rearrange("p (k n) -> p k n", n=512),
                          gw["gc_w1"].rearrange("(k p) n -> p k n", p=128), inc="ld"))
        wds.append(ev.dma("sync", g2sb[:].rearrange("p (k n) -> p k n", n=256),
                          gw["gc_w2"].rearrange("(k p) n -> p k n", p=128), inc="ld"))
        for c in range(2):
            wds.append(ev.dma("sync", g3sb[:, c:c + 1],
                              gw["gc_w3"][c * 128:(c + 1) * 128, 0:1], inc="ld"))
        for c in range(ND):
            ev.dma("sync", b1v[:, c:c + 1],
                   cb["cn_b1"].rearrange("(d o) -> d o", o=1)[c * 128:(c + 1) * 128], inc="ld")
        for c in range(4):
            ev.dma("sync", b2v[:, c:c + 1],
                   cb["cn_b2"].rearrange("(d o) -> d o", o=1)[c * 128:(c + 1) * 128], inc="ld")
            ev.dma("sync", gb1v[:, c:c + 1],
                   gb["gc_b1"].rearrange("(d o) -> d o", o=1)[c * 128:(c + 1) * 128], inc="ld")
        for c in range(2):
            ev.dma("sync", b3v[:, c:c + 1],
                   cb["cn_b3"].rearrange("(d o) -> d o", o=1)[c * 128:(c + 1) * 128], inc="ld")
            ev.dma("sync", gb2v[:, c:c + 1],
                   gb["gc_b2"].rearrange("(d o) -> d o", o=1)[c * 128:(c + 1) * 128], inc="ld")
        ev.dma("sync", b4v[:], cb["cn_b4"].rearrange("(o d) -> o d", o=1), inc="ld")
        ev.dma("sync", gb3v[:], gb["gc_b3"].rearrange("(o d) -> o d", o=1), inc="ld")
        lds_all = ev.wait_all("ld", "wd")
        gm = []
        for c in range(ND):
            r = ev.op("vector", lambda e, c=c: e.reduce_sum(gcs[:, c:c + 1], out[c][:], AX.X),
                      waits=[lds[c]], inc="vx")
            gm.append(r)
        vsc = ev.op("vector", lambda e: e.tensor_scalar_mul(gcs[:], gcs[:], 1.0 / NTOK),
                    waits=gm, inc="vx")
        h1done = []
        for m in range(4):
            acc = None
            for c in range(ND):
                acc = ev.op("tensor", lambda e, m=m, c=c: e.matmul(
                    ps_sm[:], g1sb[:, c * 512 + m * 128: c * 512 + m * 128 + 128],
                    gcs[:, c:c + 1], start=(c == 0), stop=(c == ND - 1)),
                    waits=(lds_all + [vsc] + h1done[-1:]) if c == 0 else [], inc="mm")
            h1done.append(ev.op("scalar", lambda e, m=m: e.activation(
                h1v[:, m:m + 1], ps_sm[:], AF.Gelu, bias=gb1v[:, m:m + 1]),
                waits=[acc], inc="sc"))
        h2done = []
        for m in range(2):
            acc = None
            for c in range(4):
                acc = ev.op("tensor", lambda e, m=m, c=c: e.matmul(
                    ps_sm[:], g2sb[:, c * 256 + m * 128: c * 256 + m * 128 + 128],
                    h1v[:, c:c + 1], start=(c == 0), stop=(c == 3)),
                    waits=(h1done + h2done[-1:]) if c == 0 else [], inc="mm")
            h2done.append(ev.op("scalar", lambda e, m=m: e.activation(
                h2v[:, m:m + 1], ps_sm[:], AF.Gelu, bias=gb2v[:, m:m + 1]),
                waits=[acc], inc="sc"))
        acc = None
        for c in range(2):
            acc = ev.op("tensor", lambda e, c=c: e.matmul(
                ps_sm[0:1, :], g3sb[0:128, c:c + 1], h2v[:, c:c + 1],
                start=(c == 0), stop=(c == 1)),
                waits=(h2done + h1done) if c == 0 else [], inc="mm")
        gfd = ev.op("scalar", lambda e: e.activation(
            gfv[:], ps_sm[0:1, 0:1], AF.Sigmoid, bias=gb3v[0:1, 0:1]),
            waits=[acc], inc="sc")
        v1 = ev.op("vector", lambda e: e.tensor_scalar_mul(c1v[:], arv[:], 0.6),
                   waits=[gfd] + lds_all, inc="vx")
        v2 = ev.op("vector", lambda e: e.tensor_mul(t1v[:], arv[:], gfv[:]),
                   waits=[gfd], inc="vx")
        v3 = ev.op("vector", lambda e: e.scalar_tensor_tensor(
            t1v[:], t1v[:], 1.4, arv[:], ALU.mult, ALU.subtract), waits=[v2], inc="vx")
        cc = ev.op("vector", lambda e: e.tensor_scalar_add(c0v[:], t1v[:], 0.1),
                   waits=[v3], inc="vx")
        for cyc in range(3):
            for tc in range(4):
                t0 = tc * 512
                cvt = []
                for c in range(ND):
                    cvt.append(ev.op("vector", lambda e, c=c, t0=t0: e.tensor_copy(
                        obf[:, c * 512:(c + 1) * 512], out[c][:, t0:t0 + 512]),
                        waits=ev.wait_all("mm")[-1:] if (cyc > 0 or tc > 0) else [],
                        inc="vx"))
                h1d = []
                for m in range(ND):
                    acc = None
                    for c in range(ND):
                        acc = ev.op("tensor", lambda e, m=m, c=c: e.matmul(
                            ps_m[m % 2][:], w1sb[:, c * D + m * 128: c * D + m * 128 + 128],
                            obf[:, c * 512:(c + 1) * 512],
                            start=(c == 0), stop=(c == ND - 1)),
                            waits=(cvt + lds_all + ev.wait_all("sc")[-1:]) if c == 0 else [],
                            inc="mm")
                    h1d.append(ev.op("scalar", lambda e, m=m: e.activation(
                        h1sb[:, m * 512:(m + 1) * 512], ps_m[m % 2][:], AF.Gelu,
                        bias=b1v[:, m:m + 1]), waits=[acc], inc="sc"))
                h2d = []
                for m in range(4):
                    acc = None
                    for c in range(ND):
                        acc = ev.op("tensor", lambda e, m=m, c=c: e.matmul(
                            ps_m[m % 2][:], w2sb[:, c * 512 + m * 128: c * 512 + m * 128 + 128],
                            h1sb[:, c * 512:(c + 1) * 512],
                            start=(c == 0), stop=(c == ND - 1)),
                            waits=(h1d + ev.wait_all("sc")[-1:]) if c == 0 else [],
                            inc="mm")
                    h2d.append(ev.op("scalar", lambda e, m=m: e.activation(
                        h2sb[:, m * 512:(m + 1) * 512], ps_m[m % 2][:], AF.Gelu,
                        bias=b2v[:, m:m + 1]), waits=[acc], inc="sc"))
                h3d = []
                for m in range(2):
                    acc = None
                    for c in range(4):
                        acc = ev.op("tensor", lambda e, m=m, c=c: e.matmul(
                            ps_m[m % 2][:], w3sb[:, c * 256 + m * 128: c * 256 + m * 128 + 128],
                            h2sb[:, c * 512:(c + 1) * 512],
                            start=(c == 0), stop=(c == 3)),
                            waits=(h2d + ev.wait_all("sc")[-1:]) if c == 0 else [],
                            inc="mm")
                    h3d.append(ev.op("scalar", lambda e, m=m: e.activation(
                        h3sb[:, m * 512:(m + 1) * 512], ps_m[m % 2][:], AF.Gelu,
                        bias=b3v[:, m:m + 1]), waits=[acc], inc="sc"))
                acc = None
                for c in range(2):
                    acc = ev.op("tensor", lambda e, c=c: e.matmul(
                        ps_tf[:], w4sb[0:128, c:c + 1], h3sb[:, c * 512:(c + 1) * 512],
                        start=(c == 0), stop=(c == 1)),
                        waits=h3d if c == 0 else [], inc="mm")
                tfd = ev.op("scalar", lambda e: e.activation(
                    tfrow[:], ps_tf[:], AF.Sigmoid, bias=b4v[0:1, 0:1]),
                    waits=[acc], inc="sc")
                cpd = ev.op("vector", lambda e: e.tensor_scalar(
                    coup[:], tfrow[:], c1v[0:1, 0:1], c0v[0:1, 0:1], ALU.mult, ALU.add),
                    waits=[tfd, cc], inc="vx")
                mb = ev.op("tensor", lambda e: e.matmul(
                    ps_bc[:], onesb[0:1, :], coup[0:1, :], start=True, stop=True),
                    waits=[cpd] + ev.wait_all("vx")[-1:], inc="mm")
                last = None
                for p in range(4):
                    sgn = 1.0 if p < 2 else -1.0
                    pw = [mb] + ([last] if last else [])
                    vc = ev.op("vector", lambda e, p=p, t0=t0: e.tensor_copy(
                        tcp[:], out[p][:, t0:t0 + 512]), waits=pw, inc="vx")
                    va = ev.op("vector", lambda e, p=p, t0=t0: e.tensor_tensor(
                        ttb[:], ps_bc[:], out[p + 4][:, t0:t0 + 512], ALU.mult),
                        waits=[vc], inc="vx")
                    if sgn > 0:
                        vb = ev.op("vector", lambda e, p=p, t0=t0: e.tensor_add(
                            out[p][:, t0:t0 + 512], out[p][:, t0:t0 + 512], ttb[:]),
                            waits=[va], inc="vx")
                    else:
                        vb = ev.op("vector", lambda e, p=p, t0=t0: e.tensor_sub(
                            out[p][:, t0:t0 + 512], out[p][:, t0:t0 + 512], ttb[:]),
                            waits=[va], inc="vx")
                    vd = ev.op("vector", lambda e, p=p, t0=t0: e.tensor_tensor(
                        ttb[:], ps_bc[:], tcp[:], ALU.mult), waits=[vb], inc="vx")
                    if sgn > 0:
                        last = ev.op("vector", lambda e, p=p, t0=t0: e.tensor_sub(
                            out[p + 4][:, t0:t0 + 512], out[p + 4][:, t0:t0 + 512], ttb[:]),
                            waits=[vd], inc="vx")
                    else:
                        last = ev.op("vector", lambda e, p=p, t0=t0: e.tensor_add(
                            out[p + 4][:, t0:t0 + 512], out[p + 4][:, t0:t0 + 512], ttb[:]),
                            waits=[vd], inc="vx")
        res_done = ev.wait_all("vx")
        tr_prev = None
        for t in range(NT):
            gts = []
            for c in range(ND):
                m = ev.op("tensor", lambda e, c=c, t=t: e.transpose(
                    ps_t[:], out[c][:, t * 128:(t + 1) * 128], idf[:]),
                    waits=(res_done + lds_all) if (t == 0 and c == 0) else
                          ([tr_prev] if tr_prev else []), inc="mm")
                tr_prev = ev.op("vector", lambda e, c=c: e.tensor_copy(
                    gtile[:, c * 128:(c + 1) * 128], ps_t[:]), waits=[m], inc="vx")
                gts.append(tr_prev)
            dm = ev.dma("sync", o1_o[t * 128:(t + 1) * 128, :], gtile[:],
                        waits=gts, inc="st")
            tr_prev = ("st", ev.cnt["st"])
        fin = ev.wait_all("st")
        ev.op("sync", lambda e: e.nop(), waits=fin)
        block = ctx.enter_context(nc.Block())
        ev.finalize(block)
    return nc


def build_comb2():
    nc = bass.Bass("TRN2", target_bir_lowering=False)
    x1_i = nc.dram_tensor("x1", [NTOK, D], F32, kind="ExternalInput").ap()
    s1_i = nc.dram_tensor("s1", [NTOK, D], F32, kind="ExternalInput").ap()
    o1_i = nc.dram_tensor("o1", [NTOK, D], F32, kind="ExternalInput").ap()
    y_o = nc.dram_tensor("y", [NTOK, D], F32, kind="ExternalOutput").ap()
    with ExitStack() as ctx:
        sb = lambda *a: ctx.enter_context(nc.sbuf_tensor(*a))
        xt = [sb(f"xt{i}", [128, D], F32) for i in range(2)]
        st = [sb(f"st{i}", [128, D], F32) for i in range(2)]
        ot = [sb(f"ot{i}", [128, D], F32) for i in range(2)]
        gs = sb("gs", [128, D], F32)
        yt = [sb(f"yt{i}", [128, D], F32) for i in range(2)]
        ev = EV(nc, ctx, ["ld", "ldg", "sc", "vx", "st"])
        guard = [None, None]
        for i in range(NT):
            b = i % 2
            w = [] if guard[b] is None else [guard[b]]
            l1 = ev.dma("sync", xt[b][:], x1_i[i * 128:(i + 1) * 128, :], waits=w, inc="ld")
            l2 = ev.dma("sync", st[b][:], s1_i[i * 128:(i + 1) * 128, :], waits=w, inc="ld")
            l3 = ev.dma("gpsimd", ot[b][:], o1_i[i * 128:(i + 1) * 128, :], waits=w, inc="ldg")
            sc0 = ev.op("scalar", lambda e, b=b: e.activation(
                gs[:], ot[b][:], AF.Silu), waits=[l3] + ev.wait_all("vx"), inc="sc")
            v0 = ev.op("vector", lambda e, b=b: e.tensor_mul(gs[:], st[b][:], gs[:]),
                       waits=[sc0, l2], inc="vx")
            v1 = ev.op("vector", lambda e, b=b: e.tensor_add(yt[b][:], xt[b][:], gs[:]),
                       waits=[v0, l1], inc="vx")
            guard[b] = ev.dma("sync", y_o[i * 128:(i + 1) * 128, :], yt[b][:],
                              waits=[v1], inc="st")
        fin = ev.wait_all("st")
        ev.op("sync", lambda e: e.nop(), waits=fin)
        ev.op("gpsimd", lambda e: e.nop(), waits=fin)
        block = ctx.enter_context(nc.Block())
        ev.finalize(block)
    return nc


_programs = {}


def _prog(name, fn):
    if name not in _programs:
        _programs[name] = fn()
    return _programs[name]


def _run(name, fn, in_maps):
    nc = _prog(name, fn)
    res = run_bass_kernel_spmd(nc, in_maps, core_ids=list(range(N_CORES)))
    return res.results


def kernel(**inputs):
    f32 = lambda k: np.ascontiguousarray(np.asarray(inputs[k], dtype=np.float32))
    x = f32("L0_attn_x")
    C = CONSTS
    n = N_CORES

    r = _run("ln0", build_ln0,
             [dict(x=x[i], ident=C["ident"]) for i in range(n)])
    xnT_f = [r[i]["xnT_f"] for i in range(n)]
    xnT_b = [r[i]["xnT_b"] for i in range(n)]

    r = _run("l0attn", build_l0attn,
             [dict(xnT_b=xnT_b[i], w_qkv=f32("L0_attn_w_qkv"),
                   w_out=f32("L0_attn_w_out"), b_out=f32("L0_attn_b_out"),
                   gamma=f32("L0_attn_ln_gamma"), beta=f32("L0_attn_ln_beta"),
                   ones_row=C["ones_row"]) for i in range(n)])
    s0 = [r[i]["s_out"] for i in range(n)]

    xf = xnT_f
    for layer in range(2):
        r = _run("ffn_attn", build_ffn_attn,
                 [dict(xfT=xf[i], wqkv=f32("L0_ffn_wqkv")[layer],
                       qs=f32("L0_ffn_q_scale")[layer], ks=f32("L0_ffn_k_scale")[layer],
                       wo_w=f32("L0_ffn_wo")[layer], ones_row=C["ones_row"],
                       ones2=C["ones2"], ones2T=C["ones2T"], maskM=C["maskM"],
                       cs_q=C["cs_q"], ss_q=C["ss_q"], cs_k=C["cs_k"], ss_k=C["ss_k"])
                  for i in range(n)])
        xf = [r[i]["xfT_o"] for i in range(n)]
        last = layer == 1
        r = _run("ffn_ffn_n" if last else "ffn_ffn",
                 (lambda: build_ffn_ffn(True)) if last else (lambda: build_ffn_ffn(False)),
                 [dict(xfT=xf[i], lg=f32("L0_ffn_ln_g")[layer], lb=f32("L0_ffn_ln_b")[layer],
                       w1=f32("L0_ffn_w1")[layer], b1=f32("L0_ffn_b1")[layer],
                       w2=f32("L0_ffn_w2")[layer], b2=f32("L0_ffn_b2")[layer],
                       ones_row=C["ones_row"], ident=C["ident"]) for i in range(n)])
        if last:
            g0 = [r[i]["g0"] for i in range(n)]
        else:
            xf = [r[i]["xfT_o"] for i in range(n)]

    r = _run("comb1", build_comb1,
             [dict(x=x[i], s=s0[i], g0=g0[i], ident=C["ident"]) for i in range(n)])
    x1 = [r[i]["x1"] for i in range(n)]
    xn1T_f = [r[i]["xn1T_f"] for i in range(n)]
    xn1T_b = [r[i]["xn1T_b"] for i in range(n)]
    xn1N_b = [r[i]["xn1N_b"] for i in range(n)]

    r = _run("l1attn", build_l1attn,
             [dict(xn1T_b=xn1T_b[i], xn1N_b=xn1N_b[i], QK=f32("L1_attn_QK"),
                   VO=f32("L1_attn_VO"), ones_row=C["ones_row"]) for i in range(n)])
    s1 = [r[i]["s1"] for i in range(n)]

    r = _run("l1ffn", build_l1ffn,
             [dict(xn1T_b=xn1T_b[i], xn1T_f=xn1T_f[i],
                   cn_w1=f32("L1_ffn_cn_w1"), cn_b1=f32("L1_ffn_cn_b1"),
                   cn_w2=f32("L1_ffn_cn_w2"), cn_b2=f32("L1_ffn_cn_b2"),
                   cn_w3=f32("L1_ffn_cn_w3"), cn_b3=f32("L1_ffn_cn_b3"),
                   cn_w4=f32("L1_ffn_cn_w4"), cn_b4=f32("L1_ffn_cn_b4"),
                   gc_w1=f32("L1_ffn_gc_w1"), gc_b1=f32("L1_ffn_gc_b1"),
                   gc_w2=f32("L1_ffn_gc_w2"), gc_b2=f32("L1_ffn_gc_b2"),
                   gc_w3=f32("L1_ffn_gc_w3"), gc_b3=f32("L1_ffn_gc_b3"),
                   ar=f32("L1_ffn_adaptive_range"), ones_row=C["ones_row"],
                   ident=C["ident"]) for i in range(n)])
    o1 = [r[i]["o1"] for i in range(n)]

    r = _run("comb2", build_comb2,
             [dict(x1=x1[i], s1=s1[i], o1=o1[i]) for i in range(n)])
    y = np.stack([r[i]["y"] for i in range(n)], axis=0)
    return y.astype(np.float32)


# version 03
# speedup vs baseline: 1.1141x; 1.1141x over previous
import numpy as np
import ml_dtypes
from contextlib import ExitStack
from collections import defaultdict

import concourse.bass as bass
import concourse.mybir as mybir
from concourse.bass_utils import run_bass_kernel_spmd

F32 = mybir.dt.float32
BF16 = mybir.dt.bfloat16
AF = mybir.ActivationFunctionType
ALU = mybir.AluOpType
AX = mybir.AxisListType

BF = ml_dtypes.bfloat16
N_CORES = 8
NTOK = 2048
D = 1024
NT = NTOK // 128
ND = D // 128


class EV:

    def __init__(self, nc, ctx, sem_names):
        self.nc = nc
        self.events = defaultdict(list)
        self.cnt = defaultdict(int)
        self.sems = {n: ctx.enter_context(nc.semaphore(n)) for n in sem_names}
        self.pending = {}

    def mark(self, sem):
        return (sem, self.cnt[sem])

    ENG_SEM = {"vector": "vx", "scalar": "sc"}

    def op(self, engine, fn, waits=(), inc=None, inc_n=1):
        w = self.pending.pop(engine, []) + list(waits)
        sname = self.ENG_SEM.get(engine)
        if sname is not None and sname in self.sems and self.cnt[sname] > 0:
            w.append((sname, self.cnt[sname]))
        rec = None
        if inc is not None:
            self.cnt[inc] += inc_n
            rec = (inc, inc_n)
        self.events[engine].append((tuple(w), fn, rec))
        return (inc, self.cnt[inc]) if inc is not None else None

    def dma(self, engine, out, in_, waits=(), inc=None):
        w = list(waits)
        if inc is not None and self.cnt[inc] > 0:
            w.append((inc, self.cnt[inc]))
        return self.op(engine, lambda e: e.dma_start(out=out, in_=in_),
                       waits=w, inc=inc, inc_n=16)

    def wait_all(self, *sem_names):
        return [(s, self.cnt[s]) for s in sem_names]

    def barrier(self):
        waits = self.wait_all(*self.sems.keys())
        for e in ("sync", "gpsimd", "tensor", "scalar", "vector"):
            self.pending[e] = list(waits)

    def finalize(self, block):
        if not hasattr(self, "_waited"):
            self._waited = {e: defaultdict(int)
                            for e in ("sync", "gpsimd", "tensor", "scalar", "vector")}
        for engine in ("sync", "gpsimd", "tensor", "scalar", "vector"):
            evs = self.events[engine]
            if not evs:
                continue
            sems = self.sems
            waited = self._waited[engine]

            def make(evs, waited):
                def emit(eng):
                    for waits, fn, inc in evs:
                        merged = {}
                        for wt in waits:
                            assert isinstance(wt, tuple) and len(wt) == 2, f"bad wait {wt}"
                            s, v = wt
                            if v > merged.get(s, 0):
                                merged[s] = v
                        for s, v in merged.items():
                            if v > 0 and v > waited[s]:
                                eng.wait_ge(sems[s], v)
                                waited[s] = v
                        instr = fn(eng)
                        if inc is not None:
                            instr.then_inc(sems[inc[0]], inc[1])
                return emit
            getattr(block, engine)(make(evs, waited))
        self.events = defaultdict(list)


def sem_val(ev, sem):
    return (sem, ev.cnt[sem])


def make_consts():
    c = {}
    c["ident"] = np.eye(128, dtype=np.float32)
    c["ones_row"] = np.ones((1, 128), dtype=np.float32)
    ones2 = np.zeros((128, 2), dtype=np.float32)
    ones2[:64, 0] = 1.0
    ones2[64:, 1] = 1.0
    c["ones2"] = ones2
    c["ones2T"] = ones2.T.copy()
    DH, W = 64, 512
    half = np.arange(0, DH, 2, dtype=np.float64)
    inv_freq = 1.0 / 10000.0 ** (half / DH)
    t = np.arange(NTOK, dtype=np.float64)
    freqs = t[:, None] * inv_freq[None, :]
    freqs = np.concatenate([freqs, freqs], axis=-1)
    base = (half + 0.4 * DH) / (1.4 * DH)
    power = (t - NTOK // 2) / float(W // 2)
    scale = base[None, :] ** power[:, None]
    scale = np.concatenate([scale, scale], axis=-1)
    cos, sin = np.cos(freqs), np.sin(freqs)
    for name, tab in (("cs_q", cos * scale), ("ss_q", sin * scale),
                      ("cs_k", cos / scale), ("ss_k", sin / scale)):
        tt = tab.T.astype(np.float32)
        c[name] = np.concatenate([tt, tt], axis=0)
    j = np.arange(1024)[:, None]
    i = np.arange(512)[None, :]
    allowed = (i <= j) & (j <= i + W)
    c["maskM"] = np.where(allowed, 100.0, -1e30).astype(np.float32)
    return c


CONSTS = make_consts()


def bcast_cols(ev, psum_bc, ones_row, src_row, dst, n, waits=()):
    for c0 in range(0, n, 512):
        w = min(512, n - c0)
        m = ev.op("tensor", lambda e, c0=c0, w=w: e.matmul(
            psum_bc[:, 0:w], ones_row[0:1, :], src_row[0:1, c0:c0 + w],
            start=True, stop=True),
            waits=list(waits) + ev.wait_all("vx"), inc="mm")
        ev.op("vector", lambda e, c0=c0, w=w: e.tensor_copy(
            dst[:, c0:c0 + w], psum_bc[:, 0:w]), waits=[m], inc="vx")


def emit_ln0(nc, ev, ctx, io, pfx=""):
    x = io["x"]
    ident = io["ident"]
    xnT_f = io["xnT_f"]
    xnT_b = io["xnT_b"]
    if 1:
        sb = lambda *a: ctx.enter_context(nc.sbuf_tensor(pfx + a[0], *a[1:]))
        psa = lambda *a: ctx.enter_context(nc.psum_tensor(pfx + a[0], *a[1:]))
        xt = [sb(f"xt{i}", [128, D], F32) for i in range(2)]
        xn = [sb(f"xn{i}", [128, D], F32) for i in range(2)]
        sq = sb("sq", [128, D], F32)
        s1 = sb("s1", [128, 1], F32)
        s2 = sb("s2", [128, 1], F32)
        mu = sb("mu", [128, 1], F32)
        mu2 = sb("mu2", [128, 1], F32)
        var = sb("var", [128, 1], F32)
        rt = sb("rt", [128, 1], F32)
        rr = sb("rr", [128, 1], F32)
        idf = sb("idf", [128, 128], F32)
        epsv = sb("epsv", [128, 1], F32)
        chunks = [sb(f"ch{c}", [128, NTOK], F32) for c in range(ND)]
        tp = [psa(f"tp{i}", [128, 128], F32) for i in range(2)]
        ev.dma("sync", idf[:], ident, inc="ld")
        ep = ev.op("vector", lambda e: e.memset(epsv[:], 1e-5), inc="vx")
        last_reads = [None, None]
        for i in range(NT):
            b = i % 2
            w = [] if last_reads[b] is None else [last_reads[b]]
            ld = ev.dma("sync", xt[b][:], x[i * 128:(i + 1) * 128, :],
                        waits=w, inc="ld")
            sc1 = ev.op("scalar", lambda e, b=b: e.square(sq[:], xt[b][:]),
                        waits=[ld] + ev.wait_all("vx"), inc="sc")
            v1 = ev.op("vector", lambda e, b=b: e.reduce_sum(s1[:], xt[b][:], AX.X),
                       waits=[ld], inc="vx")
            v2 = ev.op("vector", lambda e: e.reduce_sum(s2[:], sq[:], AX.X),
                       waits=[sc1], inc="vx")
            ev.op("vector", lambda e: e.tensor_scalar_mul(mu[:], s1[:], 1.0 / D),
                  waits=[v1], inc="vx")
            sc2 = ev.op("scalar", lambda e: e.square(mu2[:], mu[:]),
                        waits=ev.wait_all("vx"), inc="sc")
            v3 = ev.op("vector", lambda e: e.scalar_tensor_tensor(
                var[:], s2[:], 1.0 / D, mu2[:], ALU.mult, ALU.subtract),
                waits=[sc2, v2], inc="vx")
            sc3 = ev.op("scalar", lambda e: e.activation(
                rt[:], var[:], AF.Sqrt, bias=epsv[:]), waits=[v3, ep], inc="sc")
            v4 = ev.op("vector", lambda e: e.reciprocal(rr[:], rt[:]),
                       waits=[sc3], inc="vx")
            vn = ev.op("vector", lambda e, b=b: e.tensor_scalar(
                xn[b][:], xt[b][:], mu[:], rr[:], ALU.subtract, ALU.mult),
                waits=[v4], inc="vx")
            last_reads[b] = vn
            for c in range(ND):
                pb = c % 2
                m = ev.op("tensor", lambda e, b=b, c=c, pb=pb: e.transpose(
                    tp[pb][:], xn[b][:, c * 128:(c + 1) * 128], idf[:]),
                    waits=[vn] + ev.wait_all("vx"), inc="mm")
                ev.op("vector", lambda e, i=i, c=c, pb=pb: e.tensor_copy(
                    chunks[c][:, i * 128:(i + 1) * 128], tp[pb][:]),
                    waits=[m], inc="vx")
        w = ev.wait_all("vx")
        for c in range(ND):
            ev.dma("sync", xnT_f[c * 128:(c + 1) * 128, :], chunks[c][:],
                   waits=w, inc="st")
            ev.dma("gpsimd", xnT_b[c * 128:(c + 1) * 128, :], chunks[c][:],
                   waits=w, inc="stg")


def emit_l0attn(nc, ev, ctx, io, pfx=""):
    HG = 8
    xnT_b = io["xnT_b"]
    w_qkv = io["w_qkv"]
    w_out = io["w_out"]
    b_out = io["b_out"]
    gamma = io["gamma"]
    beta = io["beta"]
    onesr = io["onesr"]
    s_out = io["s_out"]
    if 1:
        sb = lambda *a: ctx.enter_context(nc.sbuf_tensor(pfx + a[0], *a[1:]))
        psa = lambda *a: ctx.enter_context(nc.psum_tensor(pfx + a[0], *a[1:]))
        xn = [sb(f"xn{c}", [128, NTOK], BF16) for c in range(ND)]
        oall = [sb(f"oall{c}", [128, NTOK], BF16) for c in range(ND)]
        wq = [sb(f"wq{i}", [128, 1024], BF16) for i in range(2)]
        wk = [sb(f"wk{i}", [128, 1024], BF16) for i in range(2)]
        wv = [sb(f"wv{i}", [128, 1024], BF16) for i in range(2)]
        qT = sb("qT", [128, NTOK], BF16)
        kT = sb("kT", [128, NTOK], BF16)
        vsb = sb("vsb", [128, 16 * 132], BF16)
        at = sb("at", [128, 16 * 512], BF16)
        wo = sb("wo", [128, 8 * 1024], BF16)
        brow = sb("brow", [1, 3 * D], F32)
        bbc = sb("bbc", [128, D], F32)
        gbc = sb("gbc", [128, D], F32)
        bebc = sb("bebc", [128, D], F32)
        onesb = sb("onesb", [1, 128], F32)
        rec = sb("rec", [1, 512], F32)
        nbc = sb("nbc", [32, 512], F32)
        stile = sb("stile", [128, D], F32)
        sq = sb("sq", [128, D], F32)
        st1 = sb("st1", [128, 1], F32)
        st2 = sb("st2", [128, 1], F32)
        mu = sb("mu", [128, 1], F32)
        mu2 = sb("mu2", [128, 1], F32)
        var = sb("var", [128, 1], F32)
        rt = sb("rt", [128, 1], F32)
        rr = sb("rr", [128, 1], F32)
        otile = sb("otile", [128, D], F32)

        ps_s = [psa(f"ps_s{i}", [128, 512], F32) for i in range(2)]
        ps_qk = psa("ps_qk", [128, 512], F32)
        ps_o = psa("ps_o", [33, 512], F32)
        ps_b = psa("ps_b", [32, 512], F32)
        ps_bc = psa("ps_bc", [128, 512], F32)
        ps_w = [psa(f"ps_w{i}", [128, 512], F32) for i in range(2)]
        m8v = sb("m8v", [128, 1], F32)
        epsv = sb("epsv", [128, 1], F32)

        ep0 = ev.op("vector", lambda e: e.memset(m8v[:], -8.0), inc="vx")
        ep1 = ev.op("vector", lambda e: e.memset(epsv[:], 1e-6), inc="vx")
        lds = []
        for c in range(ND):
            lds.append(ev.dma("sync", xn[c][:], xnT_b[c * 128:(c + 1) * 128, :],
                              inc="ld"))
        ev.dma("sync", onesb[:], onesr, inc="ld")
        ev.dma("sync", brow[0:1, 0:D], b_out.rearrange("(o d) -> o d", o=1), inc="ld")
        ev.dma("sync", brow[0:1, D:2 * D], gamma.rearrange("(o d) -> o d", o=1), inc="ld")
        ev.dma("sync", brow[0:1, 2 * D:3 * D], beta.rearrange("(o d) -> o d", o=1), inc="ld")
        ev.dma("gpsimd", wo[:].rearrange("p (k n) -> p k n", n=1024),
               w_out.rearrange("(k p) n -> p k n", p=128), inc="wd")
        wrow = ev.wait_all("ld")
        bcast_cols(ev, ps_bc, onesb, brow[0:1, 0:D], bbc, D, waits=wrow)
        bcast_cols(ev, ps_bc, onesb, brow[0:1, D:2 * D], gbc, D, waits=wrow)
        bcast_cols(ev, ps_bc, onesb, brow[0:1, 2 * D:3 * D], bebc, D, waits=wrow)
        vmemset = []
        for t in range(16):
            for hh in range(4):
                vmemset.append(ev.op("vector", lambda e, t=t, hh=hh: e.memset(
                    vsb[:, t * 132 + hh * 33 + 32: t * 132 + hh * 33 + 33], 1.0),
                    inc="vx"))
        def load_group(g, buf, waits=()):
            o = []
            for (dst, c0) in ((wq, g * 128), (wk, D + g * 128), (wv, 2 * D + g * 128)):
                o.append(ev.dma(
                    "gpsimd", dst[buf][:].rearrange("p (k n) -> p k n", n=128),
                    w_qkv[:, c0:c0 + 128].rearrange("(k p) n -> p k n", p=128),
                    waits=waits, inc="wd"))
            return o

        gw = [None, None]
        gw[0] = load_group(0, 0)
        wguard = [None, None]
        qf_guard = [None, None]
        qk_guard = None
        v_guard = None
        for g in range(HG):
            buf = g % 2
            if g + 1 < HG:
                nb = (g + 1) % 2
                gw[nb] = load_group(g + 1, nb,
                                    [wguard[nb]] if wguard[nb] else [])
            wds = gw[buf]
            qk_done = []
            for (dst, wsb) in ((qT, wq), (kT, wk)):
                for tc in range(4):
                    acc = []
                    for c in range(ND):
                        acc.append(ev.op("tensor", lambda e, c=c, tc=tc, wsb=wsb, buf=buf: e.matmul(
                            ps_qk[:], wsb[buf][:, c * 128:(c + 1) * 128],
                            xn[c][:, tc * 512:(tc + 1) * 512],
                            start=(c == 0), stop=(c == ND - 1)),
                            waits=(wds + lds + ([qk_guard] if qk_guard else [])) if c == 0 else [],
                            inc="mm"))
                    qk_guard = ev.op("vector", lambda e, dst=dst, tc=tc: e.tensor_copy(
                        dst[:, tc * 512:(tc + 1) * 512], ps_qk[:]),
                        waits=[acc[-1]], inc="vx")
                    qk_done.append(qk_guard)
            v_done = []
            for t in range(16):
                mv = []
                for c in range(ND):
                    mv.append(ev.op("tensor", lambda e, c=c, t=t, buf=buf: e.matmul(
                        ps_w[t % 2][:, 0:128], xn[c][:, t * 128:(t + 1) * 128],
                        wv[buf][:, c * 128:(c + 1) * 128],
                        start=(c == 0), stop=(c == ND - 1)),
                        waits=([v_guard] if (c == 0 and v_guard) else []), inc="mm"))
                for hh in range(4):
                    v_guard = ev.op("vector", lambda e, t=t, hh=hh: e.tensor_copy(
                        vsb[:, t * 132 + hh * 33: t * 132 + hh * 33 + 32],
                        ps_w[t % 2][:, hh * 32:(hh + 1) * 32]),
                        waits=[mv[-1]] + ([vmemset[-1]] if g == 0 else []), inc="vx")
                    v_done.append(v_guard)
            wguard[buf] = ev.mark("mm")
            for hh in range(4):
                hbase = hh * 32
                tp = (hbase, 0) if hbase == 96 else None
                for qc in range(4):
                    q0 = qc * 512
                    exp_done = []
                    for kt in range(16):
                        pb = kt % 2
                        warw = ev.wait_all("sc") if kt < 2 else [exp_done[kt - 2]]
                        m = ev.op("tensor", lambda e, kt=kt, pb=pb, hbase=hbase, q0=q0, tp=tp: e.matmul(
                            ps_s[pb][:], kT[hbase:hbase + 32, kt * 128:(kt + 1) * 128],
                            qT[hbase:hbase + 32, q0:q0 + 512],
                            start=True, stop=True, tile_position=tp),
                            waits=(qk_done if kt == 0 and qc == 0 else []) + warw,
                            inc="mm")
                        e1 = ev.op("scalar", lambda e, kt=kt, pb=pb: e.activation(
                            at[:, kt * 512:(kt + 1) * 512], ps_s[pb][:], AF.Exp,
                            bias=m8v[:], scale=float(32 ** -0.5)),
                            waits=[m, ep0], inc="sc")
                        exp_done.append(e1)
                    mavl = None
                    for kt in range(16):
                        mavl = ev.op("tensor", lambda e, kt=kt, hh=hh: e.matmul(
                            ps_o[:], vsb[:, kt * 132 + hh * 33: kt * 132 + hh * 33 + 33],
                            at[:, kt * 512:(kt + 1) * 512],
                            start=(kt == 0), stop=(kt == 15)),
                            waits=[exp_done[kt]] + ((v_done + ev.wait_all("vx")) if kt == 0 else []),
                            inc="mm")
                    r1 = ev.op("vector", lambda e: e.reciprocal(rec[:], ps_o[32:33, :]),
                               waits=[mavl], inc="vx")
                    mb = ev.op("tensor", lambda e: e.matmul(
                        ps_b[:], onesb[0:1, 0:32], rec[0:1, :], start=True, stop=True),
                        waits=[r1], inc="mm")
                    c1 = ev.op("vector", lambda e: e.tensor_copy(nbc[:], ps_b[:]),
                               waits=[mb], inc="vx")
                    ev.op("vector", lambda e, g=g, hbase=hbase, q0=q0: e.tensor_tensor(
                        oall[g][hbase:hbase + 32, q0:q0 + 512], ps_o[0:32, :], nbc[:],
                        ALU.mult), waits=[c1], inc="vx")
        oall_done = ev.wait_all("vx")
        wo_ld = ev.wait_all("wd")
        prev_store = None
        for t in range(NT):
            sready = []
            for n in range(2):
                acc = None
                for c in range(ND):
                    acc = ev.op("tensor", lambda e, c=c, t=t, n=n: e.matmul(
                        ps_w[n][:], oall[c][:, t * 128:(t + 1) * 128],
                        wo[:, c * 1024 + n * 512: c * 1024 + (n + 1) * 512],
                        start=(c == 0), stop=(c == ND - 1)),
                        waits=(oall_done + wo_ld) if (t == 0 and c == 0) else
                              (ev.wait_all("vx") if c == 0 else []),
                        inc="mm")
                sready.append(ev.op("vector", lambda e, n=n: e.tensor_tensor(
                    stile[:, n * 512:(n + 1) * 512], ps_w[n][:],
                    bbc[:, n * 512:(n + 1) * 512], ALU.add),
                    waits=[acc], inc="vx"))
            sc1 = ev.op("scalar", lambda e: e.square(sq[:], stile[:]),
                        waits=sready, inc="sc")
            v1 = ev.op("vector", lambda e: e.reduce_sum(st1[:], stile[:], AX.X),
                       waits=sready, inc="vx")
            v2 = ev.op("vector", lambda e: e.reduce_sum(st2[:], sq[:], AX.X),
                       waits=[sc1], inc="vx")
            ev.op("vector", lambda e: e.tensor_scalar_mul(mu[:], st1[:], 1.0 / D),
                  waits=[v1], inc="vx")
            sc2 = ev.op("scalar", lambda e: e.square(mu2[:], mu[:]),
                        waits=ev.wait_all("vx"), inc="sc")
            v3 = ev.op("vector", lambda e: e.scalar_tensor_tensor(
                var[:], st2[:], 1.0 / D, mu2[:], ALU.mult, ALU.subtract),
                waits=[sc2, v2], inc="vx")
            sc3 = ev.op("scalar", lambda e: e.activation(
                rt[:], var[:], AF.Sqrt, bias=epsv[:]), waits=[v3, ep1], inc="sc")
            v4 = ev.op("vector", lambda e: e.reciprocal(rr[:], rt[:]),
                       waits=[sc3], inc="vx")
            v5 = ev.op("vector", lambda e: e.tensor_scalar(
                sq[:], stile[:], mu[:], rr[:], ALU.subtract, ALU.mult),
                waits=[v4], inc="vx")
            v6 = ev.op("vector", lambda e: e.tensor_tensor(
                sq[:], sq[:], gbc[:], ALU.mult), waits=[v5], inc="vx")
            v7 = ev.op("vector", lambda e: e.tensor_tensor(
                otile[:], sq[:], bebc[:], ALU.add),
                waits=[v6] + ([prev_store] if prev_store else []), inc="vx")
            ev.dma("sync", s_out[t * 128:(t + 1) * 128, :], otile[:],
                   waits=[v7], inc="st")
            prev_store = ev.mark("st")


def emit_ffn_attn(nc, ev, ctx, io, pfx=""):
    xfT = io["xfT"]
    wqkv = io["wqkv"]
    qs_i = io["qs_i"]
    ks_i = io["ks_i"]
    wo_i = io["wo_i"]
    onesr = io["onesr"]
    ones2 = io["ones2"]
    ones2T = io["ones2T"]
    maskM = io["maskM"]
    ropes = io["ropes"]
    xfT_o = io["xfT_o"]
    if 1:
        sb = lambda *a: ctx.enter_context(nc.sbuf_tensor(pfx + a[0], *a[1:]))
        psa = lambda *a: ctx.enter_context(nc.psum_tensor(pfx + a[0], *a[1:]))
        xf = [sb(f"xf{c}", [128, NTOK], F32) for c in range(ND)]
        hb = [sb(f"hb{c}", [128, NTOK], BF16) for c in range(ND)]
        tabs = {n: sb(n + "_s", [128, NTOK], BF16) for n in ropes}
        mask = sb("mask", [128, 8 * 512], BF16)
        wq = [sb(f"wq{i}", [128, 1024], BF16) for i in range(2)]
        wk = [sb(f"wk{i}", [128, 1024], BF16) for i in range(2)]
        wv = [sb(f"wv{i}", [128, 1024], BF16) for i in range(2)]
        wo = sb("wo", [128, 4 * 1024], BF16)
        qf = [sb(f"qf{i}", [128, 512], F32) for i in range(2)]
        qTb = sb("qTb", [128, NTOK], BF16)
        kTb = sb("kTb", [128, NTOK], BF16)
        vsb = sb("vsb", [128, 16 * 130], BF16)
        at = sb("at", [128, 8 * 512], BF16)
        rot = sb("rot", [128, 512], F32)
        tmp = sb("tmp", [128, 512], F32)
        tmp2 = sb("tmp2", [128, 512], F32)
        tmpm = [sb(f"tmpm{i}", [128, 512], F32) for i in range(2)]
        sqv = sb("sqv", [128, 512], F32)
        ssum = sb("ssum", [2, 512], F32)
        srt = sb("srt", [2, 512], F32)
        qsv = sb("qsv", [128, 1], F32)
        ksv = sb("ksv", [128, 1], F32)
        onesb = sb("onesb", [1, 128], F32)
        o2 = sb("o2", [128, 2], F32)
        o2T = sb("o2T", [2, 128], F32)
        rec = sb("rec", [1, 512], F32)
        nbc = sb("nbc", [64, 512], F32)
        oall = [sb(f"oall{c}", [128, NTOK], BF16) for c in range(4)]

        ps_s = [psa(f"ps_s{i}", [128, 512], F32) for i in range(2)]
        ps_o = psa("ps_o", [65, 512], F32)
        ps_b = psa("ps_b", [128, 512], F32)
        ps_st = psa("ps_st", [2, 512], F32)
        ps_w = [psa(f"ps_w{i}", [128, 512], F32) for i in range(2)]
        m30v = sb("m30v", [128, 1], F32)
        eps24 = sb("eps24", [128, 1], F32)

        ep0 = ev.op("vector", lambda e: e.memset(m30v[:], -30.0), inc="vx")
        ep1 = ev.op("vector", lambda e: e.memset(eps24[:], 1e-24), inc="vx")
        lds = []
        for c in range(ND):
            lds.append(ev.dma("sync", xf[c][:], xfT[c * 128:(c + 1) * 128, :], inc="ld"))
        tab_lds = []
        for n in ropes:
            tab_lds.append(ev.dma("gpsimd", tabs[n][:], ropes[n], inc="wd"))
        for kt in range(8):
            tab_lds.append(ev.dma("gpsimd", mask[:, kt * 512:(kt + 1) * 512],
                                  maskM[kt * 128:(kt + 1) * 128, :], inc="wd"))
        ev.dma("sync", onesb[:], onesr, inc="ld")
        ev.dma("sync", o2[:], ones2, inc="ld")
        ev.dma("sync", o2T[:], ones2T, inc="ld")
        ev.dma("sync", qsv[0:64, 0:1], qs_i.rearrange("(d o) -> d o", o=1), inc="ld")
        ev.dma("sync", qsv[64:128, 0:1], qs_i.rearrange("(d o) -> d o", o=1), inc="ld")
        ev.dma("sync", ksv[0:64, 0:1], ks_i.rearrange("(d o) -> d o", o=1), inc="ld")
        ev.dma("sync", ksv[64:128, 0:1], ks_i.rearrange("(d o) -> d o", o=1), inc="ld")
        ev.dma("gpsimd", wo[:].rearrange("p (k n) -> p k n", n=1024),
               wo_i.rearrange("(k p) n -> p k n", p=128), inc="wd")
        lds_all = ev.wait_all("ld")
        hdone = []
        for c in range(ND):
            if c < 4:
                hdone.append(ev.op("vector", lambda e, c=c: e.tensor_copy(
                    hb[c][:], xf[c][:]), waits=[lds[c]], inc="vx"))
            else:
                ev.op("vector", lambda e, c=c: e.memset(hb[c][:, 0:1], 0.0), inc="vx")
                hdone.append(ev.op("vector", lambda e, c=c: e.tensor_copy(
                    hb[c][:, 1:NTOK], xf[c][:, 0:NTOK - 1]), waits=[lds[c]], inc="vx"))

        def load_group(g, buf, waits=()):
            o = []
            for (dst, c0) in ((wq, g * 128), (wk, 512 + g * 128), (wv, 1024 + g * 128)):
                o.append(ev.dma(
                    "gpsimd", dst[buf][:].rearrange("p (k n) -> p k n", n=128),
                    wqkv[:, c0:c0 + 128].rearrange("(k p) n -> p k n", p=128),
                    waits=waits, inc="wd"))
            return o

        gw = [None, None]
        gw[0] = load_group(0, 0)
        wguard = [None, None]
        qf_guard = [None, None]
        vset = []
        for t in range(16):
            for hh in range(2):
                vset.append(ev.op("vector", lambda e, t=t, hh=hh: e.memset(
                    vsb[:, t * 130 + hh * 65 + 64: t * 130 + hh * 65 + 65], 1.0),
                    inc="vx"))
        wo_acc_done = None
        for g in range(4):
            buf = g % 2
            if g + 1 < 4:
                nb = (g + 1) % 2
                gw[nb] = load_group(g + 1, nb,
                                    [wguard[nb]] if wguard[nb] else [])
            wds = gw[buf]
            qk_done = []
            for (dstb, wsb, scl, csn, ssn) in (
                    (qTb, wq, qsv, "cs_q", "ss_q"), (kTb, wk, ksv, "cs_k", "ss_k")):
                raw_done = []
                for tc in range(4):
                    t0 = tc * 512
                    acc = None
                    for c in range(ND):
                        acc = ev.op("tensor", lambda e, c=c, tc=tc, wsb=wsb, buf=buf: e.matmul(
                            ps_w[tc % 2][:], wsb[buf][:, c * 128:(c + 1) * 128],
                            hb[c][:, tc * 512:(tc + 1) * 512],
                            start=(c == 0), stop=(c == ND - 1)),
                            waits=(wds + hdone + ev.wait_all("vx")) if c == 0 else [],
                            inc="mm")
                    raw_done.append(ev.op("vector", lambda e, tc=tc: e.tensor_copy(
                        qf[tc % 2][:], ps_w[tc % 2][:]),
                        waits=[acc] + ([qf_guard[tc % 2]] if qf_guard[tc % 2] else []),
                        inc="vx"))
                    s1 = ev.op("scalar", lambda e, tc=tc: e.square(
                        sqv[:], qf[tc % 2][:]), waits=[raw_done[tc]], inc="sc")
                    qf_guard[tc % 2] = s1
                    m1 = ev.op("tensor", lambda e: e.matmul(
                        ps_st[:], o2[:], sqv[:], start=True, stop=True),
                        waits=[s1] + lds_all, inc="mm")
                    v1 = ev.op("vector", lambda e: e.tensor_copy(ssum[:], ps_st[:]),
                               waits=[m1], inc="vx")
                    s2 = ev.op("scalar", lambda e: e.activation(
                        srt[:], ssum[:], AF.Sqrt, bias=eps24[0:2, 0:1]),
                        waits=[v1, ep1], inc="sc")
                    v2 = ev.op("vector", lambda e: e.reciprocal(srt[:], srt[:]),
                               waits=[s2], inc="vx")
                    m2 = ev.op("tensor", lambda e: e.matmul(
                        ps_b[:], o2T[:], srt[:], start=True, stop=True),
                        waits=[v2], inc="mm")
                    v3 = ev.op("vector", lambda e, tc=tc, scl=scl: e.scalar_tensor_tensor(
                        tmp[:], qf[tc % 2][:], scl[:], ps_b[:],
                        ALU.mult, ALU.mult), waits=[m2], inc="vx")
                    r1 = ev.op("vector", lambda e: e.tensor_scalar_mul(
                        rot[0:32, :], tmp[32:64, :], -1.0), waits=[v3], inc="vx")
                    r2 = ev.op("vector", lambda e: e.tensor_copy(
                        rot[32:64, :], tmp[0:32, :]), waits=[v3], inc="vx")
                    r3 = ev.op("vector", lambda e: e.tensor_scalar_mul(
                        rot[64:96, :], tmp[96:128, :], -1.0), waits=[v3], inc="vx")
                    r4 = ev.op("vector", lambda e: e.tensor_copy(
                        rot[96:128, :], tmp[64:96, :]), waits=[v3], inc="vx")
                    v4 = ev.op("vector", lambda e, t0=t0, ssn=ssn: e.tensor_tensor(
                        rot[:], rot[:], tabs[ssn][:, t0:t0 + 512], ALU.mult),
                        waits=[r1, r2, r3, r4] + tab_lds, inc="vx")
                    v5 = ev.op("vector", lambda e, t0=t0, csn=csn: e.tensor_tensor(
                        tmp2[:], tmp[:], tabs[csn][:, t0:t0 + 512], ALU.mult),
                        waits=[v3], inc="vx")
                    ev.op("vector", lambda e, t0=t0, dstb=dstb: e.tensor_tensor(
                        dstb[:, t0:t0 + 512], tmp2[:], rot[:], ALU.add),
                        waits=[v4, v5], inc="vx")
            qk_done = ev.wait_all("vx")
            v_done = []
            for t in range(16):
                mv = None
                for c in range(ND):
                    mv = ev.op("tensor", lambda e, c=c, t=t, buf=buf: e.matmul(
                        ps_w[t % 2][:, 0:128], hb[c][:, t * 128:(t + 1) * 128],
                        wv[buf][:, c * 128:(c + 1) * 128],
                        start=(c == 0), stop=(c == ND - 1)),
                        waits=ev.wait_all("vx") if c == 0 else [], inc="mm")
                for hh in range(2):
                    v_done.append(ev.op("vector", lambda e, t=t, hh=hh: e.tensor_copy(
                        vsb[:, t * 130 + hh * 65: t * 130 + hh * 65 + 64],
                        ps_w[t % 2][:, hh * 64:(hh + 1) * 64]),
                        waits=[mv] + ([vset[-1]] if g == 0 else []), inc="vx"))
            wguard[buf] = ev.mark("mm")
            for hh in range(2):
                hb64 = hh * 64
                for w in range(4):
                    q0 = w * 512
                    kt_lo = 4 if w == 0 else 0
                    exp_done = {}
                    for kt in range(kt_lo, 8):
                        ktok0 = (w - 1) * 512 + kt * 128
                        pb = kt % 2
                        warw = ev.wait_all("vx") if kt < kt_lo + 2 else \
                            ([exp_done[kt - 2]] if kt - 2 in exp_done else [])
                        m = ev.op("tensor", lambda e, ktok0=ktok0, pb=pb, hb64=hb64, q0=q0: e.matmul(
                            ps_s[pb][:], kTb[hb64:hb64 + 64, ktok0:ktok0 + 128],
                            qTb[hb64:hb64 + 64, q0:q0 + 512],
                            start=True, stop=True),
                            waits=(qk_done if (kt == kt_lo and w == 0 and g >= 0) else []) + warw,
                            inc="mm")
                        v1 = ev.op("vector", lambda e, kt=kt, pb=pb: e.scalar_tensor_tensor(
                            tmpm[kt % 2][:], ps_s[pb][:], 8.0, mask[:, kt * 512:(kt + 1) * 512],
                            ALU.mult, ALU.min), waits=[m] + tab_lds, inc="vx")
                        e1 = ev.op("scalar", lambda e, kt=kt: e.activation(
                            at[:, kt * 512:(kt + 1) * 512], tmpm[kt % 2][:], AF.Exp,
                            bias=m30v[:]), waits=[v1, ep0], inc="sc")
                        exp_done[kt] = e1
                    mavl = None
                    first = True
                    for kt in range(kt_lo, 8):
                        ktok0 = (w - 1) * 512 + kt * 128
                        vt = ktok0 // 128
                        mavl = ev.op("tensor", lambda e, kt=kt, vt=vt, hh=hh, first=first: e.matmul(
                            ps_o[:], vsb[:, vt * 130 + hh * 65: vt * 130 + hh * 65 + 65],
                            at[:, kt * 512:(kt + 1) * 512],
                            start=first, stop=(kt == 7)),
                            waits=[exp_done[kt]] + ((v_done + ev.wait_all("vx")) if first else []),
                            inc="mm")
                        first = False
                    r1 = ev.op("vector", lambda e: e.reciprocal(rec[:], ps_o[64:65, :]),
                               waits=[mavl], inc="vx")
                    mb = ev.op("tensor", lambda e: e.matmul(
                        ps_b[0:64, :], onesb[0:1, 0:64], rec[0:1, :],
                        start=True, stop=True), waits=[r1], inc="mm")
                    c1 = ev.op("vector", lambda e: e.tensor_copy(nbc[:], ps_b[0:64, :]),
                               waits=[mb], inc="vx")
                    ev.op("vector", lambda e, g=g, hb64=hb64, q0=q0: e.tensor_tensor(
                        oall[g][hb64:hb64 + 64, q0:q0 + 512], ps_o[0:64, :],
                        nbc[:], ALU.mult), waits=[c1], inc="vx")
        oall_done = ev.wait_all("vx")
        wo_ld = ev.wait_all("wd")
        res_done = []
        for dc in range(ND):
            for tc in range(4):
                acc = None
                for c in range(4):
                    acc = ev.op("tensor", lambda e, c=c, dc=dc, tc=tc: e.matmul(
                        ps_w[tc % 2][:], wo[:, c * 1024 + dc * 128: c * 1024 + dc * 128 + 128],
                        oall[c][:, tc * 512:(tc + 1) * 512],
                        start=(c == 0), stop=(c == 3)),
                        waits=((oall_done + wo_ld) if (dc == 0 and tc == 0)
                               else ev.wait_all("vx")) if c == 0 else [],
                        inc="mm")
                res_done.append(ev.op("vector", lambda e, dc=dc, tc=tc: e.tensor_tensor(
                    xf[dc][:, tc * 512:(tc + 1) * 512], ps_w[tc % 2][:],
                    xf[dc][:, tc * 512:(tc + 1) * 512], ALU.add),
                    waits=[acc], inc="vx"))
        for c in range(ND):
            ev.dma("sync", xfT_o[c * 128:(c + 1) * 128, :], xf[c][:],
                   waits=res_done, inc="st")



def emit_ffn_ffn(nc, ev, ctx, io, out_normal, pfx=""):
    NH = 22
    hsz = [128] * 21 + [42]
    xfT = io["xfT"]
    lg_i = io["lg_i"]
    lb_i = io["lb_i"]
    w1_i = io["w1_i"]
    b1_i = io["b1_i"]
    w2_i = io["w2_i"]
    b2_i = io["b2_i"]
    onesr = io["onesr"]
    ident = io["ident"]
    outp = io["outp"]
    if 1:
        sb = lambda *a: ctx.enter_context(nc.sbuf_tensor(pfx + a[0], *a[1:]))
        psa = lambda *a: ctx.enter_context(nc.psum_tensor(pfx + a[0], *a[1:]))
        xf = [sb(f"xf{c}", [128, NTOK], F32) for c in range(ND)]
        xh = [sb(f"xh{c}", [128, NTOK], BF16) for c in range(ND)]
        patch = [sb(f"patch{i}", [128, 512], F32) for i in range(4)]
        w2sb = sb("w2sb", [128, NH * 1024], BF16)
        w1a = [sb(f"w1a{i}", [128, 1024], BF16) for i in range(2)]
        w1g = [sb(f"w1g{i}", [128, 1024], BF16) for i in range(2)]
        acts = sb("acts", [128, NH * 512], BF16)
        lgv = sb("lgv", [128, ND], F32)
        lbv = sb("lbv", [128, ND], F32)
        b2v = sb("b2v", [128, ND], F32)
        b1av = sb("b1av", [128, NH], F32)
        b1gv = sb("b1gv", [128, NH], F32)
        onesb = sb("onesb", [1, 128], F32)
        onesc = sb("onesc", [128, 1], F32)
        sqv = [sb(f"sqv{i}", [128, 512], F32) for i in range(2)]
        murow = sb("murow", [1, 512], F32)
        e2row = sb("e2row", [1, 512], F32)
        musq = sb("musq", [1, 512], F32)
        varrow = sb("varrow", [1, 512], F32)
        rrow = sb("rrow", [1, 512], F32)
        mubc = sb("mubc", [128, 512], F32)
        rbc = sb("rbc", [128, 512], F32)
        tta = sb("tta", [128, 512], F32)
        gbuf = [sb(f"gbuf{i}", [128, 512], BF16) for i in range(2)]
        idf = sb("idf", [128, 128], F32)
        gtile = sb("gtile", [128, D], F32)

        ps_S = psa("ps_S", [1, 512], F32)
        ps_Q = psa("ps_Q", [1, 512], F32)
        ps_bc = psa("ps_bc", [128, 512], F32)
        ps_a = psa("ps_a", [128, 512], F32)
        ps_g = psa("ps_g", [128, 512], F32)
        ps_out = [psa(f"ps_out{i}", [128, 512], F32) for i in range(2)]
        ps_t = psa("ps_t", [128, 128], F32)
        epsv = sb("epsv", [128, 1], F32)

        ep0 = ev.op("vector", lambda e: e.memset(epsv[:], 1e-5), inc="vx")
        lds = []
        for c in range(ND):
            lds.append(ev.dma("sync", xf[c][:], xfT[c * 128:(c + 1) * 128, :], inc="ld"))
        ev.dma("sync", onesb[:], onesr, inc="ld")
        ev.dma("sync", idf[:], ident, inc="ld")
        for c in range(ND):
            ev.dma("sync", lgv[:, c:c + 1], lg_i.rearrange("(d o) -> d o", o=1)[c * 128:(c + 1) * 128], inc="ld")
            ev.dma("sync", lbv[:, c:c + 1], lb_i.rearrange("(d o) -> d o", o=1)[c * 128:(c + 1) * 128], inc="ld")
            ev.dma("sync", b2v[:, c:c + 1], b2_i.rearrange("(d o) -> d o", o=1)[c * 128:(c + 1) * 128], inc="ld")
        for c in range(NH):
            sz = hsz[c]
            ev.dma("sync", b1av[0:sz, c:c + 1],
                   b1_i.rearrange("(d o) -> d o", o=1)[c * 128:c * 128 + sz], inc="ld")
            ev.dma("sync", b1gv[0:sz, c:c + 1],
                   b1_i.rearrange("(d o) -> d o", o=1)[2730 + c * 128:2730 + c * 128 + sz], inc="ld")
        w2ld = [ev.dma("gpsimd",
                       w2sb[:, 0:21 * 1024].rearrange("p (c n) -> p c n", n=1024),
                       w2_i[0:2688, :].rearrange("(c p) n -> p c n", p=128), inc="wd"),
                ev.dma("gpsimd", w2sb[0:42, 21 * 1024:22 * 1024],
                       w2_i[2688:2730, :], inc="wd")]
        mset = ev.op("vector", lambda e: e.memset(onesc[:], 1.0), inc="vx")
        lds_all = ev.wait_all("ld")
        pdone = []
        for c in range(4):
            ev.op("vector", lambda e, c=c: e.memset(patch[c][:, 0:1], 0.0), inc="vx")
            pdone.append(ev.op("vector", lambda e, c=c: e.tensor_copy(
                patch[c][:, 1:512], xf[c + 4][:, 0:511]), waits=[lds[c + 4]], inc="vx"))

        def h2slice(c, t0, wdt=512):
            if c < 4:
                return xf[c][:, t0:t0 + wdt]
            if t0 == 0:
                return patch[c - 4][:, 0:wdt]
            return xf[c][:, t0 - 1:t0 - 1 + wdt]

        for tc in range(4):
            t0 = tc * 512
            mS = None
            mQ = None
            sq_done = {}
            for c in range(ND):
                mS = ev.op("tensor", lambda e, c=c, t0=t0: e.matmul(
                    ps_S[:], onesc[:], h2slice(c, t0), start=(c == 0), stop=(c == ND - 1)),
                    waits=(lds_all + pdone + [mset] + ev.wait_all("vx")) if c == 0 else [],
                    inc="mm")
            for c in range(ND):
                sq1 = ev.op("scalar", lambda e, c=c, t0=t0: e.square(
                    sqv[c % 2][:], h2slice(c, t0)),
                    waits=(sq_done[c - 2] if c >= 2 else lds_all + pdone),
                    inc="sc")
                mQ = ev.op("tensor", lambda e, c=c: e.matmul(
                    ps_Q[:], onesc[:], sqv[c % 2][:], start=(c == 0), stop=(c == ND - 1)),
                    waits=[sq1], inc="mm")
                sq_done[c] = [mQ]
            v1 = ev.op("vector", lambda e: e.tensor_scalar_mul(murow[:], ps_S[:], 1.0 / D),
                       waits=[mS], inc="vx")
            v2 = ev.op("vector", lambda e: e.tensor_scalar_mul(e2row[:], ps_Q[:], 1.0 / D),
                       waits=[mQ], inc="vx")
            s2 = ev.op("scalar", lambda e: e.square(musq[:], murow[:]), waits=[v1], inc="sc")
            v3 = ev.op("vector", lambda e: e.tensor_sub(varrow[:], e2row[:], musq[:]),
                       waits=[s2, v2], inc="vx")
            s3 = ev.op("scalar", lambda e: e.activation(rrow[:], varrow[:], AF.Sqrt,
                                                        bias=epsv[0:1, 0:1]),
                       waits=[v3, ep0], inc="sc")
            v4 = ev.op("vector", lambda e: e.reciprocal(rrow[:], rrow[:]), waits=[s3], inc="vx")
            mb1 = ev.op("tensor", lambda e: e.matmul(
                ps_bc[:], onesb[0:1, :], murow[0:1, :], start=True, stop=True),
                waits=[v1] + ev.wait_all("vx"), inc="mm")
            vb1 = ev.op("vector", lambda e: e.tensor_copy(mubc[:], ps_bc[:]),
                        waits=[mb1], inc="vx")
            mb2 = ev.op("tensor", lambda e: e.matmul(
                ps_bc[:], onesb[0:1, :], rrow[0:1, :], start=True, stop=True),
                waits=[v4, vb1], inc="mm")
            vb2 = ev.op("vector", lambda e: e.tensor_copy(rbc[:], ps_bc[:]),
                        waits=[mb2], inc="vx")
            for c in range(ND):
                va = ev.op("vector", lambda e, c=c, t0=t0: e.tensor_sub(
                    tta[:], h2slice(c, t0), mubc[:]), waits=[vb1, vb2], inc="vx")
                vb = ev.op("vector", lambda e: e.tensor_mul(tta[:], tta[:], rbc[:]),
                           waits=[va], inc="vx")
                ev.op("vector", lambda e, c=c, t0=t0: e.tensor_scalar(
                    xh[c][:, t0:t0 + 512], tta[:], lgv[:, c:c + 1], lbv[:, c:c + 1],
                    ALU.mult, ALU.add), waits=[vb], inc="vx")
        xh_done = ev.wait_all("vx")

        def load_w1(c, buf, waits):
            sz = hsz[c]
            o = []
            for (dst, c0) in ((w1a, c * 128), (w1g, 2730 + c * 128)):
                o.append(ev.dma(
                    "gpsimd",
                    dst[buf][:].rearrange("p (k n) -> p k n", n=128)[:, :, 0:sz],
                    w1_i[:, c0:c0 + sz].rearrange("(k p) n -> p k n", p=128),
                    waits=waits, inc="wd"))
            return o

        w1ld = {}
        w1ld[0] = load_w1(0, 0, [])
        w1ld[1] = load_w1(1, 1, [])
        for tc in range(4):
            t0 = tc * 512
            act_done = {}
            mm_done = {}
            for c in range(NH):
                sz = hsz[c]
                buf = c % 2
                nxt = c + 2
                ma = mg = None
                for k in range(ND):
                    ma = ev.op("tensor", lambda e, k=k, c=c, t0=t0, buf=buf, sz=sz: e.matmul(
                        ps_a[0:sz, :], w1a[buf][:, k * 128:k * 128 + sz],
                        xh[k][:, t0:t0 + 512], start=(k == 0), stop=(k == ND - 1)),
                        waits=(w1ld[c] + (xh_done if (tc == 0 and c == 0) else []) +
                               (act_done.get(c - 1, []))) if k == 0 else [],
                        inc="mm")
                for k in range(ND):
                    mg = ev.op("tensor", lambda e, k=k, c=c, t0=t0, buf=buf, sz=sz: e.matmul(
                        ps_g[0:sz, :], w1g[buf][:, k * 128:k * 128 + sz],
                        xh[k][:, t0:t0 + 512], start=(k == 0), stop=(k == ND - 1)),
                        waits=(act_done.get(c - 1, [])) if k == 0 else [],
                        inc="mm")
                mm_done[c] = [ma, mg]
                if tc < 3 or True:
                    nc_load = c + 2
                    if nc_load < NH:
                        w1ld[nc_load] = load_w1(nc_load, nc_load % 2, mm_done.get(c, []))
                    elif tc < 3 and nc_load - NH < 2:
                        w1ld[nc_load - NH] = load_w1(nc_load - NH, (nc_load - NH) % 2,
                                                     mm_done.get(c, []))
                g1 = ev.op("scalar", lambda e, c=c, sz=sz, buf=buf: e.activation(
                    gbuf[buf][0:sz, :], ps_g[0:sz, :], AF.Gelu,
                    bias=b1gv[0:sz, c:c + 1]), waits=[mg], inc="sc")
                a1 = ev.op("vector", lambda e, c=c, sz=sz, buf=buf: e.scalar_tensor_tensor(
                    acts[0:sz, c * 512:(c + 1) * 512], ps_a[0:sz, :],
                    b1av[0:sz, c:c + 1], gbuf[buf][0:sz, :], ALU.add, ALU.mult),
                    waits=[g1], inc="vx")
                act_done[c] = [a1]
            all_act = ev.wait_all("vx")
            ev_last = []
            for dc in range(ND):
                mo = None
                for c in range(NH):
                    sz = hsz[c]
                    mo = ev.op("tensor", lambda e, c=c, dc=dc, sz=sz: e.matmul(
                        ps_out[dc % 2][:], w2sb[0:sz, c * 1024 + dc * 128: c * 1024 + dc * 128 + 128],
                        acts[0:sz, c * 512:(c + 1) * 512],
                        start=(c == 0), stop=(c == NH - 1)),
                        waits=(all_act + w2ld + (ev_last[-2:-1] if dc >= 2 else [])) if c == 0 else [],
                        inc="mm")
                ev_last.append(ev.op("vector", lambda e, dc=dc, t0=t0: e.scalar_tensor_tensor(
                    xf[dc][:, t0:t0 + 512], ps_out[dc % 2][:], b2v[:, dc:dc + 1],
                    xf[dc][:, t0:t0 + 512], ALU.add, ALU.add), waits=[mo], inc="vx"))
        res_done = ev.wait_all("vx")
        if not out_normal:
            for c in range(ND):
                ev.dma("sync", outp[c * 128:(c + 1) * 128, :], xf[c][:],
                       waits=res_done, inc="st")
        else:
            tr_prev = None
            for t in range(NT):
                gts = []
                for c in range(ND):
                    m = ev.op("tensor", lambda e, c=c, t=t: e.transpose(
                        ps_t[:], xf[c][:, t * 128:(t + 1) * 128], idf[:]),
                        waits=(res_done if (t == 0 and c == 0) else []) +
                              ([tr_prev] if tr_prev else []), inc="mm")
                    tr_prev = ev.op("vector", lambda e, c=c: e.tensor_copy(
                        gtile[:, c * 128:(c + 1) * 128], ps_t[:]), waits=[m], inc="vx")
                    gts.append(tr_prev)
                ev.dma("sync", outp[t * 128:(t + 1) * 128, :], gtile[:],
                       waits=gts, inc="st")
                tr_prev = ev.mark("st")


def emit_comb1(nc, ev, ctx, io, pfx=""):
    x_i = io["x_i"]
    s_i = io["s_i"]
    g_i = io["g_i"]
    ident = io["ident"]
    x1_o = io["x1_o"]
    xnTf_o = io["xnTf_o"]
    xnTb_o = io["xnTb_o"]
    xnN_o = io["xnN_o"]
    if 1:
        sb = lambda *a: ctx.enter_context(nc.sbuf_tensor(pfx + a[0], *a[1:]))
        psa = lambda *a: ctx.enter_context(nc.psum_tensor(pfx + a[0], *a[1:]))
        xt = [sb(f"xt{i}", [128, D], F32) for i in range(2)]
        st = [sb(f"st{i}", [128, D], F32) for i in range(2)]
        gt = [sb(f"gt{i}", [128, D], F32) for i in range(2)]
        gs = sb("gs", [128, D], F32)
        x1t = [sb(f"x1t{i}", [128, D], F32) for i in range(2)]
        xnt = [sb(f"xnt{i}", [128, D], F32) for i in range(2)]
        sq = sb("sq", [128, D], F32)
        s1 = sb("s1", [128, 1], F32)
        s2 = sb("s2", [128, 1], F32)
        mu = sb("mu", [128, 1], F32)
        mu2 = sb("mu2", [128, 1], F32)
        var = sb("var", [128, 1], F32)
        rt = sb("rt", [128, 1], F32)
        rr = sb("rr", [128, 1], F32)
        idf = sb("idf", [128, 128], F32)
        chunks = [sb(f"ch{c}", [128, NTOK], F32) for c in range(ND)]
        epsv = sb("epsv", [128, 1], F32)
        tp = [psa(f"tp{i}", [128, 128], F32) for i in range(2)]
        ev.dma("sync", idf[:], ident, inc="ld")
        ep = ev.op("vector", lambda e: e.memset(epsv[:], 1e-5), inc="vx")
        guard = [None, None]
        for i in range(NT):
            b = i % 2
            w = [] if guard[b] is None else list(guard[b])
            l1 = ev.dma("sync", xt[b][:], x_i[i * 128:(i + 1) * 128, :], waits=w, inc="ld")
            l2 = ev.dma("sync", st[b][:], s_i[i * 128:(i + 1) * 128, :], waits=w, inc="ld")
            l3 = ev.dma("gpsimd", gt[b][:], g_i[i * 128:(i + 1) * 128, :], waits=w, inc="ldg")
            sc0 = ev.op("scalar", lambda e, b=b: e.activation(
                gs[:], gt[b][:], AF.Silu), waits=[l3] + ev.wait_all("vx"), inc="sc")
            v0 = ev.op("vector", lambda e, b=b: e.tensor_mul(gs[:], st[b][:], gs[:]),
                       waits=[sc0, l2], inc="vx")
            v0b = ev.op("vector", lambda e, b=b: e.tensor_add(x1t[b][:], xt[b][:], gs[:]),
                        waits=[v0, l1], inc="vx")
            d1 = ev.dma("sync", x1_o[i * 128:(i + 1) * 128, :], x1t[b][:],
                        waits=[v0b], inc="st")
            sc1 = ev.op("scalar", lambda e, b=b: e.square(sq[:], x1t[b][:]),
                        waits=[v0b], inc="sc")
            v1 = ev.op("vector", lambda e, b=b: e.reduce_sum(s1[:], x1t[b][:], AX.X),
                       waits=[v0b], inc="vx")
            v2 = ev.op("vector", lambda e: e.reduce_sum(s2[:], sq[:], AX.X),
                       waits=[sc1], inc="vx")
            ev.op("vector", lambda e: e.tensor_scalar_mul(mu[:], s1[:], 1.0 / D),
                  waits=[v1], inc="vx")
            sc2 = ev.op("scalar", lambda e: e.square(mu2[:], mu[:]),
                        waits=ev.wait_all("vx"), inc="sc")
            v3 = ev.op("vector", lambda e: e.scalar_tensor_tensor(
                var[:], s2[:], 1.0 / D, mu2[:], ALU.mult, ALU.subtract),
                waits=[sc2, v2], inc="vx")
            sc3 = ev.op("scalar", lambda e: e.activation(
                rt[:], var[:], AF.Sqrt, bias=epsv[:]), waits=[v3, ep], inc="sc")
            v4 = ev.op("vector", lambda e: e.reciprocal(rr[:], rt[:]),
                       waits=[sc3], inc="vx")
            vn = ev.op("vector", lambda e, b=b: e.tensor_scalar(
                xnt[b][:], x1t[b][:], mu[:], rr[:], ALU.subtract, ALU.mult),
                waits=[v4], inc="vx")
            d2 = ev.dma("gpsimd", xnN_o[i * 128:(i + 1) * 128, :], xnt[b][:],
                        waits=[vn], inc="stg")
            cv = vn
            for c in range(ND):
                pb = c % 2
                m = ev.op("tensor", lambda e, b=b, c=c, pb=pb: e.transpose(
                    tp[pb][:], xnt[b][:, c * 128:(c + 1) * 128], idf[:]),
                    waits=[vn] + ev.wait_all("vx")[-1:], inc="mm")
                cv = ev.op("vector", lambda e, i=i, c=c, pb=pb: e.tensor_copy(
                    chunks[c][:, i * 128:(i + 1) * 128], tp[pb][:]),
                    waits=[m], inc="vx")
            guard[b] = [cv, d1, ev.mark("st"), ev.mark("stg")]
        w = ev.wait_all("vx")
        for c in range(ND):
            ev.dma("sync", xnTf_o[c * 128:(c + 1) * 128, :], chunks[c][:],
                   waits=w, inc="st")
            ev.dma("gpsimd", xnTb_o[c * 128:(c + 1) * 128, :], chunks[c][:],
                   waits=w, inc="stg")


def emit_l1attn(nc, ev, ctx, io, pfx=""):
    xnT_b = io["xnT_b"]
    xnN_b = io["xnN_b"]
    QK_i = io["QK_i"]
    VO_i = io["VO_i"]
    onesr = io["onesr"]
    s1_o = io["s1_o"]
    if 1:
        sb = lambda *a: ctx.enter_context(nc.sbuf_tensor(pfx + a[0], *a[1:]))
        psa = lambda *a: ctx.enter_context(nc.psum_tensor(pfx + a[0], *a[1:]))
        xn = [sb(f"xn{c}", [128, NTOK], BF16) for c in range(ND)]
        xnN = sb("xnN", [128, NT * D], BF16)
        qksb = sb("qksb", [128, ND * D], BF16)
        vosb = sb("vosb", [128, ND * D], BF16)
        xqk = [sb(f"xqk{c}", [128, NTOK], BF16) for c in range(ND)]
        at = sb("at", [128, 16 * 512], BF16)
        avsb = sb("avsb", [128, ND * 512], BF16)
        onesb = sb("onesb", [1, 128], F32)
        ones_bf = sb("ones_bf", [128, 1], BF16)
        rec = sb("rec", [1, 512], F32)
        bcs = sb("bcs", [128, 512], F32)
        stile = [sb(f"stile{i}", [128, D], F32) for i in range(2)]
        ps_s = [psa(f"ps_s{i}", [128, 512], F32) for i in range(2)]
        ps_den = psa("ps_den", [1, 512], F32)
        ps_bc = psa("ps_bc", [128, 512], F32)
        ps_av = psa("ps_av", [128, 512], F32)
        ps_vo = [psa(f"ps_vo{i}", [128, 512], F32) for i in range(2)]
        m8v = sb("m8v", [128, 1], F32)
        ep0 = ev.op("vector", lambda e: e.memset(m8v[:], -8.0), inc="vx")
        lds = []
        for c in range(ND):
            lds.append(ev.dma("sync", xn[c][:], xnT_b[c * 128:(c + 1) * 128, :], inc="ld"))
        for t in range(NT):
            lds.append(ev.dma("sync", xnN[:, t * D:(t + 1) * D],
                              xnN_b[t * 128:(t + 1) * 128, :], inc="ld"))
        ev.dma("sync", onesb[:], onesr, inc="ld")
        wd1 = ev.dma("gpsimd", qksb[:].rearrange("p (k n) -> p k n", n=D),
                     QK_i.rearrange("(k p) n -> p k n", p=128), inc="wd")
        wd2 = ev.dma("gpsimd", vosb[:].rearrange("p (k n) -> p k n", n=D),
                     VO_i.rearrange("(k p) n -> p k n", p=128), inc="wd")
        mset = ev.op("vector", lambda e: e.memset(ones_bf[:], 1.0), inc="vx")
        lds_all = ev.wait_all("ld")
        st_guard = [None, None]
        guard = None
        for ecn in range(ND):
            for tcc in range(4):
                acc = None
                for c in range(ND):
                    acc = ev.op("tensor", lambda e, c=c, ecn=ecn, tcc=tcc: e.matmul(
                        ps_s[tcc % 2][:], qksb[:, c * D + ecn * 128: c * D + ecn * 128 + 128],
                        xn[c][:, tcc * 512:(tcc + 1) * 512],
                        start=(c == 0), stop=(c == ND - 1)),
                        waits=(lds_all + [wd1] + ([guard] if guard else [])) if c == 0 else [],
                        inc="mm")
                guard = ev.op("vector", lambda e, ecn=ecn, tcc=tcc: e.tensor_copy(
                    xqk[ecn][:, tcc * 512:(tcc + 1) * 512], ps_s[tcc % 2][:]),
                    waits=[acc], inc="vx")
        xqk_done = ev.wait_all("vx")
        for tc in range(4):
            t0 = tc * 512
            exp_done = {}
            for ut in range(16):
                pb = ut % 2
                warw = ev.wait_all("sc", "vx") if ut < 2 else [exp_done[ut - 2]]
                acc = None
                for c in range(ND):
                    acc = ev.op("tensor", lambda e, c=c, ut=ut, t0=t0, pb=pb: e.matmul(
                        ps_s[pb][:], xn[c][:, ut * 128:(ut + 1) * 128],
                        xqk[c][:, t0:t0 + 512],
                        start=(c == 0), stop=(c == ND - 1)),
                        waits=((xqk_done if (tc == 0 and ut == 0) else []) + warw) if c == 0 else [],
                        inc="mm")
                e1 = ev.op("scalar", lambda e, ut=ut, pb=pb: e.activation(
                    at[:, ut * 512:(ut + 1) * 512], ps_s[pb][:], AF.Exp,
                    bias=m8v[:], scale=1.0 / 32.0), waits=[acc, ep0], inc="sc")
                exp_done[ut] = e1
            mden = None
            for ut in range(16):
                mden = ev.op("tensor", lambda e, ut=ut: e.matmul(
                    ps_den[:], ones_bf[:], at[:, ut * 512:(ut + 1) * 512],
                    start=(ut == 0), stop=(ut == 15)),
                    waits=[exp_done[ut]] + ([mset] + ev.wait_all("vx") if ut == 0 else []),
                    inc="mm")
            r1 = ev.op("vector", lambda e: e.reciprocal(rec[:], ps_den[:]),
                       waits=[mden], inc="vx")
            mb = ev.op("tensor", lambda e: e.matmul(
                ps_bc[:], onesb[0:1, :], rec[0:1, :], start=True, stop=True),
                waits=[r1], inc="mm")
            cb = ev.op("vector", lambda e: e.tensor_copy(bcs[:], ps_bc[:]),
                       waits=[mb], inc="vx")
            for dc in range(ND):
                mav = None
                for ut in range(16):
                    mav = ev.op("tensor", lambda e, dc=dc, ut=ut: e.matmul(
                        ps_av[:], xnN[:, ut * D + dc * 128: ut * D + dc * 128 + 128],
                        at[:, ut * 512:(ut + 1) * 512],
                        start=(ut == 0), stop=(ut == 15)),
                        waits=([cb] + ev.wait_all("vx")) if ut == 0 else [],
                        inc="mm")
                ev.op("vector", lambda e, dc=dc: e.tensor_tensor(
                    avsb[:, dc * 512:(dc + 1) * 512], ps_av[:], bcs[:], ALU.mult),
                    waits=[mav], inc="vx")
            av_done = ev.wait_all("vx")
            for tt in range(4):
                sready = []
                for n in range(2):
                    acc = None
                    for dc in range(ND):
                        acc = ev.op("tensor", lambda e, dc=dc, tt=tt, n=n: e.matmul(
                            ps_vo[n][:], avsb[:, dc * 512 + tt * 128: dc * 512 + tt * 128 + 128],
                            vosb[:, dc * D + n * 512: dc * D + (n + 1) * 512],
                            start=(dc == 0), stop=(dc == ND - 1)),
                            waits=(av_done + [wd2] + ev.wait_all("vx")[-1:]) if dc == 0 else [],
                            inc="mm")
                    sready.append(ev.op("vector", lambda e, tt=tt, n=n: e.tensor_copy(
                        stile[tt % 2][:, n * 512:(n + 1) * 512], ps_vo[n][:]),
                        waits=[acc] + ([st_guard[tt % 2]] if st_guard[tt % 2] else []),
                        inc="vx"))
                ev.dma("sync", s1_o[t0 + tt * 128: t0 + (tt + 1) * 128, :],
                       stile[tt % 2][:], waits=sready, inc="st")
                st_guard[tt % 2] = ev.mark("st")


def emit_l1ffn(nc, ev, ctx, io, pfx=""):
    xnT_f = io["xnT_f"]
    cw = io["cw"]
    cb = io["cb"]
    gw = io["gw"]
    gb = io["gb"]
    ar_i = io["ar_i"]
    onesr = io["onesr"]
    ident = io["ident"]
    o1_o = io["o1_o"]
    if 1:
        sb = lambda *a: ctx.enter_context(nc.sbuf_tensor(pfx + a[0], *a[1:]))
        psa = lambda *a: ctx.enter_context(nc.psum_tensor(pfx + a[0], *a[1:]))
        out = [sb(f"out{c}", [128, NTOK], F32) for c in range(ND)]
        w1sb = sb("w1sb", [128, ND * D], BF16)
        w2sb = sb("w2sb", [128, ND * 512], BF16)
        w3sb = sb("w3sb", [128, 4 * 256], BF16)
        w4sb = sb("w4sb", [128, 2], BF16)
        b1v = sb("b1v", [128, ND], F32)
        b2v = sb("b2v", [128, 4], F32)
        b3v = sb("b3v", [128, 2], F32)
        b4v = sb("b4v", [1, 1], F32)
        g1sb = sb("g1sb", [128, ND * 512], F32)
        g2sb = sb("g2sb", [128, 4 * 256], F32)
        g3sb = sb("g3sb", [128, 2], F32)
        gb1v = sb("gb1v", [128, 4], F32)
        gb2v = sb("gb2v", [128, 2], F32)
        gb3v = sb("gb3v", [1, 1], F32)
        arv = sb("arv", [1, 1], F32)
        gcs = sb("gcs", [128, ND], F32)
        h1v = sb("h1v", [128, 4], F32)
        h2v = sb("h2v", [128, 2], F32)
        gfv = sb("gfv", [1, 1], F32)
        c0v = sb("c0v", [1, 1], F32)
        c1v = sb("c1v", [1, 1], F32)
        t1v = sb("t1v", [1, 1], F32)
        obf = sb("obf", [128, ND * 512], BF16)
        h1sb = sb("h1sb", [128, ND * 512], BF16)
        h2sb = sb("h2sb", [128, 4 * 512], BF16)
        h3sb = sb("h3sb", [128, 2 * 512], BF16)
        tfrow = sb("tfrow", [1, 512], F32)
        coup = sb("coup", [1, 512], F32)
        tcp = sb("tcp", [128, 512], F32)
        ttb = sb("ttb", [128, 512], F32)
        onesb = sb("onesb", [1, 128], F32)
        idf = sb("idf", [128, 128], F32)
        gtile = sb("gtile", [128, D], F32)
        ps_m = [psa(f"ps_m{i}", [128, 512], F32) for i in range(2)]
        ps_sm = psa("ps_sm", [128, 1], F32)
        ps_tf = psa("ps_tf", [1, 512], F32)
        ps_bc = psa("ps_bc", [128, 512], F32)
        ps_t = psa("ps_t", [128, 128], F32)
        lds = []
        for c in range(ND):
            lds.append(ev.dma("sync", out[c][:], xnT_f[c * 128:(c + 1) * 128, :], inc="ld"))
        ev.dma("sync", onesb[:], onesr, inc="ld")
        ev.dma("sync", idf[:], ident, inc="ld")
        ev.dma("sync", arv[:], ar_i.rearrange("(o d) -> o d", o=1), inc="ld")
        wds = []
        wds.append(ev.dma("gpsimd", w1sb[:].rearrange("p (k n) -> p k n", n=D),
                          cw["cn_w1"].rearrange("(k p) n -> p k n", p=128), inc="wd"))
        wds.append(ev.dma("gpsimd", w2sb[:].rearrange("p (k n) -> p k n", n=512),
                          cw["cn_w2"].rearrange("(k p) n -> p k n", p=128), inc="wd"))
        wds.append(ev.dma("gpsimd", w3sb[:].rearrange("p (k n) -> p k n", n=256),
                          cw["cn_w3"].rearrange("(k p) n -> p k n", p=128), inc="wd"))
        for c in range(2):
            wds.append(ev.dma("gpsimd", w4sb[:, c:c + 1],
                              cw["cn_w4"][c * 128:(c + 1) * 128, 0:1], inc="wd"))
        wds.append(ev.dma("sync", g1sb[:].rearrange("p (k n) -> p k n", n=512),
                          gw["gc_w1"].rearrange("(k p) n -> p k n", p=128), inc="ld"))
        wds.append(ev.dma("sync", g2sb[:].rearrange("p (k n) -> p k n", n=256),
                          gw["gc_w2"].rearrange("(k p) n -> p k n", p=128), inc="ld"))
        for c in range(2):
            wds.append(ev.dma("sync", g3sb[:, c:c + 1],
                              gw["gc_w3"][c * 128:(c + 1) * 128, 0:1], inc="ld"))
        for c in range(ND):
            ev.dma("sync", b1v[:, c:c + 1],
                   cb["cn_b1"].rearrange("(d o) -> d o", o=1)[c * 128:(c + 1) * 128], inc="ld")
        for c in range(4):
            ev.dma("sync", b2v[:, c:c + 1],
                   cb["cn_b2"].rearrange("(d o) -> d o", o=1)[c * 128:(c + 1) * 128], inc="ld")
            ev.dma("sync", gb1v[:, c:c + 1],
                   gb["gc_b1"].rearrange("(d o) -> d o", o=1)[c * 128:(c + 1) * 128], inc="ld")
        for c in range(2):
            ev.dma("sync", b3v[:, c:c + 1],
                   cb["cn_b3"].rearrange("(d o) -> d o", o=1)[c * 128:(c + 1) * 128], inc="ld")
            ev.dma("sync", gb2v[:, c:c + 1],
                   gb["gc_b2"].rearrange("(d o) -> d o", o=1)[c * 128:(c + 1) * 128], inc="ld")
        ev.dma("sync", b4v[:], cb["cn_b4"].rearrange("(o d) -> o d", o=1), inc="ld")
        ev.dma("sync", gb3v[:], gb["gc_b3"].rearrange("(o d) -> o d", o=1), inc="ld")
        lds_all = ev.wait_all("ld", "wd")
        gm = []
        for c in range(ND):
            r = ev.op("vector", lambda e, c=c: e.reduce_sum(gcs[:, c:c + 1], out[c][:], AX.X),
                      waits=[lds[c]], inc="vx")
            gm.append(r)
        vsc = ev.op("vector", lambda e: e.tensor_scalar_mul(gcs[:], gcs[:], 1.0 / NTOK),
                    waits=gm, inc="vx")
        h1done = []
        for m in range(4):
            acc = None
            for c in range(ND):
                acc = ev.op("tensor", lambda e, m=m, c=c: e.matmul(
                    ps_sm[:], g1sb[:, c * 512 + m * 128: c * 512 + m * 128 + 128],
                    gcs[:, c:c + 1], start=(c == 0), stop=(c == ND - 1)),
                    waits=(lds_all + [vsc] + h1done[-1:]) if c == 0 else [], inc="mm")
            h1done.append(ev.op("scalar", lambda e, m=m: e.activation(
                h1v[:, m:m + 1], ps_sm[:], AF.Gelu, bias=gb1v[:, m:m + 1]),
                waits=[acc], inc="sc"))
        h2done = []
        for m in range(2):
            acc = None
            for c in range(4):
                acc = ev.op("tensor", lambda e, m=m, c=c: e.matmul(
                    ps_sm[:], g2sb[:, c * 256 + m * 128: c * 256 + m * 128 + 128],
                    h1v[:, c:c + 1], start=(c == 0), stop=(c == 3)),
                    waits=(h1done + h2done[-1:]) if c == 0 else [], inc="mm")
            h2done.append(ev.op("scalar", lambda e, m=m: e.activation(
                h2v[:, m:m + 1], ps_sm[:], AF.Gelu, bias=gb2v[:, m:m + 1]),
                waits=[acc], inc="sc"))
        acc = None
        for c in range(2):
            acc = ev.op("tensor", lambda e, c=c: e.matmul(
                ps_sm[0:1, :], g3sb[0:128, c:c + 1], h2v[:, c:c + 1],
                start=(c == 0), stop=(c == 1)),
                waits=(h2done + h1done) if c == 0 else [], inc="mm")
        gfd = ev.op("scalar", lambda e: e.activation(
            gfv[:], ps_sm[0:1, 0:1], AF.Sigmoid, bias=gb3v[0:1, 0:1]),
            waits=[acc], inc="sc")
        v1 = ev.op("vector", lambda e: e.tensor_scalar_mul(c1v[:], arv[:], 0.6),
                   waits=[gfd] + lds_all, inc="vx")
        v2 = ev.op("vector", lambda e: e.tensor_mul(t1v[:], arv[:], gfv[:]),
                   waits=[gfd], inc="vx")
        v3 = ev.op("vector", lambda e: e.scalar_tensor_tensor(
            t1v[:], t1v[:], 1.4, arv[:], ALU.mult, ALU.subtract), waits=[v2], inc="vx")
        cc = ev.op("vector", lambda e: e.tensor_scalar_add(c0v[:], t1v[:], 0.1),
                   waits=[v3], inc="vx")
        for cyc in range(3):
            for tc in range(4):
                t0 = tc * 512
                cvt = []
                for c in range(ND):
                    cvt.append(ev.op("vector", lambda e, c=c, t0=t0: e.tensor_copy(
                        obf[:, c * 512:(c + 1) * 512], out[c][:, t0:t0 + 512]),
                        waits=ev.wait_all("mm")[-1:] if (cyc > 0 or tc > 0) else [],
                        inc="vx"))
                h1d = []
                for m in range(ND):
                    acc = None
                    for c in range(ND):
                        acc = ev.op("tensor", lambda e, m=m, c=c: e.matmul(
                            ps_m[m % 2][:], w1sb[:, c * D + m * 128: c * D + m * 128 + 128],
                            obf[:, c * 512:(c + 1) * 512],
                            start=(c == 0), stop=(c == ND - 1)),
                            waits=(cvt + lds_all + ev.wait_all("sc")[-1:]) if c == 0 else [],
                            inc="mm")
                    h1d.append(ev.op("scalar", lambda e, m=m: e.activation(
                        h1sb[:, m * 512:(m + 1) * 512], ps_m[m % 2][:], AF.Gelu,
                        bias=b1v[:, m:m + 1]), waits=[acc], inc="sc"))
                h2d = []
                for m in range(4):
                    acc = None
                    for c in range(ND):
                        acc = ev.op("tensor", lambda e, m=m, c=c: e.matmul(
                            ps_m[m % 2][:], w2sb[:, c * 512 + m * 128: c * 512 + m * 128 + 128],
                            h1sb[:, c * 512:(c + 1) * 512],
                            start=(c == 0), stop=(c == ND - 1)),
                            waits=(h1d + ev.wait_all("sc")[-1:]) if c == 0 else [],
                            inc="mm")
                    h2d.append(ev.op("scalar", lambda e, m=m: e.activation(
                        h2sb[:, m * 512:(m + 1) * 512], ps_m[m % 2][:], AF.Gelu,
                        bias=b2v[:, m:m + 1]), waits=[acc], inc="sc"))
                h3d = []
                for m in range(2):
                    acc = None
                    for c in range(4):
                        acc = ev.op("tensor", lambda e, m=m, c=c: e.matmul(
                            ps_m[m % 2][:], w3sb[:, c * 256 + m * 128: c * 256 + m * 128 + 128],
                            h2sb[:, c * 512:(c + 1) * 512],
                            start=(c == 0), stop=(c == 3)),
                            waits=(h2d + ev.wait_all("sc")[-1:]) if c == 0 else [],
                            inc="mm")
                    h3d.append(ev.op("scalar", lambda e, m=m: e.activation(
                        h3sb[:, m * 512:(m + 1) * 512], ps_m[m % 2][:], AF.Gelu,
                        bias=b3v[:, m:m + 1]), waits=[acc], inc="sc"))
                acc = None
                for c in range(2):
                    acc = ev.op("tensor", lambda e, c=c: e.matmul(
                        ps_tf[:], w4sb[0:128, c:c + 1], h3sb[:, c * 512:(c + 1) * 512],
                        start=(c == 0), stop=(c == 1)),
                        waits=h3d if c == 0 else [], inc="mm")
                tfd = ev.op("scalar", lambda e: e.activation(
                    tfrow[:], ps_tf[:], AF.Sigmoid, bias=b4v[0:1, 0:1]),
                    waits=[acc], inc="sc")
                cpd = ev.op("vector", lambda e: e.tensor_scalar(
                    coup[:], tfrow[:], c1v[0:1, 0:1], c0v[0:1, 0:1], ALU.mult, ALU.add),
                    waits=[tfd, cc], inc="vx")
                mb = ev.op("tensor", lambda e: e.matmul(
                    ps_bc[:], onesb[0:1, :], coup[0:1, :], start=True, stop=True),
                    waits=[cpd] + ev.wait_all("vx")[-1:], inc="mm")
                last = None
                for p in range(4):
                    sgn = 1.0 if p < 2 else -1.0
                    pw = [mb] + ([last] if last else [])
                    vc = ev.op("vector", lambda e, p=p, t0=t0: e.tensor_copy(
                        tcp[:], out[p][:, t0:t0 + 512]), waits=pw, inc="vx")
                    va = ev.op("vector", lambda e, p=p, t0=t0: e.tensor_tensor(
                        ttb[:], ps_bc[:], out[p + 4][:, t0:t0 + 512], ALU.mult),
                        waits=[vc], inc="vx")
                    if sgn > 0:
                        vb = ev.op("vector", lambda e, p=p, t0=t0: e.tensor_add(
                            out[p][:, t0:t0 + 512], out[p][:, t0:t0 + 512], ttb[:]),
                            waits=[va], inc="vx")
                    else:
                        vb = ev.op("vector", lambda e, p=p, t0=t0: e.tensor_sub(
                            out[p][:, t0:t0 + 512], out[p][:, t0:t0 + 512], ttb[:]),
                            waits=[va], inc="vx")
                    vd = ev.op("vector", lambda e, p=p, t0=t0: e.tensor_tensor(
                        ttb[:], ps_bc[:], tcp[:], ALU.mult), waits=[vb], inc="vx")
                    if sgn > 0:
                        last = ev.op("vector", lambda e, p=p, t0=t0: e.tensor_sub(
                            out[p + 4][:, t0:t0 + 512], out[p + 4][:, t0:t0 + 512], ttb[:]),
                            waits=[vd], inc="vx")
                    else:
                        last = ev.op("vector", lambda e, p=p, t0=t0: e.tensor_add(
                            out[p + 4][:, t0:t0 + 512], out[p + 4][:, t0:t0 + 512], ttb[:]),
                            waits=[vd], inc="vx")
        res_done = ev.wait_all("vx")
        tr_prev = None
        for t in range(NT):
            gts = []
            for c in range(ND):
                m = ev.op("tensor", lambda e, c=c, t=t: e.transpose(
                    ps_t[:], out[c][:, t * 128:(t + 1) * 128], idf[:]),
                    waits=(res_done + lds_all) if (t == 0 and c == 0) else
                          ([tr_prev] if tr_prev else []), inc="mm")
                tr_prev = ev.op("vector", lambda e, c=c: e.tensor_copy(
                    gtile[:, c * 128:(c + 1) * 128], ps_t[:]), waits=[m], inc="vx")
                gts.append(tr_prev)
            dm = ev.dma("sync", o1_o[t * 128:(t + 1) * 128, :], gtile[:],
                        waits=gts, inc="st")
            tr_prev = ("st", ev.cnt["st"])


def emit_comb2(nc, ev, ctx, io, pfx=""):
    x1_i = io["x1_i"]
    s1_i = io["s1_i"]
    o1_i = io["o1_i"]
    y_o = io["y_o"]
    if 1:
        sb = lambda *a: ctx.enter_context(nc.sbuf_tensor(pfx + a[0], *a[1:]))
        xt = [sb(f"xt{i}", [128, D], F32) for i in range(2)]
        st = [sb(f"st{i}", [128, D], F32) for i in range(2)]
        ot = [sb(f"ot{i}", [128, D], F32) for i in range(2)]
        gs = sb("gs", [128, D], F32)
        yt = [sb(f"yt{i}", [128, D], F32) for i in range(2)]
        guard = [None, None]
        for i in range(NT):
            b = i % 2
            w = [] if guard[b] is None else [guard[b]]
            l1 = ev.dma("sync", xt[b][:], x1_i[i * 128:(i + 1) * 128, :], waits=w, inc="ld")
            l2 = ev.dma("sync", st[b][:], s1_i[i * 128:(i + 1) * 128, :], waits=w, inc="ld")
            l3 = ev.dma("gpsimd", ot[b][:], o1_i[i * 128:(i + 1) * 128, :], waits=w, inc="ldg")
            sc0 = ev.op("scalar", lambda e, b=b: e.activation(
                gs[:], ot[b][:], AF.Silu), waits=[l3] + ev.wait_all("vx"), inc="sc")
            v0 = ev.op("vector", lambda e, b=b: e.tensor_mul(gs[:], st[b][:], gs[:]),
                       waits=[sc0, l2], inc="vx")
            v1 = ev.op("vector", lambda e, b=b: e.tensor_add(yt[b][:], xt[b][:], gs[:]),
                       waits=[v0, l1], inc="vx")
            guard[b] = ev.dma("sync", y_o[i * 128:(i + 1) * 128, :], yt[b][:],
                              waits=[v1], inc="st")


ALLSEMS = ["ld", "ldg", "wd", "mm", "sc", "vx", "st", "stg"]


def build_fused():
    nc = bass.Bass("TRN2", target_bir_lowering=False)

    def dt(name, shape, kindt=F32, kind="ExternalInput"):
        return nc.dram_tensor(name, shape, kindt, kind=kind).ap()

    x = dt("x", [NTOK, D])
    l0_w_qkv = dt("l0_w_qkv", [D, 3 * D])
    l0_w_out = dt("l0_w_out", [D, D])
    l0_b_out = dt("l0_b_out", [D])
    l0_gamma = dt("l0_gamma", [D])
    l0_beta = dt("l0_beta", [D])
    f_wqkv = [dt(f"f_wqkv{l}", [D, 1536]) for l in range(2)]
    f_qs = [dt(f"f_qs{l}", [64]) for l in range(2)]
    f_ks = [dt(f"f_ks{l}", [64]) for l in range(2)]
    f_wo = [dt(f"f_wo{l}", [512, D]) for l in range(2)]
    f_lg = [dt(f"f_lg{l}", [D]) for l in range(2)]
    f_lb = [dt(f"f_lb{l}", [D]) for l in range(2)]
    f_w1 = [dt(f"f_w1{l}", [D, 5460]) for l in range(2)]
    f_b1 = [dt(f"f_b1{l}", [5460]) for l in range(2)]
    f_w2 = [dt(f"f_w2{l}", [2730, D]) for l in range(2)]
    f_b2 = [dt(f"f_b2{l}", [D]) for l in range(2)]
    QK = dt("QK", [D, D])
    VO = dt("VO", [D, D])
    cn_w = {1: dt("cn_w1", [D, D]), 2: dt("cn_w2", [D, 512]),
            3: dt("cn_w3", [512, 256]), 4: dt("cn_w4", [256, 1])}
    cn_b = {1: dt("cn_b1", [D]), 2: dt("cn_b2", [512]),
            3: dt("cn_b3", [256]), 4: dt("cn_b4", [1])}
    gc_w = {1: dt("gc_w1", [D, 512]), 2: dt("gc_w2", [512, 256]),
            3: dt("gc_w3", [256, 1])}
    gc_b = {1: dt("gc_b1", [512]), 2: dt("gc_b2", [256]), 3: dt("gc_b3", [1])}
    ar = dt("ar", [1])
    ident = dt("c_ident", [128, 128])
    onesr = dt("c_ones_row", [1, 128])
    ones2 = dt("c_ones2", [128, 2])
    ones2T = dt("c_ones2T", [2, 128])
    maskM = dt("c_maskM", [1024, 512])
    ropes = {n: dt("c_" + n, [128, NTOK]) for n in ("cs_q", "ss_q", "cs_k", "ss_k")}
    def S(name, shape, kindt=F32):
        return nc.dram_tensor(name, shape, kindt, kind="Internal").ap()
    xnT_f = S("s_xnT_f", [D, NTOK])
    xnT_b = S("s_xnT_b", [D, NTOK], BF16)
    s0 = S("s_s0", [NTOK, D])
    xfa0 = S("s_xfa0", [D, NTOK])
    xfb0 = S("s_xfb0", [D, NTOK])
    xfa1 = S("s_xfa1", [D, NTOK])
    g0 = S("s_g0", [NTOK, D])
    x1 = S("s_x1", [NTOK, D])
    xn1T_f = S("s_xn1T_f", [D, NTOK])
    xn1T_b = S("s_xn1T_b", [D, NTOK], BF16)
    xn1N_b = S("s_xn1N_b", [NTOK, D], BF16)
    s1 = S("s_s1", [NTOK, D])
    o1 = S("s_o1", [NTOK, D])
    y = dt("y", [NTOK, D], F32, kind="ExternalOutput")

    with ExitStack() as octx:
        ev = EV(nc, octx, ALLSEMS)
        block = octx.enter_context(nc.Block())

        def phase(emitfn, io, pfx, **kw):
            with ExitStack() as pctx:
                emitfn(nc, ev, pctx, io, pfx=pfx, **kw)
                ev.finalize(block)
            ev.barrier()

        phase(emit_ln0, dict(x=x, ident=ident, xnT_f=xnT_f, xnT_b=xnT_b), "p1_")
        phase(emit_l0attn, dict(xnT_b=xnT_b, w_qkv=l0_w_qkv, w_out=l0_w_out,
                                b_out=l0_b_out, gamma=l0_gamma, beta=l0_beta,
                                onesr=onesr, s_out=s0), "p2_")
        ffn_srcs = [xnT_f, xfa0, xfb0, xfa1]
        for l in range(2):
            phase(emit_ffn_attn,
                  dict(xfT=ffn_srcs[2 * l], wqkv=f_wqkv[l], qs_i=f_qs[l],
                       ks_i=f_ks[l], wo_i=f_wo[l], onesr=onesr, ones2=ones2,
                       ones2T=ones2T, maskM=maskM, ropes=ropes,
                       xfT_o=ffn_srcs[2 * l + 1]), f"p3{l}_")
            phase(emit_ffn_ffn,
                  dict(xfT=ffn_srcs[2 * l + 1], lg_i=f_lg[l], lb_i=f_lb[l],
                       w1_i=f_w1[l], b1_i=f_b1[l], w2_i=f_w2[l], b2_i=f_b2[l],
                       onesr=onesr, ident=ident,
                       outp=(g0 if l == 1 else xfb0)), f"p4{l}_",
                  out_normal=(l == 1))
        phase(emit_comb1, dict(x_i=x, s_i=s0, g_i=g0, ident=ident, x1_o=x1,
                               xnTf_o=xn1T_f, xnTb_o=xn1T_b, xnN_o=xn1N_b), "p5_")
        phase(emit_l1attn, dict(xnT_b=xn1T_b, xnN_b=xn1N_b, QK_i=QK, VO_i=VO,
                                onesr=onesr, s1_o=s1), "p6_")
        phase(emit_l1ffn, dict(xnT_f=xn1T_f,
                               cw={"cn_w1": cn_w[1], "cn_w2": cn_w[2],
                                   "cn_w3": cn_w[3], "cn_w4": cn_w[4]},
                               cb={"cn_b1": cn_b[1], "cn_b2": cn_b[2],
                                   "cn_b3": cn_b[3], "cn_b4": cn_b[4]},
                               gw={"gc_w1": gc_w[1], "gc_w2": gc_w[2],
                                   "gc_w3": gc_w[3]},
                               gb={"gc_b1": gc_b[1], "gc_b2": gc_b[2],
                                   "gc_b3": gc_b[3]},
                               ar_i=ar, onesr=onesr, ident=ident, o1_o=o1), "p7_")
        phase(emit_comb2, dict(x1_i=x1, s1_i=s1, o1_i=o1, y_o=y), "p8_")
        fin = ev.wait_all("st", "stg")
        ev.op("sync", lambda e: e.nop(), waits=fin)
        ev.op("gpsimd", lambda e: e.nop(), waits=fin)
        ev.finalize(block)
    return nc


_programs = {}


def _prog(name, fn):
    if name not in _programs:
        _programs[name] = fn()
    return _programs[name]


def kernel(**inputs):
    f32 = lambda k: np.ascontiguousarray(np.asarray(inputs[k], dtype=np.float32))
    x = f32("L0_attn_x")
    C = CONSTS
    n = N_CORES
    base = dict(
        l0_w_qkv=f32("L0_attn_w_qkv"), l0_w_out=f32("L0_attn_w_out"),
        l0_b_out=f32("L0_attn_b_out"), l0_gamma=f32("L0_attn_ln_gamma"),
        l0_beta=f32("L0_attn_ln_beta"),
        QK=f32("L1_attn_QK"), VO=f32("L1_attn_VO"),
        cn_w1=f32("L1_ffn_cn_w1"), cn_w2=f32("L1_ffn_cn_w2"),
        cn_w3=f32("L1_ffn_cn_w3"), cn_w4=f32("L1_ffn_cn_w4"),
        cn_b1=f32("L1_ffn_cn_b1"), cn_b2=f32("L1_ffn_cn_b2"),
        cn_b3=f32("L1_ffn_cn_b3"), cn_b4=f32("L1_ffn_cn_b4"),
        gc_w1=f32("L1_ffn_gc_w1"), gc_w2=f32("L1_ffn_gc_w2"),
        gc_w3=f32("L1_ffn_gc_w3"),
        gc_b1=f32("L1_ffn_gc_b1"), gc_b2=f32("L1_ffn_gc_b2"),
        gc_b3=f32("L1_ffn_gc_b3"),
        ar=f32("L1_ffn_adaptive_range"),
        c_ident=C["ident"], c_ones_row=C["ones_row"], c_ones2=C["ones2"],
        c_ones2T=C["ones2T"], c_maskM=C["maskM"],
        c_cs_q=C["cs_q"], c_ss_q=C["ss_q"], c_cs_k=C["cs_k"], c_ss_k=C["ss_k"],
    )
    for l in range(2):
        base[f"f_wqkv{l}"] = f32("L0_ffn_wqkv")[l]
        base[f"f_qs{l}"] = f32("L0_ffn_q_scale")[l]
        base[f"f_ks{l}"] = f32("L0_ffn_k_scale")[l]
        base[f"f_wo{l}"] = f32("L0_ffn_wo")[l]
        base[f"f_lg{l}"] = f32("L0_ffn_ln_g")[l]
        base[f"f_lb{l}"] = f32("L0_ffn_ln_b")[l]
        base[f"f_w1{l}"] = f32("L0_ffn_w1")[l]
        base[f"f_b1{l}"] = f32("L0_ffn_b1")[l]
        base[f"f_w2{l}"] = f32("L0_ffn_w2")[l]
        base[f"f_b2{l}"] = f32("L0_ffn_b2")[l]
    in_maps = [dict(base, x=x[i]) for i in range(n)]
    nc = _prog("fused", build_fused)
    res = run_bass_kernel_spmd(nc, in_maps, core_ids=list(range(n)))
    y = np.stack([res.results[i]["y"] for i in range(n)], axis=0)
    return y.astype(np.float32)


# version 04
# speedup vs baseline: 1.2417x; 1.1145x over previous
import numpy as np
import ml_dtypes
from contextlib import ExitStack
from collections import defaultdict

import concourse.bass as bass
import concourse.mybir as mybir
from concourse.bass_utils import run_bass_kernel_spmd

F32 = mybir.dt.float32
BF16 = mybir.dt.bfloat16
AF = mybir.ActivationFunctionType
ALU = mybir.AluOpType
AX = mybir.AxisListType

BF = ml_dtypes.bfloat16
N_CORES = 8
NTOK = 2048
D = 1024
NT = NTOK // 128
ND = D // 128


class EV:

    def __init__(self, nc, ctx, sem_names):
        self.nc = nc
        self.events = defaultdict(list)
        self.cnt = defaultdict(int)
        self.sems = {n: ctx.enter_context(nc.semaphore(n)) for n in sem_names}
        self.pending = {}

    def mark(self, sem):
        return (sem, self.cnt[sem])

    ENG_SEM = {"vector": "vx", "scalar": "sc"}

    def op(self, engine, fn, waits=(), inc=None, inc_n=1):
        w = self.pending.pop(engine, []) + list(waits)
        sname = self.ENG_SEM.get(engine)
        if sname is not None and sname in self.sems and self.cnt[sname] > 0:
            w.append((sname, self.cnt[sname]))
        rec = None
        if inc is not None:
            self.cnt[inc] += inc_n
            rec = (inc, inc_n)
        self.events[engine].append((tuple(w), fn, rec))
        return (inc, self.cnt[inc]) if inc is not None else None

    def dma(self, engine, out, in_, waits=(), inc=None):
        w = list(waits)
        if inc is not None and self.cnt[inc] > 0:
            w.append((inc, self.cnt[inc]))
        return self.op(engine, lambda e: e.dma_start(out=out, in_=in_),
                       waits=w, inc=inc, inc_n=16)

    def wait_all(self, *sem_names):
        return [(s, self.cnt[s]) for s in sem_names]

    def barrier(self):
        waits = self.wait_all(*self.sems.keys())
        for e in ("sync", "gpsimd", "tensor", "scalar", "vector"):
            self.pending[e] = list(waits)

    def finalize(self, block):
        if not hasattr(self, "_waited"):
            self._waited = {e: defaultdict(int)
                            for e in ("sync", "gpsimd", "tensor", "scalar", "vector")}
        for engine in ("sync", "gpsimd", "tensor", "scalar", "vector"):
            evs = self.events[engine]
            if not evs:
                continue
            sems = self.sems
            waited = self._waited[engine]

            def make(evs, waited):
                def emit(eng):
                    for waits, fn, inc in evs:
                        merged = {}
                        for wt in waits:
                            assert isinstance(wt, tuple) and len(wt) == 2, f"bad wait {wt}"
                            s, v = wt
                            if v > merged.get(s, 0):
                                merged[s] = v
                        for s, v in merged.items():
                            if v > 0 and v > waited[s]:
                                eng.wait_ge(sems[s], v)
                                waited[s] = v
                        instr = fn(eng)
                        if inc is not None:
                            instr.then_inc(sems[inc[0]], inc[1])
                return emit
            getattr(block, engine)(make(evs, waited))
        self.events = defaultdict(list)


def sem_val(ev, sem):
    return (sem, ev.cnt[sem])


def make_consts():
    c = {}
    c["ident"] = np.eye(128, dtype=np.float32)
    c["ones_row"] = np.ones((1, 128), dtype=np.float32)
    ones2 = np.zeros((128, 2), dtype=np.float32)
    ones2[:64, 0] = 1.0
    ones2[64:, 1] = 1.0
    c["ones2"] = ones2
    c["ones2T"] = ones2.T.copy()
    DH, W = 64, 512
    half = np.arange(0, DH, 2, dtype=np.float64)
    inv_freq = 1.0 / 10000.0 ** (half / DH)
    t = np.arange(NTOK, dtype=np.float64)
    freqs = t[:, None] * inv_freq[None, :]
    freqs = np.concatenate([freqs, freqs], axis=-1)
    base = (half + 0.4 * DH) / (1.4 * DH)
    power = (t - NTOK // 2) / float(W // 2)
    scale = base[None, :] ** power[:, None]
    scale = np.concatenate([scale, scale], axis=-1)
    cos, sin = np.cos(freqs), np.sin(freqs)
    for name, tab in (("cs_q", cos * scale), ("ss_q", sin * scale),
                      ("cs_k", cos / scale), ("ss_k", sin / scale)):
        tt = tab.T.astype(np.float32)
        c[name] = np.concatenate([tt, tt], axis=0)
    j = np.arange(1024)[:, None]
    i = np.arange(512)[None, :]
    allowed = (i <= j) & (j <= i + W)
    c["maskM"] = np.where(allowed, 100.0, -1e30).astype(np.float32)
    return c


CONSTS = make_consts()


def bcast_cols(ev, psum_bc, ones_row, src_row, dst, n, waits=()):
    for c0 in range(0, n, 512):
        w = min(512, n - c0)
        m = ev.op("tensor", lambda e, c0=c0, w=w: e.matmul(
            psum_bc[:, 0:w], ones_row[0:1, :], src_row[0:1, c0:c0 + w],
            start=True, stop=True),
            waits=list(waits) + ev.wait_all("vx"), inc="mm")
        ev.op("vector", lambda e, c0=c0, w=w: e.tensor_copy(
            dst[:, c0:c0 + w], psum_bc[:, 0:w]), waits=[m], inc="vx")


def emit_ln0(nc, ev, ctx, io, pfx=""):
    x = io["x"]
    ident = io["ident"]
    xnT_f = io["xnT_f"]
    xnT_b = io["xnT_b"]
    if 1:
        sb = lambda *a: ctx.enter_context(nc.sbuf_tensor(pfx + a[0], *a[1:]))
        psa = lambda *a: ctx.enter_context(nc.psum_tensor(pfx + a[0], *a[1:]))
        xt = [sb(f"xt{i}", [128, D], F32) for i in range(2)]
        xn = [sb(f"xn{i}", [128, D], F32) for i in range(2)]
        sq = sb("sq", [128, D], F32)
        s1 = sb("s1", [128, 1], F32)
        s2 = sb("s2", [128, 1], F32)
        mu = sb("mu", [128, 1], F32)
        mu2 = sb("mu2", [128, 1], F32)
        var = sb("var", [128, 1], F32)
        rt = sb("rt", [128, 1], F32)
        rr = sb("rr", [128, 1], F32)
        idf = sb("idf", [128, 128], F32)
        epsv = sb("epsv", [128, 1], F32)
        chunks = [sb(f"ch{c}", [128, NTOK], F32) for c in range(ND)]
        tp = [psa(f"tp{i}", [128, 128], F32) for i in range(2)]
        ev.dma("sync", idf[:], ident, inc="ld")
        ep = ev.op("vector", lambda e: e.memset(epsv[:], 1e-5), inc="vx")
        last_reads = [None, None]
        for i in range(NT):
            b = i % 2
            w = [] if last_reads[b] is None else [last_reads[b]]
            ld = ev.dma("sync", xt[b][:], x[i * 128:(i + 1) * 128, :],
                        waits=w, inc="ld")
            sc1 = ev.op("scalar", lambda e, b=b: e.square(sq[:], xt[b][:]),
                        waits=[ld] + ev.wait_all("vx"), inc="sc")
            v1 = ev.op("vector", lambda e, b=b: e.reduce_sum(s1[:], xt[b][:], AX.X),
                       waits=[ld], inc="vx")
            v2 = ev.op("vector", lambda e: e.reduce_sum(s2[:], sq[:], AX.X),
                       waits=[sc1], inc="vx")
            ev.op("vector", lambda e: e.tensor_scalar_mul(mu[:], s1[:], 1.0 / D),
                  waits=[v1], inc="vx")
            sc2 = ev.op("scalar", lambda e: e.square(mu2[:], mu[:]),
                        waits=ev.wait_all("vx"), inc="sc")
            v3 = ev.op("vector", lambda e: e.scalar_tensor_tensor(
                var[:], s2[:], 1.0 / D, mu2[:], ALU.mult, ALU.subtract),
                waits=[sc2, v2], inc="vx")
            sc3 = ev.op("scalar", lambda e: e.activation(
                rt[:], var[:], AF.Sqrt, bias=epsv[:]), waits=[v3, ep], inc="sc")
            v4 = ev.op("vector", lambda e: e.reciprocal(rr[:], rt[:]),
                       waits=[sc3], inc="vx")
            vn = ev.op("vector", lambda e, b=b: e.tensor_scalar(
                xn[b][:], xt[b][:], mu[:], rr[:], ALU.subtract, ALU.mult),
                waits=[v4], inc="vx")
            last_reads[b] = vn
            for c in range(ND):
                pb = c % 2
                m = ev.op("tensor", lambda e, b=b, c=c, pb=pb: e.transpose(
                    tp[pb][:], xn[b][:, c * 128:(c + 1) * 128], idf[:]),
                    waits=[vn] + ev.wait_all("vx"), inc="mm")
                ev.op("vector", lambda e, i=i, c=c, pb=pb: e.tensor_copy(
                    chunks[c][:, i * 128:(i + 1) * 128], tp[pb][:]),
                    waits=[m], inc="vx")
        w = ev.wait_all("vx")
        for c in range(ND):
            ev.dma("sync", xnT_f[c * 128:(c + 1) * 128, :], chunks[c][:],
                   waits=w, inc="st")
            ev.dma("gpsimd", xnT_b[c * 128:(c + 1) * 128, :], chunks[c][:],
                   waits=w, inc="stg")


def emit_l0attn(nc, ev, ctx, io, pfx=""):
    HG = 8
    xnT_b = io["xnT_b"]
    w_qkv = io["w_qkv"]
    w_out = io["w_out"]
    b_out = io["b_out"]
    gamma = io["gamma"]
    beta = io["beta"]
    onesr = io["onesr"]
    s_out = io["s_out"]
    if 1:
        sb = lambda *a: ctx.enter_context(nc.sbuf_tensor(pfx + a[0], *a[1:]))
        psa = lambda *a: ctx.enter_context(nc.psum_tensor(pfx + a[0], *a[1:]))
        xn = [sb(f"xn{c}", [128, NTOK], BF16) for c in range(ND)]
        oall = [sb(f"oall{c}", [128, NTOK], BF16) for c in range(ND)]
        wq = [sb(f"wq{i}", [128, 1024], BF16) for i in range(2)]
        wk = [sb(f"wk{i}", [128, 1024], BF16) for i in range(2)]
        wv = [sb(f"wv{i}", [128, 1024], BF16) for i in range(2)]
        qT = sb("qT", [128, NTOK], BF16)
        kT = sb("kT", [128, NTOK], BF16)
        vsb = sb("vsb", [128, 16 * 132], BF16)
        at = sb("at", [128, 16 * 512], BF16)
        wo = sb("wo", [128, 8 * 1024], BF16)
        brow = sb("brow", [1, 3 * D], F32)
        bbc = sb("bbc", [128, D], F32)
        gbc = sb("gbc", [128, D], F32)
        bebc = sb("bebc", [128, D], F32)
        onesb = sb("onesb", [1, 128], F32)
        rec = sb("rec", [1, 512], F32)
        nbc = [sb(f"nbc{i}", [32, 512], F32) for i in range(2)]
        stile = sb("stile", [128, D], F32)
        sq = sb("sq", [128, D], F32)
        st1 = sb("st1", [128, 1], F32)
        st2 = sb("st2", [128, 1], F32)
        mu = sb("mu", [128, 1], F32)
        mu2 = sb("mu2", [128, 1], F32)
        var = sb("var", [128, 1], F32)
        rt = sb("rt", [128, 1], F32)
        rr = sb("rr", [128, 1], F32)
        otile = sb("otile", [128, D], F32)

        ps_s = [psa(f"ps_s{i}", [128, 1024], F32) for i in range(2)]
        ps_qk = psa("ps_qk", [128, 512], F32)
        ps_o = [psa(f"ps_o{i}", [33, 512], F32) for i in range(2)]
        ps_b = psa("ps_b", [32, 512], F32)
        ps_bc = ps_s[1]
        ps_w = [ps_s[0], ps_qk]
        m8v = sb("m8v", [128, 1], F32)
        epsv = sb("epsv", [128, 1], F32)

        ep0 = ev.op("vector", lambda e: e.memset(m8v[:], -8.0), inc="vx")
        ep1 = ev.op("vector", lambda e: e.memset(epsv[:], 1e-6), inc="vx")
        lds = []
        for c in range(ND):
            lds.append(ev.dma("sync", xn[c][:], xnT_b[c * 128:(c + 1) * 128, :],
                              inc="ld"))
        ev.dma("sync", onesb[:], onesr, inc="ld")
        ev.dma("sync", brow[0:1, 0:D], b_out.rearrange("(o d) -> o d", o=1), inc="ld")
        ev.dma("sync", brow[0:1, D:2 * D], gamma.rearrange("(o d) -> o d", o=1), inc="ld")
        ev.dma("sync", brow[0:1, 2 * D:3 * D], beta.rearrange("(o d) -> o d", o=1), inc="ld")
        ev.dma("gpsimd", wo[:].rearrange("p (k n) -> p k n", n=1024),
               w_out.rearrange("(k p) n -> p k n", p=128), inc="wd")
        wrow = ev.wait_all("ld", "ld2")
        bcast_cols(ev, ps_bc[:, 0:512], onesb, brow[0:1, 0:D], bbc, D, waits=wrow)
        bcast_cols(ev, ps_bc[:, 0:512], onesb, brow[0:1, D:2 * D], gbc, D, waits=wrow)
        bcast_cols(ev, ps_bc[:, 0:512], onesb, brow[0:1, 2 * D:3 * D], bebc, D, waits=wrow)
        vmemset = []
        for t in range(16):
            for hh in range(4):
                vmemset.append(ev.op("vector", lambda e, t=t, hh=hh: e.memset(
                    vsb[:, t * 132 + hh * 33 + 32: t * 132 + hh * 33 + 33], 1.0),
                    inc="vx"))
        def load_group(g, buf, waits=()):
            o = []
            for (dst, c0) in ((wq, g * 128), (wk, D + g * 128), (wv, 2 * D + g * 128)):
                o.append(ev.dma(
                    "gpsimd", dst[buf][:].rearrange("p (k n) -> p k n", n=128),
                    w_qkv[:, c0:c0 + 128].rearrange("(k p) n -> p k n", p=128),
                    waits=waits, inc="wd" if buf == 0 else "wd2"))
            return o

        gw = [None, None]
        gw[0] = load_group(0, 0)
        wguard = [None, None]
        qk_guard = None
        v_guard = None
        prev_exp = None
        prev_av = None
        tt_done = [None, None]
        for g in range(HG):
            buf = g % 2
            if g + 1 < HG:
                nb = (g + 1) % 2
                gw[nb] = load_group(g + 1, nb,
                                    [wguard[nb]] if wguard[nb] else [])
            wds = gw[buf]
            qk_done = []
            for (dst, wsb) in ((qT, wq), (kT, wk)):
                for tc in range(4):
                    acc = []
                    for c in range(ND):
                        acc.append(ev.op("tensor", lambda e, c=c, tc=tc, wsb=wsb, buf=buf: e.matmul(
                            ps_qk[:], wsb[buf][:, c * 128:(c + 1) * 128],
                            xn[c][:, tc * 512:(tc + 1) * 512],
                            start=(c == 0), stop=(c == ND - 1)),
                            waits=(wds + lds + ([qk_guard] if qk_guard else [])) if c == 0 else [],
                            inc="mm"))
                    qk_guard = ev.op("vector", lambda e, dst=dst, tc=tc: e.tensor_copy(
                        dst[:, tc * 512:(tc + 1) * 512], ps_qk[:]),
                        waits=[acc[-1]], inc="vx")
                    qk_done.append(qk_guard)
            v_done = []
            for t in range(16):
                mv = []
                for c in range(ND):
                    mv.append(ev.op("tensor", lambda e, c=c, t=t, buf=buf: e.matmul(
                        ps_w[t % 2][:, 0:128], xn[c][:, t * 128:(t + 1) * 128],
                        wv[buf][:, c * 128:(c + 1) * 128],
                        start=(c == 0), stop=(c == ND - 1)),
                        waits=(([v_guard] if v_guard else []) +
                               (prev_exp[-1:] if (t < 2 and prev_exp) else []))
                              if c == 0 else [], inc="mm"))
                for hh in range(4):
                    v_guard = ev.op("vector", lambda e, t=t, hh=hh: e.tensor_copy(
                        vsb[:, t * 132 + hh * 33: t * 132 + hh * 33 + 32],
                        ps_w[t % 2][:, hh * 32:(hh + 1) * 32]),
                        waits=[mv[-1]] + ([vmemset[-1]] if g == 0 else []), inc="vx")
                    v_done.append(v_guard)
            wguard[buf] = ev.mark("mm")
            for hh in range(4):
                hbase = hh * 32
                tp = (hbase, 0) if hbase == 96 else None
                for qc in range(4):
                    q0 = qc * 512
                    it = g * 16 + hh * 4 + qc
                    po = it % 2
                    exp_done = []
                    av_marks = []
                    for kt in range(16):
                        pr = kt // 2
                        pb = pr % 2
                        half = kt % 2
                        w0 = []
                        if half == 0:
                            w0 += [exp_done[pr - 2]] if pr >= 2 else                                 (prev_exp[pr - 2 + 8:pr - 1 + 8] if prev_exp else [])
                        if kt == 0 and qc == 0 and hh == 0:
                            w0 += qk_done
                        if hh == 0 and qc == 0 and kt < 4:
                            w0 += v_done[-1:]
                        m = ev.op("tensor", lambda e, kt=kt, pb=pb, half=half, hbase=hbase, q0=q0, tp=tp: e.matmul(
                            ps_s[pb][:, half * 512:(half + 1) * 512],
                            kT[hbase:hbase + 32, kt * 128:(kt + 1) * 128],
                            qT[hbase:hbase + 32, q0:q0 + 512],
                            start=True, stop=True, tile_position=tp),
                            waits=w0, inc="mm")
                        if half == 1:
                            e1 = ev.op("scalar", lambda e, pr=pr, pb=pb: e.activation(
                                at[:, pr * 1024:(pr + 1) * 1024], ps_s[pb][:], AF.Exp,
                                bias=m8v[:], scale=float(32 ** -0.5)),
                                waits=[m, ep0] +
                                      (prev_av[2 * pr + 1:2 * pr + 2] if prev_av else []),
                                inc="sc")
                            exp_done.append(e1)
                    for kt in range(16):
                        mavl = ev.op("tensor", lambda e, kt=kt, hh=hh, po=po: e.matmul(
                            ps_o[po][:], vsb[:, kt * 132 + hh * 33: kt * 132 + hh * 33 + 33],
                            at[:, kt * 512:(kt + 1) * 512],
                            start=(kt == 0), stop=(kt == 15)),
                            waits=[exp_done[kt // 2]] +
                                  ((v_done if (hh == 0 and qc == 0) else []) +
                                   ([tt_done[po]] if tt_done[po] else []) if kt == 0 else []),
                            inc="mm")
                        av_marks.append(mavl)
                    r1 = ev.op("vector", lambda e, po=po: e.reciprocal(
                        rec[:], ps_o[po][32:33, :]), waits=[mavl], inc="vx")
                    mb = ev.op("tensor", lambda e: e.matmul(
                        ps_b[:], onesb[0:1, 0:32], rec[0:1, :], start=True, stop=True),
                        waits=[r1], inc="mm")
                    c1 = ev.op("vector", lambda e, po=po: e.tensor_copy(
                        nbc[po][:], ps_b[:]), waits=[mb], inc="vx")
                    tt_done[po] = ev.op("vector", lambda e, g=g, hbase=hbase, q0=q0, po=po: e.tensor_tensor(
                        oall[g][hbase:hbase + 32, q0:q0 + 512], ps_o[po][0:32, :],
                        nbc[po][:], ALU.mult), waits=[c1], inc="vx")
                    prev_exp = exp_done
                    prev_av = av_marks
        oall_done = ev.wait_all("vx")
        wo_ld = ev.wait_all("wd", "wd2")
        prev_store = None
        for t in range(NT):
            sready = []
            for n in range(2):
                acc = None
                for c in range(ND):
                    acc = ev.op("tensor", lambda e, c=c, t=t, n=n: e.matmul(
                        ps_w[n][:, 0:512], oall[c][:, t * 128:(t + 1) * 128],
                        wo[:, c * 1024 + n * 512: c * 1024 + (n + 1) * 512],
                        start=(c == 0), stop=(c == ND - 1)),
                        waits=(oall_done + wo_ld) if (t == 0 and c == 0) else
                              (ev.wait_all("vx") if c == 0 else []),
                        inc="mm")
                sready.append(ev.op("vector", lambda e, n=n: e.tensor_tensor(
                    stile[:, n * 512:(n + 1) * 512], ps_w[n][:, 0:512],
                    bbc[:, n * 512:(n + 1) * 512], ALU.add),
                    waits=[acc], inc="vx"))
            sc1 = ev.op("scalar", lambda e: e.square(sq[:], stile[:]),
                        waits=sready, inc="sc")
            v1 = ev.op("vector", lambda e: e.reduce_sum(st1[:], stile[:], AX.X),
                       waits=sready, inc="vx")
            v2 = ev.op("vector", lambda e: e.reduce_sum(st2[:], sq[:], AX.X),
                       waits=[sc1], inc="vx")
            ev.op("vector", lambda e: e.tensor_scalar_mul(mu[:], st1[:], 1.0 / D),
                  waits=[v1], inc="vx")
            sc2 = ev.op("scalar", lambda e: e.square(mu2[:], mu[:]),
                        waits=ev.wait_all("vx"), inc="sc")
            v3 = ev.op("vector", lambda e: e.scalar_tensor_tensor(
                var[:], st2[:], 1.0 / D, mu2[:], ALU.mult, ALU.subtract),
                waits=[sc2, v2], inc="vx")
            sc3 = ev.op("scalar", lambda e: e.activation(
                rt[:], var[:], AF.Sqrt, bias=epsv[:]), waits=[v3, ep1], inc="sc")
            v4 = ev.op("vector", lambda e: e.reciprocal(rr[:], rt[:]),
                       waits=[sc3], inc="vx")
            v5 = ev.op("vector", lambda e: e.tensor_scalar(
                sq[:], stile[:], mu[:], rr[:], ALU.subtract, ALU.mult),
                waits=[v4], inc="vx")
            v6 = ev.op("vector", lambda e: e.tensor_tensor(
                sq[:], sq[:], gbc[:], ALU.mult), waits=[v5], inc="vx")
            v7 = ev.op("vector", lambda e: e.tensor_tensor(
                otile[:], sq[:], bebc[:], ALU.add),
                waits=[v6] + ([prev_store] if prev_store else []), inc="vx")
            ev.dma("sync", s_out[t * 128:(t + 1) * 128, :], otile[:],
                   waits=[v7], inc="st")
            prev_store = ev.mark("st")


def emit_ffn_attn(nc, ev, ctx, io, pfx=""):
    xfT = io["xfT"]
    wqkv = io["wqkv"]
    qs_i = io["qs_i"]
    ks_i = io["ks_i"]
    wo_i = io["wo_i"]
    onesr = io["onesr"]
    ones2 = io["ones2"]
    ones2T = io["ones2T"]
    maskM = io["maskM"]
    ropes = io["ropes"]
    xfT_o = io["xfT_o"]
    if 1:
        sb = lambda *a: ctx.enter_context(nc.sbuf_tensor(pfx + a[0], *a[1:]))
        psa = lambda *a: ctx.enter_context(nc.psum_tensor(pfx + a[0], *a[1:]))
        xf = [sb(f"xf{c}", [128, NTOK], F32) for c in range(ND)]
        hb = [sb(f"hb{c}", [128, NTOK], BF16) for c in range(ND)]
        tabs = {n: sb(n + "_s", [128, NTOK], BF16) for n in ropes}
        mask = sb("mask", [128, 8 * 512], BF16)
        wq = [sb(f"wq{i}", [128, 1024], BF16) for i in range(2)]
        wk = [sb(f"wk{i}", [128, 1024], BF16) for i in range(2)]
        wv = [sb(f"wv{i}", [128, 1024], BF16) for i in range(2)]
        wo = sb("wo", [128, 4 * 1024], BF16)
        qf2 = [sb(f"qf{i}", [128, NTOK], F32) for i in range(2)]
        qTb = sb("qTb", [128, NTOK], BF16)
        kTb = sb("kTb", [128, NTOK], BF16)
        vsb = sb("vsb", [128, 16 * 130], BF16)
        at = sb("at", [128, 8 * 512], BF16)
        scr1 = sb("scr1", [128, NTOK], F32)
        ssum = sb("ssum", [2, 512], F32)
        qsv = sb("qsv", [128, 1], F32)
        ksv = sb("ksv", [128, 1], F32)
        onesb = sb("onesb", [1, 128], F32)
        o2 = sb("o2", [128, 2], F32)
        o2T = sb("o2T", [2, 128], F32)
        rec = sb("rec", [1, 512], F32)
        nbc = sb("nbc", [64, 512], F32)
        oall = [sb(f"oall{c}", [128, NTOK], BF16) for c in range(4)]

        ps_s = [psa(f"ps_s{i}", [128, 512], F32) for i in range(2)]
        ps_o = psa("ps_o", [65, 512], F32)
        ps_b = psa("ps_b", [128, 512], F32)
        ps_st = [psa(f"ps_st{i}", [2, 512], F32) for i in range(2)]
        ps_w = [psa(f"ps_w{i}", [128, 512], F32) for i in range(2)]
        m30v = sb("m30v", [128, 1], F32)
        eps24 = sb("eps24", [128, 1], F32)

        ep0 = ev.op("vector", lambda e: e.memset(m30v[:], -30.0), inc="vx")
        ep1 = ev.op("vector", lambda e: e.memset(eps24[:], 1e-24), inc="vx")
        lds = []
        for c in range(ND):
            lds.append(ev.dma("sync", xf[c][:], xfT[c * 128:(c + 1) * 128, :], inc="ld"))
        tab_lds = []
        for n in ropes:
            tab_lds.append(ev.dma("gpsimd", tabs[n][:], ropes[n], inc="wd"))
        for kt in range(8):
            tab_lds.append(ev.dma("gpsimd", mask[:, kt * 512:(kt + 1) * 512],
                                  maskM[kt * 128:(kt + 1) * 128, :], inc="wd"))
        ev.dma("sync", onesb[:], onesr, inc="ld")
        ev.dma("sync", o2[:], ones2, inc="ld")
        ev.dma("sync", o2T[:], ones2T, inc="ld")
        ev.dma("sync", qsv[0:64, 0:1], qs_i.rearrange("(d o) -> d o", o=1), inc="ld")
        ev.dma("sync", qsv[64:128, 0:1], qs_i.rearrange("(d o) -> d o", o=1), inc="ld")
        ev.dma("sync", ksv[0:64, 0:1], ks_i.rearrange("(d o) -> d o", o=1), inc="ld")
        ev.dma("sync", ksv[64:128, 0:1], ks_i.rearrange("(d o) -> d o", o=1), inc="ld")
        ev.dma("gpsimd", wo[:].rearrange("p (k n) -> p k n", n=1024),
               wo_i.rearrange("(k p) n -> p k n", p=128), inc="wd")
        lds_all = ev.wait_all("ld", "ld2")
        hdone = []
        for c in range(ND):
            if c < 4:
                hdone.append(ev.op("vector", lambda e, c=c: e.tensor_copy(
                    hb[c][:], xf[c][:]), waits=[lds[c]], inc="vx"))
            else:
                ev.op("vector", lambda e, c=c: e.memset(hb[c][:, 0:1], 0.0), inc="vx")
                hdone.append(ev.op("vector", lambda e, c=c: e.tensor_copy(
                    hb[c][:, 1:NTOK], xf[c][:, 0:NTOK - 1]), waits=[lds[c]], inc="vx"))

        def load_group(g, buf, waits=()):
            o = []
            for (dst, c0) in ((wq, g * 128), (wk, 512 + g * 128), (wv, 1024 + g * 128)):
                o.append(ev.dma(
                    "gpsimd", dst[buf][:].rearrange("p (k n) -> p k n", n=128),
                    wqkv[:, c0:c0 + 128].rearrange("(k p) n -> p k n", p=128),
                    waits=waits, inc="wd" if buf == 0 else "wd2"))
            return o

        gw = [None, None]
        gw[0] = load_group(0, 0)
        wguard = [None, None]
        qf_guard = [None, None]
        psw_uses = []
        vset = []
        for t in range(16):
            for hh in range(2):
                vset.append(ev.op("vector", lambda e, t=t, hh=hh: e.memset(
                    vsb[:, t * 130 + hh * 65 + 64: t * 130 + hh * 65 + 65], 1.0),
                    inc="vx"))
        wo_acc_done = None
        for g in range(4):
            buf = g % 2
            if g + 1 < 4:
                nb = (g + 1) % 2
                gw[nb] = load_group(g + 1, nb,
                                    [wguard[nb]] if wguard[nb] else [])
            wds = gw[buf]
            raws = {}
            for si, wsb in ((0, wq), (1, wk)):
                raw_done = []
                for tc in range(4):
                    acc = None
                    for c in range(ND):
                        acc = ev.op("tensor", lambda e, c=c, tc=tc, wsb=wsb, buf=buf: e.matmul(
                            ps_w[tc % 2][:], wsb[buf][:, c * 128:(c + 1) * 128],
                            hb[c][:, tc * 512:(tc + 1) * 512],
                            start=(c == 0), stop=(c == ND - 1)),
                            waits=(wds + hdone +
                                   ([qf_guard[si]] if qf_guard[si] else []) +
                                   psw_uses[-2:-1]) if c == 0 else [],
                            inc="mm")
                    cp = ev.op("vector", lambda e, tc=tc, si=si: e.tensor_copy(
                        qf2[si][:, tc * 512:(tc + 1) * 512], ps_w[tc % 2][:]),
                        waits=[acc], inc="vx")
                    raw_done.append(cp)
                    psw_uses.append(cp)
                raws[si] = raw_done
            v_done = []
            for t in range(16):
                mv = None
                for c in range(ND):
                    mv = ev.op("tensor", lambda e, c=c, t=t, buf=buf: e.matmul(
                        ps_w[t % 2][:, 0:128], hb[c][:, t * 128:(t + 1) * 128],
                        wv[buf][:, c * 128:(c + 1) * 128],
                        start=(c == 0), stop=(c == ND - 1)),
                        waits=psw_uses[-2:-1] if c == 0 else [], inc="mm")
                for hh in range(2):
                    cp = ev.op("vector", lambda e, t=t, hh=hh: e.tensor_copy(
                        vsb[:, t * 130 + hh * 65: t * 130 + hh * 65 + 64],
                        ps_w[t % 2][:, hh * 64:(hh + 1) * 64]),
                        waits=[mv] + ([vset[-1]] if g == 0 else []), inc="vx")
                    v_done.append(cp)
                psw_uses.append(v_done[-1])
            wguard[buf] = ev.mark("mm")
            chain_last = []
            for si, (dstb, scl, csn, ssn) in (
                    (0, (qTb, qsv, "cs_q", "ss_q")), (1, (kTb, ksv, "cs_k", "ss_k"))):
                s1 = ev.op("scalar", lambda e, si=si: e.square(scr1[:], qf2[si][:]),
                           waits=raws[si] + chain_last, inc="sc")
                stts = []
                m1s = []
                m2prev = []
                for tc in range(4):
                    m1 = ev.op("tensor", lambda e, tc=tc: e.matmul(
                        ps_st[tc % 2][:], o2[:], scr1[:, tc * 512:(tc + 1) * 512],
                        start=True, stop=True),
                        waits=[s1] + lds_all + (m1s[tc - 2:tc - 1] if tc >= 2 else []),
                        inc="mm")
                    m1s.append(m1)
                    sA = ev.op("scalar", lambda e, tc=tc: e.activation(
                        ssum[:], ps_st[tc % 2][:], AF.Sqrt,
                        bias=eps24[0:2, 0:1]), waits=[m1, ep1] + m2prev, inc="sc")
                    vR = ev.op("vector", lambda e: e.reciprocal(ssum[:], ssum[:]),
                               waits=[sA], inc="vx")
                    m2 = ev.op("tensor", lambda e: e.matmul(
                        ps_b[:], o2T[:], ssum[:], start=True, stop=True),
                        waits=[vR] + stts[-1:], inc="mm")
                    m2prev = [m2]
                    stts.append(ev.op("vector", lambda e, tc=tc, scl=scl, si=si: e.scalar_tensor_tensor(
                        qf2[si][:, tc * 512:(tc + 1) * 512], qf2[si][:, tc * 512:(tc + 1) * 512],
                        scl[:], ps_b[:], ALU.mult, ALU.mult), waits=[m2], inc="vx"))
                r1 = ev.op("vector", lambda e, si=si: e.tensor_scalar_mul(
                    scr1[0:32, :], qf2[si][32:64, :], -1.0), waits=stts, inc="vx")
                r2 = ev.op("vector", lambda e, si=si: e.tensor_copy(
                    scr1[32:64, :], qf2[si][0:32, :]), waits=stts, inc="vx")
                r3 = ev.op("vector", lambda e, si=si: e.tensor_scalar_mul(
                    scr1[64:96, :], qf2[si][96:128, :], -1.0), waits=stts, inc="vx")
                r4 = ev.op("vector", lambda e, si=si: e.tensor_copy(
                    scr1[96:128, :], qf2[si][64:96, :]), waits=stts, inc="vx")
                v4 = ev.op("vector", lambda e, ssn=ssn: e.tensor_tensor(
                    scr1[:], scr1[:], tabs[ssn][:], ALU.mult),
                    waits=[r1, r2, r3, r4] + tab_lds, inc="vx")
                v5 = ev.op("vector", lambda e, csn=csn, si=si: e.tensor_tensor(
                    qf2[si][:], qf2[si][:], tabs[csn][:], ALU.mult), waits=[v4], inc="vx")
                chain_last = [ev.op("vector", lambda e, dstb=dstb, si=si: e.tensor_tensor(
                    dstb[:], qf2[si][:], scr1[:], ALU.add), waits=[v4, v5], inc="vx")]
                qf_guard[si] = chain_last[0]
            qk_done = ev.wait_all("vx")
            for hh in range(2):
                hb64 = hh * 64
                for w in range(4):
                    q0 = w * 512
                    kt_lo = 4 if w == 0 else 0
                    exp_done = {}
                    for kt in range(kt_lo, 8):
                        ktok0 = (w - 1) * 512 + kt * 128
                        pb = kt % 2
                        warw = ev.wait_all("vx") if kt < kt_lo + 2 else \
                            ([exp_done[kt - 2]] if kt - 2 in exp_done else [])
                        m = ev.op("tensor", lambda e, ktok0=ktok0, pb=pb, hb64=hb64, q0=q0: e.matmul(
                            ps_s[pb][:], kTb[hb64:hb64 + 64, ktok0:ktok0 + 128],
                            qTb[hb64:hb64 + 64, q0:q0 + 512],
                            start=True, stop=True),
                            waits=(qk_done if (kt == kt_lo and w == 0 and g >= 0) else []) + warw,
                            inc="mm")
                        v1 = ev.op("vector", lambda e, kt=kt, pb=pb: e.scalar_tensor_tensor(
                            scr1[:, (kt % 2) * 512:(kt % 2 + 1) * 512], ps_s[pb][:], 8.0,
                            mask[:, kt * 512:(kt + 1) * 512],
                            ALU.mult, ALU.min), waits=[m] + tab_lds, inc="vx")
                        e1 = ev.op("scalar", lambda e, kt=kt: e.activation(
                            at[:, kt * 512:(kt + 1) * 512],
                            scr1[:, (kt % 2) * 512:(kt % 2 + 1) * 512], AF.Exp,
                            bias=m30v[:]), waits=[v1, ep0], inc="sc")
                        exp_done[kt] = e1
                    mavl = None
                    first = True
                    for kt in range(kt_lo, 8):
                        ktok0 = (w - 1) * 512 + kt * 128
                        vt = ktok0 // 128
                        mavl = ev.op("tensor", lambda e, kt=kt, vt=vt, hh=hh, first=first: e.matmul(
                            ps_o[:], vsb[:, vt * 130 + hh * 65: vt * 130 + hh * 65 + 65],
                            at[:, kt * 512:(kt + 1) * 512],
                            start=first, stop=(kt == 7)),
                            waits=[exp_done[kt]] + ((v_done + ev.wait_all("vx")) if first else []),
                            inc="mm")
                        first = False
                    r1 = ev.op("vector", lambda e: e.reciprocal(rec[:], ps_o[64:65, :]),
                               waits=[mavl], inc="vx")
                    mb = ev.op("tensor", lambda e: e.matmul(
                        ps_b[0:64, :], onesb[0:1, 0:64], rec[0:1, :],
                        start=True, stop=True), waits=[r1], inc="mm")
                    c1 = ev.op("vector", lambda e: e.tensor_copy(nbc[:], ps_b[0:64, :]),
                               waits=[mb], inc="vx")
                    ev.op("vector", lambda e, g=g, hb64=hb64, q0=q0: e.tensor_tensor(
                        oall[g][hb64:hb64 + 64, q0:q0 + 512], ps_o[0:64, :],
                        nbc[:], ALU.mult), waits=[c1], inc="vx")
        oall_done = ev.wait_all("vx")
        wo_ld = ev.wait_all("wd", "wd2")
        res_done = []
        for dc in range(ND):
            for tc in range(4):
                acc = None
                for c in range(4):
                    acc = ev.op("tensor", lambda e, c=c, dc=dc, tc=tc: e.matmul(
                        ps_w[tc % 2][:], wo[:, c * 1024 + dc * 128: c * 1024 + dc * 128 + 128],
                        oall[c][:, tc * 512:(tc + 1) * 512],
                        start=(c == 0), stop=(c == 3)),
                        waits=((oall_done + wo_ld) if (dc == 0 and tc == 0)
                               else ev.wait_all("vx")) if c == 0 else [],
                        inc="mm")
                res_done.append(ev.op("vector", lambda e, dc=dc, tc=tc: e.tensor_tensor(
                    xf[dc][:, tc * 512:(tc + 1) * 512], ps_w[tc % 2][:],
                    xf[dc][:, tc * 512:(tc + 1) * 512], ALU.add),
                    waits=[acc], inc="vx"))
        for c in range(ND):
            ev.dma("sync", xfT_o[c * 128:(c + 1) * 128, :], xf[c][:],
                   waits=res_done, inc="st")



def emit_ffn_ffn(nc, ev, ctx, io, out_normal, pfx=""):
    NH = 22
    hsz = [128] * 21 + [42]
    xfT = io["xfT"]
    lg_i = io["lg_i"]
    lb_i = io["lb_i"]
    w1_i = io["w1_i"]
    b1_i = io["b1_i"]
    w2_i = io["w2_i"]
    b2_i = io["b2_i"]
    onesr = io["onesr"]
    ident = io["ident"]
    outp = io["outp"]
    if 1:
        sb = lambda *a: ctx.enter_context(nc.sbuf_tensor(pfx + a[0], *a[1:]))
        psa = lambda *a: ctx.enter_context(nc.psum_tensor(pfx + a[0], *a[1:]))
        xf = [sb(f"xf{c}", [128, NTOK], F32) for c in range(ND)]
        xh = [sb(f"xh{c}", [128, NTOK], BF16) for c in range(ND)]
        patch = [sb(f"patch{i}", [128, 512], F32) for i in range(4)]
        w2sb = sb("w2sb", [128, NH * 1024], BF16)
        w1a = [sb(f"w1a{i}", [128, 1024], BF16) for i in range(2)]
        w1g = [sb(f"w1g{i}", [128, 1024], BF16) for i in range(2)]
        acts = sb("acts", [128, NH * 512], BF16)
        lgv = sb("lgv", [128, ND], F32)
        lbv = sb("lbv", [128, ND], F32)
        b2v = sb("b2v", [128, ND], F32)
        b1av = sb("b1av", [128, NH], F32)
        b1gv = sb("b1gv", [128, NH], F32)
        onesb = sb("onesb", [1, 128], F32)
        onesc = sb("onesc", [128, 1], F32)
        sqv = [sb(f"sqv{i}", [128, 512], F32) for i in range(2)]
        murow = sb("murow", [1, 512], F32)
        e2row = sb("e2row", [1, 512], F32)
        musq = sb("musq", [1, 512], F32)
        varrow = sb("varrow", [1, 512], F32)
        rrow = sb("rrow", [1, 512], F32)
        mubc = sb("mubc", [128, 512], F32)
        rbc = sb("rbc", [128, 512], F32)
        tta = sb("tta", [128, 512], F32)
        gbuf = [sb(f"gbuf{i}", [128, 512], BF16) for i in range(2)]
        idf = sb("idf", [128, 128], F32)
        gtile = sb("gtile", [128, D], F32)

        ps_S = psa("ps_S", [1, 512], F32)
        ps_Q = psa("ps_Q", [1, 512], F32)
        ps_bc = psa("ps_bc", [128, 512], F32)
        ps_a = [psa(f"ps_a{i}", [128, 512], F32) for i in range(2)]
        ps_g = [psa(f"ps_g{i}", [128, 512], F32) for i in range(2)]
        ps_out = psa("ps_out", [128, 512], F32)
        ps_t = ps_bc
        epsv = sb("epsv", [128, 1], F32)

        ep0 = ev.op("vector", lambda e: e.memset(epsv[:], 1e-5), inc="vx")
        lds = []
        for c in range(ND):
            lds.append(ev.dma("sync", xf[c][:], xfT[c * 128:(c + 1) * 128, :], inc="ld"))
        ev.dma("sync", onesb[:], onesr, inc="ld")
        ev.dma("sync", idf[:], ident, inc="ld")
        for c in range(ND):
            ev.dma("sync", lgv[:, c:c + 1], lg_i.rearrange("(d o) -> d o", o=1)[c * 128:(c + 1) * 128], inc="ld2")
            ev.dma("sync", lbv[:, c:c + 1], lb_i.rearrange("(d o) -> d o", o=1)[c * 128:(c + 1) * 128], inc="ld2")
            ev.dma("sync", b2v[:, c:c + 1], b2_i.rearrange("(d o) -> d o", o=1)[c * 128:(c + 1) * 128], inc="ld2")
        for c in range(NH):
            sz = hsz[c]
            ev.dma("sync", b1av[0:sz, c:c + 1],
                   b1_i.rearrange("(d o) -> d o", o=1)[c * 128:c * 128 + sz], inc="ld2")
            ev.dma("sync", b1gv[0:sz, c:c + 1],
                   b1_i.rearrange("(d o) -> d o", o=1)[2730 + c * 128:2730 + c * 128 + sz], inc="ld2")
        w2ld = [ev.dma("gpsimd",
                       w2sb[:, 0:21 * 1024].rearrange("p (c n) -> p c n", n=1024),
                       w2_i[0:2688, :].rearrange("(c p) n -> p c n", p=128), inc="wd"),
                ev.dma("gpsimd", w2sb[0:42, 21 * 1024:22 * 1024],
                       w2_i[2688:2730, :], inc="wd")]
        mset = ev.op("vector", lambda e: e.memset(onesc[:], 1.0), inc="vx")
        lds_all = ev.wait_all("ld", "ld2")
        pdone = []
        for c in range(4):
            ev.op("vector", lambda e, c=c: e.memset(patch[c][:, 0:1], 0.0), inc="vx")
            pdone.append(ev.op("vector", lambda e, c=c: e.tensor_copy(
                patch[c][:, 1:512], xf[c + 4][:, 0:511]), waits=[lds[c + 4]], inc="vx"))

        def h2slice(c, t0, wdt=512):
            if c < 4:
                return xf[c][:, t0:t0 + wdt]
            if t0 == 0:
                return patch[c - 4][:, 0:wdt]
            return xf[c][:, t0 - 1:t0 - 1 + wdt]

        for tc in range(4):
            t0 = tc * 512
            mS = None
            mQ = None
            sq_done = {}
            for c in range(ND):
                mS = ev.op("tensor", lambda e, c=c, t0=t0: e.matmul(
                    ps_S[:], onesc[:], h2slice(c, t0), start=(c == 0), stop=(c == ND - 1)),
                    waits=(lds_all + pdone + [mset] + ev.wait_all("vx")) if c == 0 else [],
                    inc="mm")
            for c in range(ND):
                sq1 = ev.op("scalar", lambda e, c=c, t0=t0: e.square(
                    sqv[c % 2][:], h2slice(c, t0)),
                    waits=(sq_done[c - 2] if c >= 2 else lds_all + pdone),
                    inc="sc")
                mQ = ev.op("tensor", lambda e, c=c: e.matmul(
                    ps_Q[:], onesc[:], sqv[c % 2][:], start=(c == 0), stop=(c == ND - 1)),
                    waits=[sq1], inc="mm")
                sq_done[c] = [mQ]
            v1 = ev.op("vector", lambda e: e.tensor_scalar_mul(murow[:], ps_S[:], 1.0 / D),
                       waits=[mS], inc="vx")
            v2 = ev.op("vector", lambda e: e.tensor_scalar_mul(e2row[:], ps_Q[:], 1.0 / D),
                       waits=[mQ], inc="vx")
            s2 = ev.op("scalar", lambda e: e.square(musq[:], murow[:]), waits=[v1], inc="sc")
            v3 = ev.op("vector", lambda e: e.tensor_sub(varrow[:], e2row[:], musq[:]),
                       waits=[s2, v2], inc="vx")
            s3 = ev.op("scalar", lambda e: e.activation(rrow[:], varrow[:], AF.Sqrt,
                                                        bias=epsv[0:1, 0:1]),
                       waits=[v3, ep0], inc="sc")
            v4 = ev.op("vector", lambda e: e.reciprocal(rrow[:], rrow[:]), waits=[s3], inc="vx")
            mb1 = ev.op("tensor", lambda e: e.matmul(
                ps_bc[:], onesb[0:1, :], murow[0:1, :], start=True, stop=True),
                waits=[v1] + ev.wait_all("vx"), inc="mm")
            vb1 = ev.op("vector", lambda e: e.tensor_copy(mubc[:], ps_bc[:]),
                        waits=[mb1], inc="vx")
            mb2 = ev.op("tensor", lambda e: e.matmul(
                ps_bc[:], onesb[0:1, :], rrow[0:1, :], start=True, stop=True),
                waits=[v4, vb1], inc="mm")
            vb2 = ev.op("vector", lambda e: e.tensor_copy(rbc[:], ps_bc[:]),
                        waits=[mb2], inc="vx")
            for c in range(ND):
                va = ev.op("vector", lambda e, c=c, t0=t0: e.tensor_sub(
                    tta[:], h2slice(c, t0), mubc[:]), waits=[vb1, vb2], inc="vx")
                vb = ev.op("vector", lambda e: e.tensor_mul(tta[:], tta[:], rbc[:]),
                           waits=[va], inc="vx")
                ev.op("vector", lambda e, c=c, t0=t0: e.tensor_scalar(
                    xh[c][:, t0:t0 + 512], tta[:], lgv[:, c:c + 1], lbv[:, c:c + 1],
                    ALU.mult, ALU.add), waits=[vb], inc="vx")
        xh_done = ev.wait_all("vx")

        def load_w1(c, buf, waits):
            sz = hsz[c]
            o = []
            for (dst, c0) in ((w1a, c * 128), (w1g, 2730 + c * 128)):
                o.append(ev.dma(
                    "gpsimd",
                    dst[buf][:].rearrange("p (k n) -> p k n", n=128)[:, :, 0:sz],
                    w1_i[:, c0:c0 + sz].rearrange("(k p) n -> p k n", p=128),
                    waits=waits, inc="wd" if buf == 0 else "wd2"))
            return o

        w1ld = {}
        w1ld[0] = load_w1(0, 0, [])
        w1ld[1] = load_w1(1, 1, [])
        for tc in range(4):
            t0 = tc * 512
            act_done = {}
            mm_done = {}
            for c in range(NH):
                sz = hsz[c]
                buf = c % 2
                nxt = c + 2
                ma = mg = None
                for k in range(ND):
                    ma = ev.op("tensor", lambda e, k=k, c=c, t0=t0, buf=buf, sz=sz: e.matmul(
                        ps_a[buf][0:sz, :], w1a[buf][:, k * 128:k * 128 + sz],
                        xh[k][:, t0:t0 + 512], start=(k == 0), stop=(k == ND - 1)),
                        waits=(w1ld[c] + (xh_done if (tc == 0 and c == 0) else []) +
                               (act_done.get(c - 2, []))) if k == 0 else [],
                        inc="mm")
                for k in range(ND):
                    mg = ev.op("tensor", lambda e, k=k, c=c, t0=t0, buf=buf, sz=sz: e.matmul(
                        ps_g[buf][0:sz, :], w1g[buf][:, k * 128:k * 128 + sz],
                        xh[k][:, t0:t0 + 512], start=(k == 0), stop=(k == ND - 1)),
                        waits=(act_done.get(c - 2, [])) if k == 0 else [],
                        inc="mm")
                mm_done[c] = [ma, mg]
                if tc < 3 or True:
                    nc_load = c + 2
                    if nc_load < NH:
                        w1ld[nc_load] = load_w1(nc_load, nc_load % 2, mm_done.get(c, []))
                    elif tc < 3 and nc_load - NH < 2:
                        w1ld[nc_load - NH] = load_w1(nc_load - NH, (nc_load - NH) % 2,
                                                     mm_done.get(c, []))
                g1 = ev.op("scalar", lambda e, c=c, sz=sz, buf=buf: e.activation(
                    gbuf[buf][0:sz, :], ps_g[buf][0:sz, :], AF.Gelu,
                    bias=b1gv[0:sz, c:c + 1]), waits=[mg], inc="sc")
                a1 = ev.op("vector", lambda e, c=c, sz=sz, buf=buf: e.scalar_tensor_tensor(
                    acts[0:sz, c * 512:(c + 1) * 512], ps_a[buf][0:sz, :],
                    b1av[0:sz, c:c + 1], gbuf[buf][0:sz, :], ALU.add, ALU.mult),
                    waits=[g1], inc="vx")
                act_done[c] = [a1]
            all_act = ev.wait_all("vx")
            ev_last = []
            for dc in range(ND):
                mo = None
                for c in range(NH):
                    sz = hsz[c]
                    mo = ev.op("tensor", lambda e, c=c, dc=dc, sz=sz: e.matmul(
                        ps_out[:], w2sb[0:sz, c * 1024 + dc * 128: c * 1024 + dc * 128 + 128],
                        acts[0:sz, c * 512:(c + 1) * 512],
                        start=(c == 0), stop=(c == NH - 1)),
                        waits=(all_act + w2ld + (ev_last[-1:] if dc >= 1 else [])) if c == 0 else [],
                        inc="mm")
                ev_last.append(ev.op("vector", lambda e, dc=dc, t0=t0: e.scalar_tensor_tensor(
                    xf[dc][:, t0:t0 + 512], ps_out[:], b2v[:, dc:dc + 1],
                    xf[dc][:, t0:t0 + 512], ALU.add, ALU.add), waits=[mo], inc="vx"))
        res_done = ev.wait_all("vx")
        if not out_normal:
            for c in range(ND):
                ev.dma("sync", outp[c * 128:(c + 1) * 128, :], xf[c][:],
                       waits=res_done, inc="st")
        else:
            tr_prev = None
            for t in range(NT):
                gts = []
                for c in range(ND):
                    m = ev.op("tensor", lambda e, c=c, t=t: e.transpose(
                        ps_t[:, 0:128], xf[c][:, t * 128:(t + 1) * 128], idf[:]),
                        waits=(res_done if (t == 0 and c == 0) else []) +
                              ([tr_prev] if tr_prev else []), inc="mm")
                    tr_prev = ev.op("vector", lambda e, c=c: e.tensor_copy(
                        gtile[:, c * 128:(c + 1) * 128], ps_t[:, 0:128]), waits=[m], inc="vx")
                    gts.append(tr_prev)
                ev.dma("sync", outp[t * 128:(t + 1) * 128, :], gtile[:],
                       waits=gts, inc="st")
                tr_prev = ev.mark("st")


def emit_comb1(nc, ev, ctx, io, pfx=""):
    x_i = io["x_i"]
    s_i = io["s_i"]
    g_i = io["g_i"]
    ident = io["ident"]
    x1_o = io["x1_o"]
    xnTf_o = io["xnTf_o"]
    xnTb_o = io["xnTb_o"]
    xnN_o = io["xnN_o"]
    if 1:
        sb = lambda *a: ctx.enter_context(nc.sbuf_tensor(pfx + a[0], *a[1:]))
        psa = lambda *a: ctx.enter_context(nc.psum_tensor(pfx + a[0], *a[1:]))
        xt = [sb(f"xt{i}", [128, D], F32) for i in range(2)]
        st = [sb(f"st{i}", [128, D], F32) for i in range(2)]
        gt = [sb(f"gt{i}", [128, D], F32) for i in range(2)]
        gs = sb("gs", [128, D], F32)
        x1t = [sb(f"x1t{i}", [128, D], F32) for i in range(2)]
        xnt = [sb(f"xnt{i}", [128, D], F32) for i in range(2)]
        sq = sb("sq", [128, D], F32)
        s1 = sb("s1", [128, 1], F32)
        s2 = sb("s2", [128, 1], F32)
        mu = sb("mu", [128, 1], F32)
        mu2 = sb("mu2", [128, 1], F32)
        var = sb("var", [128, 1], F32)
        rt = sb("rt", [128, 1], F32)
        rr = sb("rr", [128, 1], F32)
        idf = sb("idf", [128, 128], F32)
        chunks = [sb(f"ch{c}", [128, NTOK], F32) for c in range(ND)]
        epsv = sb("epsv", [128, 1], F32)
        tp = [psa(f"tp{i}", [128, 128], F32) for i in range(2)]
        ev.dma("sync", idf[:], ident, inc="ld")
        ep = ev.op("vector", lambda e: e.memset(epsv[:], 1e-5), inc="vx")
        guard = [None, None]
        for i in range(NT):
            b = i % 2
            w = [] if guard[b] is None else list(guard[b])
            l1 = ev.dma("sync", xt[b][:], x_i[i * 128:(i + 1) * 128, :], waits=w, inc="ld")
            l2 = ev.dma("sync", st[b][:], s_i[i * 128:(i + 1) * 128, :], waits=w, inc="ld")
            l3 = ev.dma("gpsimd", gt[b][:], g_i[i * 128:(i + 1) * 128, :], waits=w, inc="ldg")
            sc0 = ev.op("scalar", lambda e, b=b: e.activation(
                gs[:], gt[b][:], AF.Silu), waits=[l3] + ev.wait_all("vx"), inc="sc")
            v0 = ev.op("vector", lambda e, b=b: e.tensor_mul(gs[:], st[b][:], gs[:]),
                       waits=[sc0, l2], inc="vx")
            v0b = ev.op("vector", lambda e, b=b: e.tensor_add(x1t[b][:], xt[b][:], gs[:]),
                        waits=[v0, l1], inc="vx")
            d1 = ev.dma("sync", x1_o[i * 128:(i + 1) * 128, :], x1t[b][:],
                        waits=[v0b], inc="st")
            sc1 = ev.op("scalar", lambda e, b=b: e.square(sq[:], x1t[b][:]),
                        waits=[v0b], inc="sc")
            v1 = ev.op("vector", lambda e, b=b: e.reduce_sum(s1[:], x1t[b][:], AX.X),
                       waits=[v0b], inc="vx")
            v2 = ev.op("vector", lambda e: e.reduce_sum(s2[:], sq[:], AX.X),
                       waits=[sc1], inc="vx")
            ev.op("vector", lambda e: e.tensor_scalar_mul(mu[:], s1[:], 1.0 / D),
                  waits=[v1], inc="vx")
            sc2 = ev.op("scalar", lambda e: e.square(mu2[:], mu[:]),
                        waits=ev.wait_all("vx"), inc="sc")
            v3 = ev.op("vector", lambda e: e.scalar_tensor_tensor(
                var[:], s2[:], 1.0 / D, mu2[:], ALU.mult, ALU.subtract),
                waits=[sc2, v2], inc="vx")
            sc3 = ev.op("scalar", lambda e: e.activation(
                rt[:], var[:], AF.Sqrt, bias=epsv[:]), waits=[v3, ep], inc="sc")
            v4 = ev.op("vector", lambda e: e.reciprocal(rr[:], rt[:]),
                       waits=[sc3], inc="vx")
            vn = ev.op("vector", lambda e, b=b: e.tensor_scalar(
                xnt[b][:], x1t[b][:], mu[:], rr[:], ALU.subtract, ALU.mult),
                waits=[v4], inc="vx")
            d2 = ev.dma("gpsimd", xnN_o[i * 128:(i + 1) * 128, :], xnt[b][:],
                        waits=[vn], inc="stg")
            cv = vn
            for c in range(ND):
                pb = c % 2
                m = ev.op("tensor", lambda e, b=b, c=c, pb=pb: e.transpose(
                    tp[pb][:], xnt[b][:, c * 128:(c + 1) * 128], idf[:]),
                    waits=[vn] + ev.wait_all("vx")[-1:], inc="mm")
                cv = ev.op("vector", lambda e, i=i, c=c, pb=pb: e.tensor_copy(
                    chunks[c][:, i * 128:(i + 1) * 128], tp[pb][:]),
                    waits=[m], inc="vx")
            guard[b] = [cv, d1, ev.mark("st"), ev.mark("stg")]
        w = ev.wait_all("vx")
        for c in range(ND):
            ev.dma("sync", xnTf_o[c * 128:(c + 1) * 128, :], chunks[c][:],
                   waits=w, inc="st")
            ev.dma("gpsimd", xnTb_o[c * 128:(c + 1) * 128, :], chunks[c][:],
                   waits=w, inc="stg")


def emit_l1attn(nc, ev, ctx, io, pfx=""):
    xnT_b = io["xnT_b"]
    xnN_b = io["xnN_b"]
    QK_i = io["QK_i"]
    VO_i = io["VO_i"]
    onesr = io["onesr"]
    s1_o = io["s1_o"]
    if 1:
        sb = lambda *a: ctx.enter_context(nc.sbuf_tensor(pfx + a[0], *a[1:]))
        psa = lambda *a: ctx.enter_context(nc.psum_tensor(pfx + a[0], *a[1:]))
        xn = [sb(f"xn{c}", [128, NTOK], BF16) for c in range(ND)]
        xnN = sb("xnN", [128, NT * D], BF16)
        qksb = sb("qksb", [128, ND * D], BF16)
        vosb = sb("vosb", [128, ND * D], BF16)
        xqk = [sb(f"xqk{c}", [128, NTOK], BF16) for c in range(ND)]
        at = sb("at", [128, 16 * 512], BF16)
        avsb = sb("avsb", [128, ND * 512], BF16)
        onesb = sb("onesb", [1, 128], F32)
        ones_bf = sb("ones_bf", [128, 1], BF16)
        rec = sb("rec", [1, 512], F32)
        bcs = sb("bcs", [128, 512], F32)
        stile = [sb(f"stile{i}", [128, D], F32) for i in range(2)]
        ps_s = [psa(f"ps_s{i}", [128, 512], F32) for i in range(2)]
        ps_den = psa("ps_den", [1, 512], F32)
        ps_bc = psa("ps_bc", [128, 512], F32)
        ps_av = psa("ps_av", [128, 512], F32)
        ps_vo = [psa(f"ps_vo{i}", [128, 512], F32) for i in range(2)]
        m8v = sb("m8v", [128, 1], F32)
        ep0 = ev.op("vector", lambda e: e.memset(m8v[:], -8.0), inc="vx")
        lds = []
        for c in range(ND):
            lds.append(ev.dma("sync", xn[c][:], xnT_b[c * 128:(c + 1) * 128, :], inc="ld"))
        for t in range(NT):
            lds.append(ev.dma("sync", xnN[:, t * D:(t + 1) * D],
                              xnN_b[t * 128:(t + 1) * 128, :], inc="ld"))
        ev.dma("sync", onesb[:], onesr, inc="ld")
        wd1 = ev.dma("gpsimd", qksb[:].rearrange("p (k n) -> p k n", n=D),
                     QK_i.rearrange("(k p) n -> p k n", p=128), inc="wd")
        wd2 = ev.dma("gpsimd", vosb[:].rearrange("p (k n) -> p k n", n=D),
                     VO_i.rearrange("(k p) n -> p k n", p=128), inc="wd")
        mset = ev.op("vector", lambda e: e.memset(ones_bf[:], 1.0), inc="vx")
        lds_all = ev.wait_all("ld", "ld2")
        st_guard = [None, None]
        guard = None
        for ecn in range(ND):
            for tcc in range(4):
                acc = None
                for c in range(ND):
                    acc = ev.op("tensor", lambda e, c=c, ecn=ecn, tcc=tcc: e.matmul(
                        ps_s[tcc % 2][:], qksb[:, c * D + ecn * 128: c * D + ecn * 128 + 128],
                        xn[c][:, tcc * 512:(tcc + 1) * 512],
                        start=(c == 0), stop=(c == ND - 1)),
                        waits=(lds_all + [wd1] + ([guard] if guard else [])) if c == 0 else [],
                        inc="mm")
                guard = ev.op("vector", lambda e, ecn=ecn, tcc=tcc: e.tensor_copy(
                    xqk[ecn][:, tcc * 512:(tcc + 1) * 512], ps_s[tcc % 2][:]),
                    waits=[acc], inc="vx")
        xqk_done = ev.wait_all("vx")
        for tc in range(4):
            t0 = tc * 512
            exp_done = {}
            for ut in range(16):
                pb = ut % 2
                warw = ev.wait_all("sc", "vx") if ut < 2 else [exp_done[ut - 2]]
                acc = None
                for c in range(ND):
                    acc = ev.op("tensor", lambda e, c=c, ut=ut, t0=t0, pb=pb: e.matmul(
                        ps_s[pb][:], xn[c][:, ut * 128:(ut + 1) * 128],
                        xqk[c][:, t0:t0 + 512],
                        start=(c == 0), stop=(c == ND - 1)),
                        waits=((xqk_done if (tc == 0 and ut == 0) else []) + warw) if c == 0 else [],
                        inc="mm")
                e1 = ev.op("scalar", lambda e, ut=ut, pb=pb: e.activation(
                    at[:, ut * 512:(ut + 1) * 512], ps_s[pb][:], AF.Exp,
                    bias=m8v[:], scale=1.0 / 32.0), waits=[acc, ep0], inc="sc")
                exp_done[ut] = e1
            mden = None
            for ut in range(16):
                mden = ev.op("tensor", lambda e, ut=ut: e.matmul(
                    ps_den[:], ones_bf[:], at[:, ut * 512:(ut + 1) * 512],
                    start=(ut == 0), stop=(ut == 15)),
                    waits=[exp_done[ut]] + ([mset] + ev.wait_all("vx") if ut == 0 else []),
                    inc="mm")
            r1 = ev.op("vector", lambda e: e.reciprocal(rec[:], ps_den[:]),
                       waits=[mden], inc="vx")
            mb = ev.op("tensor", lambda e: e.matmul(
                ps_bc[:], onesb[0:1, :], rec[0:1, :], start=True, stop=True),
                waits=[r1], inc="mm")
            cb = ev.op("vector", lambda e: e.tensor_copy(bcs[:], ps_bc[:]),
                       waits=[mb], inc="vx")
            for dc in range(ND):
                mav = None
                for ut in range(16):
                    mav = ev.op("tensor", lambda e, dc=dc, ut=ut: e.matmul(
                        ps_av[:], xnN[:, ut * D + dc * 128: ut * D + dc * 128 + 128],
                        at[:, ut * 512:(ut + 1) * 512],
                        start=(ut == 0), stop=(ut == 15)),
                        waits=([cb] + ev.wait_all("vx")) if ut == 0 else [],
                        inc="mm")
                ev.op("vector", lambda e, dc=dc: e.tensor_tensor(
                    avsb[:, dc * 512:(dc + 1) * 512], ps_av[:], bcs[:], ALU.mult),
                    waits=[mav], inc="vx")
            av_done = ev.wait_all("vx")
            for tt in range(4):
                sready = []
                for n in range(2):
                    acc = None
                    for dc in range(ND):
                        acc = ev.op("tensor", lambda e, dc=dc, tt=tt, n=n: e.matmul(
                            ps_vo[n][:], avsb[:, dc * 512 + tt * 128: dc * 512 + tt * 128 + 128],
                            vosb[:, dc * D + n * 512: dc * D + (n + 1) * 512],
                            start=(dc == 0), stop=(dc == ND - 1)),
                            waits=(av_done + [wd2] + ev.wait_all("vx")[-1:]) if dc == 0 else [],
                            inc="mm")
                    sready.append(ev.op("vector", lambda e, tt=tt, n=n: e.tensor_copy(
                        stile[tt % 2][:, n * 512:(n + 1) * 512], ps_vo[n][:]),
                        waits=[acc] + ([st_guard[tt % 2]] if st_guard[tt % 2] else []),
                        inc="vx"))
                ev.dma("sync", s1_o[t0 + tt * 128: t0 + (tt + 1) * 128, :],
                       stile[tt % 2][:], waits=sready, inc="st")
                st_guard[tt % 2] = ev.mark("st")


def emit_l1ffn(nc, ev, ctx, io, pfx=""):
    xnT_f = io["xnT_f"]
    cw = io["cw"]
    cb = io["cb"]
    gw = io["gw"]
    gb = io["gb"]
    ar_i = io["ar_i"]
    onesr = io["onesr"]
    ident = io["ident"]
    o1_o = io["o1_o"]
    if 1:
        sb = lambda *a: ctx.enter_context(nc.sbuf_tensor(pfx + a[0], *a[1:]))
        psa = lambda *a: ctx.enter_context(nc.psum_tensor(pfx + a[0], *a[1:]))
        out = [sb(f"out{c}", [128, NTOK], F32) for c in range(ND)]
        w1sb = sb("w1sb", [128, ND * D], BF16)
        w2sb = sb("w2sb", [128, ND * 512], BF16)
        w3sb = sb("w3sb", [128, 4 * 256], BF16)
        w4sb = sb("w4sb", [128, 2], BF16)
        b1v = sb("b1v", [128, ND], F32)
        b2v = sb("b2v", [128, 4], F32)
        b3v = sb("b3v", [128, 2], F32)
        b4v = sb("b4v", [1, 1], F32)
        g1sb = sb("g1sb", [128, ND * 512], F32)
        g2sb = sb("g2sb", [128, 4 * 256], F32)
        g3sb = sb("g3sb", [128, 2], F32)
        gb1v = sb("gb1v", [128, 4], F32)
        gb2v = sb("gb2v", [128, 2], F32)
        gb3v = sb("gb3v", [1, 1], F32)
        arv = sb("arv", [1, 1], F32)
        gcs = sb("gcs", [128, ND], F32)
        h1v = sb("h1v", [128, 4], F32)
        h2v = sb("h2v", [128, 2], F32)
        gfv = sb("gfv", [1, 1], F32)
        c0v = sb("c0v", [1, 1], F32)
        c1v = sb("c1v", [1, 1], F32)
        t1v = sb("t1v", [1, 1], F32)
        obf = sb("obf", [128, ND * 512], BF16)
        h1sb = sb("h1sb", [128, ND * 512], BF16)
        h2sb = sb("h2sb", [128, 4 * 512], BF16)
        h3sb = sb("h3sb", [128, 2 * 512], BF16)
        tfrow = sb("tfrow", [1, 512], F32)
        coup = sb("coup", [1, 512], F32)
        tcp = sb("tcp", [128, 512], F32)
        ttb = sb("ttb", [128, 512], F32)
        onesb = sb("onesb", [1, 128], F32)
        idf = sb("idf", [128, 128], F32)
        gtile = sb("gtile", [128, D], F32)
        ps_m = [psa(f"ps_m{i}", [128, 512], F32) for i in range(2)]
        ps_sm = psa("ps_sm", [128, 1], F32)
        ps_tf = psa("ps_tf", [1, 512], F32)
        ps_bc = psa("ps_bc", [128, 512], F32)
        ps_t = psa("ps_t", [128, 128], F32)
        lds = []
        for c in range(ND):
            lds.append(ev.dma("sync", out[c][:], xnT_f[c * 128:(c + 1) * 128, :], inc="ld"))
        ev.dma("sync", onesb[:], onesr, inc="ld")
        ev.dma("sync", idf[:], ident, inc="ld")
        ev.dma("sync", arv[:], ar_i.rearrange("(o d) -> o d", o=1), inc="ld")
        wds = []
        wds.append(ev.dma("gpsimd", w1sb[:].rearrange("p (k n) -> p k n", n=D),
                          cw["cn_w1"].rearrange("(k p) n -> p k n", p=128), inc="wd"))
        wds.append(ev.dma("gpsimd", w2sb[:].rearrange("p (k n) -> p k n", n=512),
                          cw["cn_w2"].rearrange("(k p) n -> p k n", p=128), inc="wd"))
        wds.append(ev.dma("gpsimd", w3sb[:].rearrange("p (k n) -> p k n", n=256),
                          cw["cn_w3"].rearrange("(k p) n -> p k n", p=128), inc="wd"))
        for c in range(2):
            wds.append(ev.dma("gpsimd", w4sb[:, c:c + 1],
                              cw["cn_w4"][c * 128:(c + 1) * 128, 0:1], inc="wd"))
        wds.append(ev.dma("sync", g1sb[:].rearrange("p (k n) -> p k n", n=512),
                          gw["gc_w1"].rearrange("(k p) n -> p k n", p=128), inc="ld"))
        wds.append(ev.dma("sync", g2sb[:].rearrange("p (k n) -> p k n", n=256),
                          gw["gc_w2"].rearrange("(k p) n -> p k n", p=128), inc="ld"))
        for c in range(2):
            wds.append(ev.dma("sync", g3sb[:, c:c + 1],
                              gw["gc_w3"][c * 128:(c + 1) * 128, 0:1], inc="ld"))
        for c in range(ND):
            ev.dma("sync", b1v[:, c:c + 1],
                   cb["cn_b1"].rearrange("(d o) -> d o", o=1)[c * 128:(c + 1) * 128], inc="ld")
        for c in range(4):
            ev.dma("sync", b2v[:, c:c + 1],
                   cb["cn_b2"].rearrange("(d o) -> d o", o=1)[c * 128:(c + 1) * 128], inc="ld")
            ev.dma("sync", gb1v[:, c:c + 1],
                   gb["gc_b1"].rearrange("(d o) -> d o", o=1)[c * 128:(c + 1) * 128], inc="ld")
        for c in range(2):
            ev.dma("sync", b3v[:, c:c + 1],
                   cb["cn_b3"].rearrange("(d o) -> d o", o=1)[c * 128:(c + 1) * 128], inc="ld")
            ev.dma("sync", gb2v[:, c:c + 1],
                   gb["gc_b2"].rearrange("(d o) -> d o", o=1)[c * 128:(c + 1) * 128], inc="ld")
        ev.dma("sync", b4v[:], cb["cn_b4"].rearrange("(o d) -> o d", o=1), inc="ld")
        ev.dma("sync", gb3v[:], gb["gc_b3"].rearrange("(o d) -> o d", o=1), inc="ld")
        lds_all = ev.wait_all("ld", "ld2", "wd", "wd2")
        gm = []
        for c in range(ND):
            r = ev.op("vector", lambda e, c=c: e.reduce_sum(gcs[:, c:c + 1], out[c][:], AX.X),
                      waits=[lds[c]], inc="vx")
            gm.append(r)
        vsc = ev.op("vector", lambda e: e.tensor_scalar_mul(gcs[:], gcs[:], 1.0 / NTOK),
                    waits=gm, inc="vx")
        h1done = []
        for m in range(4):
            acc = None
            for c in range(ND):
                acc = ev.op("tensor", lambda e, m=m, c=c: e.matmul(
                    ps_sm[:], g1sb[:, c * 512 + m * 128: c * 512 + m * 128 + 128],
                    gcs[:, c:c + 1], start=(c == 0), stop=(c == ND - 1)),
                    waits=(lds_all + [vsc] + h1done[-1:]) if c == 0 else [], inc="mm")
            h1done.append(ev.op("scalar", lambda e, m=m: e.activation(
                h1v[:, m:m + 1], ps_sm[:], AF.Gelu, bias=gb1v[:, m:m + 1]),
                waits=[acc], inc="sc"))
        h2done = []
        for m in range(2):
            acc = None
            for c in range(4):
                acc = ev.op("tensor", lambda e, m=m, c=c: e.matmul(
                    ps_sm[:], g2sb[:, c * 256 + m * 128: c * 256 + m * 128 + 128],
                    h1v[:, c:c + 1], start=(c == 0), stop=(c == 3)),
                    waits=(h1done + h2done[-1:]) if c == 0 else [], inc="mm")
            h2done.append(ev.op("scalar", lambda e, m=m: e.activation(
                h2v[:, m:m + 1], ps_sm[:], AF.Gelu, bias=gb2v[:, m:m + 1]),
                waits=[acc], inc="sc"))
        acc = None
        for c in range(2):
            acc = ev.op("tensor", lambda e, c=c: e.matmul(
                ps_sm[0:1, :], g3sb[0:128, c:c + 1], h2v[:, c:c + 1],
                start=(c == 0), stop=(c == 1)),
                waits=(h2done + h1done) if c == 0 else [], inc="mm")
        gfd = ev.op("scalar", lambda e: e.activation(
            gfv[:], ps_sm[0:1, 0:1], AF.Sigmoid, bias=gb3v[0:1, 0:1]),
            waits=[acc], inc="sc")
        v1 = ev.op("vector", lambda e: e.tensor_scalar_mul(c1v[:], arv[:], 0.6),
                   waits=[gfd] + lds_all, inc="vx")
        v2 = ev.op("vector", lambda e: e.tensor_mul(t1v[:], arv[:], gfv[:]),
                   waits=[gfd], inc="vx")
        v3 = ev.op("vector", lambda e: e.scalar_tensor_tensor(
            t1v[:], t1v[:], 1.4, arv[:], ALU.mult, ALU.subtract), waits=[v2], inc="vx")
        cc = ev.op("vector", lambda e: e.tensor_scalar_add(c0v[:], t1v[:], 0.1),
                   waits=[v3], inc="vx")
        for cyc in range(3):
            for tc in range(4):
                t0 = tc * 512
                cvt = []
                for c in range(ND):
                    cvt.append(ev.op("vector", lambda e, c=c, t0=t0: e.tensor_copy(
                        obf[:, c * 512:(c + 1) * 512], out[c][:, t0:t0 + 512]),
                        waits=ev.wait_all("mm")[-1:] if (cyc > 0 or tc > 0) else [],
                        inc="vx"))
                h1d = []
                for m in range(ND):
                    acc = None
                    for c in range(ND):
                        acc = ev.op("tensor", lambda e, m=m, c=c: e.matmul(
                            ps_m[m % 2][:], w1sb[:, c * D + m * 128: c * D + m * 128 + 128],
                            obf[:, c * 512:(c + 1) * 512],
                            start=(c == 0), stop=(c == ND - 1)),
                            waits=(cvt + lds_all + ev.wait_all("sc")[-1:]) if c == 0 else [],
                            inc="mm")
                    h1d.append(ev.op("scalar", lambda e, m=m: e.activation(
                        h1sb[:, m * 512:(m + 1) * 512], ps_m[m % 2][:], AF.Gelu,
                        bias=b1v[:, m:m + 1]), waits=[acc], inc="sc"))
                h2d = []
                for m in range(4):
                    acc = None
                    for c in range(ND):
                        acc = ev.op("tensor", lambda e, m=m, c=c: e.matmul(
                            ps_m[m % 2][:], w2sb[:, c * 512 + m * 128: c * 512 + m * 128 + 128],
                            h1sb[:, c * 512:(c + 1) * 512],
                            start=(c == 0), stop=(c == ND - 1)),
                            waits=(h1d + ev.wait_all("sc")[-1:]) if c == 0 else [],
                            inc="mm")
                    h2d.append(ev.op("scalar", lambda e, m=m: e.activation(
                        h2sb[:, m * 512:(m + 1) * 512], ps_m[m % 2][:], AF.Gelu,
                        bias=b2v[:, m:m + 1]), waits=[acc], inc="sc"))
                h3d = []
                for m in range(2):
                    acc = None
                    for c in range(4):
                        acc = ev.op("tensor", lambda e, m=m, c=c: e.matmul(
                            ps_m[m % 2][:], w3sb[:, c * 256 + m * 128: c * 256 + m * 128 + 128],
                            h2sb[:, c * 512:(c + 1) * 512],
                            start=(c == 0), stop=(c == 3)),
                            waits=(h2d + ev.wait_all("sc")[-1:]) if c == 0 else [],
                            inc="mm")
                    h3d.append(ev.op("scalar", lambda e, m=m: e.activation(
                        h3sb[:, m * 512:(m + 1) * 512], ps_m[m % 2][:], AF.Gelu,
                        bias=b3v[:, m:m + 1]), waits=[acc], inc="sc"))
                acc = None
                for c in range(2):
                    acc = ev.op("tensor", lambda e, c=c: e.matmul(
                        ps_tf[:], w4sb[0:128, c:c + 1], h3sb[:, c * 512:(c + 1) * 512],
                        start=(c == 0), stop=(c == 1)),
                        waits=h3d if c == 0 else [], inc="mm")
                tfd = ev.op("scalar", lambda e: e.activation(
                    tfrow[:], ps_tf[:], AF.Sigmoid, bias=b4v[0:1, 0:1]),
                    waits=[acc], inc="sc")
                cpd = ev.op("vector", lambda e: e.tensor_scalar(
                    coup[:], tfrow[:], c1v[0:1, 0:1], c0v[0:1, 0:1], ALU.mult, ALU.add),
                    waits=[tfd, cc], inc="vx")
                mb = ev.op("tensor", lambda e: e.matmul(
                    ps_bc[:], onesb[0:1, :], coup[0:1, :], start=True, stop=True),
                    waits=[cpd] + ev.wait_all("vx")[-1:], inc="mm")
                last = None
                for p in range(4):
                    sgn = 1.0 if p < 2 else -1.0
                    pw = [mb] + ([last] if last else [])
                    vc = ev.op("vector", lambda e, p=p, t0=t0: e.tensor_copy(
                        tcp[:], out[p][:, t0:t0 + 512]), waits=pw, inc="vx")
                    va = ev.op("vector", lambda e, p=p, t0=t0: e.tensor_tensor(
                        ttb[:], ps_bc[:], out[p + 4][:, t0:t0 + 512], ALU.mult),
                        waits=[vc], inc="vx")
                    if sgn > 0:
                        vb = ev.op("vector", lambda e, p=p, t0=t0: e.tensor_add(
                            out[p][:, t0:t0 + 512], out[p][:, t0:t0 + 512], ttb[:]),
                            waits=[va], inc="vx")
                    else:
                        vb = ev.op("vector", lambda e, p=p, t0=t0: e.tensor_sub(
                            out[p][:, t0:t0 + 512], out[p][:, t0:t0 + 512], ttb[:]),
                            waits=[va], inc="vx")
                    vd = ev.op("vector", lambda e, p=p, t0=t0: e.tensor_tensor(
                        ttb[:], ps_bc[:], tcp[:], ALU.mult), waits=[vb], inc="vx")
                    if sgn > 0:
                        last = ev.op("vector", lambda e, p=p, t0=t0: e.tensor_sub(
                            out[p + 4][:, t0:t0 + 512], out[p + 4][:, t0:t0 + 512], ttb[:]),
                            waits=[vd], inc="vx")
                    else:
                        last = ev.op("vector", lambda e, p=p, t0=t0: e.tensor_add(
                            out[p + 4][:, t0:t0 + 512], out[p + 4][:, t0:t0 + 512], ttb[:]),
                            waits=[vd], inc="vx")
        res_done = ev.wait_all("vx")
        tr_prev = None
        for t in range(NT):
            gts = []
            for c in range(ND):
                m = ev.op("tensor", lambda e, c=c, t=t: e.transpose(
                    ps_t[:], out[c][:, t * 128:(t + 1) * 128], idf[:]),
                    waits=(res_done + lds_all) if (t == 0 and c == 0) else
                          ([tr_prev] if tr_prev else []), inc="mm")
                tr_prev = ev.op("vector", lambda e, c=c: e.tensor_copy(
                    gtile[:, c * 128:(c + 1) * 128], ps_t[:]), waits=[m], inc="vx")
                gts.append(tr_prev)
            dm = ev.dma("sync", o1_o[t * 128:(t + 1) * 128, :], gtile[:],
                        waits=gts, inc="st")
            tr_prev = ("st", ev.cnt["st"])


def emit_comb2(nc, ev, ctx, io, pfx=""):
    x1_i = io["x1_i"]
    s1_i = io["s1_i"]
    o1_i = io["o1_i"]
    y_o = io["y_o"]
    if 1:
        sb = lambda *a: ctx.enter_context(nc.sbuf_tensor(pfx + a[0], *a[1:]))
        xt = [sb(f"xt{i}", [128, D], F32) for i in range(2)]
        st = [sb(f"st{i}", [128, D], F32) for i in range(2)]
        ot = [sb(f"ot{i}", [128, D], F32) for i in range(2)]
        gs = sb("gs", [128, D], F32)
        yt = [sb(f"yt{i}", [128, D], F32) for i in range(2)]
        guard = [None, None]
        for i in range(NT):
            b = i % 2
            w = [] if guard[b] is None else [guard[b]]
            l1 = ev.dma("sync", xt[b][:], x1_i[i * 128:(i + 1) * 128, :], waits=w, inc="ld")
            l2 = ev.dma("sync", st[b][:], s1_i[i * 128:(i + 1) * 128, :], waits=w, inc="ld")
            l3 = ev.dma("gpsimd", ot[b][:], o1_i[i * 128:(i + 1) * 128, :], waits=w, inc="ldg")
            sc0 = ev.op("scalar", lambda e, b=b: e.activation(
                gs[:], ot[b][:], AF.Silu), waits=[l3] + ev.wait_all("vx"), inc="sc")
            v0 = ev.op("vector", lambda e, b=b: e.tensor_mul(gs[:], st[b][:], gs[:]),
                       waits=[sc0, l2], inc="vx")
            v1 = ev.op("vector", lambda e, b=b: e.tensor_add(yt[b][:], xt[b][:], gs[:]),
                       waits=[v0, l1], inc="vx")
            guard[b] = ev.dma("sync", y_o[i * 128:(i + 1) * 128, :], yt[b][:],
                              waits=[v1], inc="st")


ALLSEMS = ["ld", "ld2", "ldg", "wd", "wd2", "mm", "sc", "vx", "st", "stg"]


def build_fused():
    nc = bass.Bass("TRN2", target_bir_lowering=False)

    def dt(name, shape, kindt=F32, kind="ExternalInput"):
        return nc.dram_tensor(name, shape, kindt, kind=kind).ap()

    x = dt("x", [NTOK, D])
    l0_w_qkv = dt("l0_w_qkv", [D, 3 * D])
    l0_w_out = dt("l0_w_out", [D, D])
    l0_b_out = dt("l0_b_out", [D])
    l0_gamma = dt("l0_gamma", [D])
    l0_beta = dt("l0_beta", [D])
    f_wqkv = [dt(f"f_wqkv{l}", [D, 1536]) for l in range(2)]
    f_qs = [dt(f"f_qs{l}", [64]) for l in range(2)]
    f_ks = [dt(f"f_ks{l}", [64]) for l in range(2)]
    f_wo = [dt(f"f_wo{l}", [512, D]) for l in range(2)]
    f_lg = [dt(f"f_lg{l}", [D]) for l in range(2)]
    f_lb = [dt(f"f_lb{l}", [D]) for l in range(2)]
    f_w1 = [dt(f"f_w1{l}", [D, 5460]) for l in range(2)]
    f_b1 = [dt(f"f_b1{l}", [5460]) for l in range(2)]
    f_w2 = [dt(f"f_w2{l}", [2730, D]) for l in range(2)]
    f_b2 = [dt(f"f_b2{l}", [D]) for l in range(2)]
    QK = dt("QK", [D, D])
    VO = dt("VO", [D, D])
    cn_w = {1: dt("cn_w1", [D, D]), 2: dt("cn_w2", [D, 512]),
            3: dt("cn_w3", [512, 256]), 4: dt("cn_w4", [256, 1])}
    cn_b = {1: dt("cn_b1", [D]), 2: dt("cn_b2", [512]),
            3: dt("cn_b3", [256]), 4: dt("cn_b4", [1])}
    gc_w = {1: dt("gc_w1", [D, 512]), 2: dt("gc_w2", [512, 256]),
            3: dt("gc_w3", [256, 1])}
    gc_b = {1: dt("gc_b1", [512]), 2: dt("gc_b2", [256]), 3: dt("gc_b3", [1])}
    ar = dt("ar", [1])
    ident = dt("c_ident", [128, 128])
    onesr = dt("c_ones_row", [1, 128])
    ones2 = dt("c_ones2", [128, 2])
    ones2T = dt("c_ones2T", [2, 128])
    maskM = dt("c_maskM", [1024, 512])
    ropes = {n: dt("c_" + n, [128, NTOK]) for n in ("cs_q", "ss_q", "cs_k", "ss_k")}
    def S(name, shape, kindt=F32):
        return nc.dram_tensor(name, shape, kindt, kind="Internal").ap()
    xnT_f = S("s_xnT_f", [D, NTOK])
    xnT_b = S("s_xnT_b", [D, NTOK], BF16)
    s0 = S("s_s0", [NTOK, D])
    xfa0 = S("s_xfa0", [D, NTOK])
    xfb0 = S("s_xfb0", [D, NTOK])
    xfa1 = S("s_xfa1", [D, NTOK])
    g0 = S("s_g0", [NTOK, D])
    x1 = S("s_x1", [NTOK, D])
    xn1T_f = S("s_xn1T_f", [D, NTOK])
    xn1T_b = S("s_xn1T_b", [D, NTOK], BF16)
    xn1N_b = S("s_xn1N_b", [NTOK, D], BF16)
    s1 = S("s_s1", [NTOK, D])
    o1 = S("s_o1", [NTOK, D])
    y = dt("y", [NTOK, D], F32, kind="ExternalOutput")

    with ExitStack() as octx:
        ev = EV(nc, octx, ALLSEMS)
        block = octx.enter_context(nc.Block())

        def phase(emitfn, io, pfx, **kw):
            with ExitStack() as pctx:
                emitfn(nc, ev, pctx, io, pfx=pfx, **kw)
                ev.finalize(block)
            ev.barrier()

        phase(emit_ln0, dict(x=x, ident=ident, xnT_f=xnT_f, xnT_b=xnT_b), "p1_")
        phase(emit_l0attn, dict(xnT_b=xnT_b, w_qkv=l0_w_qkv, w_out=l0_w_out,
                                b_out=l0_b_out, gamma=l0_gamma, beta=l0_beta,
                                onesr=onesr, s_out=s0), "p2_")
        ffn_srcs = [xnT_f, xfa0, xfb0, xfa1]
        for l in range(2):
            phase(emit_ffn_attn,
                  dict(xfT=ffn_srcs[2 * l], wqkv=f_wqkv[l], qs_i=f_qs[l],
                       ks_i=f_ks[l], wo_i=f_wo[l], onesr=onesr, ones2=ones2,
                       ones2T=ones2T, maskM=maskM, ropes=ropes,
                       xfT_o=ffn_srcs[2 * l + 1]), f"p3{l}_")
            phase(emit_ffn_ffn,
                  dict(xfT=ffn_srcs[2 * l + 1], lg_i=f_lg[l], lb_i=f_lb[l],
                       w1_i=f_w1[l], b1_i=f_b1[l], w2_i=f_w2[l], b2_i=f_b2[l],
                       onesr=onesr, ident=ident,
                       outp=(g0 if l == 1 else xfb0)), f"p4{l}_",
                  out_normal=(l == 1))
        phase(emit_comb1, dict(x_i=x, s_i=s0, g_i=g0, ident=ident, x1_o=x1,
                               xnTf_o=xn1T_f, xnTb_o=xn1T_b, xnN_o=xn1N_b), "p5_")
        phase(emit_l1attn, dict(xnT_b=xn1T_b, xnN_b=xn1N_b, QK_i=QK, VO_i=VO,
                                onesr=onesr, s1_o=s1), "p6_")
        phase(emit_l1ffn, dict(xnT_f=xn1T_f,
                               cw={"cn_w1": cn_w[1], "cn_w2": cn_w[2],
                                   "cn_w3": cn_w[3], "cn_w4": cn_w[4]},
                               cb={"cn_b1": cn_b[1], "cn_b2": cn_b[2],
                                   "cn_b3": cn_b[3], "cn_b4": cn_b[4]},
                               gw={"gc_w1": gc_w[1], "gc_w2": gc_w[2],
                                   "gc_w3": gc_w[3]},
                               gb={"gc_b1": gc_b[1], "gc_b2": gc_b[2],
                                   "gc_b3": gc_b[3]},
                               ar_i=ar, onesr=onesr, ident=ident, o1_o=o1), "p7_")
        phase(emit_comb2, dict(x1_i=x1, s1_i=s1, o1_i=o1, y_o=y), "p8_")
        fin = ev.wait_all("st", "stg")
        ev.op("sync", lambda e: e.nop(), waits=fin)
        ev.op("gpsimd", lambda e: e.nop(), waits=fin)
        ev.finalize(block)
    return nc


_programs = {}


def _prog(name, fn):
    if name not in _programs:
        _programs[name] = fn()
    return _programs[name]


def kernel(**inputs):
    f32 = lambda k: np.ascontiguousarray(np.asarray(inputs[k], dtype=np.float32))
    x = f32("L0_attn_x")
    C = CONSTS
    n = N_CORES
    base = dict(
        l0_w_qkv=f32("L0_attn_w_qkv"), l0_w_out=f32("L0_attn_w_out"),
        l0_b_out=f32("L0_attn_b_out"), l0_gamma=f32("L0_attn_ln_gamma"),
        l0_beta=f32("L0_attn_ln_beta"),
        QK=f32("L1_attn_QK"), VO=f32("L1_attn_VO"),
        cn_w1=f32("L1_ffn_cn_w1"), cn_w2=f32("L1_ffn_cn_w2"),
        cn_w3=f32("L1_ffn_cn_w3"), cn_w4=f32("L1_ffn_cn_w4"),
        cn_b1=f32("L1_ffn_cn_b1"), cn_b2=f32("L1_ffn_cn_b2"),
        cn_b3=f32("L1_ffn_cn_b3"), cn_b4=f32("L1_ffn_cn_b4"),
        gc_w1=f32("L1_ffn_gc_w1"), gc_w2=f32("L1_ffn_gc_w2"),
        gc_w3=f32("L1_ffn_gc_w3"),
        gc_b1=f32("L1_ffn_gc_b1"), gc_b2=f32("L1_ffn_gc_b2"),
        gc_b3=f32("L1_ffn_gc_b3"),
        ar=f32("L1_ffn_adaptive_range"),
        c_ident=C["ident"], c_ones_row=C["ones_row"], c_ones2=C["ones2"],
        c_ones2T=C["ones2T"], c_maskM=C["maskM"],
        c_cs_q=C["cs_q"], c_ss_q=C["ss_q"], c_cs_k=C["cs_k"], c_ss_k=C["ss_k"],
    )
    for l in range(2):
        base[f"f_wqkv{l}"] = f32("L0_ffn_wqkv")[l]
        base[f"f_qs{l}"] = f32("L0_ffn_q_scale")[l]
        base[f"f_ks{l}"] = f32("L0_ffn_k_scale")[l]
        base[f"f_wo{l}"] = f32("L0_ffn_wo")[l]
        base[f"f_lg{l}"] = f32("L0_ffn_ln_g")[l]
        base[f"f_lb{l}"] = f32("L0_ffn_ln_b")[l]
        base[f"f_w1{l}"] = f32("L0_ffn_w1")[l]
        base[f"f_b1{l}"] = f32("L0_ffn_b1")[l]
        base[f"f_w2{l}"] = f32("L0_ffn_w2")[l]
        base[f"f_b2{l}"] = f32("L0_ffn_b2")[l]
    in_maps = [dict(base, x=x[i]) for i in range(n)]
    nc = _prog("fused", build_fused)
    res = run_bass_kernel_spmd(nc, in_maps, core_ids=list(range(n)))
    y = np.stack([res.results[i]["y"] for i in range(n)], axis=0)
    return y.astype(np.float32)


# version 05
# speedup vs baseline: 1.3462x; 1.0842x over previous
import numpy as np
import ml_dtypes
from contextlib import ExitStack
from collections import defaultdict

import concourse.bass as bass
import concourse.mybir as mybir
from concourse.bass_utils import run_bass_kernel_spmd

F32 = mybir.dt.float32
BF16 = mybir.dt.bfloat16
AF = mybir.ActivationFunctionType
ALU = mybir.AluOpType
AX = mybir.AxisListType

BF = ml_dtypes.bfloat16
N_CORES = 8
NTOK = 2048
D = 1024
NT = NTOK // 128
ND = D // 128


class EV:

    def __init__(self, nc, ctx, sem_names):
        self.nc = nc
        self.events = defaultdict(list)
        self.cnt = defaultdict(int)
        self.sems = {n: ctx.enter_context(nc.semaphore(n)) for n in sem_names}
        self.pending = {}

    def mark(self, sem):
        return (sem, self.cnt[sem])

    ENG_SEM = {"vector": "vx", "scalar": "sc"}

    def op(self, engine, fn, waits=(), inc=None, inc_n=1):
        w = self.pending.pop(engine, []) + list(waits)
        sname = self.ENG_SEM.get(engine)
        if sname is not None and sname in self.sems and self.cnt[sname] > 0:
            w.append((sname, self.cnt[sname]))
        rec = None
        if inc is not None:
            self.cnt[inc] += inc_n
            rec = (inc, inc_n)
        self.events[engine].append((tuple(w), fn, rec))
        return (inc, self.cnt[inc]) if inc is not None else None

    def dma(self, engine, out, in_, waits=(), inc=None):
        w = list(waits)
        if inc is not None and self.cnt[inc] > 0:
            w.append((inc, self.cnt[inc]))
        return self.op(engine, lambda e: e.dma_start(out=out, in_=in_),
                       waits=w, inc=inc, inc_n=16)

    def wait_all(self, *sem_names):
        return [(s, self.cnt[s]) for s in sem_names]

    def barrier(self):
        waits = self.wait_all(*self.sems.keys())
        for e in ("sync", "gpsimd", "tensor", "scalar", "vector"):
            self.pending[e] = list(waits)

    def finalize(self, block):
        if not hasattr(self, "_waited"):
            self._waited = {e: defaultdict(int)
                            for e in ("sync", "gpsimd", "tensor", "scalar", "vector")}
        for engine in ("sync", "gpsimd", "tensor", "scalar", "vector"):
            evs = self.events[engine]
            if not evs:
                continue
            sems = self.sems
            waited = self._waited[engine]

            def make(evs, waited):
                def emit(eng):
                    for waits, fn, inc in evs:
                        merged = {}
                        for wt in waits:
                            assert isinstance(wt, tuple) and len(wt) == 2, f"bad wait {wt}"
                            s, v = wt
                            if v > merged.get(s, 0):
                                merged[s] = v
                        for s, v in merged.items():
                            if v > 0 and v > waited[s]:
                                eng.wait_ge(sems[s], v)
                                waited[s] = v
                        instr = fn(eng)
                        if inc is not None:
                            instr.then_inc(sems[inc[0]], inc[1])
                return emit
            getattr(block, engine)(make(evs, waited))
        self.events = defaultdict(list)


def sem_val(ev, sem):
    return (sem, ev.cnt[sem])


def make_consts():
    c = {}
    c["ident"] = np.eye(128, dtype=np.float32)
    c["ones_row"] = np.ones((1, 128), dtype=np.float32)
    ones2 = np.zeros((128, 2), dtype=np.float32)
    ones2[:64, 0] = 1.0
    ones2[64:, 1] = 1.0
    c["ones2"] = ones2
    c["ones2T"] = ones2.T.copy()
    DH, W = 64, 512
    half = np.arange(0, DH, 2, dtype=np.float64)
    inv_freq = 1.0 / 10000.0 ** (half / DH)
    t = np.arange(NTOK, dtype=np.float64)
    freqs = t[:, None] * inv_freq[None, :]
    freqs = np.concatenate([freqs, freqs], axis=-1)
    base = (half + 0.4 * DH) / (1.4 * DH)
    power = (t - NTOK // 2) / float(W // 2)
    scale = base[None, :] ** power[:, None]
    scale = np.concatenate([scale, scale], axis=-1)
    cos, sin = np.cos(freqs), np.sin(freqs)
    for name, tab in (("cs_q", cos * scale), ("ss_q", sin * scale),
                      ("cs_k", cos / scale), ("ss_k", sin / scale)):
        tt = tab.T.astype(np.float32)
        c[name] = np.concatenate([tt, tt], axis=0)
    j = np.arange(1024)[:, None]
    i = np.arange(512)[None, :]
    allowed = (i <= j) & (j <= i + W)
    c["maskM"] = np.where(allowed, 100.0, -1e30).astype(np.float32)
    return c


CONSTS = make_consts()


def bcast_cols(ev, psum_bc, ones_row, src_row, dst, n, waits=()):
    for c0 in range(0, n, 512):
        w = min(512, n - c0)
        m = ev.op("tensor", lambda e, c0=c0, w=w: e.matmul(
            psum_bc[:, 0:w], ones_row[0:1, :], src_row[0:1, c0:c0 + w],
            start=True, stop=True),
            waits=list(waits) + ev.wait_all("vx"), inc="mm")
        ev.op("vector", lambda e, c0=c0, w=w: e.tensor_copy(
            dst[:, c0:c0 + w], psum_bc[:, 0:w]), waits=[m], inc="vx")


def emit_ln0(nc, ev, ctx, io, pfx=""):
    x = io["x"]
    ident = io["ident"]
    xnT_f = io["xnT_f"]
    xnT_b = io["xnT_b"]
    if 1:
        sb = lambda *a: ctx.enter_context(nc.sbuf_tensor(pfx + a[0], *a[1:]))
        psa = lambda *a: ctx.enter_context(nc.psum_tensor(pfx + a[0], *a[1:]))
        xt = [sb(f"xt{i}", [128, D], F32) for i in range(2)]
        xn = [sb(f"xn{i}", [128, D], F32) for i in range(2)]
        sq = sb("sq", [128, D], F32)
        s1 = sb("s1", [128, 1], F32)
        s2 = sb("s2", [128, 1], F32)
        mu = sb("mu", [128, 1], F32)
        mu2 = sb("mu2", [128, 1], F32)
        var = sb("var", [128, 1], F32)
        rt = sb("rt", [128, 1], F32)
        rr = sb("rr", [128, 1], F32)
        idf = sb("idf", [128, 128], F32)
        epsv = sb("epsv", [128, 1], F32)
        chunks = [sb(f"ch{c}", [128, NTOK], F32) for c in range(ND)]
        tp = [psa(f"tp{i}", [128, 128], F32) for i in range(2)]
        ev.dma("sync", idf[:], ident, inc="ld")
        ep = ev.op("vector", lambda e: e.memset(epsv[:], 1e-5), inc="vx")
        last_reads = [None, None]
        for i in range(NT):
            b = i % 2
            w = [] if last_reads[b] is None else [last_reads[b]]
            ld = ev.dma("sync", xt[b][:], x[i * 128:(i + 1) * 128, :],
                        waits=w, inc="ld")
            sc1 = ev.op("scalar", lambda e, b=b: e.square(sq[:], xt[b][:]),
                        waits=[ld] + ev.wait_all("vx"), inc="sc")
            v1 = ev.op("vector", lambda e, b=b: e.reduce_sum(s1[:], xt[b][:], AX.X),
                       waits=[ld], inc="vx")
            v2 = ev.op("vector", lambda e: e.reduce_sum(s2[:], sq[:], AX.X),
                       waits=[sc1], inc="vx")
            ev.op("vector", lambda e: e.tensor_scalar_mul(mu[:], s1[:], 1.0 / D),
                  waits=[v1], inc="vx")
            sc2 = ev.op("scalar", lambda e: e.square(mu2[:], mu[:]),
                        waits=ev.wait_all("vx"), inc="sc")
            v3 = ev.op("vector", lambda e: e.scalar_tensor_tensor(
                var[:], s2[:], 1.0 / D, mu2[:], ALU.mult, ALU.subtract),
                waits=[sc2, v2], inc="vx")
            sc3 = ev.op("scalar", lambda e: e.activation(
                rt[:], var[:], AF.Sqrt, bias=epsv[:]), waits=[v3, ep], inc="sc")
            v4 = ev.op("vector", lambda e: e.reciprocal(rr[:], rt[:]),
                       waits=[sc3], inc="vx")
            vn = ev.op("vector", lambda e, b=b: e.tensor_scalar(
                xn[b][:], xt[b][:], mu[:], rr[:], ALU.subtract, ALU.mult),
                waits=[v4], inc="vx")
            last_reads[b] = vn
            for c in range(ND):
                pb = c % 2
                m = ev.op("tensor", lambda e, b=b, c=c, pb=pb: e.transpose(
                    tp[pb][:], xn[b][:, c * 128:(c + 1) * 128], idf[:]),
                    waits=[vn] + ev.wait_all("vx"), inc="mm")
                ev.op("vector", lambda e, i=i, c=c, pb=pb: e.tensor_copy(
                    chunks[c][:, i * 128:(i + 1) * 128], tp[pb][:]),
                    waits=[m], inc="vx")
        w = ev.wait_all("vx")
        for c in range(ND):
            ev.dma("sync", xnT_f[c * 128:(c + 1) * 128, :], chunks[c][:],
                   waits=w, inc="st")
            ev.dma("gpsimd", xnT_b[c * 128:(c + 1) * 128, :], chunks[c][:],
                   waits=w, inc="stg")


def emit_l0attn(nc, ev, ctx, io, pfx=""):
    HG = 8
    xnT_b = io["xnT_b"]
    w_qkv = io["w_qkv"]
    w_out = io["w_out"]
    b_out = io["b_out"]
    gamma = io["gamma"]
    beta = io["beta"]
    onesr = io["onesr"]
    s_out = io["s_out"]
    if 1:
        sb = lambda *a: ctx.enter_context(nc.sbuf_tensor(pfx + a[0], *a[1:]))
        psa = lambda *a: ctx.enter_context(nc.psum_tensor(pfx + a[0], *a[1:]))
        xn = [sb(f"xn{c}", [128, NTOK], BF16) for c in range(ND)]
        oall = [sb(f"oall{c}", [128, NTOK], BF16) for c in range(ND)]
        wq = [sb(f"wq{i}", [128, 1024], BF16) for i in range(2)]
        wk = [sb(f"wk{i}", [128, 1024], BF16) for i in range(2)]
        wv = [sb(f"wv{i}", [128, 1024], BF16) for i in range(2)]
        qT = sb("qT", [128, NTOK], BF16)
        kT = sb("kT", [128, NTOK], BF16)
        vsb = sb("vsb", [128, 16 * 132], BF16)
        at = sb("at", [128, 16 * 512], BF16)
        wo = sb("wo", [128, 8 * 1024], BF16)
        brow = sb("brow", [1, 3 * D], F32)
        bbc = sb("bbc", [128, D], F32)
        gbc = sb("gbc", [128, D], F32)
        bebc = sb("bebc", [128, D], F32)
        onesb = sb("onesb", [1, 128], F32)
        rec = [sb(f"rec{i}", [1, 512], F32) for i in range(2)]
        nbc = [sb(f"nbc{i}", [32, 512], F32) for i in range(2)]
        stile = sb("stile", [128, D], F32)
        sq = sb("sq", [128, D], F32)
        st1 = sb("st1", [128, 1], F32)
        st2 = sb("st2", [128, 1], F32)
        mu = sb("mu", [128, 1], F32)
        mu2 = sb("mu2", [128, 1], F32)
        var = sb("var", [128, 1], F32)
        rt = sb("rt", [128, 1], F32)
        rr = sb("rr", [128, 1], F32)
        otile = sb("otile", [128, D], F32)

        ps_s = [psa(f"ps_s{i}", [128, 1024], F32) for i in range(2)]
        ps_qk = psa("ps_qk", [128, 512], F32)
        ps_o = [psa(f"ps_o{i}", [33, 512], F32) for i in range(2)]
        ps_b = psa("ps_b", [32, 512], F32)
        ps_bc = ps_s[1]
        ps_w = [ps_s[0], ps_qk]
        m8v = sb("m8v", [128, 1], F32)
        epsv = sb("epsv", [128, 1], F32)

        ep0 = ev.op("vector", lambda e: e.memset(m8v[:], -8.0), inc="vx")
        ep1 = ev.op("vector", lambda e: e.memset(epsv[:], 1e-6), inc="vx")
        lds = []
        for c in range(ND):
            lds.append(ev.dma("sync", xn[c][:], xnT_b[c * 128:(c + 1) * 128, :],
                              inc="ld"))
        ev.dma("sync", onesb[:], onesr, inc="ld")
        ev.dma("sync", brow[0:1, 0:D], b_out.rearrange("(o d) -> o d", o=1), inc="ld")
        ev.dma("sync", brow[0:1, D:2 * D], gamma.rearrange("(o d) -> o d", o=1), inc="ld")
        ev.dma("sync", brow[0:1, 2 * D:3 * D], beta.rearrange("(o d) -> o d", o=1), inc="ld")
        ev.dma("gpsimd", wo[:].rearrange("p (k n) -> p k n", n=1024),
               w_out.rearrange("(k p) n -> p k n", p=128), inc="wd")
        wrow = ev.wait_all("ld", "ld2")
        bcast_cols(ev, ps_bc[:, 0:512], onesb, brow[0:1, 0:D], bbc, D, waits=wrow)
        bcast_cols(ev, ps_bc[:, 0:512], onesb, brow[0:1, D:2 * D], gbc, D, waits=wrow)
        bcast_cols(ev, ps_bc[:, 0:512], onesb, brow[0:1, 2 * D:3 * D], bebc, D, waits=wrow)
        vmemset = []
        for t in range(16):
            for hh in range(4):
                vmemset.append(ev.op("vector", lambda e, t=t, hh=hh: e.memset(
                    vsb[:, t * 132 + hh * 33 + 32: t * 132 + hh * 33 + 33], 1.0),
                    inc="vx"))
        def load_group(g, buf, waits=()):
            o = []
            for (dst, c0) in ((wq, g * 128), (wk, D + g * 128), (wv, 2 * D + g * 128)):
                o.append(ev.dma(
                    "gpsimd", dst[buf][:].rearrange("p (k n) -> p k n", n=128),
                    w_qkv[:, c0:c0 + 128].rearrange("(k p) n -> p k n", p=128),
                    waits=waits, inc="wd" if buf == 0 else "wd2"))
            return o

        gw = [None, None]
        gw[0] = load_group(0, 0)
        wguard = [None, None]
        qk_guard = None
        v_guard = None
        prev_exp = None
        prev_av = None
        tt_done = [None, None]
        pending_tail = None
        for g in range(HG):
            buf = g % 2
            if g + 1 < HG:
                nb = (g + 1) % 2
                gw[nb] = load_group(g + 1, nb,
                                    [wguard[nb]] if wguard[nb] else [])
            wds = gw[buf]
            qk_done = []
            for (dst, wsb) in ((qT, wq), (kT, wk)):
                for tc in range(4):
                    acc = []
                    for c in range(ND):
                        acc.append(ev.op("tensor", lambda e, c=c, tc=tc, wsb=wsb, buf=buf: e.matmul(
                            ps_qk[:], wsb[buf][:, c * 128:(c + 1) * 128],
                            xn[c][:, tc * 512:(tc + 1) * 512],
                            start=(c == 0), stop=(c == ND - 1)),
                            waits=(wds + lds + ([qk_guard] if qk_guard else [])) if c == 0 else [],
                            inc="mm"))
                    qk_guard = ev.op("vector", lambda e, dst=dst, tc=tc: e.tensor_copy(
                        dst[:, tc * 512:(tc + 1) * 512], ps_qk[:]),
                        waits=[acc[-1]], inc="vx")
                    qk_done.append(qk_guard)
            v_done = []
            for t in range(16):
                mv = []
                for c in range(ND):
                    mv.append(ev.op("tensor", lambda e, c=c, t=t, buf=buf: e.matmul(
                        ps_w[t % 2][:, 0:128], xn[c][:, t * 128:(t + 1) * 128],
                        wv[buf][:, c * 128:(c + 1) * 128],
                        start=(c == 0), stop=(c == ND - 1)),
                        waits=(([v_guard] if v_guard else []) +
                               (prev_exp[-1:] if (t < 2 and prev_exp) else []))
                              if c == 0 else [], inc="mm"))
                for hh in range(4):
                    v_guard = ev.op("vector", lambda e, t=t, hh=hh: e.tensor_copy(
                        vsb[:, t * 132 + hh * 33: t * 132 + hh * 33 + 32],
                        ps_w[t % 2][:, hh * 32:(hh + 1) * 32]),
                        waits=[mv[-1]] + ([vmemset[-1]] if g == 0 else []), inc="vx")
                    v_done.append(v_guard)
            wguard[buf] = ev.mark("mm")
            for hh in range(4):
                hbase = hh * 32
                tp = (hbase, 0) if hbase == 96 else None
                for qc in range(4):
                    q0 = qc * 512
                    it = g * 16 + hh * 4 + qc
                    po = it % 2
                    exp_done = []
                    av_marks = []
                    for kt in range(16):
                        pr = kt // 2
                        pb = pr % 2
                        half = kt % 2
                        w0 = []
                        if half == 0:
                            w0 += [exp_done[pr - 2]] if pr >= 2 else                                 (prev_exp[pr - 2 + 8:pr - 1 + 8] if prev_exp else [])
                        if kt == 0 and qc == 0 and hh == 0:
                            w0 += qk_done
                        if hh == 0 and qc == 0 and kt < 4:
                            w0 += v_done[-1:]
                        m = ev.op("tensor", lambda e, kt=kt, pb=pb, half=half, hbase=hbase, q0=q0, tp=tp: e.matmul(
                            ps_s[pb][:, half * 512:(half + 1) * 512],
                            kT[hbase:hbase + 32, kt * 128:(kt + 1) * 128],
                            qT[hbase:hbase + 32, q0:q0 + 512],
                            start=True, stop=True, tile_position=tp),
                            waits=w0, inc="mm")
                        if half == 1:
                            e1 = ev.op("scalar", lambda e, pr=pr, pb=pb: e.activation(
                                at[:, pr * 1024:(pr + 1) * 1024], ps_s[pb][:], AF.Exp,
                                bias=m8v[:], scale=float(32 ** -0.5)),
                                waits=[m, ep0] +
                                      (prev_av[2 * pr + 1:2 * pr + 2] if prev_av else []),
                                inc="sc")
                            exp_done.append(e1)
                    if pending_tail is not None:
                        tpo, tg, thb, tq0, tr1 = pending_tail
                        mb = ev.op("tensor", lambda e, tpo=tpo: e.matmul(
                            ps_b[:], onesb[0:1, 0:32], rec[tpo][0:1, :],
                            start=True, stop=True), waits=[tr1], inc="mm")
                        c1 = ev.op("vector", lambda e, tpo=tpo: e.tensor_copy(
                            nbc[tpo][:], ps_b[:]), waits=[mb], inc="vx")
                        tt_done[tpo] = ev.op("vector", lambda e, tg=tg, thb=thb, tq0=tq0, tpo=tpo: e.tensor_tensor(
                            oall[tg][thb:thb + 32, tq0:tq0 + 512], ps_o[tpo][0:32, :],
                            nbc[tpo][:], ALU.mult), waits=[c1], inc="vx")
                        pending_tail = None
                    for kt in range(16):
                        mavl = ev.op("tensor", lambda e, kt=kt, hh=hh, po=po: e.matmul(
                            ps_o[po][:], vsb[:, kt * 132 + hh * 33: kt * 132 + hh * 33 + 33],
                            at[:, kt * 512:(kt + 1) * 512],
                            start=(kt == 0), stop=(kt == 15)),
                            waits=[exp_done[kt // 2]] +
                                  ((v_done if (hh == 0 and qc == 0) else []) +
                                   ([tt_done[po]] if tt_done[po] else []) if kt == 0 else []),
                            inc="mm")
                        av_marks.append(mavl)
                    r1 = ev.op("vector", lambda e, po=po: e.reciprocal(
                        rec[po][:], ps_o[po][32:33, :]), waits=[mavl], inc="vx")
                    pending_tail = (po, g, hbase, q0, r1)
                    prev_exp = exp_done
                    prev_av = av_marks
        if pending_tail is not None:
            tpo, tg, thb, tq0, tr1 = pending_tail
            mb = ev.op("tensor", lambda e, tpo=tpo: e.matmul(
                ps_b[:], onesb[0:1, 0:32], rec[tpo][0:1, :],
                start=True, stop=True), waits=[tr1], inc="mm")
            c1 = ev.op("vector", lambda e, tpo=tpo: e.tensor_copy(
                nbc[tpo][:], ps_b[:]), waits=[mb], inc="vx")
            tt_done[tpo] = ev.op("vector", lambda e, tg=tg, thb=thb, tq0=tq0, tpo=tpo: e.tensor_tensor(
                oall[tg][thb:thb + 32, tq0:tq0 + 512], ps_o[tpo][0:32, :],
                nbc[tpo][:], ALU.mult), waits=[c1], inc="vx")
            pending_tail = None
        oall_done = ev.wait_all("vx")
        wo_ld = ev.wait_all("wd", "wd2")
        prev_store = None
        for t in range(NT):
            sready = []
            for n in range(2):
                acc = None
                for c in range(ND):
                    acc = ev.op("tensor", lambda e, c=c, t=t, n=n: e.matmul(
                        ps_w[n][:, 0:512], oall[c][:, t * 128:(t + 1) * 128],
                        wo[:, c * 1024 + n * 512: c * 1024 + (n + 1) * 512],
                        start=(c == 0), stop=(c == ND - 1)),
                        waits=(oall_done + wo_ld) if (t == 0 and c == 0) else
                              (ev.wait_all("vx") if c == 0 else []),
                        inc="mm")
                sready.append(ev.op("vector", lambda e, n=n: e.tensor_tensor(
                    stile[:, n * 512:(n + 1) * 512], ps_w[n][:, 0:512],
                    bbc[:, n * 512:(n + 1) * 512], ALU.add),
                    waits=[acc], inc="vx"))
            sc1 = ev.op("scalar", lambda e: e.square(sq[:], stile[:]),
                        waits=sready, inc="sc")
            v1 = ev.op("vector", lambda e: e.reduce_sum(st1[:], stile[:], AX.X),
                       waits=sready, inc="vx")
            v2 = ev.op("vector", lambda e: e.reduce_sum(st2[:], sq[:], AX.X),
                       waits=[sc1], inc="vx")
            ev.op("vector", lambda e: e.tensor_scalar_mul(mu[:], st1[:], 1.0 / D),
                  waits=[v1], inc="vx")
            sc2 = ev.op("scalar", lambda e: e.square(mu2[:], mu[:]),
                        waits=ev.wait_all("vx"), inc="sc")
            v3 = ev.op("vector", lambda e: e.scalar_tensor_tensor(
                var[:], st2[:], 1.0 / D, mu2[:], ALU.mult, ALU.subtract),
                waits=[sc2, v2], inc="vx")
            sc3 = ev.op("scalar", lambda e: e.activation(
                rt[:], var[:], AF.Sqrt, bias=epsv[:]), waits=[v3, ep1], inc="sc")
            v4 = ev.op("vector", lambda e: e.reciprocal(rr[:], rt[:]),
                       waits=[sc3], inc="vx")
            v5 = ev.op("vector", lambda e: e.tensor_scalar(
                sq[:], stile[:], mu[:], rr[:], ALU.subtract, ALU.mult),
                waits=[v4], inc="vx")
            v6 = ev.op("vector", lambda e: e.tensor_tensor(
                sq[:], sq[:], gbc[:], ALU.mult), waits=[v5], inc="vx")
            v7 = ev.op("vector", lambda e: e.tensor_tensor(
                otile[:], sq[:], bebc[:], ALU.add),
                waits=[v6] + ([prev_store] if prev_store else []), inc="vx")
            ev.dma("sync", s_out[t * 128:(t + 1) * 128, :], otile[:],
                   waits=[v7], inc="st")
            prev_store = ev.mark("st")


def emit_ffn_attn(nc, ev, ctx, io, pfx=""):
    xfT = io["xfT"]
    wqkv = io["wqkv"]
    qs_i = io["qs_i"]
    ks_i = io["ks_i"]
    wo_i = io["wo_i"]
    onesr = io["onesr"]
    ones2 = io["ones2"]
    ones2T = io["ones2T"]
    maskM = io["maskM"]
    ropes = io["ropes"]
    xfT_o = io["xfT_o"]
    if 1:
        sb = lambda *a: ctx.enter_context(nc.sbuf_tensor(pfx + a[0], *a[1:]))
        psa = lambda *a: ctx.enter_context(nc.psum_tensor(pfx + a[0], *a[1:]))
        xf = [sb(f"xf{c}", [128, NTOK], F32) for c in range(ND)]
        hb = [sb(f"hb{c}", [128, NTOK], BF16) for c in range(ND)]
        tabs = {n: sb(n + "_s", [128, NTOK], BF16) for n in ropes}
        mask = sb("mask", [128, 8 * 512], BF16)
        wq = [sb(f"wq{i}", [128, 1024], BF16) for i in range(2)]
        wk = [sb(f"wk{i}", [128, 1024], BF16) for i in range(2)]
        wv = [sb(f"wv{i}", [128, 1024], BF16) for i in range(2)]
        qf2 = [sb(f"qf{i}", [128, NTOK], F32) for i in range(2)]
        qTb = sb("qTb", [128, NTOK], BF16)
        kTb = sb("kTb", [128, NTOK], BF16)
        vsb = sb("vsb", [128, 16 * 130], BF16)
        at = sb("at", [128, 8 * 512], BF16)
        scr1 = sb("scr1", [128, NTOK], F32)
        ssum = sb("ssum", [2, 512], F32)
        qsv = sb("qsv", [128, 1], F32)
        ksv = sb("ksv", [128, 1], F32)
        onesb = sb("onesb", [1, 128], F32)
        o2 = sb("o2", [128, 2], F32)
        o2T = sb("o2T", [2, 128], F32)
        rec = [sb(f"rec{i}", [1, 512], F32) for i in range(2)]
        nbc = [sb(f"nbc{i}", [64, 512], F32) for i in range(2)]
        oall = [sb(f"oall{c}", [128, NTOK], BF16) for c in range(4)]

        ps_s = [psa(f"ps_s{i}", [128, 512], F32) for i in range(2)]
        ps_o = [psa(f"ps_o{i}", [65, 512], F32) for i in range(2)]
        ps_b = psa("ps_b", [128, 512], F32)
        ps_st = [psa("ps_st0", [2, 512], F32)] * 2
        ps_w = [psa(f"ps_w{i}", [128, 512], F32) for i in range(2)]
        m30v = sb("m30v", [128, 1], F32)
        eps24 = sb("eps24", [128, 1], F32)

        ep0 = ev.op("vector", lambda e: e.memset(m30v[:], -30.0), inc="vx")
        ep1 = ev.op("vector", lambda e: e.memset(eps24[:], 1e-24), inc="vx")
        lds = []
        for c in range(ND):
            lds.append(ev.dma("sync", xf[c][:], xfT[c * 128:(c + 1) * 128, :], inc="ld"))
        tab_lds = []
        for n in ropes:
            tab_lds.append(ev.dma("gpsimd", tabs[n][:], ropes[n], inc="wd"))
        for kt in range(8):
            tab_lds.append(ev.dma("gpsimd", mask[:, kt * 512:(kt + 1) * 512],
                                  maskM[kt * 128:(kt + 1) * 128, :], inc="wd"))
        ev.dma("sync", onesb[:], onesr, inc="ld")
        ev.dma("sync", o2[:], ones2, inc="ld")
        ev.dma("sync", o2T[:], ones2T, inc="ld")
        ev.dma("sync", qsv[0:64, 0:1], qs_i.rearrange("(d o) -> d o", o=1), inc="ld")
        ev.dma("sync", qsv[64:128, 0:1], qs_i.rearrange("(d o) -> d o", o=1), inc="ld")
        ev.dma("sync", ksv[0:64, 0:1], ks_i.rearrange("(d o) -> d o", o=1), inc="ld")
        ev.dma("sync", ksv[64:128, 0:1], ks_i.rearrange("(d o) -> d o", o=1), inc="ld")
        lds_all = ev.wait_all("ld", "ld2")
        hdone = []
        for c in range(ND):
            if c < 4:
                hdone.append(ev.op("vector", lambda e, c=c: e.tensor_copy(
                    hb[c][:], xf[c][:]), waits=[lds[c]], inc="vx"))
            else:
                ev.op("vector", lambda e, c=c: e.memset(hb[c][:, 0:1], 0.0), inc="vx")
                hdone.append(ev.op("vector", lambda e, c=c: e.tensor_copy(
                    hb[c][:, 1:NTOK], xf[c][:, 0:NTOK - 1]), waits=[lds[c]], inc="vx"))

        def load_group(g, buf, waits=()):
            o = []
            for (dst, c0) in ((wq, g * 128), (wk, 512 + g * 128), (wv, 1024 + g * 128)):
                o.append(ev.dma(
                    "gpsimd", dst[buf][:].rearrange("p (k n) -> p k n", n=128),
                    wqkv[:, c0:c0 + 128].rearrange("(k p) n -> p k n", p=128),
                    waits=waits, inc="wd" if buf == 0 else "wd2"))
            return o

        gw = [None, None]
        gw[0] = load_group(0, 0)
        wguard = [None, None]
        qf_guard = [None, None]
        psw_uses = []
        pend_f = None
        tt_f = [None, None]
        v1_bank = {0: None, 1: None}
        vset = []
        for t in range(16):
            for hh in range(2):
                vset.append(ev.op("vector", lambda e, t=t, hh=hh: e.memset(
                    vsb[:, t * 130 + hh * 65 + 64: t * 130 + hh * 65 + 65], 1.0),
                    inc="vx"))
        wo_acc_done = None
        for g in range(4):
            buf = g % 2
            if g + 1 < 4:
                nb = (g + 1) % 2
                gw[nb] = load_group(g + 1, nb,
                                    [wguard[nb]] if wguard[nb] else [])
            wds = gw[buf]
            raws = {}
            for si, wsb in ((0, wq), (1, wk)):
                raw_done = []
                for tc in range(4):
                    acc = None
                    for c in range(ND):
                        acc = ev.op("tensor", lambda e, c=c, tc=tc, wsb=wsb, buf=buf: e.matmul(
                            ps_w[tc % 2][:], wsb[buf][:, c * 128:(c + 1) * 128],
                            hb[c][:, tc * 512:(tc + 1) * 512],
                            start=(c == 0), stop=(c == ND - 1)),
                            waits=(wds + hdone +
                                   ([qf_guard[si]] if qf_guard[si] else []) +
                                   psw_uses[-2:-1]) if c == 0 else [],
                            inc="mm")
                    cp = ev.op("vector", lambda e, tc=tc, si=si: e.tensor_copy(
                        qf2[si][:, tc * 512:(tc + 1) * 512], ps_w[tc % 2][:]),
                        waits=[acc], inc="vx")
                    raw_done.append(cp)
                    psw_uses.append(cp)
                raws[si] = raw_done
            v_done = []
            for t in range(16):
                mv = None
                for c in range(ND):
                    mv = ev.op("tensor", lambda e, c=c, t=t, buf=buf: e.matmul(
                        ps_w[t % 2][:, 0:128], hb[c][:, t * 128:(t + 1) * 128],
                        wv[buf][:, c * 128:(c + 1) * 128],
                        start=(c == 0), stop=(c == ND - 1)),
                        waits=psw_uses[-2:-1] if c == 0 else [], inc="mm")
                for hh in range(2):
                    cp = ev.op("vector", lambda e, t=t, hh=hh: e.tensor_copy(
                        vsb[:, t * 130 + hh * 65: t * 130 + hh * 65 + 64],
                        ps_w[t % 2][:, hh * 64:(hh + 1) * 64]),
                        waits=[mv] + ([vset[-1]] if g == 0 else []), inc="vx")
                    v_done.append(cp)
                psw_uses.append(v_done[-1])
            wguard[buf] = ev.mark("mm")
            chain_last = []
            for si, (dstb, scl, csn, ssn) in (
                    (0, (qTb, qsv, "cs_q", "ss_q")), (1, (kTb, ksv, "cs_k", "ss_k"))):
                s1 = ev.op("scalar", lambda e, si=si: e.square(scr1[:], qf2[si][:]),
                           waits=raws[si] + chain_last, inc="sc")
                stts = []
                m1s = []
                m2prev = []
                for tc in range(4):
                    m1 = ev.op("tensor", lambda e, tc=tc: e.matmul(
                        ps_st[tc % 2][:], o2[:], scr1[:, tc * 512:(tc + 1) * 512],
                        start=True, stop=True),
                        waits=[s1] + lds_all + m2prev, inc="mm")
                    m1s.append(m1)
                    sA = ev.op("scalar", lambda e, tc=tc: e.activation(
                        ssum[:], ps_st[tc % 2][:], AF.Sqrt,
                        bias=eps24[0:2, 0:1]), waits=[m1, ep1] + m2prev, inc="sc")
                    vR = ev.op("vector", lambda e: e.reciprocal(ssum[:], ssum[:]),
                               waits=[sA], inc="vx")
                    m2 = ev.op("tensor", lambda e: e.matmul(
                        ps_b[:], o2T[:], ssum[:], start=True, stop=True),
                        waits=[vR] + stts[-1:], inc="mm")
                    m2prev = [m2]
                    stts.append(ev.op("vector", lambda e, tc=tc, scl=scl, si=si: e.scalar_tensor_tensor(
                        qf2[si][:, tc * 512:(tc + 1) * 512], qf2[si][:, tc * 512:(tc + 1) * 512],
                        scl[:], ps_b[:], ALU.mult, ALU.mult), waits=[m2], inc="vx"))
                r1 = ev.op("vector", lambda e, si=si: e.tensor_scalar_mul(
                    scr1[0:32, :], qf2[si][32:64, :], -1.0), waits=stts, inc="vx")
                r2 = ev.op("vector", lambda e, si=si: e.tensor_copy(
                    scr1[32:64, :], qf2[si][0:32, :]), waits=stts, inc="vx")
                r3 = ev.op("vector", lambda e, si=si: e.tensor_scalar_mul(
                    scr1[64:96, :], qf2[si][96:128, :], -1.0), waits=stts, inc="vx")
                r4 = ev.op("vector", lambda e, si=si: e.tensor_copy(
                    scr1[96:128, :], qf2[si][64:96, :]), waits=stts, inc="vx")
                v4 = ev.op("vector", lambda e, ssn=ssn: e.tensor_tensor(
                    scr1[:], scr1[:], tabs[ssn][:], ALU.mult),
                    waits=[r1, r2, r3, r4] + tab_lds, inc="vx")
                v5 = ev.op("vector", lambda e, csn=csn, si=si: e.tensor_tensor(
                    qf2[si][:], qf2[si][:], tabs[csn][:], ALU.mult), waits=[v4], inc="vx")
                chain_last = [ev.op("vector", lambda e, dstb=dstb, si=si: e.tensor_tensor(
                    dstb[:], qf2[si][:], scr1[:], ALU.add), waits=[v4, v5], inc="vx")]
                qf_guard[si] = chain_last[0]
            qk_done = ev.wait_all("vx")
            def flush_tail_f():
                nonlocal pend_f
                if pend_f is None:
                    return
                tpo, tg, thb, tq0, tr1 = pend_f
                mb = ev.op("tensor", lambda e, tpo=tpo: e.matmul(
                    ps_b[0:64, :], onesb[0:1, 0:64], rec[tpo][0:1, :],
                    start=True, stop=True), waits=[tr1], inc="mm")
                c1 = ev.op("vector", lambda e, tpo=tpo: e.tensor_copy(
                    nbc[tpo][:], ps_b[0:64, :]), waits=[mb], inc="vx")
                tt_f[tpo] = ev.op("vector", lambda e, tg=tg, thb=thb, tq0=tq0, tpo=tpo: e.tensor_tensor(
                    oall[tg][thb:thb + 64, tq0:tq0 + 512], ps_o[tpo][0:64, :],
                    nbc[tpo][:], ALU.mult), waits=[c1], inc="vx")
                pend_f = None

            for hh in range(2):
                hb64 = hh * 64
                for w in range(4):
                    q0 = w * 512
                    it = hh * 4 + w
                    po = it % 2
                    kt_lo = 4 if w == 0 else 0
                    exp_done = {}
                    for kt in range(kt_lo, 8):
                        ktok0 = (w - 1) * 512 + kt * 128
                        pb = kt % 2
                        warw = ([v1_bank[pb]] if v1_bank[pb] else []) if kt < kt_lo + 2 \
                            else ([exp_done[kt - 2]] if kt - 2 in exp_done else [])
                        m = ev.op("tensor", lambda e, ktok0=ktok0, pb=pb, hb64=hb64, q0=q0: e.matmul(
                            ps_s[pb][:], kTb[hb64:hb64 + 64, ktok0:ktok0 + 128],
                            qTb[hb64:hb64 + 64, q0:q0 + 512],
                            start=True, stop=True),
                            waits=(qk_done if (kt == kt_lo and w == 0) else []) + warw,
                            inc="mm")
                        v1 = ev.op("vector", lambda e, kt=kt, pb=pb: e.scalar_tensor_tensor(
                            scr1[:, (kt % 2) * 512:(kt % 2 + 1) * 512], ps_s[pb][:], 8.0,
                            mask[:, kt * 512:(kt + 1) * 512],
                            ALU.mult, ALU.min), waits=[m] + tab_lds, inc="vx")
                        v1_bank[pb] = v1
                        e1 = ev.op("scalar", lambda e, kt=kt: e.activation(
                            at[:, kt * 512:(kt + 1) * 512],
                            scr1[:, (kt % 2) * 512:(kt % 2 + 1) * 512], AF.Exp,
                            bias=m30v[:]), waits=[v1, ep0], inc="sc")
                        exp_done[kt] = e1
                    flush_tail_f()
                    mavl = None
                    first = True
                    for kt in range(kt_lo, 8):
                        ktok0 = (w - 1) * 512 + kt * 128
                        vt = ktok0 // 128
                        mavl = ev.op("tensor", lambda e, kt=kt, vt=vt, hh=hh, first=first, po=po: e.matmul(
                            ps_o[po][:], vsb[:, vt * 130 + hh * 65: vt * 130 + hh * 65 + 65],
                            at[:, kt * 512:(kt + 1) * 512],
                            start=first, stop=(kt == 7)),
                            waits=[exp_done[kt]] +
                                  (((v_done if (hh == 0 and w == 0) else []) +
                                    ([tt_f[po]] if tt_f[po] else [])) if first else []),
                            inc="mm")
                        first = False
                    r1 = ev.op("vector", lambda e, po=po: e.reciprocal(
                        rec[po][:], ps_o[po][64:65, :]), waits=[mavl], inc="vx")
                    pend_f = (po, g, hb64, q0, r1)
        flush_tail_f()
        wo_sb = at
        wo_ld = [ev.dma("gpsimd", wo_sb[:].rearrange("p (k n) -> p k n", n=1024),
                        wo_i.rearrange("(k p) n -> p k n", p=128),
                        waits=ev.wait_all("mm", "sc"), inc="wd")]
        oall_done = ev.wait_all("vx")
        res_done = []
        for dc in range(ND):
            for tc in range(4):
                acc = None
                for c in range(4):
                    acc = ev.op("tensor", lambda e, c=c, dc=dc, tc=tc: e.matmul(
                        ps_w[tc % 2][:], wo_sb[:, c * 1024 + dc * 128: c * 1024 + dc * 128 + 128],
                        oall[c][:, tc * 512:(tc + 1) * 512],
                        start=(c == 0), stop=(c == 3)),
                        waits=((oall_done + wo_ld) if (dc == 0 and tc == 0)
                               else ev.wait_all("vx")) if c == 0 else [],
                        inc="mm")
                res_done.append(ev.op("vector", lambda e, dc=dc, tc=tc: e.tensor_tensor(
                    xf[dc][:, tc * 512:(tc + 1) * 512], ps_w[tc % 2][:],
                    xf[dc][:, tc * 512:(tc + 1) * 512], ALU.add),
                    waits=[acc], inc="vx"))
        for c in range(ND):
            ev.dma("sync", xfT_o[c * 128:(c + 1) * 128, :], xf[c][:],
                   waits=res_done, inc="st")



def emit_ffn_ffn(nc, ev, ctx, io, out_normal, pfx=""):
    NH = 22
    hsz = [128] * 21 + [42]
    xfT = io["xfT"]
    lg_i = io["lg_i"]
    lb_i = io["lb_i"]
    w1_i = io["w1_i"]
    b1_i = io["b1_i"]
    w2_i = io["w2_i"]
    b2_i = io["b2_i"]
    onesr = io["onesr"]
    ident = io["ident"]
    outp = io["outp"]
    if 1:
        sb = lambda *a: ctx.enter_context(nc.sbuf_tensor(pfx + a[0], *a[1:]))
        psa = lambda *a: ctx.enter_context(nc.psum_tensor(pfx + a[0], *a[1:]))
        xf = [sb(f"xf{c}", [128, NTOK], F32) for c in range(ND)]
        xh = [sb(f"xh{c}", [128, NTOK], BF16) for c in range(ND)]
        patch = [sb(f"patch{i}", [128, 512], F32) for i in range(4)]
        w2sb = sb("w2sb", [128, NH * 1024], BF16)
        w1a = [sb(f"w1a{i}", [128, 1024], BF16) for i in range(2)]
        w1g = [sb(f"w1g{i}", [128, 1024], BF16) for i in range(2)]
        acts = sb("acts", [128, NH * 512], BF16)
        lgv = sb("lgv", [128, ND], F32)
        lbv = sb("lbv", [128, ND], F32)
        b2v = sb("b2v", [128, ND], F32)
        b1av = sb("b1av", [128, NH], F32)
        b1gv = sb("b1gv", [128, NH], F32)
        onesb = sb("onesb", [1, 128], F32)
        onesc = sb("onesc", [128, 1], F32)
        sqv = [sb(f"sqv{i}", [128, 512], F32) for i in range(2)]
        murow = sb("murow", [1, 512], F32)
        e2row = sb("e2row", [1, 512], F32)
        musq = sb("musq", [1, 512], F32)
        varrow = sb("varrow", [1, 512], F32)
        rrow = sb("rrow", [1, 512], F32)
        mubc = sb("mubc", [128, 512], F32)
        rbc = sb("rbc", [128, 512], F32)
        tta = sb("tta", [128, 512], F32)
        gbuf = [sb(f"gbuf{i}", [128, 512], BF16) for i in range(2)]
        idf = sb("idf", [128, 128], F32)
        gtile = sb("gtile", [128, D], F32)

        ps_S = psa("ps_S", [1, 512], F32)
        ps_Q = psa("ps_Q", [1, 512], F32)
        ps_bc = psa("ps_bc", [128, 512], F32)
        ps_a = [psa(f"ps_a{i}", [128, 512], F32) for i in range(2)]
        ps_g = [psa(f"ps_g{i}", [128, 512], F32) for i in range(2)]
        ps_out = psa("ps_out", [128, 512], F32)
        ps_t = ps_bc
        epsv = sb("epsv", [128, 1], F32)

        ep0 = ev.op("vector", lambda e: e.memset(epsv[:], 1e-5), inc="vx")
        lds = []
        for c in range(ND):
            lds.append(ev.dma("sync", xf[c][:], xfT[c * 128:(c + 1) * 128, :], inc="ld"))
        ev.dma("sync", onesb[:], onesr, inc="ld")
        ev.dma("sync", idf[:], ident, inc="ld")
        for c in range(ND):
            ev.dma("sync", lgv[:, c:c + 1], lg_i.rearrange("(d o) -> d o", o=1)[c * 128:(c + 1) * 128], inc="ld2")
            ev.dma("sync", lbv[:, c:c + 1], lb_i.rearrange("(d o) -> d o", o=1)[c * 128:(c + 1) * 128], inc="ld2")
            ev.dma("sync", b2v[:, c:c + 1], b2_i.rearrange("(d o) -> d o", o=1)[c * 128:(c + 1) * 128], inc="ld2")
        for c in range(NH):
            sz = hsz[c]
            ev.dma("sync", b1av[0:sz, c:c + 1],
                   b1_i.rearrange("(d o) -> d o", o=1)[c * 128:c * 128 + sz], inc="ld2")
            ev.dma("sync", b1gv[0:sz, c:c + 1],
                   b1_i.rearrange("(d o) -> d o", o=1)[2730 + c * 128:2730 + c * 128 + sz], inc="ld2")
        w2ld = [ev.dma("gpsimd",
                       w2sb[:, 0:21 * 1024].rearrange("p (c n) -> p c n", n=1024),
                       w2_i[0:2688, :].rearrange("(c p) n -> p c n", p=128), inc="wd"),
                ev.dma("gpsimd", w2sb[0:42, 21 * 1024:22 * 1024],
                       w2_i[2688:2730, :], inc="wd")]
        mset = ev.op("vector", lambda e: e.memset(onesc[:], 1.0), inc="vx")
        lds_all = ev.wait_all("ld", "ld2")
        pdone = []
        for c in range(4):
            ev.op("vector", lambda e, c=c: e.memset(patch[c][:, 0:1], 0.0), inc="vx")
            pdone.append(ev.op("vector", lambda e, c=c: e.tensor_copy(
                patch[c][:, 1:512], xf[c + 4][:, 0:511]), waits=[lds[c + 4]], inc="vx"))

        def h2slice(c, t0, wdt=512):
            if c < 4:
                return xf[c][:, t0:t0 + wdt]
            if t0 == 0:
                return patch[c - 4][:, 0:wdt]
            return xf[c][:, t0 - 1:t0 - 1 + wdt]

        xh_tc_done = {}

        def emit_ln_tc(tc):
            t0 = tc * 512
            mS = None
            mQ = None
            sq_done = {}
            for c in range(ND):
                mS = ev.op("tensor", lambda e, c=c, t0=t0: e.matmul(
                    ps_S[:], onesc[:], h2slice(c, t0), start=(c == 0), stop=(c == ND - 1)),
                    waits=(lds_all + pdone + [mset] + ev.wait_all("vx")) if c == 0 else [],
                    inc="mm")
            for c in range(ND):
                sq1 = ev.op("scalar", lambda e, c=c, t0=t0: e.square(
                    sqv[c % 2][:], h2slice(c, t0)),
                    waits=(sq_done[c - 2] if c >= 2 else lds_all + pdone),
                    inc="sc")
                mQ = ev.op("tensor", lambda e, c=c: e.matmul(
                    ps_Q[:], onesc[:], sqv[c % 2][:], start=(c == 0), stop=(c == ND - 1)),
                    waits=[sq1], inc="mm")
                sq_done[c] = [mQ]
            v1 = ev.op("vector", lambda e: e.tensor_scalar_mul(murow[:], ps_S[:], 1.0 / D),
                       waits=[mS], inc="vx")
            v2 = ev.op("vector", lambda e: e.tensor_scalar_mul(e2row[:], ps_Q[:], 1.0 / D),
                       waits=[mQ], inc="vx")
            s2 = ev.op("scalar", lambda e: e.square(musq[:], murow[:]), waits=[v1], inc="sc")
            v3 = ev.op("vector", lambda e: e.tensor_sub(varrow[:], e2row[:], musq[:]),
                       waits=[s2, v2], inc="vx")
            s3 = ev.op("scalar", lambda e: e.activation(rrow[:], varrow[:], AF.Sqrt,
                                                        bias=epsv[0:1, 0:1]),
                       waits=[v3, ep0], inc="sc")
            v4 = ev.op("vector", lambda e: e.reciprocal(rrow[:], rrow[:]), waits=[s3], inc="vx")
            mb1 = ev.op("tensor", lambda e: e.matmul(
                ps_bc[:], onesb[0:1, :], murow[0:1, :], start=True, stop=True),
                waits=[v1] + ev.wait_all("vx"), inc="mm")
            vb1 = ev.op("vector", lambda e: e.tensor_copy(mubc[:], ps_bc[:]),
                        waits=[mb1], inc="vx")
            mb2 = ev.op("tensor", lambda e: e.matmul(
                ps_bc[:], onesb[0:1, :], rrow[0:1, :], start=True, stop=True),
                waits=[v4, vb1], inc="mm")
            vb2 = ev.op("vector", lambda e: e.tensor_copy(rbc[:], ps_bc[:]),
                        waits=[mb2], inc="vx")
            for c in range(ND):
                va = ev.op("vector", lambda e, c=c, t0=t0: e.tensor_sub(
                    tta[:], h2slice(c, t0), mubc[:]), waits=[vb1, vb2], inc="vx")
                vb = ev.op("vector", lambda e: e.tensor_mul(tta[:], tta[:], rbc[:]),
                           waits=[va], inc="vx")
                ev.op("vector", lambda e, c=c, t0=t0: e.tensor_scalar(
                    xh[c][:, t0:t0 + 512], tta[:], lgv[:, c:c + 1], lbv[:, c:c + 1],
                    ALU.mult, ALU.add), waits=[vb], inc="vx")
            xh_tc_done[tc] = ev.wait_all("vx")

        emit_ln_tc(0)
        emit_ln_tc(1)

        def load_w1(c, buf, waits):
            sz = hsz[c]
            o = []
            for (dst, c0) in ((w1a, c * 128), (w1g, 2730 + c * 128)):
                o.append(ev.dma(
                    "gpsimd",
                    dst[buf][:].rearrange("p (k n) -> p k n", n=128)[:, :, 0:sz],
                    w1_i[:, c0:c0 + sz].rearrange("(k p) n -> p k n", p=128),
                    waits=waits, inc="wd" if buf == 0 else "wd2"))
            return o

        w1ld = {}
        w1ld[0] = load_w1(0, 0, [])
        w1ld[1] = load_w1(1, 1, [])
        for tc in range(4):
            t0 = tc * 512
            if tc + 2 < 4:
                pass
            act_done = {}
            mm_done = {}
            for c in range(NH):
                sz = hsz[c]
                buf = c % 2
                nxt = c + 2
                ma = mg = None
                for k in range(ND):
                    ma = ev.op("tensor", lambda e, k=k, c=c, t0=t0, buf=buf, sz=sz: e.matmul(
                        ps_a[buf][0:sz, :], w1a[buf][:, k * 128:k * 128 + sz],
                        xh[k][:, t0:t0 + 512], start=(k == 0), stop=(k == ND - 1)),
                        waits=(w1ld[c] + (xh_tc_done[tc] if c == 0 else []) +
                               (act_done.get(c - 2, []))) if k == 0 else [],
                        inc="mm")
                for k in range(ND):
                    mg = ev.op("tensor", lambda e, k=k, c=c, t0=t0, buf=buf, sz=sz: e.matmul(
                        ps_g[buf][0:sz, :], w1g[buf][:, k * 128:k * 128 + sz],
                        xh[k][:, t0:t0 + 512], start=(k == 0), stop=(k == ND - 1)),
                        waits=(act_done.get(c - 2, [])) if k == 0 else [],
                        inc="mm")
                mm_done[c] = [ma, mg]
                if tc < 3 or True:
                    nc_load = c + 2
                    if nc_load < NH:
                        w1ld[nc_load] = load_w1(nc_load, nc_load % 2, mm_done.get(c, []))
                    elif tc < 3 and nc_load - NH < 2:
                        w1ld[nc_load - NH] = load_w1(nc_load - NH, (nc_load - NH) % 2,
                                                     mm_done.get(c, []))
                g1 = ev.op("scalar", lambda e, c=c, sz=sz, buf=buf: e.activation(
                    gbuf[buf][0:sz, :], ps_g[buf][0:sz, :], AF.Gelu,
                    bias=b1gv[0:sz, c:c + 1]), waits=[mg], inc="sc")
                a1 = ev.op("vector", lambda e, c=c, sz=sz, buf=buf: e.scalar_tensor_tensor(
                    acts[0:sz, c * 512:(c + 1) * 512], ps_a[buf][0:sz, :],
                    b1av[0:sz, c:c + 1], gbuf[buf][0:sz, :], ALU.add, ALU.mult),
                    waits=[g1], inc="vx")
                act_done[c] = [a1]
            all_act = ev.wait_all("vx")
            ev_last = []
            for dc in range(ND):
                mo = None
                for c in range(NH):
                    sz = hsz[c]
                    mo = ev.op("tensor", lambda e, c=c, dc=dc, sz=sz: e.matmul(
                        ps_out[:], w2sb[0:sz, c * 1024 + dc * 128: c * 1024 + dc * 128 + 128],
                        acts[0:sz, c * 512:(c + 1) * 512],
                        start=(c == 0), stop=(c == NH - 1)),
                        waits=(all_act + w2ld + (ev_last[-1:] if dc >= 1 else [])) if c == 0 else [],
                        inc="mm")
                ev_last.append(ev.op("vector", lambda e, dc=dc, t0=t0: e.scalar_tensor_tensor(
                    xf[dc][:, t0:t0 + 512], ps_out[:], b2v[:, dc:dc + 1],
                    xf[dc][:, t0:t0 + 512], ALU.add, ALU.add), waits=[mo], inc="vx"))
            if tc + 2 < 4:
                emit_ln_tc(tc + 2)
        res_done = ev.wait_all("vx")
        if not out_normal:
            for c in range(ND):
                ev.dma("sync", outp[c * 128:(c + 1) * 128, :], xf[c][:],
                       waits=res_done, inc="st")
        else:
            tr_prev = None
            for t in range(NT):
                gts = []
                for c in range(ND):
                    m = ev.op("tensor", lambda e, c=c, t=t: e.transpose(
                        ps_t[:, 0:128], xf[c][:, t * 128:(t + 1) * 128], idf[:]),
                        waits=(res_done if (t == 0 and c == 0) else []) +
                              ([tr_prev] if tr_prev else []), inc="mm")
                    tr_prev = ev.op("vector", lambda e, c=c: e.tensor_copy(
                        gtile[:, c * 128:(c + 1) * 128], ps_t[:, 0:128]), waits=[m], inc="vx")
                    gts.append(tr_prev)
                ev.dma("sync", outp[t * 128:(t + 1) * 128, :], gtile[:],
                       waits=gts, inc="st")
                tr_prev = ev.mark("st")


def emit_comb1(nc, ev, ctx, io, pfx=""):
    x_i = io["x_i"]
    s_i = io["s_i"]
    g_i = io["g_i"]
    ident = io["ident"]
    x1_o = io["x1_o"]
    xnTf_o = io["xnTf_o"]
    xnTb_o = io["xnTb_o"]
    xnN_o = io["xnN_o"]
    if 1:
        sb = lambda *a: ctx.enter_context(nc.sbuf_tensor(pfx + a[0], *a[1:]))
        psa = lambda *a: ctx.enter_context(nc.psum_tensor(pfx + a[0], *a[1:]))
        xt = [sb(f"xt{i}", [128, D], F32) for i in range(2)]
        st = [sb(f"st{i}", [128, D], F32) for i in range(2)]
        gt = [sb(f"gt{i}", [128, D], F32) for i in range(2)]
        gs = sb("gs", [128, D], F32)
        x1t = [sb(f"x1t{i}", [128, D], F32) for i in range(2)]
        xnt = [sb(f"xnt{i}", [128, D], F32) for i in range(2)]
        sq = sb("sq", [128, D], F32)
        s1 = sb("s1", [128, 1], F32)
        s2 = sb("s2", [128, 1], F32)
        mu = sb("mu", [128, 1], F32)
        mu2 = sb("mu2", [128, 1], F32)
        var = sb("var", [128, 1], F32)
        rt = sb("rt", [128, 1], F32)
        rr = sb("rr", [128, 1], F32)
        idf = sb("idf", [128, 128], F32)
        chunks = [sb(f"ch{c}", [128, NTOK], F32) for c in range(ND)]
        epsv = sb("epsv", [128, 1], F32)
        tp = [psa(f"tp{i}", [128, 128], F32) for i in range(2)]
        ev.dma("sync", idf[:], ident, inc="ld")
        ep = ev.op("vector", lambda e: e.memset(epsv[:], 1e-5), inc="vx")
        guard = [None, None]
        for i in range(NT):
            b = i % 2
            w = [] if guard[b] is None else list(guard[b])
            l1 = ev.dma("sync", xt[b][:], x_i[i * 128:(i + 1) * 128, :], waits=w, inc="ld")
            l2 = ev.dma("sync", st[b][:], s_i[i * 128:(i + 1) * 128, :], waits=w, inc="ld")
            l3 = ev.dma("gpsimd", gt[b][:], g_i[i * 128:(i + 1) * 128, :], waits=w, inc="ldg")
            sc0 = ev.op("scalar", lambda e, b=b: e.activation(
                gs[:], gt[b][:], AF.Silu), waits=[l3] + ev.wait_all("vx"), inc="sc")
            v0 = ev.op("vector", lambda e, b=b: e.tensor_mul(gs[:], st[b][:], gs[:]),
                       waits=[sc0, l2], inc="vx")
            v0b = ev.op("vector", lambda e, b=b: e.tensor_add(x1t[b][:], xt[b][:], gs[:]),
                        waits=[v0, l1], inc="vx")
            d1 = ev.dma("sync", x1_o[i * 128:(i + 1) * 128, :], x1t[b][:],
                        waits=[v0b], inc="st")
            sc1 = ev.op("scalar", lambda e, b=b: e.square(sq[:], x1t[b][:]),
                        waits=[v0b], inc="sc")
            v1 = ev.op("vector", lambda e, b=b: e.reduce_sum(s1[:], x1t[b][:], AX.X),
                       waits=[v0b], inc="vx")
            v2 = ev.op("vector", lambda e: e.reduce_sum(s2[:], sq[:], AX.X),
                       waits=[sc1], inc="vx")
            ev.op("vector", lambda e: e.tensor_scalar_mul(mu[:], s1[:], 1.0 / D),
                  waits=[v1], inc="vx")
            sc2 = ev.op("scalar", lambda e: e.square(mu2[:], mu[:]),
                        waits=ev.wait_all("vx"), inc="sc")
            v3 = ev.op("vector", lambda e: e.scalar_tensor_tensor(
                var[:], s2[:], 1.0 / D, mu2[:], ALU.mult, ALU.subtract),
                waits=[sc2, v2], inc="vx")
            sc3 = ev.op("scalar", lambda e: e.activation(
                rt[:], var[:], AF.Sqrt, bias=epsv[:]), waits=[v3, ep], inc="sc")
            v4 = ev.op("vector", lambda e: e.reciprocal(rr[:], rt[:]),
                       waits=[sc3], inc="vx")
            vn = ev.op("vector", lambda e, b=b: e.tensor_scalar(
                xnt[b][:], x1t[b][:], mu[:], rr[:], ALU.subtract, ALU.mult),
                waits=[v4], inc="vx")
            d2 = ev.dma("gpsimd", xnN_o[i * 128:(i + 1) * 128, :], xnt[b][:],
                        waits=[vn], inc="stg")
            cv = vn
            for c in range(ND):
                pb = c % 2
                m = ev.op("tensor", lambda e, b=b, c=c, pb=pb: e.transpose(
                    tp[pb][:], xnt[b][:, c * 128:(c + 1) * 128], idf[:]),
                    waits=[vn] + ev.wait_all("vx")[-1:], inc="mm")
                cv = ev.op("vector", lambda e, i=i, c=c, pb=pb: e.tensor_copy(
                    chunks[c][:, i * 128:(i + 1) * 128], tp[pb][:]),
                    waits=[m], inc="vx")
            guard[b] = [cv, d1, ev.mark("st"), ev.mark("stg")]
        w = ev.wait_all("vx")
        for c in range(ND):
            ev.dma("sync", xnTf_o[c * 128:(c + 1) * 128, :], chunks[c][:],
                   waits=w, inc="st")
            ev.dma("gpsimd", xnTb_o[c * 128:(c + 1) * 128, :], chunks[c][:],
                   waits=w, inc="stg")


def emit_l1attn(nc, ev, ctx, io, pfx=""):
    xnT_b = io["xnT_b"]
    xnN_b = io["xnN_b"]
    QK_i = io["QK_i"]
    VO_i = io["VO_i"]
    onesr = io["onesr"]
    s1_o = io["s1_o"]
    if 1:
        sb = lambda *a: ctx.enter_context(nc.sbuf_tensor(pfx + a[0], *a[1:]))
        psa = lambda *a: ctx.enter_context(nc.psum_tensor(pfx + a[0], *a[1:]))
        xn = [sb(f"xn{c}", [128, NTOK], BF16) for c in range(ND)]
        xnN = sb("xnN", [128, NT * D], BF16)
        qksb = sb("qksb", [128, ND * D], BF16)
        vosb = sb("vosb", [128, ND * D], BF16)
        xqk = [sb(f"xqk{c}", [128, NTOK], BF16) for c in range(ND)]
        at = sb("at", [128, 16 * 512], BF16)
        avsb = sb("avsb", [128, ND * 512], BF16)
        onesb = sb("onesb", [1, 128], F32)
        ones_bf = sb("ones_bf", [128, 1], BF16)
        rec = sb("rec", [1, 512], F32)
        bcs = sb("bcs", [128, 512], F32)
        stile = [sb(f"stile{i}", [128, D], F32) for i in range(2)]
        ps_s = [psa(f"ps_s{i}", [128, 512], F32) for i in range(2)]
        ps_den = psa("ps_den", [1, 512], F32)
        ps_bc = psa("ps_bc", [128, 512], F32)
        ps_av = psa("ps_av", [128, 512], F32)
        ps_vo = [psa(f"ps_vo{i}", [128, 512], F32) for i in range(2)]
        m8v = sb("m8v", [128, 1], F32)
        ep0 = ev.op("vector", lambda e: e.memset(m8v[:], -8.0), inc="vx")
        lds = []
        for c in range(ND):
            lds.append(ev.dma("sync", xn[c][:], xnT_b[c * 128:(c + 1) * 128, :], inc="ld"))
        for t in range(NT):
            lds.append(ev.dma("sync", xnN[:, t * D:(t + 1) * D],
                              xnN_b[t * 128:(t + 1) * 128, :], inc="ld"))
        ev.dma("sync", onesb[:], onesr, inc="ld")
        wd1 = ev.dma("gpsimd", qksb[:].rearrange("p (k n) -> p k n", n=D),
                     QK_i.rearrange("(k p) n -> p k n", p=128), inc="wd")
        wd2 = ev.dma("gpsimd", vosb[:].rearrange("p (k n) -> p k n", n=D),
                     VO_i.rearrange("(k p) n -> p k n", p=128), inc="wd")
        mset = ev.op("vector", lambda e: e.memset(ones_bf[:], 1.0), inc="vx")
        lds_all = ev.wait_all("ld", "ld2")
        st_guard = [None, None]
        guard = None
        for ecn in range(ND):
            for tcc in range(4):
                acc = None
                for c in range(ND):
                    acc = ev.op("tensor", lambda e, c=c, ecn=ecn, tcc=tcc: e.matmul(
                        ps_s[tcc % 2][:], qksb[:, c * D + ecn * 128: c * D + ecn * 128 + 128],
                        xn[c][:, tcc * 512:(tcc + 1) * 512],
                        start=(c == 0), stop=(c == ND - 1)),
                        waits=(lds_all + [wd1] + ([guard] if guard else [])) if c == 0 else [],
                        inc="mm")
                guard = ev.op("vector", lambda e, ecn=ecn, tcc=tcc: e.tensor_copy(
                    xqk[ecn][:, tcc * 512:(tcc + 1) * 512], ps_s[tcc % 2][:]),
                    waits=[acc], inc="vx")
        xqk_done = ev.wait_all("vx")
        for tc in range(4):
            t0 = tc * 512
            exp_done = {}
            for ut in range(16):
                pb = ut % 2
                warw = ev.wait_all("sc", "vx") if ut < 2 else [exp_done[ut - 2]]
                acc = None
                for c in range(ND):
                    acc = ev.op("tensor", lambda e, c=c, ut=ut, t0=t0, pb=pb: e.matmul(
                        ps_s[pb][:], xn[c][:, ut * 128:(ut + 1) * 128],
                        xqk[c][:, t0:t0 + 512],
                        start=(c == 0), stop=(c == ND - 1)),
                        waits=((xqk_done if (tc == 0 and ut == 0) else []) + warw) if c == 0 else [],
                        inc="mm")
                e1 = ev.op("scalar", lambda e, ut=ut, pb=pb: e.activation(
                    at[:, ut * 512:(ut + 1) * 512], ps_s[pb][:], AF.Exp,
                    bias=m8v[:], scale=1.0 / 32.0), waits=[acc, ep0], inc="sc")
                exp_done[ut] = e1
            mden = None
            for ut in range(16):
                mden = ev.op("tensor", lambda e, ut=ut: e.matmul(
                    ps_den[:], ones_bf[:], at[:, ut * 512:(ut + 1) * 512],
                    start=(ut == 0), stop=(ut == 15)),
                    waits=[exp_done[ut]] + ([mset] + ev.wait_all("vx") if ut == 0 else []),
                    inc="mm")
            r1 = ev.op("vector", lambda e: e.reciprocal(rec[:], ps_den[:]),
                       waits=[mden], inc="vx")
            mb = ev.op("tensor", lambda e: e.matmul(
                ps_bc[:], onesb[0:1, :], rec[0:1, :], start=True, stop=True),
                waits=[r1], inc="mm")
            cb = ev.op("vector", lambda e: e.tensor_copy(bcs[:], ps_bc[:]),
                       waits=[mb], inc="vx")
            for dc in range(ND):
                mav = None
                for ut in range(16):
                    mav = ev.op("tensor", lambda e, dc=dc, ut=ut: e.matmul(
                        ps_av[:], xnN[:, ut * D + dc * 128: ut * D + dc * 128 + 128],
                        at[:, ut * 512:(ut + 1) * 512],
                        start=(ut == 0), stop=(ut == 15)),
                        waits=([cb] + ev.wait_all("vx")) if ut == 0 else [],
                        inc="mm")
                ev.op("vector", lambda e, dc=dc: e.tensor_tensor(
                    avsb[:, dc * 512:(dc + 1) * 512], ps_av[:], bcs[:], ALU.mult),
                    waits=[mav], inc="vx")
            av_done = ev.wait_all("vx")
            for tt in range(4):
                sready = []
                for n in range(2):
                    acc = None
                    for dc in range(ND):
                        acc = ev.op("tensor", lambda e, dc=dc, tt=tt, n=n: e.matmul(
                            ps_vo[n][:], avsb[:, dc * 512 + tt * 128: dc * 512 + tt * 128 + 128],
                            vosb[:, dc * D + n * 512: dc * D + (n + 1) * 512],
                            start=(dc == 0), stop=(dc == ND - 1)),
                            waits=(av_done + [wd2] + ev.wait_all("vx")[-1:]) if dc == 0 else [],
                            inc="mm")
                    sready.append(ev.op("vector", lambda e, tt=tt, n=n: e.tensor_copy(
                        stile[tt % 2][:, n * 512:(n + 1) * 512], ps_vo[n][:]),
                        waits=[acc] + ([st_guard[tt % 2]] if st_guard[tt % 2] else []),
                        inc="vx"))
                ev.dma("sync", s1_o[t0 + tt * 128: t0 + (tt + 1) * 128, :],
                       stile[tt % 2][:], waits=sready, inc="st")
                st_guard[tt % 2] = ev.mark("st")


def emit_l1ffn(nc, ev, ctx, io, pfx=""):
    xnT_f = io["xnT_f"]
    cw = io["cw"]
    cb = io["cb"]
    gw = io["gw"]
    gb = io["gb"]
    ar_i = io["ar_i"]
    onesr = io["onesr"]
    ident = io["ident"]
    o1_o = io["o1_o"]
    if 1:
        sb = lambda *a: ctx.enter_context(nc.sbuf_tensor(pfx + a[0], *a[1:]))
        psa = lambda *a: ctx.enter_context(nc.psum_tensor(pfx + a[0], *a[1:]))
        out = [sb(f"out{c}", [128, NTOK], F32) for c in range(ND)]
        w1sb = sb("w1sb", [128, ND * D], BF16)
        w2sb = sb("w2sb", [128, ND * 512], BF16)
        w3sb = sb("w3sb", [128, 4 * 256], BF16)
        w4sb = sb("w4sb", [128, 2], BF16)
        b1v = sb("b1v", [128, ND], F32)
        b2v = sb("b2v", [128, 4], F32)
        b3v = sb("b3v", [128, 2], F32)
        b4v = sb("b4v", [1, 1], F32)
        g1sb = sb("g1sb", [128, ND * 512], F32)
        g2sb = sb("g2sb", [128, 4 * 256], F32)
        g3sb = sb("g3sb", [128, 2], F32)
        gb1v = sb("gb1v", [128, 4], F32)
        gb2v = sb("gb2v", [128, 2], F32)
        gb3v = sb("gb3v", [1, 1], F32)
        arv = sb("arv", [1, 1], F32)
        gcs = sb("gcs", [128, ND], F32)
        h1v = sb("h1v", [128, 4], F32)
        h2v = sb("h2v", [128, 2], F32)
        gfv = sb("gfv", [1, 1], F32)
        c0v = sb("c0v", [1, 1], F32)
        c1v = sb("c1v", [1, 1], F32)
        t1v = sb("t1v", [1, 1], F32)
        obf = sb("obf", [128, ND * NTOK], BF16)
        h1sb = sb("h1sb", [128, ND * 512], BF16)
        h2sb = sb("h2sb", [128, 4 * 512], BF16)
        h3sb = sb("h3sb", [128, 2 * 512], BF16)
        tfrow = sb("tfrow", [1, 512], F32)
        coup = sb("coup", [1, 512], F32)
        tcp = sb("tcp", [128, 512], F32)
        ttb = sb("ttb", [128, 512], F32)
        onesb = sb("onesb", [1, 128], F32)
        idf = sb("idf", [128, 128], F32)
        gtile = sb("gtile", [128, D], F32)
        ps_m = [psa(f"ps_m{i}", [128, 512], F32) for i in range(2)]
        ps_sm = psa("ps_sm", [128, 1], F32)
        ps_tf = psa("ps_tf", [1, 512], F32)
        ps_bc = psa("ps_bc", [128, 512], F32)
        ps_t = psa("ps_t", [128, 128], F32)
        lds = []
        for c in range(ND):
            lds.append(ev.dma("sync", out[c][:], xnT_f[c * 128:(c + 1) * 128, :], inc="ld"))
        ev.dma("sync", onesb[:], onesr, inc="ld")
        ev.dma("sync", idf[:], ident, inc="ld")
        ev.dma("sync", arv[:], ar_i.rearrange("(o d) -> o d", o=1), inc="ld")
        wds = []
        wds.append(ev.dma("gpsimd", w1sb[:].rearrange("p (k n) -> p k n", n=D),
                          cw["cn_w1"].rearrange("(k p) n -> p k n", p=128), inc="wd"))
        wds.append(ev.dma("gpsimd", w2sb[:].rearrange("p (k n) -> p k n", n=512),
                          cw["cn_w2"].rearrange("(k p) n -> p k n", p=128), inc="wd"))
        wds.append(ev.dma("gpsimd", w3sb[:].rearrange("p (k n) -> p k n", n=256),
                          cw["cn_w3"].rearrange("(k p) n -> p k n", p=128), inc="wd"))
        for c in range(2):
            wds.append(ev.dma("gpsimd", w4sb[:, c:c + 1],
                              cw["cn_w4"][c * 128:(c + 1) * 128, 0:1], inc="wd"))
        wds.append(ev.dma("sync", g1sb[:].rearrange("p (k n) -> p k n", n=512),
                          gw["gc_w1"].rearrange("(k p) n -> p k n", p=128), inc="ld"))
        wds.append(ev.dma("sync", g2sb[:].rearrange("p (k n) -> p k n", n=256),
                          gw["gc_w2"].rearrange("(k p) n -> p k n", p=128), inc="ld"))
        for c in range(2):
            wds.append(ev.dma("sync", g3sb[:, c:c + 1],
                              gw["gc_w3"][c * 128:(c + 1) * 128, 0:1], inc="ld"))
        for c in range(ND):
            ev.dma("sync", b1v[:, c:c + 1],
                   cb["cn_b1"].rearrange("(d o) -> d o", o=1)[c * 128:(c + 1) * 128], inc="ld")
        for c in range(4):
            ev.dma("sync", b2v[:, c:c + 1],
                   cb["cn_b2"].rearrange("(d o) -> d o", o=1)[c * 128:(c + 1) * 128], inc="ld")
            ev.dma("sync", gb1v[:, c:c + 1],
                   gb["gc_b1"].rearrange("(d o) -> d o", o=1)[c * 128:(c + 1) * 128], inc="ld")
        for c in range(2):
            ev.dma("sync", b3v[:, c:c + 1],
                   cb["cn_b3"].rearrange("(d o) -> d o", o=1)[c * 128:(c + 1) * 128], inc="ld")
            ev.dma("sync", gb2v[:, c:c + 1],
                   gb["gc_b2"].rearrange("(d o) -> d o", o=1)[c * 128:(c + 1) * 128], inc="ld")
        ev.dma("sync", b4v[:], cb["cn_b4"].rearrange("(o d) -> o d", o=1), inc="ld")
        ev.dma("sync", gb3v[:], gb["gc_b3"].rearrange("(o d) -> o d", o=1), inc="ld")
        lds_all = ev.wait_all("ld", "ld2", "wd", "wd2")
        gm = []
        for c in range(ND):
            r = ev.op("vector", lambda e, c=c: e.reduce_sum(gcs[:, c:c + 1], out[c][:], AX.X),
                      waits=[lds[c]], inc="vx")
            gm.append(r)
        vsc = ev.op("vector", lambda e: e.tensor_scalar_mul(gcs[:], gcs[:], 1.0 / NTOK),
                    waits=gm, inc="vx")
        h1done = []
        for m in range(4):
            acc = None
            for c in range(ND):
                acc = ev.op("tensor", lambda e, m=m, c=c: e.matmul(
                    ps_sm[:], g1sb[:, c * 512 + m * 128: c * 512 + m * 128 + 128],
                    gcs[:, c:c + 1], start=(c == 0), stop=(c == ND - 1)),
                    waits=(lds_all + [vsc] + h1done[-1:]) if c == 0 else [], inc="mm")
            h1done.append(ev.op("scalar", lambda e, m=m: e.activation(
                h1v[:, m:m + 1], ps_sm[:], AF.Gelu, bias=gb1v[:, m:m + 1]),
                waits=[acc], inc="sc"))
        h2done = []
        for m in range(2):
            acc = None
            for c in range(4):
                acc = ev.op("tensor", lambda e, m=m, c=c: e.matmul(
                    ps_sm[:], g2sb[:, c * 256 + m * 128: c * 256 + m * 128 + 128],
                    h1v[:, c:c + 1], start=(c == 0), stop=(c == 3)),
                    waits=(h1done + h2done[-1:]) if c == 0 else [], inc="mm")
            h2done.append(ev.op("scalar", lambda e, m=m: e.activation(
                h2v[:, m:m + 1], ps_sm[:], AF.Gelu, bias=gb2v[:, m:m + 1]),
                waits=[acc], inc="sc"))
        acc = None
        for c in range(2):
            acc = ev.op("tensor", lambda e, c=c: e.matmul(
                ps_sm[0:1, :], g3sb[0:128, c:c + 1], h2v[:, c:c + 1],
                start=(c == 0), stop=(c == 1)),
                waits=(h2done + h1done) if c == 0 else [], inc="mm")
        gfd = ev.op("scalar", lambda e: e.activation(
            gfv[:], ps_sm[0:1, 0:1], AF.Sigmoid, bias=gb3v[0:1, 0:1]),
            waits=[acc], inc="sc")
        v1 = ev.op("vector", lambda e: e.tensor_scalar_mul(c1v[:], arv[:], 0.6),
                   waits=[gfd] + lds_all, inc="vx")
        v2 = ev.op("vector", lambda e: e.tensor_mul(t1v[:], arv[:], gfv[:]),
                   waits=[gfd], inc="vx")
        v3 = ev.op("vector", lambda e: e.scalar_tensor_tensor(
            t1v[:], t1v[:], 1.4, arv[:], ALU.mult, ALU.subtract), waits=[v2], inc="vx")
        cc = ev.op("vector", lambda e: e.tensor_scalar_add(c0v[:], t1v[:], 0.1),
                   waits=[v3], inc="vx")
        h1_last_prev = []
        for cyc in range(3):
            cvt = []
            for c in range(ND):
                cvt.append(ev.op("vector", lambda e, c=c: e.tensor_copy(
                    obf[:, c * NTOK:(c + 1) * NTOK], out[c][:]),
                    waits=h1_last_prev, inc="vx"))
            for tc in range(4):
                t0 = tc * 512
                h1d = []
                for m in range(ND):
                    acc = None
                    for c in range(ND):
                        acc = ev.op("tensor", lambda e, m=m, c=c, t0=t0: e.matmul(
                            ps_m[m % 2][:], w1sb[:, c * D + m * 128: c * D + m * 128 + 128],
                            obf[:, c * NTOK + t0: c * NTOK + t0 + 512],
                            start=(c == 0), stop=(c == ND - 1)),
                            waits=(cvt + lds_all + ev.wait_all("sc")[-1:]) if c == 0 else [],
                            inc="mm")
                    h1d.append(ev.op("scalar", lambda e, m=m: e.activation(
                        h1sb[:, m * 512:(m + 1) * 512], ps_m[m % 2][:], AF.Gelu,
                        bias=b1v[:, m:m + 1]), waits=[acc], inc="sc"))
                h1_last_prev = [acc]
                h2d = []
                for m in range(4):
                    acc = None
                    for c in range(ND):
                        acc = ev.op("tensor", lambda e, m=m, c=c: e.matmul(
                            ps_m[m % 2][:], w2sb[:, c * 512 + m * 128: c * 512 + m * 128 + 128],
                            h1sb[:, c * 512:(c + 1) * 512],
                            start=(c == 0), stop=(c == ND - 1)),
                            waits=(h1d + ev.wait_all("sc")[-1:]) if c == 0 else [],
                            inc="mm")
                    h2d.append(ev.op("scalar", lambda e, m=m: e.activation(
                        h2sb[:, m * 512:(m + 1) * 512], ps_m[m % 2][:], AF.Gelu,
                        bias=b2v[:, m:m + 1]), waits=[acc], inc="sc"))
                h3d = []
                for m in range(2):
                    acc = None
                    for c in range(4):
                        acc = ev.op("tensor", lambda e, m=m, c=c: e.matmul(
                            ps_m[m % 2][:], w3sb[:, c * 256 + m * 128: c * 256 + m * 128 + 128],
                            h2sb[:, c * 512:(c + 1) * 512],
                            start=(c == 0), stop=(c == 3)),
                            waits=(h2d + ev.wait_all("sc")[-1:]) if c == 0 else [],
                            inc="mm")
                    h3d.append(ev.op("scalar", lambda e, m=m: e.activation(
                        h3sb[:, m * 512:(m + 1) * 512], ps_m[m % 2][:], AF.Gelu,
                        bias=b3v[:, m:m + 1]), waits=[acc], inc="sc"))
                acc = None
                for c in range(2):
                    acc = ev.op("tensor", lambda e, c=c: e.matmul(
                        ps_tf[:], w4sb[0:128, c:c + 1], h3sb[:, c * 512:(c + 1) * 512],
                        start=(c == 0), stop=(c == 1)),
                        waits=h3d if c == 0 else [], inc="mm")
                tfd = ev.op("scalar", lambda e: e.activation(
                    tfrow[:], ps_tf[:], AF.Sigmoid, bias=b4v[0:1, 0:1]),
                    waits=[acc], inc="sc")
                cpd = ev.op("vector", lambda e: e.tensor_scalar(
                    coup[:], tfrow[:], c1v[0:1, 0:1], c0v[0:1, 0:1], ALU.mult, ALU.add),
                    waits=[tfd, cc], inc="vx")
                mb = ev.op("tensor", lambda e: e.matmul(
                    ps_bc[:], onesb[0:1, :], coup[0:1, :], start=True, stop=True),
                    waits=[cpd] + ev.wait_all("vx")[-1:], inc="mm")
                last = None
                for p in range(4):
                    sgn = 1.0 if p < 2 else -1.0
                    pw = [mb] + ([last] if last else [])
                    vc = ev.op("vector", lambda e, p=p, t0=t0: e.tensor_copy(
                        tcp[:], out[p][:, t0:t0 + 512]), waits=pw, inc="vx")
                    va = ev.op("vector", lambda e, p=p, t0=t0: e.tensor_tensor(
                        ttb[:], ps_bc[:], out[p + 4][:, t0:t0 + 512], ALU.mult),
                        waits=[vc], inc="vx")
                    if sgn > 0:
                        vb = ev.op("vector", lambda e, p=p, t0=t0: e.tensor_add(
                            out[p][:, t0:t0 + 512], out[p][:, t0:t0 + 512], ttb[:]),
                            waits=[va], inc="vx")
                    else:
                        vb = ev.op("vector", lambda e, p=p, t0=t0: e.tensor_sub(
                            out[p][:, t0:t0 + 512], out[p][:, t0:t0 + 512], ttb[:]),
                            waits=[va], inc="vx")
                    vd = ev.op("vector", lambda e, p=p, t0=t0: e.tensor_tensor(
                        ttb[:], ps_bc[:], tcp[:], ALU.mult), waits=[vb], inc="vx")
                    if sgn > 0:
                        last = ev.op("vector", lambda e, p=p, t0=t0: e.tensor_sub(
                            out[p + 4][:, t0:t0 + 512], out[p + 4][:, t0:t0 + 512], ttb[:]),
                            waits=[vd], inc="vx")
                    else:
                        last = ev.op("vector", lambda e, p=p, t0=t0: e.tensor_add(
                            out[p + 4][:, t0:t0 + 512], out[p + 4][:, t0:t0 + 512], ttb[:]),
                            waits=[vd], inc="vx")
        res_done = ev.wait_all("vx")
        tr_prev = None
        for t in range(NT):
            gts = []
            for c in range(ND):
                m = ev.op("tensor", lambda e, c=c, t=t: e.transpose(
                    ps_t[:], out[c][:, t * 128:(t + 1) * 128], idf[:]),
                    waits=(res_done + lds_all) if (t == 0 and c == 0) else
                          ([tr_prev] if tr_prev else []), inc="mm")
                tr_prev = ev.op("vector", lambda e, c=c: e.tensor_copy(
                    gtile[:, c * 128:(c + 1) * 128], ps_t[:]), waits=[m], inc="vx")
                gts.append(tr_prev)
            dm = ev.dma("sync", o1_o[t * 128:(t + 1) * 128, :], gtile[:],
                        waits=gts, inc="st")
            tr_prev = ("st", ev.cnt["st"])


def emit_comb2(nc, ev, ctx, io, pfx=""):
    x1_i = io["x1_i"]
    s1_i = io["s1_i"]
    o1_i = io["o1_i"]
    y_o = io["y_o"]
    if 1:
        sb = lambda *a: ctx.enter_context(nc.sbuf_tensor(pfx + a[0], *a[1:]))
        xt = [sb(f"xt{i}", [128, D], F32) for i in range(2)]
        st = [sb(f"st{i}", [128, D], F32) for i in range(2)]
        ot = [sb(f"ot{i}", [128, D], F32) for i in range(2)]
        gs = sb("gs", [128, D], F32)
        yt = [sb(f"yt{i}", [128, D], F32) for i in range(2)]
        guard = [None, None]
        for i in range(NT):
            b = i % 2
            w = [] if guard[b] is None else [guard[b]]
            l1 = ev.dma("sync", xt[b][:], x1_i[i * 128:(i + 1) * 128, :], waits=w, inc="ld")
            l2 = ev.dma("sync", st[b][:], s1_i[i * 128:(i + 1) * 128, :], waits=w, inc="ld")
            l3 = ev.dma("gpsimd", ot[b][:], o1_i[i * 128:(i + 1) * 128, :], waits=w, inc="ldg")
            sc0 = ev.op("scalar", lambda e, b=b: e.activation(
                gs[:], ot[b][:], AF.Silu), waits=[l3] + ev.wait_all("vx"), inc="sc")
            v0 = ev.op("vector", lambda e, b=b: e.tensor_mul(gs[:], st[b][:], gs[:]),
                       waits=[sc0, l2], inc="vx")
            v1 = ev.op("vector", lambda e, b=b: e.tensor_add(yt[b][:], xt[b][:], gs[:]),
                       waits=[v0, l1], inc="vx")
            guard[b] = ev.dma("sync", y_o[i * 128:(i + 1) * 128, :], yt[b][:],
                              waits=[v1], inc="st")


ALLSEMS = ["ld", "ld2", "ldg", "wd", "wd2", "mm", "sc", "vx", "st", "stg"]


def build_fused():
    nc = bass.Bass("TRN2", target_bir_lowering=False)

    def dt(name, shape, kindt=F32, kind="ExternalInput"):
        return nc.dram_tensor(name, shape, kindt, kind=kind).ap()

    x = dt("x", [NTOK, D])
    l0_w_qkv = dt("l0_w_qkv", [D, 3 * D])
    l0_w_out = dt("l0_w_out", [D, D])
    l0_b_out = dt("l0_b_out", [D])
    l0_gamma = dt("l0_gamma", [D])
    l0_beta = dt("l0_beta", [D])
    f_wqkv = [dt(f"f_wqkv{l}", [D, 1536]) for l in range(2)]
    f_qs = [dt(f"f_qs{l}", [64]) for l in range(2)]
    f_ks = [dt(f"f_ks{l}", [64]) for l in range(2)]
    f_wo = [dt(f"f_wo{l}", [512, D]) for l in range(2)]
    f_lg = [dt(f"f_lg{l}", [D]) for l in range(2)]
    f_lb = [dt(f"f_lb{l}", [D]) for l in range(2)]
    f_w1 = [dt(f"f_w1{l}", [D, 5460]) for l in range(2)]
    f_b1 = [dt(f"f_b1{l}", [5460]) for l in range(2)]
    f_w2 = [dt(f"f_w2{l}", [2730, D]) for l in range(2)]
    f_b2 = [dt(f"f_b2{l}", [D]) for l in range(2)]
    QK = dt("QK", [D, D])
    VO = dt("VO", [D, D])
    cn_w = {1: dt("cn_w1", [D, D]), 2: dt("cn_w2", [D, 512]),
            3: dt("cn_w3", [512, 256]), 4: dt("cn_w4", [256, 1])}
    cn_b = {1: dt("cn_b1", [D]), 2: dt("cn_b2", [512]),
            3: dt("cn_b3", [256]), 4: dt("cn_b4", [1])}
    gc_w = {1: dt("gc_w1", [D, 512]), 2: dt("gc_w2", [512, 256]),
            3: dt("gc_w3", [256, 1])}
    gc_b = {1: dt("gc_b1", [512]), 2: dt("gc_b2", [256]), 3: dt("gc_b3", [1])}
    ar = dt("ar", [1])
    ident = dt("c_ident", [128, 128])
    onesr = dt("c_ones_row", [1, 128])
    ones2 = dt("c_ones2", [128, 2])
    ones2T = dt("c_ones2T", [2, 128])
    maskM = dt("c_maskM", [1024, 512])
    ropes = {n: dt("c_" + n, [128, NTOK]) for n in ("cs_q", "ss_q", "cs_k", "ss_k")}
    def S(name, shape, kindt=F32):
        return nc.dram_tensor(name, shape, kindt, kind="Internal").ap()
    xnT_f = S("s_xnT_f", [D, NTOK])
    xnT_b = S("s_xnT_b", [D, NTOK], BF16)
    s0 = S("s_s0", [NTOK, D])
    xfa0 = S("s_xfa0", [D, NTOK])
    xfb0 = S("s_xfb0", [D, NTOK])
    xfa1 = S("s_xfa1", [D, NTOK])
    g0 = S("s_g0", [NTOK, D])
    x1 = S("s_x1", [NTOK, D])
    xn1T_f = S("s_xn1T_f", [D, NTOK])
    xn1T_b = S("s_xn1T_b", [D, NTOK], BF16)
    xn1N_b = S("s_xn1N_b", [NTOK, D], BF16)
    s1 = S("s_s1", [NTOK, D])
    o1 = S("s_o1", [NTOK, D])
    y = dt("y", [NTOK, D], F32, kind="ExternalOutput")

    with ExitStack() as octx:
        ev = EV(nc, octx, ALLSEMS)
        block = octx.enter_context(nc.Block())

        def phase(emitfn, io, pfx, **kw):
            with ExitStack() as pctx:
                emitfn(nc, ev, pctx, io, pfx=pfx, **kw)
                ev.finalize(block)
            ev.barrier()

        phase(emit_ln0, dict(x=x, ident=ident, xnT_f=xnT_f, xnT_b=xnT_b), "p1_")
        phase(emit_l0attn, dict(xnT_b=xnT_b, w_qkv=l0_w_qkv, w_out=l0_w_out,
                                b_out=l0_b_out, gamma=l0_gamma, beta=l0_beta,
                                onesr=onesr, s_out=s0), "p2_")
        ffn_srcs = [xnT_f, xfa0, xfb0, xfa1]
        for l in range(2):
            phase(emit_ffn_attn,
                  dict(xfT=ffn_srcs[2 * l], wqkv=f_wqkv[l], qs_i=f_qs[l],
                       ks_i=f_ks[l], wo_i=f_wo[l], onesr=onesr, ones2=ones2,
                       ones2T=ones2T, maskM=maskM, ropes=ropes,
                       xfT_o=ffn_srcs[2 * l + 1]), f"p3{l}_")
            phase(emit_ffn_ffn,
                  dict(xfT=ffn_srcs[2 * l + 1], lg_i=f_lg[l], lb_i=f_lb[l],
                       w1_i=f_w1[l], b1_i=f_b1[l], w2_i=f_w2[l], b2_i=f_b2[l],
                       onesr=onesr, ident=ident,
                       outp=(g0 if l == 1 else xfb0)), f"p4{l}_",
                  out_normal=(l == 1))
        phase(emit_comb1, dict(x_i=x, s_i=s0, g_i=g0, ident=ident, x1_o=x1,
                               xnTf_o=xn1T_f, xnTb_o=xn1T_b, xnN_o=xn1N_b), "p5_")
        phase(emit_l1attn, dict(xnT_b=xn1T_b, xnN_b=xn1N_b, QK_i=QK, VO_i=VO,
                                onesr=onesr, s1_o=s1), "p6_")
        phase(emit_l1ffn, dict(xnT_f=xn1T_f,
                               cw={"cn_w1": cn_w[1], "cn_w2": cn_w[2],
                                   "cn_w3": cn_w[3], "cn_w4": cn_w[4]},
                               cb={"cn_b1": cn_b[1], "cn_b2": cn_b[2],
                                   "cn_b3": cn_b[3], "cn_b4": cn_b[4]},
                               gw={"gc_w1": gc_w[1], "gc_w2": gc_w[2],
                                   "gc_w3": gc_w[3]},
                               gb={"gc_b1": gc_b[1], "gc_b2": gc_b[2],
                                   "gc_b3": gc_b[3]},
                               ar_i=ar, onesr=onesr, ident=ident, o1_o=o1), "p7_")
        phase(emit_comb2, dict(x1_i=x1, s1_i=s1, o1_i=o1, y_o=y), "p8_")
        fin = ev.wait_all("st", "stg")
        ev.op("sync", lambda e: e.nop(), waits=fin)
        ev.op("gpsimd", lambda e: e.nop(), waits=fin)
        ev.finalize(block)
    return nc


_programs = {}


def _prog(name, fn):
    if name not in _programs:
        _programs[name] = fn()
    return _programs[name]


def kernel(**inputs):
    f32 = lambda k: np.ascontiguousarray(np.asarray(inputs[k], dtype=np.float32))
    x = f32("L0_attn_x")
    C = CONSTS
    n = N_CORES
    base = dict(
        l0_w_qkv=f32("L0_attn_w_qkv"), l0_w_out=f32("L0_attn_w_out"),
        l0_b_out=f32("L0_attn_b_out"), l0_gamma=f32("L0_attn_ln_gamma"),
        l0_beta=f32("L0_attn_ln_beta"),
        QK=f32("L1_attn_QK"), VO=f32("L1_attn_VO"),
        cn_w1=f32("L1_ffn_cn_w1"), cn_w2=f32("L1_ffn_cn_w2"),
        cn_w3=f32("L1_ffn_cn_w3"), cn_w4=f32("L1_ffn_cn_w4"),
        cn_b1=f32("L1_ffn_cn_b1"), cn_b2=f32("L1_ffn_cn_b2"),
        cn_b3=f32("L1_ffn_cn_b3"), cn_b4=f32("L1_ffn_cn_b4"),
        gc_w1=f32("L1_ffn_gc_w1"), gc_w2=f32("L1_ffn_gc_w2"),
        gc_w3=f32("L1_ffn_gc_w3"),
        gc_b1=f32("L1_ffn_gc_b1"), gc_b2=f32("L1_ffn_gc_b2"),
        gc_b3=f32("L1_ffn_gc_b3"),
        ar=f32("L1_ffn_adaptive_range"),
        c_ident=C["ident"], c_ones_row=C["ones_row"], c_ones2=C["ones2"],
        c_ones2T=C["ones2T"], c_maskM=C["maskM"],
        c_cs_q=C["cs_q"], c_ss_q=C["ss_q"], c_cs_k=C["cs_k"], c_ss_k=C["ss_k"],
    )
    for l in range(2):
        base[f"f_wqkv{l}"] = f32("L0_ffn_wqkv")[l]
        base[f"f_qs{l}"] = f32("L0_ffn_q_scale")[l]
        base[f"f_ks{l}"] = f32("L0_ffn_k_scale")[l]
        base[f"f_wo{l}"] = f32("L0_ffn_wo")[l]
        base[f"f_lg{l}"] = f32("L0_ffn_ln_g")[l]
        base[f"f_lb{l}"] = f32("L0_ffn_ln_b")[l]
        base[f"f_w1{l}"] = f32("L0_ffn_w1")[l]
        base[f"f_b1{l}"] = f32("L0_ffn_b1")[l]
        base[f"f_w2{l}"] = f32("L0_ffn_w2")[l]
        base[f"f_b2{l}"] = f32("L0_ffn_b2")[l]
    in_maps = [dict(base, x=x[i]) for i in range(n)]
    nc = _prog("fused", build_fused)
    res = run_bass_kernel_spmd(nc, in_maps, core_ids=list(range(n)))
    y = np.stack([res.results[i]["y"] for i in range(n)], axis=0)
    return y.astype(np.float32)


# version 06
# speedup vs baseline: 1.3821x; 1.0267x over previous
import numpy as np
import ml_dtypes
from contextlib import ExitStack
from collections import defaultdict

import concourse.bass as bass
import concourse.mybir as mybir
from concourse.bass_utils import run_bass_kernel_spmd

F32 = mybir.dt.float32
BF16 = mybir.dt.bfloat16
AF = mybir.ActivationFunctionType
ALU = mybir.AluOpType
AX = mybir.AxisListType

BF = ml_dtypes.bfloat16
N_CORES = 8
NTOK = 2048
D = 1024
NT = NTOK // 128
ND = D // 128


class EV:

    def __init__(self, nc, ctx, sem_names):
        self.nc = nc
        self.events = defaultdict(list)
        self.cnt = defaultdict(int)
        self.sems = {n: ctx.enter_context(nc.semaphore(n)) for n in sem_names}
        self.pending = {}

    def mark(self, sem):
        return (sem, self.cnt[sem])

    ENG_SEM = {"vector": "vx", "scalar": "sc"}

    def op(self, engine, fn, waits=(), inc=None, inc_n=1):
        w = self.pending.pop(engine, []) + list(waits)
        sname = self.ENG_SEM.get(engine)
        if sname is not None and sname in self.sems and self.cnt[sname] > 0:
            w.append((sname, self.cnt[sname]))
        rec = None
        if inc is not None:
            self.cnt[inc] += inc_n
            rec = (inc, inc_n)
        self.events[engine].append((tuple(w), fn, rec))
        return (inc, self.cnt[inc]) if inc is not None else None

    def dma(self, engine, out, in_, waits=(), inc=None):
        w = list(waits)
        if inc is not None and self.cnt[inc] > 0:
            w.append((inc, self.cnt[inc]))
        return self.op(engine, lambda e: e.dma_start(out=out, in_=in_),
                       waits=w, inc=inc, inc_n=16)

    def wait_all(self, *sem_names):
        return [(s, self.cnt[s]) for s in sem_names]

    def barrier(self):
        waits = self.wait_all(*self.sems.keys())
        for e in ("sync", "gpsimd", "tensor", "scalar", "vector"):
            self.pending[e] = list(waits)

    def finalize(self, block):
        if not hasattr(self, "_waited"):
            self._waited = {e: defaultdict(int)
                            for e in ("sync", "gpsimd", "tensor", "scalar", "vector")}
        for engine in ("sync", "gpsimd", "tensor", "scalar", "vector"):
            evs = self.events[engine]
            if not evs:
                continue
            sems = self.sems
            waited = self._waited[engine]

            def make(evs, waited):
                def emit(eng):
                    for waits, fn, inc in evs:
                        merged = {}
                        for wt in waits:
                            assert isinstance(wt, tuple) and len(wt) == 2, f"bad wait {wt}"
                            s, v = wt
                            if v > merged.get(s, 0):
                                merged[s] = v
                        for s, v in merged.items():
                            if v > 0 and v > waited[s]:
                                eng.wait_ge(sems[s], v)
                                waited[s] = v
                        instr = fn(eng)
                        if inc is not None:
                            instr.then_inc(sems[inc[0]], inc[1])
                return emit
            getattr(block, engine)(make(evs, waited))
        self.events = defaultdict(list)


def sem_val(ev, sem):
    return (sem, ev.cnt[sem])


def make_consts():
    c = {}
    c["ident"] = np.eye(128, dtype=np.float32)
    c["ones_row"] = np.ones((1, 128), dtype=np.float32)
    ones2 = np.zeros((128, 2), dtype=np.float32)
    ones2[:64, 0] = 1.0
    ones2[64:, 1] = 1.0
    c["ones2"] = ones2
    c["ones2T"] = ones2.T.copy()
    DH, W = 64, 512
    half = np.arange(0, DH, 2, dtype=np.float64)
    inv_freq = 1.0 / 10000.0 ** (half / DH)
    t = np.arange(NTOK, dtype=np.float64)
    freqs = t[:, None] * inv_freq[None, :]
    freqs = np.concatenate([freqs, freqs], axis=-1)
    base = (half + 0.4 * DH) / (1.4 * DH)
    power = (t - NTOK // 2) / float(W // 2)
    scale = base[None, :] ** power[:, None]
    scale = np.concatenate([scale, scale], axis=-1)
    cos, sin = np.cos(freqs), np.sin(freqs)
    for name, tab in (("cs_q", cos * scale), ("ss_q", sin * scale),
                      ("cs_k", cos / scale), ("ss_k", sin / scale)):
        tt = tab.T.astype(np.float32)
        c[name] = np.concatenate([tt, tt], axis=0)
    j = np.arange(1024)[:, None]
    i = np.arange(512)[None, :]
    allowed = (i <= j) & (j <= i + W)
    c["maskM"] = np.where(allowed, 100.0, -1e30).astype(np.float32)
    return c


CONSTS = make_consts()


def bcast_cols(ev, psum_bc, ones_row, src_row, dst, n, waits=()):
    for c0 in range(0, n, 512):
        w = min(512, n - c0)
        m = ev.op("tensor", lambda e, c0=c0, w=w: e.matmul(
            psum_bc[:, 0:w], ones_row[0:1, :], src_row[0:1, c0:c0 + w],
            start=True, stop=True),
            waits=list(waits) + ev.wait_all("vx"), inc="mm")
        ev.op("vector", lambda e, c0=c0, w=w: e.tensor_copy(
            dst[:, c0:c0 + w], psum_bc[:, 0:w]), waits=[m], inc="vx")


def emit_ln0(nc, ev, ctx, io, pfx=""):
    x = io["x"]
    ident = io["ident"]
    xnT_f = io["xnT_f"]
    xnT_b = io["xnT_b"]
    if 1:
        sb = lambda *a: ctx.enter_context(nc.sbuf_tensor(pfx + a[0], *a[1:]))
        psa = lambda *a: ctx.enter_context(nc.psum_tensor(pfx + a[0], *a[1:]))
        xt = [sb(f"xt{i}", [128, D], F32) for i in range(2)]
        xn = [sb(f"xn{i}", [128, D], F32) for i in range(2)]
        sq = sb("sq", [128, D], F32)
        s1 = sb("s1", [128, 1], F32)
        s2 = sb("s2", [128, 1], F32)
        mu = sb("mu", [128, 1], F32)
        mu2 = sb("mu2", [128, 1], F32)
        var = sb("var", [128, 1], F32)
        rt = sb("rt", [128, 1], F32)
        rr = sb("rr", [128, 1], F32)
        idf = sb("idf", [128, 128], F32)
        epsv = sb("epsv", [128, 1], F32)
        chunks = [sb(f"ch{c}", [128, NTOK], F32) for c in range(ND)]
        tp = [psa(f"tp{i}", [128, 128], F32) for i in range(2)]
        ev.dma("sync", idf[:], ident, inc="ld")
        ep = ev.op("vector", lambda e: e.memset(epsv[:], 1e-5), inc="vx")
        last_reads = [None, None]
        for i in range(NT):
            b = i % 2
            w = [] if last_reads[b] is None else [last_reads[b]]
            ld = ev.dma("sync", xt[b][:], x[i * 128:(i + 1) * 128, :],
                        waits=w, inc="ld")
            sc1 = ev.op("scalar", lambda e, b=b: e.square(sq[:], xt[b][:]),
                        waits=[ld] + ev.wait_all("vx"), inc="sc")
            v1 = ev.op("vector", lambda e, b=b: e.reduce_sum(s1[:], xt[b][:], AX.X),
                       waits=[ld], inc="vx")
            v2 = ev.op("vector", lambda e: e.reduce_sum(s2[:], sq[:], AX.X),
                       waits=[sc1], inc="vx")
            ev.op("vector", lambda e: e.tensor_scalar_mul(mu[:], s1[:], 1.0 / D),
                  waits=[v1], inc="vx")
            sc2 = ev.op("scalar", lambda e: e.square(mu2[:], mu[:]),
                        waits=ev.wait_all("vx"), inc="sc")
            v3 = ev.op("vector", lambda e: e.scalar_tensor_tensor(
                var[:], s2[:], 1.0 / D, mu2[:], ALU.mult, ALU.subtract),
                waits=[sc2, v2], inc="vx")
            sc3 = ev.op("scalar", lambda e: e.activation(
                rt[:], var[:], AF.Sqrt, bias=epsv[:]), waits=[v3, ep], inc="sc")
            v4 = ev.op("vector", lambda e: e.reciprocal(rr[:], rt[:]),
                       waits=[sc3], inc="vx")
            vn = ev.op("vector", lambda e, b=b: e.tensor_scalar(
                xn[b][:], xt[b][:], mu[:], rr[:], ALU.subtract, ALU.mult),
                waits=[v4], inc="vx")
            last_reads[b] = vn
            for c in range(ND):
                pb = c % 2
                m = ev.op("tensor", lambda e, b=b, c=c, pb=pb: e.transpose(
                    tp[pb][:], xn[b][:, c * 128:(c + 1) * 128], idf[:]),
                    waits=[vn] + ev.wait_all("vx"), inc="mm")
                ev.op("vector", lambda e, i=i, c=c, pb=pb: e.tensor_copy(
                    chunks[c][:, i * 128:(i + 1) * 128], tp[pb][:]),
                    waits=[m], inc="vx")
        w = ev.wait_all("vx")
        for c in range(ND):
            ev.dma("sync", xnT_f[c * 128:(c + 1) * 128, :], chunks[c][:],
                   waits=w, inc="st")
            ev.dma("gpsimd", xnT_b[c * 128:(c + 1) * 128, :], chunks[c][:],
                   waits=w, inc="stg")


def emit_l0attn(nc, ev, ctx, io, pfx=""):
    HG = 8
    xnT_b = io["xnT_b"]
    w_qkv = io["w_qkv"]
    w_out = io["w_out"]
    b_out = io["b_out"]
    gamma = io["gamma"]
    beta = io["beta"]
    onesr = io["onesr"]
    s_out = io["s_out"]
    if 1:
        sb = lambda *a: ctx.enter_context(nc.sbuf_tensor(pfx + a[0], *a[1:]))
        psa = lambda *a: ctx.enter_context(nc.psum_tensor(pfx + a[0], *a[1:]))
        xn = [sb(f"xn{c}", [128, NTOK], BF16) for c in range(ND)]
        oall = [sb(f"oall{c}", [128, NTOK], BF16) for c in range(ND)]
        wq = [sb(f"wq{i}", [128, 1024], BF16) for i in range(2)]
        wk = [sb(f"wk{i}", [128, 1024], BF16) for i in range(2)]
        wv = [sb(f"wv{i}", [128, 1024], BF16) for i in range(2)]
        qT = [sb(f"qT{i}", [128, NTOK], BF16) for i in range(2)]
        kT = [sb(f"kT{i}", [128, NTOK], BF16) for i in range(2)]
        vsb = [sb(f"vsb{i}", [128, 16 * 132], BF16) for i in range(2)]
        at = sb("at", [128, 16 * 512], BF16)
        wo = sb("wo", [128, 8 * 1024], BF16)
        brow = sb("brow", [1, 3 * D], F32)
        bbc = sb("bbc", [128, D], F32)
        gbc = sb("gbc", [128, D], F32)
        bebc = sb("bebc", [128, D], F32)
        onesb = sb("onesb", [1, 128], F32)
        rec = [sb(f"rec{i}", [1, 512], F32) for i in range(2)]
        nbc = [sb(f"nbc{i}", [32, 512], F32) for i in range(2)]
        stile = sb("stile", [128, D], F32)
        sq = sb("sq", [128, D], F32)
        st1 = sb("st1", [128, 1], F32)
        st2 = sb("st2", [128, 1], F32)
        mu = sb("mu", [128, 1], F32)
        mu2 = sb("mu2", [128, 1], F32)
        var = sb("var", [128, 1], F32)
        rt = sb("rt", [128, 1], F32)
        rr = sb("rr", [128, 1], F32)
        otile = sb("otile", [128, D], F32)

        ps_s = [psa(f"ps_s{i}", [128, 1024], F32) for i in range(2)]
        ps_qk = psa("ps_qk", [128, 512], F32)
        ps_o = [psa(f"ps_o{i}", [33, 512], F32) for i in range(2)]
        ps_b = psa("ps_b", [32, 512], F32)
        ps_bc = ps_s[1]
        ps_w = [ps_s[0], ps_qk]
        m8v = sb("m8v", [128, 1], F32)
        epsv = sb("epsv", [128, 1], F32)

        ep0 = ev.op("vector", lambda e: e.memset(m8v[:], -8.0), inc="vx")
        ep1 = ev.op("vector", lambda e: e.memset(epsv[:], 1e-6), inc="vx")
        lds = []
        for c in range(ND):
            lds.append(ev.dma("sync", xn[c][:], xnT_b[c * 128:(c + 1) * 128, :],
                              inc="ld"))
        ev.dma("sync", onesb[:], onesr, inc="ld")
        ev.dma("sync", brow[0:1, 0:D], b_out.rearrange("(o d) -> o d", o=1), inc="ld")
        ev.dma("sync", brow[0:1, D:2 * D], gamma.rearrange("(o d) -> o d", o=1), inc="ld")
        ev.dma("sync", brow[0:1, 2 * D:3 * D], beta.rearrange("(o d) -> o d", o=1), inc="ld")
        ev.dma("gpsimd", wo[:].rearrange("p (k n) -> p k n", n=1024),
               w_out.rearrange("(k p) n -> p k n", p=128), inc="wd")
        wrow = ev.wait_all("ld", "ld2")
        bcast_cols(ev, ps_bc[:, 0:512], onesb, brow[0:1, 0:D], bbc, D, waits=wrow)
        bcast_cols(ev, ps_bc[:, 0:512], onesb, brow[0:1, D:2 * D], gbc, D, waits=wrow)
        bcast_cols(ev, ps_bc[:, 0:512], onesb, brow[0:1, 2 * D:3 * D], bebc, D, waits=wrow)
        vmemset = []
        for vb in range(2):
            for t in range(16):
                for hh in range(4):
                    vmemset.append(ev.op("vector", lambda e, vb=vb, t=t, hh=hh: e.memset(
                        vsb[vb][:, t * 132 + hh * 33 + 32: t * 132 + hh * 33 + 33], 1.0),
                        inc="vx"))
        def load_group(g, buf, waits=()):
            o = []
            for (dst, c0) in ((wq, g * 128), (wk, D + g * 128), (wv, 2 * D + g * 128)):
                o.append(ev.dma(
                    "gpsimd", dst[buf][:].rearrange("p (k n) -> p k n", n=128),
                    w_qkv[:, c0:c0 + 128].rearrange("(k p) n -> p k n", p=128),
                    waits=waits, inc="wd" if buf == 0 else "wd2"))
            return o

        gw = [None, None]
        wguard = [None, None]
        unit_guard = [None]
        prev_exp = None
        prev_av = None
        tt_done = [None, None]
        pending_tail = None

        def qkv_units(g, buf):
            st = {"qk_done": [], "v_done": []}
            units = []
            for (dst, wsb) in ((qT, wq), (kT, wk)):
                for tc in range(4):
                    def u(dst=dst, wsb=wsb, tc=tc, buf=buf, st=st):
                        acc = None
                        for c in range(ND):
                            acc = ev.op("tensor", lambda e, c=c, tc=tc, wsb=wsb, buf=buf: e.matmul(
                                ps_qk[:], wsb[buf][:, c * 128:(c + 1) * 128],
                                xn[c][:, tc * 512:(tc + 1) * 512],
                                start=(c == 0), stop=(c == ND - 1)),
                                waits=(gw[buf] + lds +
                                       ([unit_guard[0]] if unit_guard[0] else []))
                                      if c == 0 else [],
                                inc="mm")
                        cp = ev.op("vector", lambda e, dst=dst, tc=tc, buf=buf: e.tensor_copy(
                            dst[buf][:, tc * 512:(tc + 1) * 512], ps_qk[:]),
                            waits=[acc], inc="vx")
                        st["qk_done"].append(cp)
                        unit_guard[0] = cp
                    units.append(u)
            for t in range(16):
                def u(t=t, buf=buf, st=st):
                    mv = None
                    for c in range(ND):
                        mv = ev.op("tensor", lambda e, c=c, t=t, buf=buf: e.matmul(
                            ps_qk[:, 0:128], xn[c][:, t * 128:(t + 1) * 128],
                            wv[buf][:, c * 128:(c + 1) * 128],
                            start=(c == 0), stop=(c == ND - 1)),
                            waits=([unit_guard[0]] if unit_guard[0] else [])
                                  if c == 0 else [], inc="mm")
                    for hh in range(4):
                        cp = ev.op("vector", lambda e, t=t, hh=hh, buf=buf: e.tensor_copy(
                            vsb[buf][:, t * 132 + hh * 33: t * 132 + hh * 33 + 32],
                            ps_qk[:, hh * 32:(hh + 1) * 32]),
                            waits=[mv, vmemset[-1]], inc="vx")
                        st["v_done"].append(cp)
                    unit_guard[0] = st["v_done"][-1]
                units.append(u)
            return units, st

        gw[0] = load_group(0, 0)
        gw[1] = load_group(1, 1)
        units0, st0 = qkv_units(0, 0)
        for u in units0:
            u()
        states = {0: st0}
        wguard[0] = ev.mark("mm")
        for g in range(HG):
            buf = g % 2
            queue = []
            if g + 1 < HG:
                units_n, st_n = qkv_units(g + 1, (g + 1) % 2)
                states[g + 1] = st_n
                queue = list(units_n)
            if g + 2 < HG:
                nb = (g + 2) % 2
                gw[nb] = load_group(g + 2, nb,
                                    [wguard[nb]] if wguard[nb] else [])
            qk_done = states[g]["qk_done"]
            v_done = states[g]["v_done"]
            wg_set = False
            for hh in range(4):
                hbase = hh * 32
                tp = (hbase, 0) if hbase == 96 else None
                for qc in range(4):
                    q0 = qc * 512
                    it = g * 16 + hh * 4 + qc
                    po = it % 2
                    nfill = 2 if (hh * 4 + qc) < 8 else 1
                    for _ in range(nfill):
                        if queue:
                            queue.pop(0)()
                    if not queue and (g + 1) in states and not wg_set:
                        wguard[(g + 1) % 2] = ev.mark("mm")
                        wg_set = True
                    exp_done = []
                    av_marks = []
                    for kt in range(16):
                        pr = kt // 2
                        pb = pr % 2
                        half = kt % 2
                        w0 = []
                        if half == 0:
                            w0 += [exp_done[pr - 2]] if pr >= 2 else                                 (prev_exp[pr - 2 + 8:pr - 1 + 8] if prev_exp else [])
                        if kt == 0 and qc == 0 and hh == 0:
                            w0 += qk_done
                        if hh == 0 and qc == 0 and kt < 4:
                            w0 += v_done[-1:]
                        m = ev.op("tensor", lambda e, kt=kt, pb=pb, half=half, hbase=hbase, q0=q0, tp=tp, buf=buf: e.matmul(
                            ps_s[pb][:, half * 512:(half + 1) * 512],
                            kT[buf][hbase:hbase + 32, kt * 128:(kt + 1) * 128],
                            qT[buf][hbase:hbase + 32, q0:q0 + 512],
                            start=True, stop=True, tile_position=tp),
                            waits=w0, inc="mm")
                        if half == 1:
                            e1 = ev.op("scalar", lambda e, pr=pr, pb=pb: e.activation(
                                at[:, pr * 1024:(pr + 1) * 1024], ps_s[pb][:], AF.Exp,
                                bias=m8v[:], scale=float(32 ** -0.5)),
                                waits=[m, ep0] +
                                      (prev_av[2 * pr + 1:2 * pr + 2] if prev_av else []),
                                inc="sc")
                            exp_done.append(e1)
                    if pending_tail is not None:
                        tpo, tg, thb, tq0, tr1 = pending_tail
                        mb = ev.op("tensor", lambda e, tpo=tpo: e.matmul(
                            ps_b[:], onesb[0:1, 0:32], rec[tpo][0:1, :],
                            start=True, stop=True), waits=[tr1], inc="mm")
                        c1 = ev.op("vector", lambda e, tpo=tpo: e.tensor_copy(
                            nbc[tpo][:], ps_b[:]), waits=[mb], inc="vx")
                        tt_done[tpo] = ev.op("vector", lambda e, tg=tg, thb=thb, tq0=tq0, tpo=tpo: e.tensor_tensor(
                            oall[tg][thb:thb + 32, tq0:tq0 + 512], ps_o[tpo][0:32, :],
                            nbc[tpo][:], ALU.mult), waits=[c1], inc="vx")
                        pending_tail = None
                    for kt in range(16):
                        mavl = ev.op("tensor", lambda e, kt=kt, hh=hh, po=po, buf=buf: e.matmul(
                            ps_o[po][:], vsb[buf][:, kt * 132 + hh * 33: kt * 132 + hh * 33 + 33],
                            at[:, kt * 512:(kt + 1) * 512],
                            start=(kt == 0), stop=(kt == 15)),
                            waits=[exp_done[kt // 2]] +
                                  ((v_done if (hh == 0 and qc == 0) else []) +
                                   ([tt_done[po]] if tt_done[po] else []) if kt == 0 else []),
                            inc="mm")
                        av_marks.append(mavl)
                    r1 = ev.op("vector", lambda e, po=po: e.reciprocal(
                        rec[po][:], ps_o[po][32:33, :]), waits=[mavl], inc="vx")
                    pending_tail = (po, g, hbase, q0, r1)
                    prev_exp = exp_done
                    prev_av = av_marks
        if pending_tail is not None:
            tpo, tg, thb, tq0, tr1 = pending_tail
            mb = ev.op("tensor", lambda e, tpo=tpo: e.matmul(
                ps_b[:], onesb[0:1, 0:32], rec[tpo][0:1, :],
                start=True, stop=True), waits=[tr1], inc="mm")
            c1 = ev.op("vector", lambda e, tpo=tpo: e.tensor_copy(
                nbc[tpo][:], ps_b[:]), waits=[mb], inc="vx")
            tt_done[tpo] = ev.op("vector", lambda e, tg=tg, thb=thb, tq0=tq0, tpo=tpo: e.tensor_tensor(
                oall[tg][thb:thb + 32, tq0:tq0 + 512], ps_o[tpo][0:32, :],
                nbc[tpo][:], ALU.mult), waits=[c1], inc="vx")
            pending_tail = None
        oall_done = ev.wait_all("vx")
        wo_ld = ev.wait_all("wd", "wd2")
        prev_store = None
        for t in range(NT):
            sready = []
            for n in range(2):
                acc = None
                for c in range(ND):
                    acc = ev.op("tensor", lambda e, c=c, t=t, n=n: e.matmul(
                        ps_w[n][:, 0:512], oall[c][:, t * 128:(t + 1) * 128],
                        wo[:, c * 1024 + n * 512: c * 1024 + (n + 1) * 512],
                        start=(c == 0), stop=(c == ND - 1)),
                        waits=(oall_done + wo_ld) if (t == 0 and c == 0) else
                              (ev.wait_all("vx") if c == 0 else []),
                        inc="mm")
                sready.append(ev.op("vector", lambda e, n=n: e.tensor_tensor(
                    stile[:, n * 512:(n + 1) * 512], ps_w[n][:, 0:512],
                    bbc[:, n * 512:(n + 1) * 512], ALU.add),
                    waits=[acc], inc="vx"))
            sc1 = ev.op("scalar", lambda e: e.square(sq[:], stile[:]),
                        waits=sready, inc="sc")
            v1 = ev.op("vector", lambda e: e.reduce_sum(st1[:], stile[:], AX.X),
                       waits=sready, inc="vx")
            v2 = ev.op("vector", lambda e: e.reduce_sum(st2[:], sq[:], AX.X),
                       waits=[sc1], inc="vx")
            ev.op("vector", lambda e: e.tensor_scalar_mul(mu[:], st1[:], 1.0 / D),
                  waits=[v1], inc="vx")
            sc2 = ev.op("scalar", lambda e: e.square(mu2[:], mu[:]),
                        waits=ev.wait_all("vx"), inc="sc")
            v3 = ev.op("vector", lambda e: e.scalar_tensor_tensor(
                var[:], st2[:], 1.0 / D, mu2[:], ALU.mult, ALU.subtract),
                waits=[sc2, v2], inc="vx")
            sc3 = ev.op("scalar", lambda e: e.activation(
                rt[:], var[:], AF.Sqrt, bias=epsv[:]), waits=[v3, ep1], inc="sc")
            v4 = ev.op("vector", lambda e: e.reciprocal(rr[:], rt[:]),
                       waits=[sc3], inc="vx")
            v5 = ev.op("vector", lambda e: e.tensor_scalar(
                sq[:], stile[:], mu[:], rr[:], ALU.subtract, ALU.mult),
                waits=[v4], inc="vx")
            v6 = ev.op("vector", lambda e: e.tensor_tensor(
                sq[:], sq[:], gbc[:], ALU.mult), waits=[v5], inc="vx")
            v7 = ev.op("vector", lambda e: e.tensor_tensor(
                otile[:], sq[:], bebc[:], ALU.add),
                waits=[v6] + ([prev_store] if prev_store else []), inc="vx")
            ev.dma("sync", s_out[t * 128:(t + 1) * 128, :], otile[:],
                   waits=[v7], inc="st")
            prev_store = ev.mark("st")


def emit_ffn_attn(nc, ev, ctx, io, pfx=""):
    xfT = io["xfT"]
    wqkv = io["wqkv"]
    qs_i = io["qs_i"]
    ks_i = io["ks_i"]
    wo_i = io["wo_i"]
    onesr = io["onesr"]
    ones2 = io["ones2"]
    ones2T = io["ones2T"]
    maskM = io["maskM"]
    ropes = io["ropes"]
    xfT_o = io["xfT_o"]
    if 1:
        sb = lambda *a: ctx.enter_context(nc.sbuf_tensor(pfx + a[0], *a[1:]))
        psa = lambda *a: ctx.enter_context(nc.psum_tensor(pfx + a[0], *a[1:]))
        xf = [sb(f"xf{c}", [128, NTOK], F32) for c in range(ND)]
        hb = [sb(f"hb{c}", [128, NTOK], BF16) for c in range(ND)]
        tabs = {n: sb(n + "_s", [128, NTOK], BF16) for n in ropes}
        mask = sb("mask", [128, 8 * 512], BF16)
        wq = [sb(f"wq{i}", [128, 1024], BF16) for i in range(2)]
        wk = [sb(f"wk{i}", [128, 1024], BF16) for i in range(2)]
        wv = [sb(f"wv{i}", [128, 1024], BF16) for i in range(2)]
        qf2 = [sb(f"qf{i}", [128, NTOK], F32) for i in range(2)]
        qTb = sb("qTb", [128, NTOK], BF16)
        kTb = sb("kTb", [128, NTOK], BF16)
        vsb = sb("vsb", [128, 16 * 130], BF16)
        at = sb("at", [128, 8 * 512], BF16)
        scr1 = sb("scr1", [128, NTOK], F32)
        ssum = sb("ssum", [2, 512], F32)
        qsv = sb("qsv", [128, 1], F32)
        ksv = sb("ksv", [128, 1], F32)
        onesb = sb("onesb", [1, 128], F32)
        o2 = sb("o2", [128, 2], F32)
        o2T = sb("o2T", [2, 128], F32)
        rec = [sb(f"rec{i}", [1, 512], F32) for i in range(2)]
        nbc = [sb(f"nbc{i}", [64, 512], F32) for i in range(2)]
        oall = [sb(f"oall{c}", [128, NTOK], BF16) for c in range(4)]

        ps_s = [psa(f"ps_s{i}", [128, 512], F32) for i in range(2)]
        ps_o = [psa(f"ps_o{i}", [65, 512], F32) for i in range(2)]
        ps_b = psa("ps_b", [128, 512], F32)
        ps_st = [psa("ps_st0", [2, 512], F32)] * 2
        ps_w = [psa(f"ps_w{i}", [128, 512], F32) for i in range(2)]
        m30v = sb("m30v", [128, 1], F32)
        eps24 = sb("eps24", [128, 1], F32)

        ep0 = ev.op("vector", lambda e: e.memset(m30v[:], -30.0), inc="vx")
        ep1 = ev.op("vector", lambda e: e.memset(eps24[:], 1e-24), inc="vx")
        lds = []
        for c in range(ND):
            lds.append(ev.dma("sync", xf[c][:], xfT[c * 128:(c + 1) * 128, :], inc="ld"))
        tab_lds = []
        for n in ropes:
            tab_lds.append(ev.dma("gpsimd", tabs[n][:], ropes[n], inc="wd"))
        for kt in range(8):
            tab_lds.append(ev.dma("gpsimd", mask[:, kt * 512:(kt + 1) * 512],
                                  maskM[kt * 128:(kt + 1) * 128, :], inc="wd"))
        ev.dma("sync", onesb[:], onesr, inc="ld")
        ev.dma("sync", o2[:], ones2, inc="ld")
        ev.dma("sync", o2T[:], ones2T, inc="ld")
        ev.dma("sync", qsv[0:64, 0:1], qs_i.rearrange("(d o) -> d o", o=1), inc="ld")
        ev.dma("sync", qsv[64:128, 0:1], qs_i.rearrange("(d o) -> d o", o=1), inc="ld")
        ev.dma("sync", ksv[0:64, 0:1], ks_i.rearrange("(d o) -> d o", o=1), inc="ld")
        ev.dma("sync", ksv[64:128, 0:1], ks_i.rearrange("(d o) -> d o", o=1), inc="ld")
        lds_all = ev.wait_all("ld", "ld2")
        hdone = []
        for c in range(ND):
            if c < 4:
                hdone.append(ev.op("vector", lambda e, c=c: e.tensor_copy(
                    hb[c][:], xf[c][:]), waits=[lds[c]], inc="vx"))
            else:
                ev.op("vector", lambda e, c=c: e.memset(hb[c][:, 0:1], 0.0), inc="vx")
                hdone.append(ev.op("vector", lambda e, c=c: e.tensor_copy(
                    hb[c][:, 1:NTOK], xf[c][:, 0:NTOK - 1]), waits=[lds[c]], inc="vx"))

        def load_group(g, buf, waits=()):
            o = []
            for (dst, c0) in ((wq, g * 128), (wk, 512 + g * 128), (wv, 1024 + g * 128)):
                o.append(ev.dma(
                    "gpsimd", dst[buf][:].rearrange("p (k n) -> p k n", n=128),
                    wqkv[:, c0:c0 + 128].rearrange("(k p) n -> p k n", p=128),
                    waits=waits, inc="wd" if buf == 0 else "wd2"))
            return o

        gw = [None, None]
        gw[0] = load_group(0, 0)
        wguard = [None, None]
        qf_guard = [None, None]
        psw_uses = []
        pend_f = None
        tt_f = [None, None]
        v1_bank = {0: None, 1: None}
        vset = []
        for t in range(16):
            for hh in range(2):
                vset.append(ev.op("vector", lambda e, t=t, hh=hh: e.memset(
                    vsb[:, t * 130 + hh * 65 + 64: t * 130 + hh * 65 + 65], 1.0),
                    inc="vx"))
        wo_acc_done = None
        for g in range(4):
            buf = g % 2
            if g + 1 < 4:
                nb = (g + 1) % 2
                gw[nb] = load_group(g + 1, nb,
                                    [wguard[nb]] if wguard[nb] else [])
            wds = gw[buf]
            raws = {}
            for si, wsb in ((0, wq), (1, wk)):
                raw_done = []
                for tc in range(4):
                    acc = None
                    for c in range(ND):
                        acc = ev.op("tensor", lambda e, c=c, tc=tc, wsb=wsb, buf=buf: e.matmul(
                            ps_w[tc % 2][:], wsb[buf][:, c * 128:(c + 1) * 128],
                            hb[c][:, tc * 512:(tc + 1) * 512],
                            start=(c == 0), stop=(c == ND - 1)),
                            waits=(wds + hdone +
                                   ([qf_guard[si]] if qf_guard[si] else []) +
                                   psw_uses[-2:-1]) if c == 0 else [],
                            inc="mm")
                    cp = ev.op("vector", lambda e, tc=tc, si=si: e.tensor_copy(
                        qf2[si][:, tc * 512:(tc + 1) * 512], ps_w[tc % 2][:]),
                        waits=[acc], inc="vx")
                    raw_done.append(cp)
                    psw_uses.append(cp)
                raws[si] = raw_done
            v_done = []
            for t in range(16):
                mv = None
                for c in range(ND):
                    mv = ev.op("tensor", lambda e, c=c, t=t, buf=buf: e.matmul(
                        ps_w[t % 2][:, 0:128], hb[c][:, t * 128:(t + 1) * 128],
                        wv[buf][:, c * 128:(c + 1) * 128],
                        start=(c == 0), stop=(c == ND - 1)),
                        waits=psw_uses[-2:-1] if c == 0 else [], inc="mm")
                for hh in range(2):
                    cp = ev.op("vector", lambda e, t=t, hh=hh: e.tensor_copy(
                        vsb[:, t * 130 + hh * 65: t * 130 + hh * 65 + 64],
                        ps_w[t % 2][:, hh * 64:(hh + 1) * 64]),
                        waits=[mv] + ([vset[-1]] if g == 0 else []), inc="vx")
                    v_done.append(cp)
                psw_uses.append(v_done[-1])
            wguard[buf] = ev.mark("mm")
            chain_last = []
            for si, (dstb, scl, csn, ssn) in (
                    (0, (qTb, qsv, "cs_q", "ss_q")), (1, (kTb, ksv, "cs_k", "ss_k"))):
                s1 = ev.op("scalar", lambda e, si=si: e.square(scr1[:], qf2[si][:]),
                           waits=raws[si] + chain_last, inc="sc")
                stts = []
                m1s = []
                m2prev = []
                for tc in range(4):
                    m1 = ev.op("tensor", lambda e, tc=tc: e.matmul(
                        ps_st[tc % 2][:], o2[:], scr1[:, tc * 512:(tc + 1) * 512],
                        start=True, stop=True),
                        waits=[s1] + lds_all + m2prev, inc="mm")
                    m1s.append(m1)
                    sA = ev.op("scalar", lambda e, tc=tc: e.activation(
                        ssum[:], ps_st[tc % 2][:], AF.Sqrt,
                        bias=eps24[0:2, 0:1]), waits=[m1, ep1] + m2prev, inc="sc")
                    vR = ev.op("vector", lambda e: e.reciprocal(ssum[:], ssum[:]),
                               waits=[sA], inc="vx")
                    m2 = ev.op("tensor", lambda e: e.matmul(
                        ps_b[:], o2T[:], ssum[:], start=True, stop=True),
                        waits=[vR] + stts[-1:], inc="mm")
                    m2prev = [m2]
                    stts.append(ev.op("vector", lambda e, tc=tc, scl=scl, si=si: e.scalar_tensor_tensor(
                        qf2[si][:, tc * 512:(tc + 1) * 512], qf2[si][:, tc * 512:(tc + 1) * 512],
                        scl[:], ps_b[:], ALU.mult, ALU.mult), waits=[m2], inc="vx"))
                r1 = ev.op("vector", lambda e, si=si: e.tensor_scalar_mul(
                    scr1[0:32, :], qf2[si][32:64, :], -1.0), waits=stts, inc="vx")
                r2 = ev.op("vector", lambda e, si=si: e.tensor_copy(
                    scr1[32:64, :], qf2[si][0:32, :]), waits=stts, inc="vx")
                r3 = ev.op("vector", lambda e, si=si: e.tensor_scalar_mul(
                    scr1[64:96, :], qf2[si][96:128, :], -1.0), waits=stts, inc="vx")
                r4 = ev.op("vector", lambda e, si=si: e.tensor_copy(
                    scr1[96:128, :], qf2[si][64:96, :]), waits=stts, inc="vx")
                v4 = ev.op("vector", lambda e, ssn=ssn: e.tensor_tensor(
                    scr1[:], scr1[:], tabs[ssn][:], ALU.mult),
                    waits=[r1, r2, r3, r4] + tab_lds, inc="vx")
                v5 = ev.op("vector", lambda e, csn=csn, si=si: e.tensor_tensor(
                    qf2[si][:], qf2[si][:], tabs[csn][:], ALU.mult), waits=[v4], inc="vx")
                chain_last = [ev.op("vector", lambda e, dstb=dstb, si=si: e.tensor_tensor(
                    dstb[:], qf2[si][:], scr1[:], ALU.add), waits=[v4, v5], inc="vx")]
                qf_guard[si] = chain_last[0]
            qk_done = ev.wait_all("vx")
            def flush_tail_f():
                nonlocal pend_f
                if pend_f is None:
                    return
                tpo, tg, thb, tq0, tr1 = pend_f
                mb = ev.op("tensor", lambda e, tpo=tpo: e.matmul(
                    ps_b[0:64, :], onesb[0:1, 0:64], rec[tpo][0:1, :],
                    start=True, stop=True), waits=[tr1], inc="mm")
                c1 = ev.op("vector", lambda e, tpo=tpo: e.tensor_copy(
                    nbc[tpo][:], ps_b[0:64, :]), waits=[mb], inc="vx")
                tt_f[tpo] = ev.op("vector", lambda e, tg=tg, thb=thb, tq0=tq0, tpo=tpo: e.tensor_tensor(
                    oall[tg][thb:thb + 64, tq0:tq0 + 512], ps_o[tpo][0:64, :],
                    nbc[tpo][:], ALU.mult), waits=[c1], inc="vx")
                pend_f = None

            for hh in range(2):
                hb64 = hh * 64
                for w in range(4):
                    q0 = w * 512
                    it = hh * 4 + w
                    po = it % 2
                    kt_lo = 4 if w == 0 else 0
                    exp_done = {}
                    for kt in range(kt_lo, 8):
                        ktok0 = (w - 1) * 512 + kt * 128
                        pb = kt % 2
                        warw = ([v1_bank[pb]] if v1_bank[pb] else []) if kt < kt_lo + 2 \
                            else ([exp_done[kt - 2]] if kt - 2 in exp_done else [])
                        m = ev.op("tensor", lambda e, ktok0=ktok0, pb=pb, hb64=hb64, q0=q0: e.matmul(
                            ps_s[pb][:], kTb[hb64:hb64 + 64, ktok0:ktok0 + 128],
                            qTb[hb64:hb64 + 64, q0:q0 + 512],
                            start=True, stop=True),
                            waits=(qk_done if (kt == kt_lo and w == 0) else []) + warw,
                            inc="mm")
                        v1 = ev.op("vector", lambda e, kt=kt, pb=pb: e.scalar_tensor_tensor(
                            scr1[:, (kt % 2) * 512:(kt % 2 + 1) * 512], ps_s[pb][:], 8.0,
                            mask[:, kt * 512:(kt + 1) * 512],
                            ALU.mult, ALU.min), waits=[m] + tab_lds, inc="vx")
                        v1_bank[pb] = v1
                        e1 = ev.op("scalar", lambda e, kt=kt: e.activation(
                            at[:, kt * 512:(kt + 1) * 512],
                            scr1[:, (kt % 2) * 512:(kt % 2 + 1) * 512], AF.Exp,
                            bias=m30v[:]), waits=[v1, ep0], inc="sc")
                        exp_done[kt] = e1
                    flush_tail_f()
                    mavl = None
                    first = True
                    for kt in range(kt_lo, 8):
                        ktok0 = (w - 1) * 512 + kt * 128
                        vt = ktok0 // 128
                        mavl = ev.op("tensor", lambda e, kt=kt, vt=vt, hh=hh, first=first, po=po: e.matmul(
                            ps_o[po][:], vsb[:, vt * 130 + hh * 65: vt * 130 + hh * 65 + 65],
                            at[:, kt * 512:(kt + 1) * 512],
                            start=first, stop=(kt == 7)),
                            waits=[exp_done[kt]] +
                                  (((v_done if (hh == 0 and w == 0) else []) +
                                    ([tt_f[po]] if tt_f[po] else [])) if first else []),
                            inc="mm")
                        first = False
                    r1 = ev.op("vector", lambda e, po=po: e.reciprocal(
                        rec[po][:], ps_o[po][64:65, :]), waits=[mavl], inc="vx")
                    pend_f = (po, g, hb64, q0, r1)
        flush_tail_f()
        wo_sb = at
        wo_ld = [ev.dma("gpsimd", wo_sb[:].rearrange("p (k n) -> p k n", n=1024),
                        wo_i.rearrange("(k p) n -> p k n", p=128),
                        waits=ev.wait_all("mm", "sc"), inc="wd")]
        oall_done = ev.wait_all("vx")
        res_done = []
        for dc in range(ND):
            for tc in range(4):
                acc = None
                for c in range(4):
                    acc = ev.op("tensor", lambda e, c=c, dc=dc, tc=tc: e.matmul(
                        ps_w[tc % 2][:], wo_sb[:, c * 1024 + dc * 128: c * 1024 + dc * 128 + 128],
                        oall[c][:, tc * 512:(tc + 1) * 512],
                        start=(c == 0), stop=(c == 3)),
                        waits=((oall_done + wo_ld) if (dc == 0 and tc == 0)
                               else ev.wait_all("vx")) if c == 0 else [],
                        inc="mm")
                res_done.append(ev.op("vector", lambda e, dc=dc, tc=tc: e.tensor_tensor(
                    xf[dc][:, tc * 512:(tc + 1) * 512], ps_w[tc % 2][:],
                    xf[dc][:, tc * 512:(tc + 1) * 512], ALU.add),
                    waits=[acc], inc="vx"))
        for c in range(ND):
            ev.dma("sync", xfT_o[c * 128:(c + 1) * 128, :], xf[c][:],
                   waits=res_done, inc="st")



def emit_ffn_ffn(nc, ev, ctx, io, out_normal, pfx=""):
    NH = 22
    hsz = [128] * 21 + [42]
    xfT = io["xfT"]
    lg_i = io["lg_i"]
    lb_i = io["lb_i"]
    w1_i = io["w1_i"]
    b1_i = io["b1_i"]
    w2_i = io["w2_i"]
    b2_i = io["b2_i"]
    onesr = io["onesr"]
    ident = io["ident"]
    outp = io["outp"]
    if 1:
        sb = lambda *a: ctx.enter_context(nc.sbuf_tensor(pfx + a[0], *a[1:]))
        psa = lambda *a: ctx.enter_context(nc.psum_tensor(pfx + a[0], *a[1:]))
        xf = [sb(f"xf{c}", [128, NTOK], F32) for c in range(ND)]
        xh = [sb(f"xh{c}", [128, NTOK], BF16) for c in range(ND)]
        patch = [sb(f"patch{i}", [128, 512], F32) for i in range(4)]
        w2sb = sb("w2sb", [128, NH * 1024], BF16)
        w1a = [sb(f"w1a{i}", [128, 1024], BF16) for i in range(2)]
        w1g = [sb(f"w1g{i}", [128, 1024], BF16) for i in range(2)]
        acts = sb("acts", [128, NH * 512], BF16)
        lgv = sb("lgv", [128, ND], F32)
        lbv = sb("lbv", [128, ND], F32)
        b2v = sb("b2v", [128, ND], F32)
        b1av = sb("b1av", [128, NH], F32)
        b1gv = sb("b1gv", [128, NH], F32)
        onesb = sb("onesb", [1, 128], F32)
        onesc = sb("onesc", [128, 1], F32)
        sqv = [sb(f"sqv{i}", [128, 512], F32) for i in range(2)]
        murow = sb("murow", [1, 512], F32)
        e2row = sb("e2row", [1, 512], F32)
        musq = sb("musq", [1, 512], F32)
        varrow = sb("varrow", [1, 512], F32)
        rrow = sb("rrow", [1, 512], F32)
        mubc = sb("mubc", [128, 512], F32)
        rbc = sb("rbc", [128, 512], F32)
        tta = sb("tta", [128, 512], F32)
        gbuf = [sb(f"gbuf{i}", [128, 512], BF16) for i in range(2)]
        idf = sb("idf", [128, 128], F32)
        gtile = sb("gtile", [128, D], F32)

        ps_S = psa("ps_S", [1, 512], F32)
        ps_Q = psa("ps_Q", [1, 512], F32)
        ps_bc = psa("ps_bc", [128, 512], F32)
        ps_a = [psa(f"ps_a{i}", [128, 512], F32) for i in range(2)]
        ps_g = [psa(f"ps_g{i}", [128, 512], F32) for i in range(2)]
        ps_out = psa("ps_out", [128, 512], F32)
        ps_t = ps_bc
        epsv = sb("epsv", [128, 1], F32)

        ep0 = ev.op("vector", lambda e: e.memset(epsv[:], 1e-5), inc="vx")
        ev.dma("sync", onesb[:], onesr, inc="ld")
        ev.dma("sync", idf[:], ident, inc="ld")
        lds = []
        for c in range(ND):
            lds.append(ev.dma("sync", xf[c][:], xfT[c * 128:(c + 1) * 128, :], inc="ld"))
        xf_lds = ev.wait_all("ld")
        for c in range(ND):
            ev.dma("sync", lgv[:, c:c + 1], lg_i.rearrange("(d o) -> d o", o=1)[c * 128:(c + 1) * 128], inc="ld2")
            ev.dma("sync", lbv[:, c:c + 1], lb_i.rearrange("(d o) -> d o", o=1)[c * 128:(c + 1) * 128], inc="ld2")
            ev.dma("sync", b2v[:, c:c + 1], b2_i.rearrange("(d o) -> d o", o=1)[c * 128:(c + 1) * 128], inc="ld2")
        for c in range(NH):
            sz = hsz[c]
            ev.dma("sync", b1av[0:sz, c:c + 1],
                   b1_i.rearrange("(d o) -> d o", o=1)[c * 128:c * 128 + sz],
                   inc="ld2" if c % 2 == 0 else "ld")
            ev.dma("sync", b1gv[0:sz, c:c + 1],
                   b1_i.rearrange("(d o) -> d o", o=1)[2730 + c * 128:2730 + c * 128 + sz],
                   inc="ld2" if c % 2 == 1 else "ld")
        w2ld = [ev.dma("gpsimd",
                       w2sb[:, 0:21 * 1024].rearrange("p (c n) -> p c n", n=1024),
                       w2_i[0:2688, :].rearrange("(c p) n -> p c n", p=128), inc="wd"),
                ev.dma("gpsimd", w2sb[0:42, 21 * 1024:22 * 1024],
                       w2_i[2688:2730, :], inc="wd")]
        mset = ev.op("vector", lambda e: e.memset(onesc[:], 1.0), inc="vx")
        lds_all = ev.wait_all("ld", "ld2")
        pdone = []
        for c in range(4):
            ev.op("vector", lambda e, c=c: e.memset(patch[c][:, 0:1], 0.0), inc="vx")
            pdone.append(ev.op("vector", lambda e, c=c: e.tensor_copy(
                patch[c][:, 1:512], xf[c + 4][:, 0:511]), waits=[lds[c + 4]], inc="vx"))

        def h2slice(c, t0, wdt=512):
            if c < 4:
                return xf[c][:, t0:t0 + wdt]
            if t0 == 0:
                return patch[c - 4][:, 0:wdt]
            return xf[c][:, t0 - 1:t0 - 1 + wdt]

        xh_tc_done = {}

        def emit_ln_tc(tc):
            t0 = tc * 512
            mS = None
            mQ = None
            sq_done = {}
            for c in range(ND):
                mS = ev.op("tensor", lambda e, c=c, t0=t0: e.matmul(
                    ps_S[:], onesc[:], h2slice(c, t0), start=(c == 0), stop=(c == ND - 1)),
                    waits=(xf_lds + pdone + [mset] + ev.wait_all("vx")) if c == 0 else [],
                    inc="mm")
            for c in range(ND):
                sq1 = ev.op("scalar", lambda e, c=c, t0=t0: e.square(
                    sqv[c % 2][:], h2slice(c, t0)),
                    waits=(sq_done[c - 2] if c >= 2 else xf_lds + pdone),
                    inc="sc")
                mQ = ev.op("tensor", lambda e, c=c: e.matmul(
                    ps_Q[:], onesc[:], sqv[c % 2][:], start=(c == 0), stop=(c == ND - 1)),
                    waits=[sq1], inc="mm")
                sq_done[c] = [mQ]
            v1 = ev.op("vector", lambda e: e.tensor_scalar_mul(murow[:], ps_S[:], 1.0 / D),
                       waits=[mS], inc="vx")
            v2 = ev.op("vector", lambda e: e.tensor_scalar_mul(e2row[:], ps_Q[:], 1.0 / D),
                       waits=[mQ], inc="vx")
            s2 = ev.op("scalar", lambda e: e.square(musq[:], murow[:]), waits=[v1], inc="sc")
            v3 = ev.op("vector", lambda e: e.tensor_sub(varrow[:], e2row[:], musq[:]),
                       waits=[s2, v2], inc="vx")
            s3 = ev.op("scalar", lambda e: e.activation(rrow[:], varrow[:], AF.Sqrt,
                                                        bias=epsv[0:1, 0:1]),
                       waits=[v3, ep0], inc="sc")
            v4 = ev.op("vector", lambda e: e.reciprocal(rrow[:], rrow[:]), waits=[s3], inc="vx")
            mb1 = ev.op("tensor", lambda e: e.matmul(
                ps_bc[:], onesb[0:1, :], murow[0:1, :], start=True, stop=True),
                waits=[v1] + ev.wait_all("vx"), inc="mm")
            vb1 = ev.op("vector", lambda e: e.tensor_copy(mubc[:], ps_bc[:]),
                        waits=[mb1], inc="vx")
            mb2 = ev.op("tensor", lambda e: e.matmul(
                ps_bc[:], onesb[0:1, :], rrow[0:1, :], start=True, stop=True),
                waits=[v4, vb1], inc="mm")
            vb2 = ev.op("vector", lambda e: e.tensor_copy(rbc[:], ps_bc[:]),
                        waits=[mb2], inc="vx")
            for c in range(ND):
                va = ev.op("vector", lambda e, c=c, t0=t0: e.tensor_sub(
                    tta[:], h2slice(c, t0), mubc[:]), waits=[vb1, vb2], inc="vx")
                vb = ev.op("vector", lambda e: e.tensor_mul(tta[:], tta[:], rbc[:]),
                           waits=[va], inc="vx")
                ev.op("vector", lambda e, c=c, t0=t0: e.tensor_scalar(
                    xh[c][:, t0:t0 + 512], tta[:], lgv[:, c:c + 1], lbv[:, c:c + 1],
                    ALU.mult, ALU.add), waits=[vb] + ev.wait_all("ld2"), inc="vx")
            xh_tc_done[tc] = ev.wait_all("vx")

        emit_ln_tc(0)
        emit_ln_tc(1)

        def load_w1(c, buf, waits):
            sz = hsz[c]
            o = []
            for (dst, c0) in ((w1a, c * 128), (w1g, 2730 + c * 128)):
                o.append(ev.dma(
                    "gpsimd",
                    dst[buf][:].rearrange("p (k n) -> p k n", n=128)[:, :, 0:sz],
                    w1_i[:, c0:c0 + sz].rearrange("(k p) n -> p k n", p=128),
                    waits=waits, inc="wd" if buf == 0 else "wd2"))
            return o

        w1ld = {}
        w1ld[0] = load_w1(0, 0, [])
        w1ld[1] = load_w1(1, 1, [])
        for tc in range(4):
            t0 = tc * 512
            if tc + 2 < 4:
                pass
            act_done = {}
            mm_done = {}
            for c in range(NH):
                sz = hsz[c]
                buf = c % 2
                nxt = c + 2
                ma = mg = None
                for k in range(ND):
                    ma = ev.op("tensor", lambda e, k=k, c=c, t0=t0, buf=buf, sz=sz: e.matmul(
                        ps_a[buf][0:sz, :], w1a[buf][:, k * 128:k * 128 + sz],
                        xh[k][:, t0:t0 + 512], start=(k == 0), stop=(k == ND - 1)),
                        waits=(w1ld[c] + (xh_tc_done[tc] if c == 0 else []) +
                               (act_done.get(c - 2, []))) if k == 0 else [],
                        inc="mm")
                for k in range(ND):
                    mg = ev.op("tensor", lambda e, k=k, c=c, t0=t0, buf=buf, sz=sz: e.matmul(
                        ps_g[buf][0:sz, :], w1g[buf][:, k * 128:k * 128 + sz],
                        xh[k][:, t0:t0 + 512], start=(k == 0), stop=(k == ND - 1)),
                        waits=(act_done.get(c - 2, [])) if k == 0 else [],
                        inc="mm")
                mm_done[c] = [ma, mg]
                if tc < 3 or True:
                    nc_load = c + 2
                    if nc_load < NH:
                        w1ld[nc_load] = load_w1(nc_load, nc_load % 2, mm_done.get(c, []))
                    elif tc < 3 and nc_load - NH < 2:
                        w1ld[nc_load - NH] = load_w1(nc_load - NH, (nc_load - NH) % 2,
                                                     mm_done.get(c, []))
                g1 = ev.op("scalar", lambda e, c=c, sz=sz, buf=buf: e.activation(
                    gbuf[buf][0:sz, :], ps_g[buf][0:sz, :], AF.Gelu,
                    bias=b1gv[0:sz, c:c + 1]),
                    waits=[mg] + (lds_all if tc == 0 and c < 2 else []), inc="sc")
                a1 = ev.op("vector", lambda e, c=c, sz=sz, buf=buf: e.scalar_tensor_tensor(
                    acts[0:sz, c * 512:(c + 1) * 512], ps_a[buf][0:sz, :],
                    b1av[0:sz, c:c + 1], gbuf[buf][0:sz, :], ALU.add, ALU.mult),
                    waits=[g1], inc="vx")
                act_done[c] = [a1]
            all_act = ev.wait_all("vx")
            ev_last = []
            for dc in range(ND):
                mo = None
                for c in range(NH):
                    sz = hsz[c]
                    mo = ev.op("tensor", lambda e, c=c, dc=dc, sz=sz: e.matmul(
                        ps_out[:], w2sb[0:sz, c * 1024 + dc * 128: c * 1024 + dc * 128 + 128],
                        acts[0:sz, c * 512:(c + 1) * 512],
                        start=(c == 0), stop=(c == NH - 1)),
                        waits=(all_act + w2ld + (ev_last[-1:] if dc >= 1 else [])) if c == 0 else [],
                        inc="mm")
                ev_last.append(ev.op("vector", lambda e, dc=dc, t0=t0: e.scalar_tensor_tensor(
                    xf[dc][:, t0:t0 + 512], ps_out[:], b2v[:, dc:dc + 1],
                    xf[dc][:, t0:t0 + 512], ALU.add, ALU.add), waits=[mo], inc="vx"))
            if tc + 2 < 4:
                emit_ln_tc(tc + 2)
        res_done = ev.wait_all("vx")
        if not out_normal:
            for c in range(ND):
                ev.dma("sync", outp[c * 128:(c + 1) * 128, :], xf[c][:],
                       waits=res_done, inc="st")
        else:
            tr_prev = None
            for t in range(NT):
                gts = []
                for c in range(ND):
                    m = ev.op("tensor", lambda e, c=c, t=t: e.transpose(
                        ps_t[:, 0:128], xf[c][:, t * 128:(t + 1) * 128], idf[:]),
                        waits=(res_done if (t == 0 and c == 0) else []) +
                              ([tr_prev] if tr_prev else []), inc="mm")
                    tr_prev = ev.op("vector", lambda e, c=c: e.tensor_copy(
                        gtile[:, c * 128:(c + 1) * 128], ps_t[:, 0:128]), waits=[m], inc="vx")
                    gts.append(tr_prev)
                ev.dma("sync", outp[t * 128:(t + 1) * 128, :], gtile[:],
                       waits=gts, inc="st")
                tr_prev = ev.mark("st")


def emit_comb1(nc, ev, ctx, io, pfx=""):
    x_i = io["x_i"]
    s_i = io["s_i"]
    g_i = io["g_i"]
    ident = io["ident"]
    x1_o = io["x1_o"]
    xnTf_o = io["xnTf_o"]
    xnTb_o = io["xnTb_o"]
    xnN_o = io["xnN_o"]
    if 1:
        sb = lambda *a: ctx.enter_context(nc.sbuf_tensor(pfx + a[0], *a[1:]))
        psa = lambda *a: ctx.enter_context(nc.psum_tensor(pfx + a[0], *a[1:]))
        xt = [sb(f"xt{i}", [128, D], F32) for i in range(2)]
        st = [sb(f"st{i}", [128, D], F32) for i in range(2)]
        gt = [sb(f"gt{i}", [128, D], F32) for i in range(2)]
        gs = sb("gs", [128, D], F32)
        x1t = [sb(f"x1t{i}", [128, D], F32) for i in range(2)]
        xnt = [sb(f"xnt{i}", [128, D], F32) for i in range(2)]
        sq = sb("sq", [128, D], F32)
        s1 = sb("s1", [128, 1], F32)
        s2 = sb("s2", [128, 1], F32)
        mu = sb("mu", [128, 1], F32)
        mu2 = sb("mu2", [128, 1], F32)
        var = sb("var", [128, 1], F32)
        rt = sb("rt", [128, 1], F32)
        rr = sb("rr", [128, 1], F32)
        idf = sb("idf", [128, 128], F32)
        chunks = [sb(f"ch{c}", [128, NTOK], F32) for c in range(ND)]
        epsv = sb("epsv", [128, 1], F32)
        tp = [psa(f"tp{i}", [128, 128], F32) for i in range(2)]
        ev.dma("sync", idf[:], ident, inc="ld")
        ep = ev.op("vector", lambda e: e.memset(epsv[:], 1e-5), inc="vx")
        guard = [None, None]
        for i in range(NT):
            b = i % 2
            w = [] if guard[b] is None else list(guard[b])
            l1 = ev.dma("sync", xt[b][:], x_i[i * 128:(i + 1) * 128, :], waits=w, inc="ld")
            l2 = ev.dma("sync", st[b][:], s_i[i * 128:(i + 1) * 128, :], waits=w, inc="ld")
            l3 = ev.dma("gpsimd", gt[b][:], g_i[i * 128:(i + 1) * 128, :], waits=w, inc="ldg")
            sc0 = ev.op("scalar", lambda e, b=b: e.activation(
                gs[:], gt[b][:], AF.Silu), waits=[l3] + ev.wait_all("vx"), inc="sc")
            v0 = ev.op("vector", lambda e, b=b: e.tensor_mul(gs[:], st[b][:], gs[:]),
                       waits=[sc0, l2], inc="vx")
            v0b = ev.op("vector", lambda e, b=b: e.tensor_add(x1t[b][:], xt[b][:], gs[:]),
                        waits=[v0, l1], inc="vx")
            d1 = ev.dma("sync", x1_o[i * 128:(i + 1) * 128, :], x1t[b][:],
                        waits=[v0b], inc="st")
            sc1 = ev.op("scalar", lambda e, b=b: e.square(sq[:], x1t[b][:]),
                        waits=[v0b], inc="sc")
            v1 = ev.op("vector", lambda e, b=b: e.reduce_sum(s1[:], x1t[b][:], AX.X),
                       waits=[v0b], inc="vx")
            v2 = ev.op("vector", lambda e: e.reduce_sum(s2[:], sq[:], AX.X),
                       waits=[sc1], inc="vx")
            ev.op("vector", lambda e: e.tensor_scalar_mul(mu[:], s1[:], 1.0 / D),
                  waits=[v1], inc="vx")
            sc2 = ev.op("scalar", lambda e: e.square(mu2[:], mu[:]),
                        waits=ev.wait_all("vx"), inc="sc")
            v3 = ev.op("vector", lambda e: e.scalar_tensor_tensor(
                var[:], s2[:], 1.0 / D, mu2[:], ALU.mult, ALU.subtract),
                waits=[sc2, v2], inc="vx")
            sc3 = ev.op("scalar", lambda e: e.activation(
                rt[:], var[:], AF.Sqrt, bias=epsv[:]), waits=[v3, ep], inc="sc")
            v4 = ev.op("vector", lambda e: e.reciprocal(rr[:], rt[:]),
                       waits=[sc3], inc="vx")
            vn = ev.op("vector", lambda e, b=b: e.tensor_scalar(
                xnt[b][:], x1t[b][:], mu[:], rr[:], ALU.subtract, ALU.mult),
                waits=[v4], inc="vx")
            d2 = ev.dma("gpsimd", xnN_o[i * 128:(i + 1) * 128, :], xnt[b][:],
                        waits=[vn], inc="stg")
            cv = vn
            for c in range(ND):
                pb = c % 2
                m = ev.op("tensor", lambda e, b=b, c=c, pb=pb: e.transpose(
                    tp[pb][:], xnt[b][:, c * 128:(c + 1) * 128], idf[:]),
                    waits=[vn] + ev.wait_all("vx")[-1:], inc="mm")
                cv = ev.op("vector", lambda e, i=i, c=c, pb=pb: e.tensor_copy(
                    chunks[c][:, i * 128:(i + 1) * 128], tp[pb][:]),
                    waits=[m], inc="vx")
            guard[b] = [cv, d1, ev.mark("st"), ev.mark("stg")]
        w = ev.wait_all("vx")
        for c in range(ND):
            ev.dma("sync", xnTf_o[c * 128:(c + 1) * 128, :], chunks[c][:],
                   waits=w, inc="st")
            ev.dma("gpsimd", xnTb_o[c * 128:(c + 1) * 128, :], chunks[c][:],
                   waits=w, inc="stg")


def emit_l1attn(nc, ev, ctx, io, pfx=""):
    xnT_b = io["xnT_b"]
    xnN_b = io["xnN_b"]
    QK_i = io["QK_i"]
    VO_i = io["VO_i"]
    onesr = io["onesr"]
    s1_o = io["s1_o"]
    if 1:
        sb = lambda *a: ctx.enter_context(nc.sbuf_tensor(pfx + a[0], *a[1:]))
        psa = lambda *a: ctx.enter_context(nc.psum_tensor(pfx + a[0], *a[1:]))
        xn = [sb(f"xn{c}", [128, NTOK], BF16) for c in range(ND)]
        xnN = sb("xnN", [128, NT * D], BF16)
        qksb = sb("qksb", [128, ND * D], BF16)
        vosb = sb("vosb", [128, ND * D], BF16)
        xqk = [sb(f"xqk{c}", [128, NTOK], BF16) for c in range(ND)]
        at = sb("at", [128, 16 * 512], BF16)
        avsb = sb("avsb", [128, ND * 512], BF16)
        onesb = sb("onesb", [1, 128], F32)
        ones_bf = sb("ones_bf", [128, 1], BF16)
        rec = sb("rec", [1, 512], F32)
        bcs = sb("bcs", [128, 512], F32)
        stile = [sb(f"stile{i}", [128, D], F32) for i in range(2)]
        ps_s = [psa(f"ps_s{i}", [128, 512], F32) for i in range(2)]
        ps_den = psa("ps_den", [1, 512], F32)
        ps_bc = psa("ps_bc", [128, 512], F32)
        ps_av = psa("ps_av", [128, 512], F32)
        ps_vo = [psa(f"ps_vo{i}", [128, 512], F32) for i in range(2)]
        m8v = sb("m8v", [128, 1], F32)
        ep0 = ev.op("vector", lambda e: e.memset(m8v[:], -8.0), inc="vx")
        lds = []
        for c in range(ND):
            lds.append(ev.dma("sync", xn[c][:], xnT_b[c * 128:(c + 1) * 128, :], inc="ld"))
        for t in range(NT):
            lds.append(ev.dma("sync", xnN[:, t * D:(t + 1) * D],
                              xnN_b[t * 128:(t + 1) * 128, :], inc="ld"))
        ev.dma("sync", onesb[:], onesr, inc="ld")
        wd1 = ev.dma("gpsimd", qksb[:].rearrange("p (k n) -> p k n", n=D),
                     QK_i.rearrange("(k p) n -> p k n", p=128), inc="wd")
        wd2 = ev.dma("gpsimd", vosb[:].rearrange("p (k n) -> p k n", n=D),
                     VO_i.rearrange("(k p) n -> p k n", p=128), inc="wd")
        mset = ev.op("vector", lambda e: e.memset(ones_bf[:], 1.0), inc="vx")
        lds_all = ev.wait_all("ld", "ld2")
        st_guard = [None, None]
        guard = None
        for ecn in range(ND):
            for tcc in range(4):
                acc = None
                for c in range(ND):
                    acc = ev.op("tensor", lambda e, c=c, ecn=ecn, tcc=tcc: e.matmul(
                        ps_s[tcc % 2][:], qksb[:, c * D + ecn * 128: c * D + ecn * 128 + 128],
                        xn[c][:, tcc * 512:(tcc + 1) * 512],
                        start=(c == 0), stop=(c == ND - 1)),
                        waits=(lds_all + [wd1] + ([guard] if guard else [])) if c == 0 else [],
                        inc="mm")
                guard = ev.op("vector", lambda e, ecn=ecn, tcc=tcc: e.tensor_copy(
                    xqk[ecn][:, tcc * 512:(tcc + 1) * 512], ps_s[tcc % 2][:]),
                    waits=[acc], inc="vx")
        xqk_done = ev.wait_all("vx")
        for tc in range(4):
            t0 = tc * 512
            exp_done = {}
            for ut in range(16):
                pb = ut % 2
                warw = ev.wait_all("sc", "vx") if ut < 2 else [exp_done[ut - 2]]
                acc = None
                for c in range(ND):
                    acc = ev.op("tensor", lambda e, c=c, ut=ut, t0=t0, pb=pb: e.matmul(
                        ps_s[pb][:], xn[c][:, ut * 128:(ut + 1) * 128],
                        xqk[c][:, t0:t0 + 512],
                        start=(c == 0), stop=(c == ND - 1)),
                        waits=((xqk_done if (tc == 0 and ut == 0) else []) + warw) if c == 0 else [],
                        inc="mm")
                e1 = ev.op("scalar", lambda e, ut=ut, pb=pb: e.activation(
                    at[:, ut * 512:(ut + 1) * 512], ps_s[pb][:], AF.Exp,
                    bias=m8v[:], scale=1.0 / 32.0), waits=[acc, ep0], inc="sc")
                exp_done[ut] = e1
            mden = None
            for ut in range(16):
                mden = ev.op("tensor", lambda e, ut=ut: e.matmul(
                    ps_den[:], ones_bf[:], at[:, ut * 512:(ut + 1) * 512],
                    start=(ut == 0), stop=(ut == 15)),
                    waits=[exp_done[ut]] + ([mset] + ev.wait_all("vx") if ut == 0 else []),
                    inc="mm")
            r1 = ev.op("vector", lambda e: e.reciprocal(rec[:], ps_den[:]),
                       waits=[mden], inc="vx")
            mb = ev.op("tensor", lambda e: e.matmul(
                ps_bc[:], onesb[0:1, :], rec[0:1, :], start=True, stop=True),
                waits=[r1], inc="mm")
            cb = ev.op("vector", lambda e: e.tensor_copy(bcs[:], ps_bc[:]),
                       waits=[mb], inc="vx")
            for dc in range(ND):
                mav = None
                for ut in range(16):
                    mav = ev.op("tensor", lambda e, dc=dc, ut=ut: e.matmul(
                        ps_av[:], xnN[:, ut * D + dc * 128: ut * D + dc * 128 + 128],
                        at[:, ut * 512:(ut + 1) * 512],
                        start=(ut == 0), stop=(ut == 15)),
                        waits=([cb] + ev.wait_all("vx")) if ut == 0 else [],
                        inc="mm")
                ev.op("vector", lambda e, dc=dc: e.tensor_tensor(
                    avsb[:, dc * 512:(dc + 1) * 512], ps_av[:], bcs[:], ALU.mult),
                    waits=[mav], inc="vx")
            av_done = ev.wait_all("vx")
            for tt in range(4):
                sready = []
                for n in range(2):
                    acc = None
                    for dc in range(ND):
                        acc = ev.op("tensor", lambda e, dc=dc, tt=tt, n=n: e.matmul(
                            ps_vo[n][:], avsb[:, dc * 512 + tt * 128: dc * 512 + tt * 128 + 128],
                            vosb[:, dc * D + n * 512: dc * D + (n + 1) * 512],
                            start=(dc == 0), stop=(dc == ND - 1)),
                            waits=(av_done + [wd2] + ev.wait_all("vx")[-1:]) if dc == 0 else [],
                            inc="mm")
                    sready.append(ev.op("vector", lambda e, tt=tt, n=n: e.tensor_copy(
                        stile[tt % 2][:, n * 512:(n + 1) * 512], ps_vo[n][:]),
                        waits=[acc] + ([st_guard[tt % 2]] if st_guard[tt % 2] else []),
                        inc="vx"))
                ev.dma("sync", s1_o[t0 + tt * 128: t0 + (tt + 1) * 128, :],
                       stile[tt % 2][:], waits=sready, inc="st")
                st_guard[tt % 2] = ev.mark("st")


def emit_l1ffn(nc, ev, ctx, io, pfx=""):
    xnT_f = io["xnT_f"]
    cw = io["cw"]
    cb = io["cb"]
    gw = io["gw"]
    gb = io["gb"]
    ar_i = io["ar_i"]
    onesr = io["onesr"]
    ident = io["ident"]
    o1_o = io["o1_o"]
    if 1:
        sb = lambda *a: ctx.enter_context(nc.sbuf_tensor(pfx + a[0], *a[1:]))
        psa = lambda *a: ctx.enter_context(nc.psum_tensor(pfx + a[0], *a[1:]))
        out = [sb(f"out{c}", [128, NTOK], F32) for c in range(ND)]
        w1sb = sb("w1sb", [128, ND * D], BF16)
        w2sb = sb("w2sb", [128, ND * 512], BF16)
        w3sb = sb("w3sb", [128, 4 * 256], BF16)
        w4sb = sb("w4sb", [128, 2], BF16)
        b1v = sb("b1v", [128, ND], F32)
        b2v = sb("b2v", [128, 4], F32)
        b3v = sb("b3v", [128, 2], F32)
        b4v = sb("b4v", [1, 1], F32)
        g1sb = sb("g1sb", [128, ND * 512], F32)
        g2sb = sb("g2sb", [128, 4 * 256], F32)
        g3sb = sb("g3sb", [128, 2], F32)
        gb1v = sb("gb1v", [128, 4], F32)
        gb2v = sb("gb2v", [128, 2], F32)
        gb3v = sb("gb3v", [1, 1], F32)
        arv = sb("arv", [1, 1], F32)
        gcs = sb("gcs", [128, ND], F32)
        h1v = sb("h1v", [128, 4], F32)
        h2v = sb("h2v", [128, 2], F32)
        gfv = sb("gfv", [1, 1], F32)
        c0v = sb("c0v", [1, 1], F32)
        c1v = sb("c1v", [1, 1], F32)
        t1v = sb("t1v", [1, 1], F32)
        obf = sb("obf", [128, ND * NTOK], BF16)
        h1sb = sb("h1sb", [128, ND * 512], BF16)
        h2sb = sb("h2sb", [128, 4 * 512], BF16)
        h3sb = sb("h3sb", [128, 2 * 512], BF16)
        tfrow = sb("tfrow", [1, 512], F32)
        coup = sb("coup", [1, 512], F32)
        tcp = sb("tcp", [128, 512], F32)
        ttb = sb("ttb", [128, 512], F32)
        onesb = sb("onesb", [1, 128], F32)
        idf = sb("idf", [128, 128], F32)
        gtile = sb("gtile", [128, D], F32)
        ps_m = [psa(f"ps_m{i}", [128, 512], F32) for i in range(2)]
        ps_sm = psa("ps_sm", [128, 1], F32)
        ps_tf = psa("ps_tf", [1, 512], F32)
        ps_bc = psa("ps_bc", [128, 512], F32)
        ps_t = psa("ps_t", [128, 128], F32)
        lds = []
        for c in range(ND):
            lds.append(ev.dma("sync", out[c][:], xnT_f[c * 128:(c + 1) * 128, :], inc="ld"))
        ev.dma("sync", onesb[:], onesr, inc="ld")
        ev.dma("sync", idf[:], ident, inc="ld")
        ev.dma("sync", arv[:], ar_i.rearrange("(o d) -> o d", o=1), inc="ld")
        wds = []
        wds.append(ev.dma("gpsimd", w1sb[:].rearrange("p (k n) -> p k n", n=D),
                          cw["cn_w1"].rearrange("(k p) n -> p k n", p=128), inc="wd"))
        wds.append(ev.dma("gpsimd", w2sb[:].rearrange("p (k n) -> p k n", n=512),
                          cw["cn_w2"].rearrange("(k p) n -> p k n", p=128), inc="wd"))
        wds.append(ev.dma("gpsimd", w3sb[:].rearrange("p (k n) -> p k n", n=256),
                          cw["cn_w3"].rearrange("(k p) n -> p k n", p=128), inc="wd"))
        for c in range(2):
            wds.append(ev.dma("gpsimd", w4sb[:, c:c + 1],
                              cw["cn_w4"][c * 128:(c + 1) * 128, 0:1], inc="wd"))
        wds.append(ev.dma("sync", g1sb[:].rearrange("p (k n) -> p k n", n=512),
                          gw["gc_w1"].rearrange("(k p) n -> p k n", p=128), inc="ld"))
        wds.append(ev.dma("sync", g2sb[:].rearrange("p (k n) -> p k n", n=256),
                          gw["gc_w2"].rearrange("(k p) n -> p k n", p=128), inc="ld"))
        for c in range(2):
            wds.append(ev.dma("sync", g3sb[:, c:c + 1],
                              gw["gc_w3"][c * 128:(c + 1) * 128, 0:1], inc="ld"))
        for c in range(ND):
            ev.dma("sync", b1v[:, c:c + 1],
                   cb["cn_b1"].rearrange("(d o) -> d o", o=1)[c * 128:(c + 1) * 128], inc="ld")
        for c in range(4):
            ev.dma("sync", b2v[:, c:c + 1],
                   cb["cn_b2"].rearrange("(d o) -> d o", o=1)[c * 128:(c + 1) * 128], inc="ld")
            ev.dma("sync", gb1v[:, c:c + 1],
                   gb["gc_b1"].rearrange("(d o) -> d o", o=1)[c * 128:(c + 1) * 128], inc="ld")
        for c in range(2):
            ev.dma("sync", b3v[:, c:c + 1],
                   cb["cn_b3"].rearrange("(d o) -> d o", o=1)[c * 128:(c + 1) * 128], inc="ld")
            ev.dma("sync", gb2v[:, c:c + 1],
                   gb["gc_b2"].rearrange("(d o) -> d o", o=1)[c * 128:(c + 1) * 128], inc="ld")
        ev.dma("sync", b4v[:], cb["cn_b4"].rearrange("(o d) -> o d", o=1), inc="ld")
        ev.dma("sync", gb3v[:], gb["gc_b3"].rearrange("(o d) -> o d", o=1), inc="ld")
        lds_all = ev.wait_all("ld", "ld2", "wd", "wd2")
        def emit_gf():
            gm = []
            for c in range(ND):
                r = ev.op("vector", lambda e, c=c: e.reduce_sum(gcs[:, c:c + 1], out[c][:], AX.X),
                          waits=[lds[c]], inc="vx")
                gm.append(r)
            vsc = ev.op("vector", lambda e: e.tensor_scalar_mul(gcs[:], gcs[:], 1.0 / NTOK),
                        waits=gm, inc="vx")
            h1done = []
            for m in range(4):
                acc = None
                for c in range(ND):
                    acc = ev.op("tensor", lambda e, m=m, c=c: e.matmul(
                        ps_sm[:], g1sb[:, c * 512 + m * 128: c * 512 + m * 128 + 128],
                        gcs[:, c:c + 1], start=(c == 0), stop=(c == ND - 1)),
                        waits=(lds_all + [vsc] + h1done[-1:]) if c == 0 else [], inc="mm")
                h1done.append(ev.op("scalar", lambda e, m=m: e.activation(
                    h1v[:, m:m + 1], ps_sm[:], AF.Gelu, bias=gb1v[:, m:m + 1]),
                    waits=[acc], inc="sc"))
            h2done = []
            for m in range(2):
                acc = None
                for c in range(4):
                    acc = ev.op("tensor", lambda e, m=m, c=c: e.matmul(
                        ps_sm[:], g2sb[:, c * 256 + m * 128: c * 256 + m * 128 + 128],
                        h1v[:, c:c + 1], start=(c == 0), stop=(c == 3)),
                        waits=(h1done + h2done[-1:]) if c == 0 else [], inc="mm")
                h2done.append(ev.op("scalar", lambda e, m=m: e.activation(
                    h2v[:, m:m + 1], ps_sm[:], AF.Gelu, bias=gb2v[:, m:m + 1]),
                    waits=[acc], inc="sc"))
            acc = None
            for c in range(2):
                acc = ev.op("tensor", lambda e, c=c: e.matmul(
                    ps_sm[0:1, :], g3sb[0:128, c:c + 1], h2v[:, c:c + 1],
                    start=(c == 0), stop=(c == 1)),
                    waits=(h2done + h1done) if c == 0 else [], inc="mm")
            gfd = ev.op("scalar", lambda e: e.activation(
                gfv[:], ps_sm[0:1, 0:1], AF.Sigmoid, bias=gb3v[0:1, 0:1]),
                waits=[acc], inc="sc")
            v1 = ev.op("vector", lambda e: e.tensor_scalar_mul(c1v[:], arv[:], 0.6),
                       waits=[gfd] + lds_all, inc="vx")
            v2 = ev.op("vector", lambda e: e.tensor_mul(t1v[:], arv[:], gfv[:]),
                       waits=[gfd], inc="vx")
            v3 = ev.op("vector", lambda e: e.scalar_tensor_tensor(
                t1v[:], t1v[:], 1.4, arv[:], ALU.mult, ALU.subtract), waits=[v2], inc="vx")
            return ev.op("vector", lambda e: e.tensor_scalar_add(c0v[:], t1v[:], 0.1),
                         waits=[v3], inc="vx")

        cc = None
        h1_last_prev = []
        gf_emitted = False
        for cyc in range(3):
            cvt = []
            for c in range(ND):
                cvt.append(ev.op("vector", lambda e, c=c: e.tensor_copy(
                    obf[:, c * NTOK:(c + 1) * NTOK], out[c][:]),
                    waits=h1_last_prev + ([lds[c]] if cyc == 0 else []), inc="vx"))
            for tc in range(4):
                t0 = tc * 512
                h1d = []
                for m in range(ND):
                    acc = None
                    for c in range(ND):
                        acc = ev.op("tensor", lambda e, m=m, c=c, t0=t0: e.matmul(
                            ps_m[m % 2][:], w1sb[:, c * D + m * 128: c * D + m * 128 + 128],
                            obf[:, c * NTOK + t0: c * NTOK + t0 + 512],
                            start=(c == 0), stop=(c == ND - 1)),
                            waits=(cvt + lds_all + ev.wait_all("sc")[-1:]) if c == 0 else [],
                            inc="mm")
                    h1d.append(ev.op("scalar", lambda e, m=m: e.activation(
                        h1sb[:, m * 512:(m + 1) * 512], ps_m[m % 2][:], AF.Gelu,
                        bias=b1v[:, m:m + 1]), waits=[acc], inc="sc"))
                h1_last_prev = [acc]
                if not gf_emitted:
                    cc = emit_gf()
                    gf_emitted = True
                h2d = []
                for m in range(4):
                    acc = None
                    for c in range(ND):
                        acc = ev.op("tensor", lambda e, m=m, c=c: e.matmul(
                            ps_m[m % 2][:], w2sb[:, c * 512 + m * 128: c * 512 + m * 128 + 128],
                            h1sb[:, c * 512:(c + 1) * 512],
                            start=(c == 0), stop=(c == ND - 1)),
                            waits=(h1d + ev.wait_all("sc")[-1:]) if c == 0 else [],
                            inc="mm")
                    h2d.append(ev.op("scalar", lambda e, m=m: e.activation(
                        h2sb[:, m * 512:(m + 1) * 512], ps_m[m % 2][:], AF.Gelu,
                        bias=b2v[:, m:m + 1]), waits=[acc], inc="sc"))
                h3d = []
                for m in range(2):
                    acc = None
                    for c in range(4):
                        acc = ev.op("tensor", lambda e, m=m, c=c: e.matmul(
                            ps_m[m % 2][:], w3sb[:, c * 256 + m * 128: c * 256 + m * 128 + 128],
                            h2sb[:, c * 512:(c + 1) * 512],
                            start=(c == 0), stop=(c == 3)),
                            waits=(h2d + ev.wait_all("sc")[-1:]) if c == 0 else [],
                            inc="mm")
                    h3d.append(ev.op("scalar", lambda e, m=m: e.activation(
                        h3sb[:, m * 512:(m + 1) * 512], ps_m[m % 2][:], AF.Gelu,
                        bias=b3v[:, m:m + 1]), waits=[acc], inc="sc"))
                acc = None
                for c in range(2):
                    acc = ev.op("tensor", lambda e, c=c: e.matmul(
                        ps_tf[:], w4sb[0:128, c:c + 1], h3sb[:, c * 512:(c + 1) * 512],
                        start=(c == 0), stop=(c == 1)),
                        waits=h3d if c == 0 else [], inc="mm")
                tfd = ev.op("scalar", lambda e: e.activation(
                    tfrow[:], ps_tf[:], AF.Sigmoid, bias=b4v[0:1, 0:1]),
                    waits=[acc], inc="sc")
                cpd = ev.op("vector", lambda e: e.tensor_scalar(
                    coup[:], tfrow[:], c1v[0:1, 0:1], c0v[0:1, 0:1], ALU.mult, ALU.add),
                    waits=[tfd, cc], inc="vx")
                mb = ev.op("tensor", lambda e: e.matmul(
                    ps_bc[:], onesb[0:1, :], coup[0:1, :], start=True, stop=True),
                    waits=[cpd] + ev.wait_all("vx")[-1:], inc="mm")
                last = None
                for p in range(4):
                    sgn = 1.0 if p < 2 else -1.0
                    pw = [mb] + ([last] if last else [])
                    vc = ev.op("vector", lambda e, p=p, t0=t0: e.tensor_copy(
                        tcp[:], out[p][:, t0:t0 + 512]), waits=pw, inc="vx")
                    va = ev.op("vector", lambda e, p=p, t0=t0: e.tensor_tensor(
                        ttb[:], ps_bc[:], out[p + 4][:, t0:t0 + 512], ALU.mult),
                        waits=[vc], inc="vx")
                    if sgn > 0:
                        vb = ev.op("vector", lambda e, p=p, t0=t0: e.tensor_add(
                            out[p][:, t0:t0 + 512], out[p][:, t0:t0 + 512], ttb[:]),
                            waits=[va], inc="vx")
                    else:
                        vb = ev.op("vector", lambda e, p=p, t0=t0: e.tensor_sub(
                            out[p][:, t0:t0 + 512], out[p][:, t0:t0 + 512], ttb[:]),
                            waits=[va], inc="vx")
                    vd = ev.op("vector", lambda e, p=p, t0=t0: e.tensor_tensor(
                        ttb[:], ps_bc[:], tcp[:], ALU.mult), waits=[vb], inc="vx")
                    if sgn > 0:
                        last = ev.op("vector", lambda e, p=p, t0=t0: e.tensor_sub(
                            out[p + 4][:, t0:t0 + 512], out[p + 4][:, t0:t0 + 512], ttb[:]),
                            waits=[vd], inc="vx")
                    else:
                        last = ev.op("vector", lambda e, p=p, t0=t0: e.tensor_add(
                            out[p + 4][:, t0:t0 + 512], out[p + 4][:, t0:t0 + 512], ttb[:]),
                            waits=[vd], inc="vx")
        res_done = ev.wait_all("vx")
        tr_prev = None
        for t in range(NT):
            gts = []
            for c in range(ND):
                m = ev.op("tensor", lambda e, c=c, t=t: e.transpose(
                    ps_t[:], out[c][:, t * 128:(t + 1) * 128], idf[:]),
                    waits=(res_done + lds_all) if (t == 0 and c == 0) else
                          ([tr_prev] if tr_prev else []), inc="mm")
                tr_prev = ev.op("vector", lambda e, c=c: e.tensor_copy(
                    gtile[:, c * 128:(c + 1) * 128], ps_t[:]), waits=[m], inc="vx")
                gts.append(tr_prev)
            dm = ev.dma("sync", o1_o[t * 128:(t + 1) * 128, :], gtile[:],
                        waits=gts, inc="st")
            tr_prev = ("st", ev.cnt["st"])


def emit_comb2(nc, ev, ctx, io, pfx=""):
    x1_i = io["x1_i"]
    s1_i = io["s1_i"]
    o1_i = io["o1_i"]
    y_o = io["y_o"]
    if 1:
        sb = lambda *a: ctx.enter_context(nc.sbuf_tensor(pfx + a[0], *a[1:]))
        xt = [sb(f"xt{i}", [128, D], F32) for i in range(2)]
        st = [sb(f"st{i}", [128, D], F32) for i in range(2)]
        ot = [sb(f"ot{i}", [128, D], F32) for i in range(2)]
        gs = sb("gs", [128, D], F32)
        yt = [sb(f"yt{i}", [128, D], F32) for i in range(2)]
        guard = [None, None]
        for i in range(NT):
            b = i % 2
            w = [] if guard[b] is None else [guard[b]]
            l1 = ev.dma("sync", xt[b][:], x1_i[i * 128:(i + 1) * 128, :], waits=w, inc="ld")
            l2 = ev.dma("sync", st[b][:], s1_i[i * 128:(i + 1) * 128, :], waits=w, inc="ld")
            l3 = ev.dma("gpsimd", ot[b][:], o1_i[i * 128:(i + 1) * 128, :], waits=w, inc="ldg")
            sc0 = ev.op("scalar", lambda e, b=b: e.activation(
                gs[:], ot[b][:], AF.Silu), waits=[l3] + ev.wait_all("vx"), inc="sc")
            v0 = ev.op("vector", lambda e, b=b: e.tensor_mul(gs[:], st[b][:], gs[:]),
                       waits=[sc0, l2], inc="vx")
            v1 = ev.op("vector", lambda e, b=b: e.tensor_add(yt[b][:], xt[b][:], gs[:]),
                       waits=[v0, l1], inc="vx")
            guard[b] = ev.dma("sync", y_o[i * 128:(i + 1) * 128, :], yt[b][:],
                              waits=[v1], inc="st")


ALLSEMS = ["ld", "ld2", "ldg", "wd", "wd2", "mm", "sc", "vx", "st", "stg"]


def build_fused():
    nc = bass.Bass("TRN2", target_bir_lowering=False)

    def dt(name, shape, kindt=F32, kind="ExternalInput"):
        return nc.dram_tensor(name, shape, kindt, kind=kind).ap()

    x = dt("x", [NTOK, D])
    l0_w_qkv = dt("l0_w_qkv", [D, 3 * D])
    l0_w_out = dt("l0_w_out", [D, D])
    l0_b_out = dt("l0_b_out", [D])
    l0_gamma = dt("l0_gamma", [D])
    l0_beta = dt("l0_beta", [D])
    f_wqkv = [dt(f"f_wqkv{l}", [D, 1536]) for l in range(2)]
    f_qs = [dt(f"f_qs{l}", [64]) for l in range(2)]
    f_ks = [dt(f"f_ks{l}", [64]) for l in range(2)]
    f_wo = [dt(f"f_wo{l}", [512, D]) for l in range(2)]
    f_lg = [dt(f"f_lg{l}", [D]) for l in range(2)]
    f_lb = [dt(f"f_lb{l}", [D]) for l in range(2)]
    f_w1 = [dt(f"f_w1{l}", [D, 5460]) for l in range(2)]
    f_b1 = [dt(f"f_b1{l}", [5460]) for l in range(2)]
    f_w2 = [dt(f"f_w2{l}", [2730, D]) for l in range(2)]
    f_b2 = [dt(f"f_b2{l}", [D]) for l in range(2)]
    QK = dt("QK", [D, D])
    VO = dt("VO", [D, D])
    cn_w = {1: dt("cn_w1", [D, D]), 2: dt("cn_w2", [D, 512]),
            3: dt("cn_w3", [512, 256]), 4: dt("cn_w4", [256, 1])}
    cn_b = {1: dt("cn_b1", [D]), 2: dt("cn_b2", [512]),
            3: dt("cn_b3", [256]), 4: dt("cn_b4", [1])}
    gc_w = {1: dt("gc_w1", [D, 512]), 2: dt("gc_w2", [512, 256]),
            3: dt("gc_w3", [256, 1])}
    gc_b = {1: dt("gc_b1", [512]), 2: dt("gc_b2", [256]), 3: dt("gc_b3", [1])}
    ar = dt("ar", [1])
    ident = dt("c_ident", [128, 128])
    onesr = dt("c_ones_row", [1, 128])
    ones2 = dt("c_ones2", [128, 2])
    ones2T = dt("c_ones2T", [2, 128])
    maskM = dt("c_maskM", [1024, 512])
    ropes = {n: dt("c_" + n, [128, NTOK]) for n in ("cs_q", "ss_q", "cs_k", "ss_k")}
    def S(name, shape, kindt=F32):
        return nc.dram_tensor(name, shape, kindt, kind="Internal").ap()
    xnT_f = S("s_xnT_f", [D, NTOK])
    xnT_b = S("s_xnT_b", [D, NTOK], BF16)
    s0 = S("s_s0", [NTOK, D])
    xfa0 = S("s_xfa0", [D, NTOK])
    xfb0 = S("s_xfb0", [D, NTOK])
    xfa1 = S("s_xfa1", [D, NTOK])
    g0 = S("s_g0", [NTOK, D])
    x1 = S("s_x1", [NTOK, D])
    xn1T_f = S("s_xn1T_f", [D, NTOK])
    xn1T_b = S("s_xn1T_b", [D, NTOK], BF16)
    xn1N_b = S("s_xn1N_b", [NTOK, D], BF16)
    s1 = S("s_s1", [NTOK, D])
    o1 = S("s_o1", [NTOK, D])
    y = dt("y", [NTOK, D], F32, kind="ExternalOutput")

    with ExitStack() as octx:
        ev = EV(nc, octx, ALLSEMS)
        block = octx.enter_context(nc.Block())

        def phase(emitfn, io, pfx, **kw):
            with ExitStack() as pctx:
                emitfn(nc, ev, pctx, io, pfx=pfx, **kw)
                ev.finalize(block)
            ev.barrier()

        phase(emit_ln0, dict(x=x, ident=ident, xnT_f=xnT_f, xnT_b=xnT_b), "p1_")
        phase(emit_l0attn, dict(xnT_b=xnT_b, w_qkv=l0_w_qkv, w_out=l0_w_out,
                                b_out=l0_b_out, gamma=l0_gamma, beta=l0_beta,
                                onesr=onesr, s_out=s0), "p2_")
        ffn_srcs = [xnT_f, xfa0, xfb0, xfa1]
        for l in range(2):
            phase(emit_ffn_attn,
                  dict(xfT=ffn_srcs[2 * l], wqkv=f_wqkv[l], qs_i=f_qs[l],
                       ks_i=f_ks[l], wo_i=f_wo[l], onesr=onesr, ones2=ones2,
                       ones2T=ones2T, maskM=maskM, ropes=ropes,
                       xfT_o=ffn_srcs[2 * l + 1]), f"p3{l}_")
            phase(emit_ffn_ffn,
                  dict(xfT=ffn_srcs[2 * l + 1], lg_i=f_lg[l], lb_i=f_lb[l],
                       w1_i=f_w1[l], b1_i=f_b1[l], w2_i=f_w2[l], b2_i=f_b2[l],
                       onesr=onesr, ident=ident,
                       outp=(g0 if l == 1 else xfb0)), f"p4{l}_",
                  out_normal=(l == 1))
        phase(emit_comb1, dict(x_i=x, s_i=s0, g_i=g0, ident=ident, x1_o=x1,
                               xnTf_o=xn1T_f, xnTb_o=xn1T_b, xnN_o=xn1N_b), "p5_")
        phase(emit_l1attn, dict(xnT_b=xn1T_b, xnN_b=xn1N_b, QK_i=QK, VO_i=VO,
                                onesr=onesr, s1_o=s1), "p6_")
        phase(emit_l1ffn, dict(xnT_f=xn1T_f,
                               cw={"cn_w1": cn_w[1], "cn_w2": cn_w[2],
                                   "cn_w3": cn_w[3], "cn_w4": cn_w[4]},
                               cb={"cn_b1": cn_b[1], "cn_b2": cn_b[2],
                                   "cn_b3": cn_b[3], "cn_b4": cn_b[4]},
                               gw={"gc_w1": gc_w[1], "gc_w2": gc_w[2],
                                   "gc_w3": gc_w[3]},
                               gb={"gc_b1": gc_b[1], "gc_b2": gc_b[2],
                                   "gc_b3": gc_b[3]},
                               ar_i=ar, onesr=onesr, ident=ident, o1_o=o1), "p7_")
        phase(emit_comb2, dict(x1_i=x1, s1_i=s1, o1_i=o1, y_o=y), "p8_")
        fin = ev.wait_all("st", "stg")
        ev.op("sync", lambda e: e.nop(), waits=fin)
        ev.op("gpsimd", lambda e: e.nop(), waits=fin)
        ev.finalize(block)
    return nc


_programs = {}


def _prog(name, fn):
    if name not in _programs:
        _programs[name] = fn()
    return _programs[name]


def kernel(**inputs):
    f32 = lambda k: np.ascontiguousarray(np.asarray(inputs[k], dtype=np.float32))
    x = f32("L0_attn_x")
    C = CONSTS
    n = N_CORES
    base = dict(
        l0_w_qkv=f32("L0_attn_w_qkv"), l0_w_out=f32("L0_attn_w_out"),
        l0_b_out=f32("L0_attn_b_out"), l0_gamma=f32("L0_attn_ln_gamma"),
        l0_beta=f32("L0_attn_ln_beta"),
        QK=f32("L1_attn_QK"), VO=f32("L1_attn_VO"),
        cn_w1=f32("L1_ffn_cn_w1"), cn_w2=f32("L1_ffn_cn_w2"),
        cn_w3=f32("L1_ffn_cn_w3"), cn_w4=f32("L1_ffn_cn_w4"),
        cn_b1=f32("L1_ffn_cn_b1"), cn_b2=f32("L1_ffn_cn_b2"),
        cn_b3=f32("L1_ffn_cn_b3"), cn_b4=f32("L1_ffn_cn_b4"),
        gc_w1=f32("L1_ffn_gc_w1"), gc_w2=f32("L1_ffn_gc_w2"),
        gc_w3=f32("L1_ffn_gc_w3"),
        gc_b1=f32("L1_ffn_gc_b1"), gc_b2=f32("L1_ffn_gc_b2"),
        gc_b3=f32("L1_ffn_gc_b3"),
        ar=f32("L1_ffn_adaptive_range"),
        c_ident=C["ident"], c_ones_row=C["ones_row"], c_ones2=C["ones2"],
        c_ones2T=C["ones2T"], c_maskM=C["maskM"],
        c_cs_q=C["cs_q"], c_ss_q=C["ss_q"], c_cs_k=C["cs_k"], c_ss_k=C["ss_k"],
    )
    for l in range(2):
        base[f"f_wqkv{l}"] = f32("L0_ffn_wqkv")[l]
        base[f"f_qs{l}"] = f32("L0_ffn_q_scale")[l]
        base[f"f_ks{l}"] = f32("L0_ffn_k_scale")[l]
        base[f"f_wo{l}"] = f32("L0_ffn_wo")[l]
        base[f"f_lg{l}"] = f32("L0_ffn_ln_g")[l]
        base[f"f_lb{l}"] = f32("L0_ffn_ln_b")[l]
        base[f"f_w1{l}"] = f32("L0_ffn_w1")[l]
        base[f"f_b1{l}"] = f32("L0_ffn_b1")[l]
        base[f"f_w2{l}"] = f32("L0_ffn_w2")[l]
        base[f"f_b2{l}"] = f32("L0_ffn_b2")[l]
    in_maps = [dict(base, x=x[i]) for i in range(n)]
    nc = _prog("fused", build_fused)
    res = run_bass_kernel_spmd(nc, in_maps, core_ids=list(range(n)))
    y = np.stack([res.results[i]["y"] for i in range(n)], axis=0)
    return y.astype(np.float32)
